# version 16
# baseline (speedup 1.0000x reference)
import numpy as np
import concourse.bass as bass
import concourse.mybir as mybir
from concourse.bass_utils import run_bass_kernel_spmd

F32 = mybir.dt.float32
BF16 = mybir.dt.bfloat16
AF = mybir.ActivationFunctionType
OP = mybir.AluOpType
AX = mybir.AxisListType

NCORES = 8
SEQS = 2
T = 2048
D = 1024
NF = 8
ICW = 512
NIC = T // ICW
NJT = T // 128
HA = 8
HB = 16
NE = 8
SCL_A = 1.0 / 128.0
SCL_B = 0.125
EPS = 1e-5
VST = HB * 65


class Prog:
    ENGS = ("tensor", "vector", "scalar", "gpsimd", "sync")

    def __init__(self, nc):
        self.nc = nc
        self.q = {e: [] for e in self.ENGS}
        self.sem_handles = {}
        self.cnt = {}
        self.fences = {}

    def sem(self, name):
        if name not in self.sem_handles:
            self.sem_handles[name] = self.nc.alloc_semaphore(name)
            self.cnt[name] = 0
        return self.sem_handles[name]

    def emit(self, eng, fn, wait=None, inc=None, inc_n=1):
        sems = self.sem_handles
        if inc is not None:
            self.sem(inc)
            self.cnt[inc] += inc_n
        wait = [(s, v) for (s, v) in (wait or []) if v > 0]
        for s, _ in wait:
            self.sem(s)

        def run(engine, fn=fn, wait=wait, inc=inc, inc_n=inc_n):
            for s, v in wait:
                engine.wait_ge(sems[s], v)
            ins = fn(engine)
            if inc is not None and ins is not None:
                ins.then_inc(sems[inc], inc_n)

        self.q[eng].append(run)

    def val(self, name):
        self.sem(name)
        return self.cnt[name]

    def dma(self, eng, out, in_, sem, wait=None):
        self.emit(eng, lambda e, o=out, i=in_: e.dma_start(out=o, in_=i),
                  wait=wait, inc=sem, inc_n=16)
        return self.cnt[sem]

    def release(self, eng, sem):
        self.emit(eng, lambda e: e.nop(), inc=sem)
        return self.cnt[sem]

    def barrier(self):
        marks = []
        for e, sname in (("vector", "bv"), ("scalar", "bs"),
                         ("gpsimd", "bg"), ("sync", "by")):
            fence = self.fences.get(e)
            if fence is not None:
                self.emit(e, fence, inc=sname)
            else:
                self.emit(e, lambda eng: eng.nop(), inc=sname)
            marks.append((sname, self.cnt[sname]))
        self.emit("tensor", self.fences["tensor"], wait=list(marks), inc="bt")
        marks.append(("bt", self.cnt["bt"]))
        for e in ("tensor", "vector", "scalar"):
            self.emit(e, lambda eng: None, wait=marks)


def build_program(debug=False, direct_we=False, nocb=False, skipadds=False, widn=False, xdirect=False):
    nc = bass.Bass(target_bir_lowering=False, trn_type="TRN2")

    x_in = nc.dram_tensor("x", [SEQS, D, T], F32, kind="ExternalInput").ap()
    wq_in = nc.dram_tensor("wq", [D, D], F32, kind="ExternalInput").ap()
    wk_in = nc.dram_tensor("wk", [D, D], F32, kind="ExternalInput").ap()
    wv_in = nc.dram_tensor("wv", [D, D], F32, kind="ExternalInput").ap()
    wo_in = nc.dram_tensor("wo", [HA, 128, D], F32, kind="ExternalInput").ap()
    wqkv_in = nc.dram_tensor("wqkv", [D, 3 * D], F32, kind="ExternalInput").ap()
    bqkv_in = nc.dram_tensor("bqkv", [1, 3 * D], F32, kind="ExternalInput").ap()
    wgate_in = nc.dram_tensor("wgate", [D, NE], F32, kind="ExternalInput").ap()
    bgate_in = nc.dram_tensor("bgate", [1, NE], F32, kind="ExternalInput").ap()
    wexp_in = nc.dram_tensor("wexp", [NE, D, D], F32, kind="ExternalInput").ap()
    bexp_in = nc.dram_tensor("bexp", [NE, D], F32, kind="ExternalInput").ap()
    ctab_in = nc.dram_tensor("ctab", [128, T], F32, kind="ExternalInput").ap()
    stab_in = nc.dram_tensor("stab", [128, T], F32, kind="ExternalInput").ap()
    perm_in = nc.dram_tensor("perm", [128, 128], F32, kind="ExternalInput").ap()
    stair_in = nc.dram_tensor("stair", [128, 896], F32, kind="ExternalInput").ap()
    idn_in = nc.dram_tensor("idn", [128, 128], F32, kind="ExternalInput").ap()
    sel_in = nc.dram_tensor("sel", [8, 1024], F32, kind="ExternalInput").ap()
    out_hbm = nc.dram_tensor("out", [SEQS, D, T], F32, kind="ExternalOutput").ap()

    x1_scr = nc.dram_tensor("x1scr", [SEQS, D, T], BF16).ap()
    dbg1 = (nc.dram_tensor("dbg1", [SEQS, D, T], F32, kind="ExternalOutput").ap()
            if debug else None)
    dbg2 = (nc.dram_tensor("dbg2", [SEQS, D, T], F32, kind="ExternalOutput").ap()
            if debug else None)
    dbg3 = (nc.dram_tensor("dbg3", [SEQS, 8, T], F32, kind="ExternalOutput").ap()
            if debug else None)
    dbg4 = (nc.dram_tensor("dbg4", [SEQS, D, T], F32, kind="ExternalOutput").ap()
            if debug else None)
    dbg5 = (nc.dram_tensor("dbg5", [NIC, 128, 4096], F32, kind="ExternalOutput").ap()
            if debug else None)
    we_scr = nc.dram_tensor("wescr", [NE, D, D], BF16).ap()

    c_ctab = nc.alloc_sbuf_tensor("c_ctab", [128, T], BF16).ap()
    c_stab = nc.alloc_sbuf_tensor("c_stab", [128, T], BF16).ap()
    c_perm = nc.alloc_sbuf_tensor("c_perm", [128, 128], BF16).ap()
    c_stair = nc.alloc_sbuf_tensor("c_stair", [128, 896], BF16).ap()
    c_idn = nc.alloc_sbuf_tensor("c_idn", [128, 128], BF16).ap()
    c_ones = nc.alloc_sbuf_tensor("c_ones", [128, 512], BF16).ap()
    c_wgate = nc.alloc_sbuf_tensor("c_wgate", [128, 64], BF16).ap()
    c_bgate = nc.alloc_sbuf_tensor("c_bgate", [1, 8], BF16).ap()
    c_brow = nc.alloc_sbuf_tensor("c_brow", [1, 3 * D], BF16).ap()
    c_bexp = nc.alloc_sbuf_tensor("c_bexp", [8, D], BF16).ap()
    c_sel = nc.alloc_sbuf_tensor("c_sel", [8, 1024], BF16).ap()
    combT = nc.alloc_sbuf_tensor("combT", [8, T], BF16).ap()
    mf = nc.alloc_sbuf_tensor("miscf", [128, 3264], F32).ap()
    a_bf = nc.alloc_sbuf_tensor("a_bf", [1, 512], BF16).ap()
    b_bf = nc.alloc_sbuf_tensor("b_bf", [1, 512], BF16).ap()
    bcB = nc.alloc_sbuf_tensor("bcB", [128, 512], BF16).ap()
    rsc = nc.alloc_sbuf_tensor("ropescr2", [128, 1024], BF16).ap()
    lrow = nc.alloc_sbuf_tensor("lrow", [128, 512], BF16).ap()
    gcomb_bf = nc.alloc_sbuf_tensor("gcombbf", [128, 32], BF16).ap()
    bnc = [nc.alloc_sbuf_tensor(f"bnc{i}", [128, 512], BF16).ap() for i in range(2)]

    R1 = nc.alloc_sbuf_tensor("R1", [128, 16384], BF16).ap()
    R2 = nc.alloc_sbuf_tensor("R2", [128, 16384], BF16).ap()
    R3 = nc.alloc_sbuf_tensor("R3", [128, 16640], BF16).ap()
    POOL = nc.alloc_sbuf_tensor("POOL", [128, 32768], BF16).ap()

    pA = nc.alloc_psum_tensor("pA", [128, 2048], F32).ap()
    pB = nc.alloc_psum_tensor("pB", [128, 2048], F32).ap()

    K_, Q_, V_ = R1, R2, R3
    X1F = R1
    T1OUT = R2[:, 0:8192].bitcast(F32)
    Y2B = R2[:, 8192:12288]
    YSQB = R2[:, 12288:16384]
    CBSB = R3[:, 0:4096]
    WER = [R3[:, 4096 + 1024 * i: 4096 + 1024 * (i + 1)] for i in range(4)]
    XER = [R3[:, 8192 + 512 * i: 8192 + 512 * (i + 1)] for i in range(4)]

    wq_t = POOL[:, 0:8192]
    wk_t = POOL[:, 8192:16384]
    wv_t = POOL[:, 16384:24576]
    x0c = [POOL[:, 24576:28672], POOL[:, 28672:32768]]
    wo_t = POOL[:, 0:8192]
    ptil = POOL[:, 8192:10240]
    oA = POOL[:, 10240:14336]
    ysqA = POOL[:, 10240:14336]
    ybuf = POOL[:, 14336:18432]
    x1buf = [POOL[:, 18432:22528], POOL[:, 22528:26624]]
    x0c2 = POOL[:, 26624:30720]
    wqkv_t = POOL[:, 0:12288]
    x1c = [POOL[:, 12288:16384], POOL[:, 16384:20480]]
    ptilB = POOL[:, 0:4096]
    ATB = POOL[:, 4096:20480]
    OEV = POOL[:, 20480:21504]
    a_sb = POOL[:, 0:1024].bitcast(F32)
    b_sb = POOL[:, 1024:2048].bitcast(F32)

    linv = mf[:, 0:512]
    mean_r = mf[0:1, 512:1024]
    es2_r = mf[0:1, 1024:1536]
    msq_r = mf[0:1, 1536:2048]
    var_r = mf[0:1, 2048:2560]
    lnv_r = mf[0:1, 1536:2048]
    rstd_r = mf[0:1, 2560:3072]
    bneg_r = mf[0:1, 1024:1536]
    linvo_r = mf[0:1, 2048:2560]
    linve_r = mf[64:65, 2048:2560]
    gexp = mf[:, 3072:3104]
    gsum = mf[:, 3104:3108]
    grs = mf[:, 3108:3112]
    gm1 = mf[:, 3112:3116]
    gm2 = mf[:, 3116:3120]
    gsm = mf[:, 3120:3152]
    gmk = mf[:, 3152:3184]
    gzap = mf[:, 3184:3216]
    negm_g = mf[:, 3216:3220]
    eps_r = mf[0:1, 3220:3221]

    P = Prog(nc)
    fence_f = mf[0:1, 3221:3222]
    fence_g = mf[0:1, 3222:3223]
    P.fences = {
        "tensor": lambda e: e.matmul(pA[0:1, 0:1], c_ones[0:1, 0:1], c_ones[0:1, 0:1],
                                     start=True, stop=True),
        "vector": lambda e: e.tensor_copy(fence_f, fence_g),
        "scalar": lambda e: e.copy(fence_g, fence_f),
    }

    def fm(t, chunk, ic, w=ICW):
        return t[:, chunk * T + ic * w: chunk * T + ic * w + w]

    P.dma("gpsimd", c_ctab, ctab_in, "swC")
    P.dma("gpsimd", c_stab, stab_in, "swC")
    P.dma("gpsimd", c_perm, perm_in, "swC")
    P.dma("gpsimd", c_stair, stair_in, "swC")
    P.dma("gpsimd", c_idn, idn_in, "swC")
    P.dma("gpsimd", c_wgate.rearrange("p (c e) -> p c e", c=8),
          wgate_in.rearrange("(c p) e -> p c e", p=128), "swC")
    P.dma("gpsimd", c_bgate, bgate_in, "swC")
    P.dma("gpsimd", c_brow, bqkv_in, "swC")
    P.dma("gpsimd", c_bexp, bexp_in, "swC")
    P.dma("gpsimd", c_sel, sel_in, "swC")
    swC_total = P.val("swC")
    P.emit("vector", lambda e: e.memset(c_ones, 1.0))
    P.emit("vector", lambda e: e.memset(eps_r, float(EPS)), inc="svones")
    sv_ones = P.val("svones")

    wescr_idx = [0]

    def emit_wescr_chunk():
        idx = wescr_idx[0]
        if idx >= NE * NF * 2:
            return
        wescr_idx[0] += 1
        ed, hf = divmod(idx, 2)
        e_, d_ = divmod(ed, NF)
        par = idx % 2
        w = [("sbs%d" % par, 16 * (idx // 2))] if idx >= 2 else None
        P.dma("gpsimd", bnc[par],
              wexp_in[e_, d_ * 128:(d_ + 1) * 128, hf * 512:(hf + 1) * 512],
              "sbl%d" % par, wait=w)
        P.dma("gpsimd", we_scr[e_, d_ * 128:(d_ + 1) * 128, hf * 512:(hf + 1) * 512],
              bnc[par], "sbs%d" % par, wait=[("sbl%d" % par, 16 * (idx // 2 + 1))])

    def pump_wescr(n):
        for _ in range(n):
            emit_wescr_chunk()

    x0gate = {}
    x1gate = {}
    pa_gate = {}
    pb_gate = {}
    for s in range(SEQS):
        swa_w = None
        if s > 0:
            swa_w = [("stB3m", P.val("stB3m")), ("ssB2f", P.val("ssB2f")),
                     ("stB1g", P.val("stB1g")), ("ssB2", P.val("ssB2")),
                     ("svB3o", P.val("svB3o"))]
        for hh in range(2):
            P.dma("gpsimd", wq_t.rearrange("p (c n) -> p c n", c=8)[:, hh * 4:(hh + 1) * 4],
                  wq_in.rearrange("(c p) n -> p c n", p=128)[:, hh * 4:(hh + 1) * 4],
                  "swAh%d" % hh, wait=swa_w if hh == 0 else None)
            P.dma("gpsimd", wk_t.rearrange("p (c n) -> p c n", c=8)[:, hh * 4:(hh + 1) * 4],
                  wk_in.rearrange("(c p) n -> p c n", p=128)[:, hh * 4:(hh + 1) * 4],
                  "swAh%d" % hh)
            P.dma("gpsimd", wv_t.rearrange("p (c n) -> p c n", c=8)[:, hh * 4:(hh + 1) * 4],
                  wv_in.rearrange("(c p) n -> p c n", p=128)[:, hh * 4:(hh + 1) * 4],
                  "swAh%d" % hh)
        swA_h = [P.val("swAh0"), P.val("swAh1")]

        pa_gate = dict(pa_gate) if s > 0 else {}
        pb_gate = dict(pb_gate) if s > 0 else {}
        qk_idx = [0]
        for ic in range(NIC):
            par = ic % 2
            w = [("stA1v", x0gate.get(par, 0))]
            if s > 0 and ic < 2:
                w += [("sspill1", P.val("sspill1")), ("stA2y", P.val("stA2y"))]
            P.dma("gpsimd", x0c[par].rearrange("p (f t) -> p f t", f=NF),
                  x_in[s].rearrange("(f p) t -> p f t", p=128)[:, :, ic * ICW:(ic + 1) * ICW],
                  "sx%d" % par, wait=w)
            sxv = P.val("sx%d" % par)
            pump_wescr(4)

            base = [("sx%d" % par, sxv), ("swC", swC_total),
                    ("svones", sv_ones)]
            for kind in ("q", "k"):
                wt = wq_t if kind == "q" else wk_t
                dst = Q_ if kind == "q" else K_
                for dt in range(NF):
                    pp = qk_idx[0] % 4
                    qk_idx[0] += 1
                    ps = pA[:, pp * 512:(pp + 1) * 512]
                    wv_ = [pa_gate[pp]] if pp in pa_gate else []
                    for kc in range(NF):
                        kw = (base + wv_) if kc == 0 else None
                        if ic == 0 and dt < 2 and kc in (0, 4):
                            kw = (kw or []) + [("swAh%d" % (kc // 4), swA_h[kc // 4])]
                        P.emit("tensor",
                               lambda e, ps=ps, wt=wt, kc=kc, dt=dt, par=par:
                               e.matmul(ps, wt[:, kc * 1024 + dt * 128: kc * 1024 + dt * 128 + 128],
                                        x0c[par][:, kc * 512:(kc + 1) * 512],
                                        start=(kc == 0), stop=(kc == NF - 1)),
                               wait=kw,
                               inc="stA1" if kc == NF - 1 else None)
                    stv = P.val("stA1")
                    P.emit("scalar",
                           lambda e, ps=ps, dst=dst, dt=dt, ic=ic: e.copy(fm(dst, dt, ic), ps),
                           wait=[("stA1", stv)], inc="svA1")
                    pa_gate[pp] = ("svA1", P.val("svA1"))
            for jt4 in range(4):
                jt = ic * 4 + jt4
                pp = jt4 % 2
                ps = pB[:, pp * 1024:(pp + 1) * 1024]
                wv_ = [g for b in (2 * pp, 2 * pp + 1) for g in ([pb_gate[b]] if b in pb_gate else [])]
                if ic == 0 and jt4 == 0:
                    wv_ = wv_ + [("swAh0", swA_h[0]), ("swAh1", swA_h[1])]
                for kc in range(NF):
                    for half in range(2):
                        P.emit("tensor",
                               lambda e, ps=ps, kc=kc, jt4=jt4, par=par, half=half:
                               e.matmul(ps[:, half * 512:(half + 1) * 512],
                                        x0c[par][:, kc * 512 + jt4 * 128: kc * 512 + jt4 * 128 + 128],
                                        wv_t[:, kc * 1024 + half * 512: kc * 1024 + (half + 1) * 512],
                                        start=(kc == 0), stop=(kc == NF - 1)),
                               wait=(base + wv_) if (kc == 0 and half == 0) else None,
                               inc="stA1v" if (kc == NF - 1 and half == 1) else None)
                stv = P.val("stA1v")
                P.emit("scalar",
                       lambda e, ps=ps, jt=jt: e.copy(V_[:, jt * 1024:(jt + 1) * 1024], ps),
                       wait=[("stA1v", stv)], inc="svA1v")
                pb_gate[2 * pp] = ("svA1v", P.val("svA1v"))
                pb_gate[2 * pp + 1] = ("svA1v", P.val("svA1v"))
            x0gate[par] = P.val("stA1v")

            svq = P.val("svA1")
            first = True
            rslot = [0]
            for X in (Q_, K_):
                for pr in range(4):
                    h0 = 2 * pr
                    sl0, sl1 = rslot[0] % 4, (rslot[0] + 1) % 4
                    rslot[0] += 2
                    rvals = {}
                    for (h, slot) in ((h0, sl0), (h0 + 1, sl1)):
                        rps = pB[:, slot * 512:(slot + 1) * 512]
                        w = [pb_gate[slot]] if slot in pb_gate else []
                        if first:
                            w += [("svA1", svq), ("swC", swC_total)]
                        P.emit("tensor",
                               lambda e, rps=rps, X=X, h=h, ic=ic:
                               e.matmul(rps, c_perm,
                                        X[:, h * T + ic * ICW: h * T + ic * ICW + ICW],
                                        start=True, stop=True),
                               wait=w, inc="strp")
                        first = False
                        rvals[h] = P.val("strp")
                    for (h, slot, ri) in ((h0, sl0, 0), (h0 + 1, sl1, 1)):
                        rps = pB[:, slot * 512:(slot + 1) * 512]
                        P.emit("vector",
                               lambda e, X=X, h=h, ic=ic:
                               e.tensor_tensor(
                                   X[:, h * T + ic * ICW: h * T + ic * ICW + ICW],
                                   X[:, h * T + ic * ICW: h * T + ic * ICW + ICW],
                                   c_ctab[:, ic * ICW:(ic + 1) * ICW], op=OP.mult),
                               wait=[("strp", rvals[h])])
                        P.emit("vector",
                               lambda e, rps=rps, ri=ri, ic=ic:
                               e.tensor_tensor(
                                   rsc[:, ri * 512:(ri + 1) * 512], rps,
                                   c_stab[:, ic * ICW:(ic + 1) * ICW], op=OP.mult),
                               inc="svrp")
                        pb_gate[slot] = ("svrp", P.val("svrp"))
                    P.emit("vector", lambda e: e.drain())
                    for (h, ri) in ((h0, 0), (h0 + 1, 1)):
                        P.emit("vector",
                               lambda e, X=X, h=h, ic=ic, ri=ri:
                               e.tensor_tensor(
                                   X[:, h * T + ic * ICW: h * T + ic * ICW + ICW],
                                   X[:, h * T + ic * ICW: h * T + ic * ICW + ICW],
                                   rsc[:, ri * 512:(ri + 1) * 512], op=OP.add))
        P.barrier()

        P.dma("gpsimd", wo_t.rearrange("p (h m) -> p h m", h=HA),
              wo_in.rearrange("h p m -> p h m"), "swO",
              wait=[("stA1", P.val("stA1"))])
        swO_total = P.val("swO")

        A2expv = {}
        pend_ln = [None]
        x0gate2 = [0 if s == 0 else x0gate2[0]]
        for ic in range(NIC):
            njt = 4 * (ic + 1)
            w = [("stA2y", x0gate2[0])] if x0gate2[0] > 0 else []
            if ic == 0:
                w = list(w) + [("stA1v", P.val("stA1v"))]
            P.dma("gpsimd", x0c2.rearrange("p (f t) -> p f t", f=NF),
                  x_in[s].rearrange("(f p) t -> p f t", p=128)[:, :, ic * ICW:(ic + 1) * ICW],
                  "sx2", wait=w)
            sx2v = P.val("sx2")
            pump_wescr(4)

            A2maskv = {}

            for h in range(HA):
                if h == 1 and pend_ln[0] is not None:
                    pend_ln[0]()
                    pend_ln[0] = None
                ngrp = (njt + 1) // 2

                def a2_s(jg, h=h, ic=ic, njt=njt):
                    gpar = jg % 2
                    ps_s = pA[:, gpar * 1024:(gpar + 1) * 1024]
                    nt = min(2, njt - 2 * jg)
                    wargs = [("ssA2", A2expv.get(gpar, 0))]
                    for t_ in range(nt):
                        jt = 2 * jg + t_
                        P.emit("tensor",
                               lambda e, ps_s=ps_s, t_=t_, jt=jt, h=h, ic=ic:
                               e.matmul(ps_s[:, t_ * 512:(t_ + 1) * 512],
                                        K_[:, h * T + jt * 128: h * T + jt * 128 + 128],
                                        Q_[:, h * T + ic * ICW: h * T + ic * ICW + ICW],
                                        start=True, stop=True),
                               wait=wargs if t_ == 0 else None,
                               inc="stA2s" if t_ == nt - 1 else None)
                    stv = P.val("stA2s")
                    P.emit("scalar",
                           lambda e, gpar=gpar, ps_s=ps_s, nt=nt:
                           e.activation(ptil[:, gpar * 1024: gpar * 1024 + nt * 512],
                                        ps_s[:, 0:nt * 512], AF.Exp, scale=SCL_A),
                           wait=[("stA2s", stv)], inc="ssA2")
                    A2expv[gpar] = P.val("ssA2")
                    masked = (2 * jg) >= 4 * ic
                    if masked:
                        ssv = P.val("ssA2")
                        for t_ in range(nt):
                            jt = 2 * jg + t_
                            toff = (jt - 4 * ic) * 128
                            P.emit("vector",
                                   lambda e, gpar=gpar, t_=t_, toff=toff:
                                   e.tensor_tensor(
                                       ptil[:, gpar * 1024 + t_ * 512: gpar * 1024 + (t_ + 1) * 512],
                                       ptil[:, gpar * 1024 + t_ * 512: gpar * 1024 + (t_ + 1) * 512],
                                       c_stair[:, 384 - toff: 384 - toff + 512],
                                       op=OP.mult),
                                   wait=[("ssA2", ssv)] if t_ == 0 else None,
                                   inc="svA2m" if t_ == nt - 1 else None)
                        A2maskv[gpar] = P.val("svA2m")
                    else:
                        A2maskv[gpar] = None

                if s == 0:
                    pump_wescr(3)
                a2_s(0)
                for jg in range(ngrp):
                    gpar = jg % 2
                    nt = min(2, njt - 2 * jg)
                    if jg + 1 < ngrp:
                        a2_s(jg + 1)
                    if A2maskv[gpar] is not None:
                        pwait = [("svA2m", A2maskv[gpar])]
                    else:
                        pwait = [("ssA2", A2expv[gpar])]
                    if jg == 0:
                        pwait = pwait + [("svA2o", P.val("svA2o"))]
                    for t_ in range(nt):
                        jt = 2 * jg + t_
                        P.emit("tensor",
                               lambda e, jt=jt, gpar=gpar, t_=t_, h=h:
                               e.matmul(pB[:, 0:512],
                                        V_[:, jt * 1024 + h * 128: jt * 1024 + h * 128 + 128],
                                        ptil[:, gpar * 1024 + t_ * 512: gpar * 1024 + (t_ + 1) * 512],
                                        start=(jt == 0), stop=(jt == njt - 1)),
                               wait=pwait if t_ == 0 else None)
                        P.emit("tensor",
                               lambda e, jt=jt, gpar=gpar, t_=t_:
                               e.matmul(pB[:, 512:1024],
                                        c_ones[:, 0:128],
                                        ptil[:, gpar * 1024 + t_ * 512: gpar * 1024 + (t_ + 1) * 512],
                                        start=(jt == 0), stop=(jt == njt - 1)),
                               inc="stA2o" if jt == njt - 1 else None)
                stv = P.val("stA2o")
                P.emit("scalar", lambda e: e.activation(linv, pB[:, 512:1024], AF.Ln),
                       wait=[("stA2o", stv)])
                P.emit("scalar", lambda e: e.drain())
                P.emit("scalar", lambda e: e.activation(linv, linv, AF.Exp, scale=-1.0),
                       inc="ssA2r")
                ssrv = P.val("ssA2r")
                ovwait = [("ssA2r", ssrv)]
                if h == 0 and ic > 0:
                    ovwait.append(("stA2st", P.val("stA2st")))
                P.emit("vector",
                       lambda e, h=h: e.tensor_tensor(
                           oA[:, h * 512:(h + 1) * 512], pB[:, 0:512], linv, op=OP.mult),
                       wait=ovwait, inc="svA2o")
            sv_heads = P.val("svA2o")
            evy = {}
            for mc in range(NF):
                ypar = mc % 2
                ps_y = pB[:, 1024 + ypar * 512: 1024 + (ypar + 1) * 512]
                wargs = [("swO", swO_total), ("svA2o", sv_heads), ("sx2", sx2v),
                         ("svA2y", evy.get(ypar, 0))]
                if mc == 0:
                    wargs = wargs + [("ssL", P.val("ssL")), ("svA2x1", P.val("svA2x1"))]
                for h in range(HA):
                    P.emit("tensor",
                           lambda e, ps_y=ps_y, h=h, mc=mc:
                           e.matmul(ps_y, wo_t[:, h * 1024 + mc * 128: h * 1024 + mc * 128 + 128],
                                    oA[:, h * 512:(h + 1) * 512],
                                    start=(h == 0), stop=False),
                           wait=wargs if h == 0 else None)
                P.emit("tensor",
                       lambda e, ps_y=ps_y, mc=mc:
                       e.matmul(ps_y, c_idn, x0c2[:, mc * 512:(mc + 1) * 512],
                                start=False, stop=True),
                       inc="stA2y")
                stv = P.val("stA2y")
                P.emit("vector",
                       lambda e, ps_y=ps_y, mc=mc: e.tensor_copy(ybuf[:, mc * 512:(mc + 1) * 512], ps_y),
                       wait=[("stA2y", stv)], inc="svA2y")
                evy[ypar] = P.val("svA2y")
            sv_y = P.val("svA2y")
            x0gate2[0] = P.val("stA2y")
            for mc in range(NF):
                P.emit("vector",
                       lambda e, mc=mc: e.tensor_tensor(
                           ysqA[:, mc * 512:(mc + 1) * 512],
                           ybuf[:, mc * 512:(mc + 1) * 512],
                           ybuf[:, mc * 512:(mc + 1) * 512], op=OP.mult),
                       wait=[("svA2y", sv_y)] if mc == 0 else None,
                       inc="svA2q" if mc == NF - 1 else None)
            sv_q = P.val("svA2q")
            for mc in range(NF):
                P.emit("tensor",
                       lambda e, mc=mc: e.matmul(pB[0:1, 1024:1536], c_ones[:, 0:1],
                                                 ybuf[:, mc * 512:(mc + 1) * 512],
                                                 start=(mc == 0), stop=(mc == NF - 1)),
                       wait=[("svA2y", sv_y)] if mc == 0 else None)
            for mc in range(NF):
                P.emit("tensor",
                       lambda e, mc=mc: e.matmul(pB[0:1, 1536:2048], c_ones[:, 0:1],
                                                 ysqA[:, mc * 512:(mc + 1) * 512],
                                                 start=(mc == 0), stop=(mc == NF - 1)),
                       wait=[("svA2q", sv_q)] if mc == 0 else None,
                       inc="stA2st" if mc == NF - 1 else None)

            def a2_ln(ic=ic, stv=P.val("stA2st"), sv_y=sv_y):
                P.emit("scalar", lambda e: e.mul(mean_r, pB[0:1, 1024:1536], 1.0 / D),
                       wait=[("stA2st", stv)])
                P.emit("scalar", lambda e: e.mul(es2_r, pB[0:1, 1536:2048], 1.0 / D), inc="ssL")
                ssv = P.val("ssL")
                P.emit("vector", lambda e: e.tensor_tensor(msq_r, mean_r, mean_r, op=OP.mult),
                       wait=[("ssL", ssv)])
                P.emit("vector", lambda e: e.drain())
                P.emit("vector", lambda e: e.tensor_tensor(var_r, es2_r, msq_r, op=OP.subtract),
                       inc="svL")
                svv = P.val("svL")
                P.emit("scalar", lambda e: e.activation(lnv_r, var_r, AF.Ln, bias=eps_r),
                       wait=[("svL", svv)])
                P.emit("scalar", lambda e: e.drain())
                P.emit("scalar", lambda e: e.activation(rstd_r, lnv_r, AF.Exp, scale=-0.5),
                       inc="ssL2")
                ssv = P.val("ssL2")
                P.emit("vector", lambda e: e.tensor_copy(a_bf, rstd_r), wait=[("ssL2", ssv)])
                P.emit("vector", lambda e: e.scalar_tensor_tensor(
                    bneg_r, mean_r, -1.0, rstd_r, op0=OP.mult, op1=OP.mult))
                P.emit("vector", lambda e: e.drain())
                P.emit("vector", lambda e: e.tensor_copy(b_bf, bneg_r), inc="svL2")
                svv = P.val("svL2")
                P.emit("tensor", lambda e: e.matmul(pB[:, 1024:1536], c_ones[0:1, 0:128], a_bf,
                                                    start=True, stop=True),
                       wait=[("svL2", svv), ("svA2y", sv_y)])
                P.emit("tensor", lambda e: e.matmul(pB[:, 1536:2048], c_ones[0:1, 0:128], b_bf,
                                                    start=True, stop=True), inc="stA2b")
                stv2 = P.val("stA2b")
                xpar = ic % 2
                wspill = [("sspill%d" % xpar, P.val("sspill%d" % xpar))]
                for mc in range(NF):
                    P.emit("vector",
                           lambda e, mc=mc, xpar=xpar: e.tensor_tensor(
                               x1buf[xpar][:, mc * 512:(mc + 1) * 512],
                               ybuf[:, mc * 512:(mc + 1) * 512], pB[:, 1024:1536], op=OP.mult),
                           wait=([("stA2b", stv2)] + wspill) if mc == 0 else None)
                for mc in range(NF):
                    P.emit("vector",
                           lambda e, mc=mc, xpar=xpar: e.tensor_tensor(
                               x1buf[xpar][:, mc * 512:(mc + 1) * 512],
                               x1buf[xpar][:, mc * 512:(mc + 1) * 512], pB[:, 1536:2048], op=OP.add),
                           inc="svA2x1" if mc == NF - 1 else None)
                svv = P.val("svA2x1")
                P.dma("sync",
                      x1_scr[s].rearrange("(f p) t -> p f t", p=128)[:, :, ic * ICW:(ic + 1) * ICW],
                      x1buf[xpar].rearrange("p (f t) -> p f t", f=NF),
                      "sspill%d" % xpar, wait=[("svA2x1", svv)])

            if ic == NIC - 1:
                a2_ln()
                pend_ln[0] = None
            else:
                pend_ln[0] = a2_ln

        spill_tot = [("sspill0", P.val("sspill0")), ("sspill1", P.val("sspill1"))]
        evb = {}
        evbv = {}
        for half in range(2):
            for kcp in range(8):
                if half == 0 and kcp == 0:
                    wq_w = [("ssA2", P.val("ssA2")), ("stA2o", P.val("stA2o")),
                            ("stA2y", P.val("stA2y")), ("svA2o", P.val("svA2o")),
                            ("svA2q", P.val("svA2q")), ("stA2st", P.val("stA2st")),
                            ("ssB2", P.val("ssB2")), ("stB3m", P.val("stB3m"))]
                elif half == 1 and kcp == 0:
                    wq_w = [("stB1", P.val("stB1"))]
                else:
                    wq_w = None
                P.dma("gpsimd", wqkv_t.rearrange("p (c n) -> p c n", c=8)[:, kcp:kcp + 1],
                      wqkv_in.rearrange("(c p) n -> p c n", p=128)[:, kcp:kcp + 1,
                                                                   half * 1536:(half + 1) * 1536],
                      "swQk%d" % kcp, wait=wq_w)
            swq_piece = [P.val("swQk%d" % kcp) for kcp in range(8)]
            for ic in range(NIC):
                par = ic % 2
                gs_, gv_ = x1gate.get(par, ("stB1", 0))
                w = [(gs_, gv_)]
                if half == 0:
                    w = w + spill_tot
                P.dma("sync", x1c[par].rearrange("p (f t) -> p f t", f=NF),
                      x1_scr[s].rearrange("(f p) t -> p f t", p=128)[:, :, ic * ICW:(ic + 1) * ICW],
                      "sx1%d" % par, wait=w)
                sx1v = P.val("sx1%d" % par)
                pump_wescr(4)
                base = [("swQk7", swq_piece[7]), ("sx1%d" % par, sx1v)]
                for oci in range(12):
                    oc = half * 12 + oci
                    if oc >= 16:
                        continue
                    dst = Q_ if oc < 8 else K_
                    dt = oc % 8
                    pp = oci % 2
                    ps = pA[:, pp * 512:(pp + 1) * 512]
                    wv_ = [("svB1", evb.get(pp, 0))]
                    for kc in range(NF):
                        kw = (base + wv_) if kc == 0 else None
                        if ic == 0 and oci < 2:
                            kw = (kw or []) + [("swQk%d" % kc, swq_piece[kc])]
                        P.emit("tensor",
                               lambda e, ps=ps, kc=kc, oci=oci, par=par:
                               e.matmul(ps, wqkv_t[:, kc * 1536 + oci * 128: kc * 1536 + oci * 128 + 128],
                                        x1c[par][:, kc * 512:(kc + 1) * 512],
                                        start=(kc == 0), stop=False),
                               wait=kw)
                    P.emit("tensor",
                           lambda e, ps=ps, oc=oc:
                           e.matmul(ps, c_brow[0:1, oc * 128:(oc + 1) * 128],
                                    c_ones[0:1, 0:512], start=False, stop=True),
                           inc="stB1")
                    stv = P.val("stB1")
                    P.emit("vector",
                           lambda e, ps=ps, dst=dst, dt=dt, ic=ic: e.tensor_copy(fm(dst, dt, ic), ps),
                           wait=[("stB1", stv)], inc="svB1")
                    evb[pp] = P.val("svB1")
                if half == 1:
                    for jt4 in range(4):
                        jt = ic * 4 + jt4
                        pp = jt4 % 2
                        ps = pB[:, pp * 1024:(pp + 1) * 1024]
                        wv_ = [("svB1v", evbv.get(pp, 0))]
                        if ic == 0 and jt4 == 0:
                            wv_ = wv_ + [("swQk%d" % k2, swq_piece[k2]) for k2 in range(8)]
                        for kc in range(NF):
                            for vh in range(2):
                                P.emit("tensor",
                                       lambda e, ps=ps, kc=kc, jt4=jt4, par=par, vh=vh:
                                       e.matmul(ps[:, vh * 512:(vh + 1) * 512],
                                                x1c[par][:, kc * 512 + jt4 * 128: kc * 512 + jt4 * 128 + 128],
                                                wqkv_t[:, kc * 1536 + 512 + vh * 512: kc * 1536 + 512 + (vh + 1) * 512],
                                                start=(kc == 0), stop=False),
                                       wait=(base + wv_) if (kc == 0 and vh == 0) else None)
                        for vh in range(2):
                            P.emit("tensor",
                                   lambda e, ps=ps, vh=vh:
                                   e.matmul(ps[:, vh * 512:(vh + 1) * 512],
                                            c_ones[0:1, 0:128],
                                            c_brow[0:1, 2048 + vh * 512: 2048 + (vh + 1) * 512],
                                            start=False, stop=True),
                                   inc="stB1v" if vh == 1 else None)
                        stv = P.val("stB1v")
                        P.emit("vector",
                               lambda e, ps=ps, jt=jt:
                               e.tensor_copy(
                                   V_[:, jt * VST: jt * VST + VST].rearrange("p (h c) -> p h c", h=HB)[:, :, 0:64],
                                   ps.rearrange("p (h c) -> p h c", h=HB)),
                               wait=[("stB1v", stv)], inc="svB1v")
                        evbv[pp] = P.val("svB1v")
                    for it in range(4):
                        wv_ = [("ssB1g", P.val("ssB1g"))]
                        for kc in range(NF):
                            P.emit("tensor",
                                   lambda e, kc=kc, it=it, par=par:
                                   e.matmul(pA[:, 1024:1032],
                                            x1c[par][:, kc * 512 + it * 128: kc * 512 + it * 128 + 128],
                                            c_wgate[:, kc * 8:(kc + 1) * 8],
                                            start=(kc == 0), stop=False),
                                   wait=(base + wv_) if kc == 0 else None)
                        P.emit("tensor",
                               lambda e: e.matmul(pA[:, 1024:1032], c_ones[0:1, 0:128],
                                                  c_bgate, start=False, stop=True),
                               inc="stB1g")
                        stv = P.val("stB1g")
                        P.emit("vector",
                               lambda e, it=it: e.reduce_max(negm_g[:, it:it + 1],
                                                             pA[:, 1024:1032], axis=AX.X,
                                                             negate=True),
                               wait=[("stB1g", stv)], inc="svB1g")
                        svv = P.val("svB1g")
                        P.emit("scalar",
                               lambda e, it=it: e.activation(gexp[:, it * 8:(it + 1) * 8],
                                                             pA[:, 1024:1032], AF.Exp,
                                                             bias=negm_g[:, it:it + 1],
                                                             accum_out=gsum[:, it:it + 1]),
                               wait=[("svB1g", svv)], inc="ssB1g")
                    ssv = P.val("ssB1g")
                    for it in range(4):
                        P.emit("vector", lambda e, it=it: e.reciprocal(grs[:, it:it + 1], gsum[:, it:it + 1]),
                               wait=[("ssB1g", ssv)] if it == 0 else None)
                    for it in range(4):
                        P.emit("vector", lambda e, it=it: e.tensor_tensor(
                            gsm[:, it * 8:(it + 1) * 8], gexp[:, it * 8:(it + 1) * 8],
                            grs[:, it:it + 1].broadcast_to([128, 8]), op=OP.mult))
                    for it in range(4):
                        P.emit("vector", lambda e, it=it: e.reduce_max(
                            gm1[:, it:it + 1], gsm[:, it * 8:(it + 1) * 8], axis=AX.X))
                    for it in range(4):
                        P.emit("vector", lambda e, it=it: e.tensor_tensor(
                            gmk[:, it * 8:(it + 1) * 8], gsm[:, it * 8:(it + 1) * 8],
                            gm1[:, it:it + 1].broadcast_to([128, 8]), op=OP.is_ge))
                    for it in range(4):
                        P.emit("vector", lambda e, it=it: e.tensor_tensor(
                            gzap[:, it * 8:(it + 1) * 8], gsm[:, it * 8:(it + 1) * 8],
                            gmk[:, it * 8:(it + 1) * 8], op=OP.subtract))
                    for it in range(4):
                        P.emit("vector", lambda e, it=it: e.reduce_max(
                            gm2[:, it:it + 1], gzap[:, it * 8:(it + 1) * 8], axis=AX.X))
                    for it in range(4):
                        P.emit("vector", lambda e, it=it: e.tensor_tensor(
                            gmk[:, it * 8:(it + 1) * 8], gsm[:, it * 8:(it + 1) * 8],
                            gm2[:, it:it + 1].broadcast_to([128, 8]), op=OP.is_ge))
                    for it in range(4):
                        P.emit("vector", lambda e, it=it: e.tensor_tensor(
                            gsm[:, it * 8:(it + 1) * 8], gsm[:, it * 8:(it + 1) * 8],
                            gmk[:, it * 8:(it + 1) * 8], op=OP.mult))
                    for it in range(4):
                        P.emit("vector", lambda e, it=it: e.tensor_copy(
                            gcomb_bf[:, it * 8:(it + 1) * 8], gsm[:, it * 8:(it + 1) * 8]),
                               inc="svB1c" if it == 3 else None)
                    svv = P.val("svB1c")
                    for it in range(4):
                        P.emit("tensor",
                               lambda e, it=it: e.transpose(
                                   pA[0:8, 1536:1600].bitcast(BF16),
                                   gcomb_bf[:, it * 8:(it + 1) * 8],
                                   c_idn),
                               wait=([("svB1c", svv)] if it == 0 else
                                     [("svB1t", P.val("svB1t"))]),
                               inc="stB1t")
                        stv = P.val("stB1t")
                        P.emit("vector",
                               lambda e, it=it, ic=ic: e.tensor_copy(
                                   combT[:, ic * ICW + it * 128: ic * ICW + (it + 1) * 128],
                                   pA[0:8, 1536:1600].bitcast(BF16)),
                               wait=[("stB1t", stv)], inc="svB1t")
                x1gate[par] = (("stB1g", P.val("stB1g")) if half == 1
                               else ("stB1", P.val("stB1")))
        P.emit("vector",
               lambda e: e.memset(
                   V_[:, 0:NJT * VST].rearrange("p (j h c) -> p j h c", j=NJT, h=HB)[:, :, :, 64:65], 1.0),
               inc="svB1o")
        sv_vones = P.val("svB1o")
        b1_done = [("svB1", P.val("svB1")), ("svB1v", P.val("svB1v")),
                   ("ssB1g", P.val("ssB1g")), ("svB1t", P.val("svB1t"))]

        B2expv = {}
        pend_epi = [None]
        pend_epi2 = [None]
        for ic in range(NIC):
            for g in range(8):
                hA2, hB2 = 2 * g, 2 * g + 1

                def b2_s(jt, g=g, ic=ic):
                    gpar = jt % 2
                    ps_s = pA[:, gpar * 1024:(gpar + 1) * 1024]
                    wargs = [("ssB2", B2expv.get(gpar, 0))]
                    if g == 0 and ic == 0 and jt <= 1:
                        wargs = wargs + [("svB1o", sv_vones)] + b1_done
                    P.emit("tensor",
                           lambda e, ps_s=ps_s, g=g, jt=jt, ic=ic:
                           e.matmul(ps_s[:, 0:512],
                                    K_[0:64, g * T + jt * 128: g * T + jt * 128 + 128],
                                    Q_[0:64, g * T + ic * ICW: g * T + ic * ICW + ICW],
                                    start=True, stop=True),
                           wait=wargs)
                    P.emit("tensor",
                           lambda e, ps_s=ps_s, g=g, jt=jt, ic=ic:
                           e.matmul(ps_s[:, 512:1024],
                                    K_[64:128, g * T + jt * 128: g * T + jt * 128 + 128],
                                    Q_[64:128, g * T + ic * ICW: g * T + ic * ICW + ICW],
                                    start=True, stop=True, tile_position=(64, 0)),
                           inc="stB2s")
                    stv = P.val("stB2s")
                    P.emit("scalar",
                           lambda e, gpar=gpar, ps_s=ps_s:
                           e.activation(ptilB[:, gpar * 1024:(gpar + 1) * 1024],
                                        ps_s, AF.Exp, scale=SCL_B),
                           wait=[("stB2s", stv)], inc="ssB2")
                    B2expv[gpar] = P.val("ssB2")

                b2_s(0)
                for jt in range(NJT):
                    gpar = jt % 2
                    if jt + 1 < NJT:
                        b2_s(jt + 1)
                    if jt == 0 and pend_epi[0] is not None:
                        pend_epi[0]()
                        pend_epi[0] = None
                    if jt == 3 and pend_epi2[0] is not None:
                        pend_epi2[0]()
                        pend_epi2[0] = None
                    ssv = B2expv[gpar]
                    owait = [("ssB2", ssv)]
                    if jt == 0:
                        owait = owait + [("svB2e", P.val("svB2e"))]
                    P.emit("tensor",
                           lambda e, jt=jt, gpar=gpar, hA2=hA2:
                           e.matmul(pB[0:65, 0:512],
                                    V_[:, jt * VST + hA2 * 65: jt * VST + hA2 * 65 + 65],
                                    ptilB[:, gpar * 1024: gpar * 1024 + 512],
                                    start=(jt == 0), stop=(jt == NJT - 1)),
                           wait=owait)
                    P.emit("tensor",
                           lambda e, jt=jt, gpar=gpar, hB2=hB2:
                           e.matmul(pB[64:128, 512:1024],
                                    V_[:, jt * VST + hB2 * 65: jt * VST + hB2 * 65 + 64],
                                    ptilB[:, gpar * 1024 + 512: gpar * 1024 + 1024],
                                    start=(jt == 0), stop=(jt == NJT - 1),
                                    tile_position=(0, 64)))
                    P.emit("tensor",
                           lambda e, jt=jt, gpar=gpar:
                           e.matmul(pB[:, 1024:1536],
                                    c_ones[:, 0:128],
                                    ptilB[:, gpar * 1024 + 512: gpar * 1024 + 1024],
                                    start=(jt == 0), stop=(jt == NJT - 1)),
                           inc="stB2o" if jt == NJT - 1 else None)
                stv = P.val("stB2o")

                def epilogue(g=g, ic=ic, stv=stv):
                    P.emit("vector", lambda e: e.tensor_copy(OEV[0:64, 0:512], pB[0:64, 0:512]),
                           wait=[("stB2o", stv)])
                    P.emit("vector", lambda e: e.tensor_copy(OEV[64:128, 512:1024],
                                                             pB[64:128, 512:1024]))
                    P.emit("vector", lambda e: e.tensor_copy(linve_r, pB[64:65, 0:512]))
                    P.emit("vector", lambda e: e.tensor_copy(linvo_r, pB[0:1, 1024:1536]),
                           inc="svB2e")
                    svev = P.val("svB2e")
                    P.emit("scalar", lambda e: e.activation(linve_r, linve_r, AF.Ln),
                           wait=[("svB2e", svev)])
                    P.emit("scalar", lambda e: e.activation(linvo_r, linvo_r, AF.Ln))
                    P.emit("scalar", lambda e: e.drain())
                    P.emit("scalar", lambda e: e.activation(lrow[64:65, :], linve_r, AF.Exp, scale=-1.0))
                    P.emit("scalar", lambda e: e.activation(lrow[0:1, :], linvo_r, AF.Exp, scale=-1.0),
                           inc="ssB2r")
                    svv = P.val("ssB2r")

                    def epi2(g=g, ic=ic, svv=svv):
                        P.emit("tensor", lambda e: e.matmul(pB[0:64, 1536:2048], c_ones[64:65, 0:64],
                                                            lrow[64:65, :], start=True, stop=True,
                                                            tile_position=(64, 0)),
                               wait=[("ssB2r", svv), ("ssB2f", P.val("ssB2f"))])
                        P.emit("tensor", lambda e: e.matmul(pB[64:128, 1536:2048], c_ones[0:1, 0:64],
                                                            lrow[0:1, :], start=True, stop=True,
                                                            tile_position=(0, 64)),
                               inc="stB2b")
                        stv2 = P.val("stB2b")
                        P.emit("vector", lambda e: e.tensor_copy(bcB, pB[:, 1536:2048]),
                               wait=[("stB2b", stv2)])
                        P.emit("vector", lambda e: e.drain())
                        P.emit("vector",
                               lambda e, g=g, ic=ic: e.tensor_tensor(
                                   ATB[0:64, g * T + ic * ICW: g * T + ic * ICW + ICW],
                                   OEV[0:64, 0:512], bcB[0:64, :], op=OP.mult))
                        P.emit("vector",
                               lambda e, g=g, ic=ic: e.tensor_tensor(
                                   ATB[64:128, g * T + ic * ICW: g * T + ic * ICW + ICW],
                                   OEV[64:128, 512:1024], bcB[64:128, :], op=OP.mult),
                               inc="ssB2f")

                    pend_epi2[0] = epi2

                pend_epi[0] = epilogue
        if pend_epi[0] is not None:
            pend_epi[0]()
            pend_epi[0] = None
        if pend_epi2[0] is not None:
            pend_epi2[0]()
            pend_epi2[0] = None
        sv_atb = P.val("ssB2f")
        if debug:
            P.dma("gpsimd", dbg2[s].rearrange("(f p) t -> p f t", p=128),
                  ATB.rearrange("p (f t) -> p f t", f=NF), "sdbg2",
                  wait=[("ssB2f", sv_atb)])
            P.dma("gpsimd", dbg3[s], combT, "sdbg2")
        b2_done = [("ssB2", P.val("ssB2"))]
        pump_wescr(NE * NF * 2)
        wescr_wait = [("sbs0", P.val("sbs0")), ("sbs1", P.val("sbs1"))]

        for dp in range(8):
            xw = [("stB2s", P.val("stB2s"))] if dp == 0 else None
            P.dma("sync", X1F.rearrange("p (f t) -> p f t", f=NF)[:, dp:dp + 1],
                  x1_scr[s].rearrange("(f p) t -> p f t", p=128)[:, dp:dp + 1], "sxfp%d" % dp,
                  wait=xw)
        sxf_piece = [P.val("sxfp%d" % dp) for dp in range(8)]
        def cb_emit(iq):
            cwait_extra = [("svB3y", P.val("svB3y"))] + (b2_done if iq == 0 else [])
            for e_ in range(NE):
                cps = pA[:, 1024:1536] if e_ % 2 == 0 else pA[:, 1536:2048]
                w = [("svB3c", P.val("svB3c") - 1 if P.val("svB3c") >= 2 else 0)]
                if e_ == 0:
                    w = w + cwait_extra
                P.emit("tensor",
                       lambda e, cps=cps, e_=e_, iq=iq: e.matmul(
                           cps, c_sel[:, e_ * 128:(e_ + 1) * 128],
                           combT[:, iq * ICW:(iq + 1) * ICW],
                           start=True, stop=True),
                       wait=w, inc="stB3c")
                stv = P.val("stB3c")
                P.emit("vector",
                       lambda e, cps=cps, e_=e_: e.tensor_copy(
                           CBSB[:, e_ * 512:(e_ + 1) * 512], cps),
                       wait=[("stB3c", stv)], inc="svB3c")
            return P.val("svB3c")

        cb_next = [None]
        for iq in range(NIC):
            sv_cb = cb_emit(0) if iq == 0 else cb_next[0]
            if debug and s == 0:
                P.dma("gpsimd", dbg5[iq], CBSB, "sdbg2", wait=[("svB3c", sv_cb)])
            mmfirst = True
            for e_ in range(NE):
                for d_ in range(NF):
                    idx = e_ * NF + d_
                    ring = idx % 4
                    w = [("stB3r%d" % ring, P.val("stB3r%d" % ring))]
                    if iq == 0 and idx < 4:
                        w = w + wescr_wait + [("stB2o", P.val("stB2o"))]
                    if direct_we:
                        P.dma("gpsimd", WER[ring], wexp_in[e_, d_ * 128:(d_ + 1) * 128, :],
                              "seW%d" % ring, wait=w)
                    else:
                        P.dma("sync", WER[ring], we_scr[e_, d_ * 128:(d_ + 1) * 128, :],
                              "seW%d" % ring, wait=w)
                    sew = P.val("seW%d" % ring)
                    xr = idx % 4
                    wx = [("stB3r%d" % xr, P.val("stB3r%d" % xr))]
                    if iq == 0 and e_ == 0:
                        wx = wx + [("sxfp%d" % d_, sxf_piece[d_])]
                    if idx == 0:
                        wx = wx + [("svB3c", sv_cb)]
                    first_mm_extra = ([("svB3L3", P.val("svB3L3"))] if idx == 0 else [])
                    if nocb:
                        P.emit("vector",
                               lambda e, xr=xr, d_=d_, iq=iq: e.tensor_copy(
                                   XER[xr], X1F[:, d_ * T + iq * ICW: d_ * T + iq * ICW + ICW]),
                               wait=wx, inc="svB3e")
                    else:
                        P.emit("vector",
                               lambda e, xr=xr, d_=d_, e_=e_, iq=iq: e.tensor_tensor(
                                   XER[xr], X1F[:, d_ * T + iq * ICW: d_ * T + iq * ICW + ICW],
                                   CBSB[:, e_ * 512:(e_ + 1) * 512], op=OP.mult),
                               wait=wx)
                        P.emit("vector", lambda e: e.drain(), inc="svB3e")
                    svv = P.val("svB3e")
                    fcs = [2, 3, 4, 5, 6, 7, 0, 1]
                    for fi, fc in enumerate(fcs):
                        pbank = (pA if fc < 4 else pB)[:, (fc % 4) * 512:((fc % 4) + 1) * 512]
                        lhs_ap = c_idn if widn else WER[ring][:, fc * 128:(fc + 1) * 128]
                        rhs_ap = (X1F[:, d_ * T + iq * ICW: d_ * T + iq * ICW + ICW]
                                  if xdirect else XER[xr])
                        mmw = None
                        if fi == 0:
                            mmw = [("seW%d" % ring, sew), ("svB3e", svv)]
                        if fc == 0:
                            mmw = (mmw or []) + first_mm_extra
                        P.emit("tensor",
                               lambda e, pbank=pbank, lhs_ap=lhs_ap, rhs_ap=rhs_ap, st=(e_ == 0 and d_ == 0):
                               e.matmul(pbank, lhs_ap, rhs_ap, start=st, stop=False),
                               wait=mmw,
                               inc="stB3r%d" % ring if fi == NF - 1 else None)
            for fc in range(NF):
                pbank = (pA if fc < 4 else pB)[:, (fc % 4) * 512:((fc % 4) + 1) * 512]
                if skipadds:
                    P.emit("tensor",
                           lambda e, pbank=pbank, fc=fc, iq=iq:
                           e.matmul(pbank, c_bexp[:, fc * 128:(fc + 1) * 128],
                                    combT[:, iq * ICW:(iq + 1) * ICW], start=False, stop=True),
                           wait=[("ssB2f", sv_atb)] if fc == 0 else None,
                           inc="stB3m")
                    continue
                P.emit("tensor",
                       lambda e, pbank=pbank, fc=fc, iq=iq:
                       e.matmul(pbank, c_bexp[:, fc * 128:(fc + 1) * 128],
                                combT[:, iq * ICW:(iq + 1) * ICW], start=False, stop=False))
                P.emit("tensor",
                       lambda e, pbank=pbank, fc=fc, iq=iq:
                       e.matmul(pbank, c_idn, ATB[:, fc * T + iq * ICW: fc * T + iq * ICW + ICW],
                                start=False, stop=False),
                       wait=[("ssB2f", sv_atb)] if fc == 0 else None)
                P.emit("tensor",
                       lambda e, pbank=pbank, fc=fc, iq=iq:
                       e.matmul(pbank, c_idn, X1F[:, fc * T + iq * ICW: fc * T + iq * ICW + ICW],
                                start=False, stop=True),
                       wait=([("sxfp%d" % fc, sxf_piece[fc])] if iq == 0 else None),
                       inc="stB3m")
            stv = P.val("stB3m")
            for fc in range(NF):
                pbank = (pA if fc < 4 else pB)[:, (fc % 4) * 512:((fc % 4) + 1) * 512]
                P.emit("vector",
                       lambda e, pbank=pbank, fc=fc: e.tensor_copy(
                           Y2B[:, fc * 512:(fc + 1) * 512], pbank),
                       wait=[("stB3m", stv)] if fc == 0 else None,
                       inc="svB3y" if fc == NF - 1 else None)
            sv_y2 = P.val("svB3y")
            if debug:
                P.dma("gpsimd", dbg4[s].rearrange("(f p) t -> p f t", p=128)[:, :, iq * ICW:(iq + 1) * ICW],
                      Y2B.rearrange("p (f t) -> p f t", f=NF), "sdbg2",
                      wait=[("svB3y", sv_y2)])
            for fc in range(NF):
                P.emit("vector",
                       lambda e, fc=fc: e.tensor_tensor(
                           YSQB[:, fc * 512:(fc + 1) * 512],
                           Y2B[:, fc * 512:(fc + 1) * 512],
                           Y2B[:, fc * 512:(fc + 1) * 512], op=OP.mult),
                       inc="svB3q" if fc == NF - 1 else None)
            sv_q2 = P.val("svB3q")
            for fc in range(NF):
                P.emit("tensor",
                       lambda e, fc=fc: e.matmul(pA[0:1, 0:512], c_ones[:, 0:1],
                                                 Y2B[:, fc * 512:(fc + 1) * 512],
                                                 start=(fc == 0), stop=(fc == NF - 1)),
                       wait=[("svB3y", sv_y2)] if fc == 0 else None)
            for fc in range(NF):
                P.emit("tensor",
                       lambda e, fc=fc: e.matmul(pA[0:1, 512:1024], c_ones[:, 0:1],
                                                 YSQB[:, fc * 512:(fc + 1) * 512],
                                                 start=(fc == 0), stop=(fc == NF - 1)),
                       wait=[("svB3q", sv_q2)] if fc == 0 else None,
                       inc="stB3st" if fc == NF - 1 else None)
            stv = P.val("stB3st")
            if iq + 1 < NIC:
                cb_next[0] = cb_emit(iq + 1)
            P.emit("scalar", lambda e: e.mul(mean_r, pA[0:1, 0:512], 1.0 / D),
                   wait=[("stB3st", stv)])
            P.emit("scalar", lambda e: e.mul(es2_r, pA[0:1, 512:1024], 1.0 / D), inc="ssL")
            ssv = P.val("ssL")
            P.emit("vector", lambda e: e.tensor_tensor(msq_r, mean_r, mean_r, op=OP.mult),
                   wait=[("ssL", ssv)])
            P.emit("vector", lambda e: e.drain())
            P.emit("vector", lambda e: e.tensor_tensor(var_r, es2_r, msq_r, op=OP.subtract),
                   inc="svL")
            svv = P.val("svL")
            P.emit("scalar", lambda e: e.activation(lnv_r, var_r, AF.Ln, bias=eps_r),
                   wait=[("svL", svv)])
            P.emit("scalar", lambda e: e.drain())
            P.emit("scalar", lambda e: e.activation(rstd_r, lnv_r, AF.Exp, scale=-0.5),
                   inc="ssL2")
            ssv = P.val("ssL2")
            P.emit("vector", lambda e: e.tensor_copy(a_bf, rstd_r), wait=[("ssL2", ssv)])
            P.emit("vector", lambda e: e.scalar_tensor_tensor(
                bneg_r, mean_r, -1.0, rstd_r, op0=OP.mult, op1=OP.mult))
            P.emit("vector", lambda e: e.drain())
            P.emit("vector", lambda e: e.tensor_copy(b_bf, bneg_r), inc="svL2")
            svv = P.val("svL2")
            P.emit("tensor", lambda e: e.matmul(pA[:, 0:512], c_ones[0:1, 0:128], a_bf,
                                                start=True, stop=True),
                   wait=[("svL2", svv)])
            P.emit("tensor", lambda e: e.matmul(pA[:, 512:1024], c_ones[0:1, 0:128], b_bf,
                                                start=True, stop=True), inc="stB3b")
            stv = P.val("stB3b")
            P.emit("vector", lambda e: e.tensor_copy(a_sb, pA[:, 0:512]),
                   wait=[("stB3b", stv)])
            P.emit("vector", lambda e: e.tensor_copy(b_sb, pA[:, 512:1024]),
                   inc="svB3L3")
            P.emit("vector", lambda e: e.drain())
            wout = [("sout", P.val("sout"))]
            for fc in range(NF):
                P.emit("vector",
                       lambda e, fc=fc: e.tensor_tensor(
                           T1OUT[:, fc * 512:(fc + 1) * 512],
                           Y2B[:, fc * 512:(fc + 1) * 512], a_sb, op=OP.mult),
                       wait=wout if fc == 0 else None)
            for fc in range(NF):
                P.emit("vector",
                       lambda e, fc=fc: e.tensor_tensor(
                           T1OUT[:, fc * 512:(fc + 1) * 512],
                           T1OUT[:, fc * 512:(fc + 1) * 512],
                           b_sb, op=OP.add),
                       inc="svB3o" if fc == NF - 1 else None)
            svv = P.val("svB3o")
            P.dma("sync",
                  out_hbm[s].rearrange("(f p) t -> p f t", p=128)[:, :, iq * ICW:(iq + 1) * ICW],
                  T1OUT.rearrange("p (f t) -> p f t", f=NF),
                  "sout", wait=[("svB3o", svv)])

    if debug:
        P.dma("gpsimd", dbg1, x1_scr, "sdbg")
        P.emit("gpsimd", lambda e: None,
               wait=[("sdbg", P.val("sdbg")), ("sdbg2", P.val("sdbg2"))])
    P.emit("sync", lambda e: None, wait=[("sout", P.val("sout"))])

    with nc.Block() as block:
        @block.tensor
        def _(eng):
            for fn in P.q["tensor"]:
                fn(eng)

        @block.vector
        def _(eng):
            for fn in P.q["vector"]:
                fn(eng)

        @block.scalar
        def _(eng):
            for fn in P.q["scalar"]:
                fn(eng)

        @block.gpsimd
        def _(eng):
            for fn in P.q["gpsimd"]:
                fn(eng)

        @block.sync
        def _(eng):
            for fn in P.q["sync"]:
                fn(eng)

    return nc


_NC_CACHE = [None]


def _host_constants():
    pos = np.arange(T, dtype=np.float64)[:, None]
    w = np.arange(32, dtype=np.float64)[None, :]
    ang = np.power(10000.0, -w / 32.0)
    rad = (pos * ang).astype(np.float32)
    cos = np.cos(rad).T.astype(np.float32)
    sin = np.sin(rad).T.astype(np.float32)
    ctab = np.ones((128, T), dtype=np.float32)
    ctab[0:32] = cos
    ctab[32:64] = cos
    stab = np.zeros((128, T), dtype=np.float32)
    stab[0:32] = -sin
    stab[32:64] = sin
    perm = np.zeros((128, 128), dtype=np.float32)
    for i in range(32):
        perm[32 + i, i] = 1.0
        perm[i, 32 + i] = 1.0
    r = np.arange(128)[:, None]
    u = np.arange(896)[None, :]
    stair = (r <= u - 384).astype(np.float32)
    idn = np.eye(128, dtype=np.float32)
    sel = np.zeros((8, 1024), dtype=np.float32)
    for e in range(8):
        sel[e, e * 128:(e + 1) * 128] = 1.0
    return ctab, stab, perm, stair, idn, sel


def kernel(L0_attn_x, L0_attn_wq, L0_attn_wk, L0_attn_wv, L0_attn_wo,
           L0_ffn_W_qkv, L0_ffn_b_qkv, L0_ffn_W_gate, L0_ffn_b_gate,
           L0_ffn_W_experts, L0_ffn_b_experts):
    x = np.ascontiguousarray(np.asarray(L0_attn_x, dtype=np.float32).transpose(0, 2, 1))
    wq = np.ascontiguousarray(np.asarray(L0_attn_wq, dtype=np.float32).reshape(D, D))
    wk = np.ascontiguousarray(np.asarray(L0_attn_wk, dtype=np.float32).reshape(D, D))
    wv = np.ascontiguousarray(np.asarray(L0_attn_wv, dtype=np.float32).reshape(D, D))
    wo = np.ascontiguousarray(np.asarray(L0_attn_wo, dtype=np.float32))
    wqkv = np.ascontiguousarray(np.asarray(L0_ffn_W_qkv, dtype=np.float32))
    bqkv = np.ascontiguousarray(np.asarray(L0_ffn_b_qkv, dtype=np.float32).reshape(1, 3 * D))
    wgate = np.ascontiguousarray(np.asarray(L0_ffn_W_gate, dtype=np.float32))
    bgate = np.ascontiguousarray(np.asarray(L0_ffn_b_gate, dtype=np.float32).reshape(1, NE))
    wexp = np.ascontiguousarray(np.asarray(L0_ffn_W_experts, dtype=np.float32))
    bexp = np.ascontiguousarray(np.asarray(L0_ffn_b_experts, dtype=np.float32))
    ctab, stab, perm, stair, idn, sel = _host_constants()

    if _NC_CACHE[0] is None:
        _NC_CACHE[0] = build_program()
    nc = _NC_CACHE[0]

    in_maps = []
    for c in range(NCORES):
        in_maps.append({
            "x": np.ascontiguousarray(x[c * SEQS:(c + 1) * SEQS]),
            "wq": wq, "wk": wk, "wv": wv, "wo": wo,
            "wqkv": wqkv, "bqkv": bqkv, "wgate": wgate, "bgate": bgate,
            "wexp": wexp, "bexp": bexp,
            "ctab": ctab, "stab": stab, "perm": perm, "stair": stair, "idn": idn, "sel": sel,
        })
    res = run_bass_kernel_spmd(nc, in_maps, core_ids=list(range(NCORES)))
    outs = [r["out"] for r in res.results]
    full = np.concatenate(outs, axis=0)
    return np.ascontiguousarray(full.transpose(0, 2, 1))


# version 17
# speedup vs baseline: 1.0025x; 1.0025x over previous
import numpy as np
import concourse.bass as bass
import concourse.mybir as mybir
from concourse.bass_utils import run_bass_kernel_spmd

F32 = mybir.dt.float32
BF16 = mybir.dt.bfloat16
AF = mybir.ActivationFunctionType
OP = mybir.AluOpType
AX = mybir.AxisListType

NCORES = 8
SEQS = 2
T = 2048
D = 1024
NF = 8
ICW = 512
NIC = T // ICW
NJT = T // 128
HA = 8
HB = 16
NE = 8
SCL_A = 1.0 / 128.0
SCL_B = 0.125
EPS = 1e-5
VST = HB * 65


class Prog:
    ENGS = ("tensor", "vector", "scalar", "gpsimd", "sync")

    def __init__(self, nc):
        self.nc = nc
        self.q = {e: [] for e in self.ENGS}
        self.sem_handles = {}
        self.cnt = {}
        self.fences = {}

    def sem(self, name):
        if name not in self.sem_handles:
            self.sem_handles[name] = self.nc.alloc_semaphore(name)
            self.cnt[name] = 0
        return self.sem_handles[name]

    def emit(self, eng, fn, wait=None, inc=None, inc_n=1):
        sems = self.sem_handles
        if inc is not None:
            self.sem(inc)
            self.cnt[inc] += inc_n
        wait = [(s, v) for (s, v) in (wait or []) if v > 0]
        for s, _ in wait:
            self.sem(s)

        def run(engine, fn=fn, wait=wait, inc=inc, inc_n=inc_n):
            for s, v in wait:
                engine.wait_ge(sems[s], v)
            ins = fn(engine)
            if inc is not None and ins is not None:
                ins.then_inc(sems[inc], inc_n)

        self.q[eng].append(run)

    def val(self, name):
        self.sem(name)
        return self.cnt[name]

    def dma(self, eng, out, in_, sem, wait=None):
        self.emit(eng, lambda e, o=out, i=in_: e.dma_start(out=o, in_=i),
                  wait=wait, inc=sem, inc_n=16)
        return self.cnt[sem]

    def release(self, eng, sem):
        self.emit(eng, lambda e: e.nop(), inc=sem)
        return self.cnt[sem]

    def barrier(self):
        marks = []
        for e, sname in (("vector", "bv"), ("scalar", "bs"),
                         ("gpsimd", "bg"), ("sync", "by")):
            fence = self.fences.get(e)
            if fence is not None:
                self.emit(e, fence, inc=sname)
            else:
                self.emit(e, lambda eng: eng.nop(), inc=sname)
            marks.append((sname, self.cnt[sname]))
        self.emit("tensor", self.fences["tensor"], wait=list(marks), inc="bt")
        marks.append(("bt", self.cnt["bt"]))
        for e in ("tensor", "vector", "scalar"):
            self.emit(e, lambda eng: None, wait=marks)


def build_program(debug=False, direct_we=False, nocb=False, skipadds=False, widn=False, xdirect=False):
    nc = bass.Bass(target_bir_lowering=False, trn_type="TRN2")

    x_in = nc.dram_tensor("x", [SEQS, D, T], F32, kind="ExternalInput").ap()
    wq_in = nc.dram_tensor("wq", [D, D], F32, kind="ExternalInput").ap()
    wk_in = nc.dram_tensor("wk", [D, D], F32, kind="ExternalInput").ap()
    wv_in = nc.dram_tensor("wv", [D, D], F32, kind="ExternalInput").ap()
    wo_in = nc.dram_tensor("wo", [HA, 128, D], F32, kind="ExternalInput").ap()
    wqkv_in = nc.dram_tensor("wqkv", [D, 3 * D], F32, kind="ExternalInput").ap()
    bqkv_in = nc.dram_tensor("bqkv", [1, 3 * D], F32, kind="ExternalInput").ap()
    wgate_in = nc.dram_tensor("wgate", [D, NE], F32, kind="ExternalInput").ap()
    bgate_in = nc.dram_tensor("bgate", [1, NE], F32, kind="ExternalInput").ap()
    wexp_in = nc.dram_tensor("wexp", [NE, D, D], F32, kind="ExternalInput").ap()
    bexp_in = nc.dram_tensor("bexp", [NE, D], F32, kind="ExternalInput").ap()
    ctab_in = nc.dram_tensor("ctab", [128, T], F32, kind="ExternalInput").ap()
    stab_in = nc.dram_tensor("stab", [128, T], F32, kind="ExternalInput").ap()
    perm_in = nc.dram_tensor("perm", [128, 128], F32, kind="ExternalInput").ap()
    stair_in = nc.dram_tensor("stair", [128, 896], F32, kind="ExternalInput").ap()
    idn_in = nc.dram_tensor("idn", [128, 128], F32, kind="ExternalInput").ap()
    sel_in = nc.dram_tensor("sel", [8, 1024], F32, kind="ExternalInput").ap()
    out_hbm = nc.dram_tensor("out", [SEQS, D, T], F32, kind="ExternalOutput").ap()

    x1_scr = nc.dram_tensor("x1scr", [SEQS, D, T], BF16).ap()
    dbg1 = (nc.dram_tensor("dbg1", [SEQS, D, T], F32, kind="ExternalOutput").ap()
            if debug else None)
    dbg2 = (nc.dram_tensor("dbg2", [SEQS, D, T], F32, kind="ExternalOutput").ap()
            if debug else None)
    dbg3 = (nc.dram_tensor("dbg3", [SEQS, 8, T], F32, kind="ExternalOutput").ap()
            if debug else None)
    dbg4 = (nc.dram_tensor("dbg4", [SEQS, D, T], F32, kind="ExternalOutput").ap()
            if debug else None)
    dbg5 = (nc.dram_tensor("dbg5", [NIC, 128, 4096], F32, kind="ExternalOutput").ap()
            if debug else None)
    we_scr = nc.dram_tensor("wescr", [NE, D, D], BF16).ap()

    c_ctab = nc.alloc_sbuf_tensor("c_ctab", [128, T], BF16).ap()
    c_stab = nc.alloc_sbuf_tensor("c_stab", [128, T], BF16).ap()
    c_perm = nc.alloc_sbuf_tensor("c_perm", [128, 128], BF16).ap()
    c_stair = nc.alloc_sbuf_tensor("c_stair", [128, 896], BF16).ap()
    c_idn = nc.alloc_sbuf_tensor("c_idn", [128, 128], BF16).ap()
    c_ones = nc.alloc_sbuf_tensor("c_ones", [128, 512], BF16).ap()
    c_wgate = nc.alloc_sbuf_tensor("c_wgate", [128, 64], BF16).ap()
    c_bgate = nc.alloc_sbuf_tensor("c_bgate", [1, 8], BF16).ap()
    c_brow = nc.alloc_sbuf_tensor("c_brow", [1, 3 * D], BF16).ap()
    c_bexp = nc.alloc_sbuf_tensor("c_bexp", [8, D], BF16).ap()
    c_sel = nc.alloc_sbuf_tensor("c_sel", [8, 1024], BF16).ap()
    combT = nc.alloc_sbuf_tensor("combT", [8, T], BF16).ap()
    mf = nc.alloc_sbuf_tensor("miscf", [128, 3264], F32).ap()
    a_bf = nc.alloc_sbuf_tensor("a_bf", [1, 512], BF16).ap()
    b_bf = nc.alloc_sbuf_tensor("b_bf", [1, 512], BF16).ap()
    bcB = nc.alloc_sbuf_tensor("bcB", [128, 512], BF16).ap()
    rsc = nc.alloc_sbuf_tensor("ropescr2", [128, 1024], BF16).ap()
    lrow = nc.alloc_sbuf_tensor("lrow", [128, 512], BF16).ap()
    gcomb_bf = nc.alloc_sbuf_tensor("gcombbf", [128, 32], BF16).ap()
    bnc = [nc.alloc_sbuf_tensor(f"bnc{i}", [128, 512], BF16).ap() for i in range(2)]

    R1 = nc.alloc_sbuf_tensor("R1", [128, 16384], BF16).ap()
    R2 = nc.alloc_sbuf_tensor("R2", [128, 16384], BF16).ap()
    R3 = nc.alloc_sbuf_tensor("R3", [128, 16640], BF16).ap()
    POOL = nc.alloc_sbuf_tensor("POOL", [128, 32768], BF16).ap()

    pA = nc.alloc_psum_tensor("pA", [128, 2048], F32).ap()
    pB = nc.alloc_psum_tensor("pB", [128, 2048], F32).ap()

    K_, Q_, V_ = R1, R2, R3
    X1F = R1
    T1OUT = R2[:, 0:8192].bitcast(F32)
    Y2B = R2[:, 8192:12288]
    YSQB = R2[:, 12288:16384]
    CBSB = R3[:, 0:4096]
    WER = [R3[:, 4096 + 1024 * i: 4096 + 1024 * (i + 1)] for i in range(4)]
    XER = [R3[:, 8192 + 512 * i: 8192 + 512 * (i + 1)] for i in range(4)]

    wq_t = POOL[:, 0:8192]
    wk_t = POOL[:, 8192:16384]
    wv_t = POOL[:, 16384:24576]
    x0c = [POOL[:, 24576:28672], POOL[:, 28672:32768]]
    wo_t = POOL[:, 0:8192]
    ptil = POOL[:, 8192:10240]
    oA = POOL[:, 10240:14336]
    ysqA = POOL[:, 10240:14336]
    ybuf = POOL[:, 14336:18432]
    x1buf = [POOL[:, 18432:22528], POOL[:, 22528:26624]]
    x0c2 = POOL[:, 26624:30720]
    wqkv_t = POOL[:, 0:12288]
    x1c = [POOL[:, 12288:16384], POOL[:, 16384:20480]]
    ptilB = POOL[:, 0:4096]
    ATB = POOL[:, 4096:20480]
    OEV = POOL[:, 20480:21504]
    a_sb = POOL[:, 0:1024].bitcast(F32)
    b_sb = POOL[:, 1024:2048].bitcast(F32)

    linv = mf[:, 0:512]
    mean_r = mf[0:1, 512:1024]
    es2_r = mf[0:1, 1024:1536]
    msq_r = mf[0:1, 1536:2048]
    var_r = mf[0:1, 2048:2560]
    lnv_r = mf[0:1, 1536:2048]
    rstd_r = mf[0:1, 2560:3072]
    bneg_r = mf[0:1, 1024:1536]
    linvo_r = mf[0:1, 2048:2560]
    linve_r = mf[64:65, 2048:2560]
    gexp = mf[:, 3072:3104]
    gsum = mf[:, 3104:3108]
    grs = mf[:, 3108:3112]
    gm1 = mf[:, 3112:3116]
    gm2 = mf[:, 3116:3120]
    gsm = mf[:, 3120:3152]
    gmk = mf[:, 3152:3184]
    gzap = mf[:, 3184:3216]
    negm_g = mf[:, 3216:3220]
    eps_r = mf[0:1, 3220:3221]

    P = Prog(nc)
    fence_f = mf[0:1, 3221:3222]
    fence_g = mf[0:1, 3222:3223]
    P.fences = {
        "tensor": lambda e: e.matmul(pA[0:1, 0:1], c_ones[0:1, 0:1], c_ones[0:1, 0:1],
                                     start=True, stop=True),
        "vector": lambda e: e.tensor_copy(fence_f, fence_g),
        "scalar": lambda e: e.copy(fence_g, fence_f),
    }

    def fm(t, chunk, ic, w=ICW):
        return t[:, chunk * T + ic * w: chunk * T + ic * w + w]

    P.dma("gpsimd", c_ctab, ctab_in, "swC")
    P.dma("gpsimd", c_stab, stab_in, "swC")
    P.dma("gpsimd", c_perm, perm_in, "swC")
    P.dma("gpsimd", c_stair, stair_in, "swC")
    P.dma("gpsimd", c_idn, idn_in, "swC")
    P.dma("gpsimd", c_wgate.rearrange("p (c e) -> p c e", c=8),
          wgate_in.rearrange("(c p) e -> p c e", p=128), "swC")
    P.dma("gpsimd", c_bgate, bgate_in, "swC")
    P.dma("gpsimd", c_brow, bqkv_in, "swC")
    P.dma("gpsimd", c_bexp, bexp_in, "swC")
    P.dma("gpsimd", c_sel, sel_in, "swC")
    swC_total = P.val("swC")
    P.emit("vector", lambda e: e.memset(c_ones, 1.0))
    P.emit("vector", lambda e: e.memset(eps_r, float(EPS)), inc="svones")
    sv_ones = P.val("svones")

    wescr_idx = [0]

    def emit_wescr_chunk():
        idx = wescr_idx[0]
        if idx >= NE * NF * 2:
            return
        wescr_idx[0] += 1
        ed, hf = divmod(idx, 2)
        e_, d_ = divmod(ed, NF)
        par = idx % 2
        w = [("sbs%d" % par, 16 * (idx // 2))] if idx >= 2 else None
        P.dma("gpsimd", bnc[par],
              wexp_in[e_, d_ * 128:(d_ + 1) * 128, hf * 512:(hf + 1) * 512],
              "sbl%d" % par, wait=w)
        P.dma("gpsimd", we_scr[e_, d_ * 128:(d_ + 1) * 128, hf * 512:(hf + 1) * 512],
              bnc[par], "sbs%d" % par, wait=[("sbl%d" % par, 16 * (idx // 2 + 1))])

    def pump_wescr(n):
        for _ in range(n):
            emit_wescr_chunk()

    x0gate = {}
    x1gate = {}
    pa_gate = {}
    pb_gate = {}
    for s in range(SEQS):
        swa_w = None
        if s > 0:
            swa_w = [("stB3m", P.val("stB3m")), ("ssB2f", P.val("ssB2f")),
                     ("stB1g", P.val("stB1g")), ("ssB2", P.val("ssB2")),
                     ("svB3o", P.val("svB3o"))]
        for hh in range(2):
            P.dma("gpsimd", wq_t.rearrange("p (c n) -> p c n", c=8)[:, hh * 4:(hh + 1) * 4],
                  wq_in.rearrange("(c p) n -> p c n", p=128)[:, hh * 4:(hh + 1) * 4],
                  "swAh%d" % hh, wait=swa_w if hh == 0 else None)
            P.dma("gpsimd", wk_t.rearrange("p (c n) -> p c n", c=8)[:, hh * 4:(hh + 1) * 4],
                  wk_in.rearrange("(c p) n -> p c n", p=128)[:, hh * 4:(hh + 1) * 4],
                  "swAh%d" % hh)
            P.dma("gpsimd", wv_t.rearrange("p (c n) -> p c n", c=8)[:, hh * 4:(hh + 1) * 4],
                  wv_in.rearrange("(c p) n -> p c n", p=128)[:, hh * 4:(hh + 1) * 4],
                  "swAh%d" % hh)
        swA_h = [P.val("swAh0"), P.val("swAh1")]

        pa_gate = dict(pa_gate) if s > 0 else {}
        pb_gate = dict(pb_gate) if s > 0 else {}
        qk_idx = [0]
        for ic in range(NIC):
            par = ic % 2
            w = [("stA1v", x0gate.get(par, 0))]
            if s > 0 and ic < 2:
                w += [("sspill1", P.val("sspill1")), ("stA2y", P.val("stA2y"))]
            P.dma("gpsimd", x0c[par].rearrange("p (f t) -> p f t", f=NF),
                  x_in[s].rearrange("(f p) t -> p f t", p=128)[:, :, ic * ICW:(ic + 1) * ICW],
                  "sx%d" % par, wait=w)
            sxv = P.val("sx%d" % par)
            pump_wescr(4)

            base = [("sx%d" % par, sxv), ("swC", swC_total),
                    ("svones", sv_ones)]
            for kind in ("q", "k"):
                wt = wq_t if kind == "q" else wk_t
                dst = Q_ if kind == "q" else K_
                for dt in range(NF):
                    pp = qk_idx[0] % 4
                    qk_idx[0] += 1
                    ps = pA[:, pp * 512:(pp + 1) * 512]
                    wv_ = [pa_gate[pp]] if pp in pa_gate else []
                    for kc in range(NF):
                        kw = (base + wv_) if kc == 0 else None
                        if ic == 0 and dt < 2 and kc in (0, 4):
                            kw = (kw or []) + [("swAh%d" % (kc // 4), swA_h[kc // 4])]
                        P.emit("tensor",
                               lambda e, ps=ps, wt=wt, kc=kc, dt=dt, par=par:
                               e.matmul(ps, wt[:, kc * 1024 + dt * 128: kc * 1024 + dt * 128 + 128],
                                        x0c[par][:, kc * 512:(kc + 1) * 512],
                                        start=(kc == 0), stop=(kc == NF - 1)),
                               wait=kw,
                               inc="stA1" if kc == NF - 1 else None)
                    stv = P.val("stA1")
                    P.emit("scalar",
                           lambda e, ps=ps, dst=dst, dt=dt, ic=ic: e.copy(fm(dst, dt, ic), ps),
                           wait=[("stA1", stv)], inc="svA1")
                    pa_gate[pp] = ("svA1", P.val("svA1"))
            for jt4 in range(4):
                jt = ic * 4 + jt4
                pp = jt4 % 2
                ps = pB[:, pp * 1024:(pp + 1) * 1024]
                wv_ = [g for b in (2 * pp, 2 * pp + 1) for g in ([pb_gate[b]] if b in pb_gate else [])]
                if ic == 0 and jt4 == 0:
                    wv_ = wv_ + [("swAh0", swA_h[0]), ("swAh1", swA_h[1])]
                for kc in range(NF):
                    for half in range(2):
                        P.emit("tensor",
                               lambda e, ps=ps, kc=kc, jt4=jt4, par=par, half=half:
                               e.matmul(ps[:, half * 512:(half + 1) * 512],
                                        x0c[par][:, kc * 512 + jt4 * 128: kc * 512 + jt4 * 128 + 128],
                                        wv_t[:, kc * 1024 + half * 512: kc * 1024 + (half + 1) * 512],
                                        start=(kc == 0), stop=(kc == NF - 1)),
                               wait=(base + wv_) if (kc == 0 and half == 0) else None,
                               inc="stA1v" if (kc == NF - 1 and half == 1) else None)
                stv = P.val("stA1v")
                P.emit("scalar",
                       lambda e, ps=ps, jt=jt: e.copy(V_[:, jt * 1024:(jt + 1) * 1024], ps),
                       wait=[("stA1v", stv)], inc="svA1v")
                pb_gate[2 * pp] = ("svA1v", P.val("svA1v"))
                pb_gate[2 * pp + 1] = ("svA1v", P.val("svA1v"))
            x0gate[par] = P.val("stA1v")

            svq = P.val("svA1")
            first = True
            rslot = [0]
            for X in (Q_, K_):
                for pr in range(4):
                    h0 = 2 * pr
                    sl0, sl1 = rslot[0] % 4, (rslot[0] + 1) % 4
                    rslot[0] += 2
                    rvals = {}
                    for (h, slot) in ((h0, sl0), (h0 + 1, sl1)):
                        rps = pB[:, slot * 512:(slot + 1) * 512]
                        w = [pb_gate[slot]] if slot in pb_gate else []
                        if first:
                            w += [("svA1", svq), ("swC", swC_total)]
                        P.emit("tensor",
                               lambda e, rps=rps, X=X, h=h, ic=ic:
                               e.matmul(rps, c_perm,
                                        X[:, h * T + ic * ICW: h * T + ic * ICW + ICW],
                                        start=True, stop=True),
                               wait=w, inc="strp")
                        first = False
                        rvals[h] = P.val("strp")
                    for (h, slot, ri) in ((h0, sl0, 0), (h0 + 1, sl1, 1)):
                        rps = pB[:, slot * 512:(slot + 1) * 512]
                        P.emit("vector",
                               lambda e, X=X, h=h, ic=ic:
                               e.tensor_tensor(
                                   X[:, h * T + ic * ICW: h * T + ic * ICW + ICW],
                                   X[:, h * T + ic * ICW: h * T + ic * ICW + ICW],
                                   c_ctab[:, ic * ICW:(ic + 1) * ICW], op=OP.mult),
                               wait=[("strp", rvals[h])])
                        P.emit("vector",
                               lambda e, rps=rps, ri=ri, ic=ic:
                               e.tensor_tensor(
                                   rsc[:, ri * 512:(ri + 1) * 512], rps,
                                   c_stab[:, ic * ICW:(ic + 1) * ICW], op=OP.mult),
                               inc="svrp")
                        pb_gate[slot] = ("svrp", P.val("svrp"))
                    P.emit("vector", lambda e: e.drain())
                    for (h, ri) in ((h0, 0), (h0 + 1, 1)):
                        P.emit("vector",
                               lambda e, X=X, h=h, ic=ic, ri=ri:
                               e.tensor_tensor(
                                   X[:, h * T + ic * ICW: h * T + ic * ICW + ICW],
                                   X[:, h * T + ic * ICW: h * T + ic * ICW + ICW],
                                   rsc[:, ri * 512:(ri + 1) * 512], op=OP.add))
        P.barrier()

        P.dma("gpsimd", wo_t.rearrange("p (h m) -> p h m", h=HA),
              wo_in.rearrange("h p m -> p h m"), "swO",
              wait=[("stA1", P.val("stA1"))])
        swO_total = P.val("swO")

        A2expv = {}
        pend_ln = [None]
        x0gate2 = [0 if s == 0 else x0gate2[0]]
        for ic in range(NIC):
            njt = 4 * (ic + 1)
            w = [("stA2y", x0gate2[0])] if x0gate2[0] > 0 else []
            if ic == 0:
                w = list(w) + [("stA1v", P.val("stA1v"))]
            P.dma("gpsimd", x0c2.rearrange("p (f t) -> p f t", f=NF),
                  x_in[s].rearrange("(f p) t -> p f t", p=128)[:, :, ic * ICW:(ic + 1) * ICW],
                  "sx2", wait=w)
            sx2v = P.val("sx2")
            pump_wescr(4)

            A2maskv = {}

            for h in range(HA):
                if h == 1 and pend_ln[0] is not None:
                    pend_ln[0]()
                    pend_ln[0] = None
                ngrp = (njt + 1) // 2

                def a2_s(jg, h=h, ic=ic, njt=njt):
                    gpar = jg % 2
                    ps_s = pA[:, gpar * 1024:(gpar + 1) * 1024]
                    nt = min(2, njt - 2 * jg)
                    wargs = [("ssA2", A2expv.get(gpar, 0))]
                    for t_ in range(nt):
                        jt = 2 * jg + t_
                        P.emit("tensor",
                               lambda e, ps_s=ps_s, t_=t_, jt=jt, h=h, ic=ic:
                               e.matmul(ps_s[:, t_ * 512:(t_ + 1) * 512],
                                        K_[:, h * T + jt * 128: h * T + jt * 128 + 128],
                                        Q_[:, h * T + ic * ICW: h * T + ic * ICW + ICW],
                                        start=True, stop=True),
                               wait=wargs if t_ == 0 else None,
                               inc="stA2s" if t_ == nt - 1 else None)
                    stv = P.val("stA2s")
                    P.emit("scalar",
                           lambda e, gpar=gpar, ps_s=ps_s, nt=nt:
                           e.activation(ptil[:, gpar * 1024: gpar * 1024 + nt * 512],
                                        ps_s[:, 0:nt * 512], AF.Exp, scale=SCL_A),
                           wait=[("stA2s", stv)], inc="ssA2")
                    A2expv[gpar] = P.val("ssA2")
                    masked = (2 * jg) >= 4 * ic
                    if masked:
                        ssv = P.val("ssA2")
                        for t_ in range(nt):
                            jt = 2 * jg + t_
                            toff = (jt - 4 * ic) * 128
                            P.emit("vector",
                                   lambda e, gpar=gpar, t_=t_, toff=toff:
                                   e.tensor_tensor(
                                       ptil[:, gpar * 1024 + t_ * 512: gpar * 1024 + (t_ + 1) * 512],
                                       ptil[:, gpar * 1024 + t_ * 512: gpar * 1024 + (t_ + 1) * 512],
                                       c_stair[:, 384 - toff: 384 - toff + 512],
                                       op=OP.mult),
                                   wait=[("ssA2", ssv)] if t_ == 0 else None,
                                   inc="svA2m" if t_ == nt - 1 else None)
                        A2maskv[gpar] = P.val("svA2m")
                    else:
                        A2maskv[gpar] = None

                if s == 0:
                    pump_wescr(3)
                a2_s(0)
                for jg in range(ngrp):
                    gpar = jg % 2
                    nt = min(2, njt - 2 * jg)
                    if jg + 1 < ngrp:
                        a2_s(jg + 1)
                    if A2maskv[gpar] is not None:
                        pwait = [("svA2m", A2maskv[gpar])]
                    else:
                        pwait = [("ssA2", A2expv[gpar])]
                    if jg == 0:
                        pwait = pwait + [("svA2o", P.val("svA2o"))]
                    for t_ in range(nt):
                        jt = 2 * jg + t_
                        P.emit("tensor",
                               lambda e, jt=jt, gpar=gpar, t_=t_, h=h:
                               e.matmul(pB[:, 0:512],
                                        V_[:, jt * 1024 + h * 128: jt * 1024 + h * 128 + 128],
                                        ptil[:, gpar * 1024 + t_ * 512: gpar * 1024 + (t_ + 1) * 512],
                                        start=(jt == 0), stop=(jt == njt - 1)),
                               wait=pwait if t_ == 0 else None)
                        P.emit("tensor",
                               lambda e, jt=jt, gpar=gpar, t_=t_:
                               e.matmul(pB[:, 512:1024],
                                        c_ones[:, 0:128],
                                        ptil[:, gpar * 1024 + t_ * 512: gpar * 1024 + (t_ + 1) * 512],
                                        start=(jt == 0), stop=(jt == njt - 1)),
                               inc="stA2o" if jt == njt - 1 else None)
                stv = P.val("stA2o")
                P.emit("scalar", lambda e: e.activation(linv, pB[:, 512:1024], AF.Ln),
                       wait=[("stA2o", stv)])
                P.emit("scalar", lambda e: e.drain())
                P.emit("scalar", lambda e: e.activation(linv, linv, AF.Exp, scale=-1.0),
                       inc="ssA2r")
                ssrv = P.val("ssA2r")
                ovwait = [("ssA2r", ssrv)]
                if h == 0 and ic > 0:
                    ovwait.append(("stA2st", P.val("stA2st")))
                P.emit("vector",
                       lambda e, h=h: e.tensor_tensor(
                           oA[:, h * 512:(h + 1) * 512], pB[:, 0:512], linv, op=OP.mult),
                       wait=ovwait, inc="svA2o")
            sv_heads = P.val("svA2o")
            evy = {}
            for mc in range(NF):
                ypar = mc % 2
                ps_y = pB[:, 1024 + ypar * 512: 1024 + (ypar + 1) * 512]
                wargs = [("swO", swO_total), ("svA2o", sv_heads), ("sx2", sx2v),
                         ("svA2y", evy.get(ypar, 0))]
                if mc == 0:
                    wargs = wargs + [("ssL", P.val("ssL")), ("svA2x1", P.val("svA2x1"))]
                for h in range(HA):
                    P.emit("tensor",
                           lambda e, ps_y=ps_y, h=h, mc=mc:
                           e.matmul(ps_y, wo_t[:, h * 1024 + mc * 128: h * 1024 + mc * 128 + 128],
                                    oA[:, h * 512:(h + 1) * 512],
                                    start=(h == 0), stop=False),
                           wait=wargs if h == 0 else None)
                P.emit("tensor",
                       lambda e, ps_y=ps_y, mc=mc:
                       e.matmul(ps_y, c_idn, x0c2[:, mc * 512:(mc + 1) * 512],
                                start=False, stop=True),
                       inc="stA2y")
                stv = P.val("stA2y")
                P.emit("vector",
                       lambda e, ps_y=ps_y, mc=mc: e.tensor_copy(ybuf[:, mc * 512:(mc + 1) * 512], ps_y),
                       wait=[("stA2y", stv)], inc="svA2y")
                evy[ypar] = P.val("svA2y")
            sv_y = P.val("svA2y")
            x0gate2[0] = P.val("stA2y")
            for mc in range(NF):
                P.emit("vector",
                       lambda e, mc=mc: e.tensor_tensor(
                           ysqA[:, mc * 512:(mc + 1) * 512],
                           ybuf[:, mc * 512:(mc + 1) * 512],
                           ybuf[:, mc * 512:(mc + 1) * 512], op=OP.mult),
                       wait=[("svA2y", sv_y)] if mc == 0 else None,
                       inc="svA2q" if mc == NF - 1 else None)
            sv_q = P.val("svA2q")
            for mc in range(NF):
                P.emit("tensor",
                       lambda e, mc=mc: e.matmul(pB[0:1, 1024:1536], c_ones[:, 0:1],
                                                 ybuf[:, mc * 512:(mc + 1) * 512],
                                                 start=(mc == 0), stop=(mc == NF - 1)),
                       wait=[("svA2y", sv_y)] if mc == 0 else None)
            for mc in range(NF):
                P.emit("tensor",
                       lambda e, mc=mc: e.matmul(pB[0:1, 1536:2048], c_ones[:, 0:1],
                                                 ysqA[:, mc * 512:(mc + 1) * 512],
                                                 start=(mc == 0), stop=(mc == NF - 1)),
                       wait=[("svA2q", sv_q)] if mc == 0 else None,
                       inc="stA2st" if mc == NF - 1 else None)

            def a2_ln(ic=ic, stv=P.val("stA2st"), sv_y=sv_y):
                P.emit("scalar", lambda e: e.mul(mean_r, pB[0:1, 1024:1536], 1.0 / D),
                       wait=[("stA2st", stv)])
                P.emit("scalar", lambda e: e.mul(es2_r, pB[0:1, 1536:2048], 1.0 / D), inc="ssL")
                ssv = P.val("ssL")
                P.emit("vector", lambda e: e.tensor_tensor(msq_r, mean_r, mean_r, op=OP.mult),
                       wait=[("ssL", ssv)])
                P.emit("vector", lambda e: e.drain())
                P.emit("vector", lambda e: e.tensor_tensor(var_r, es2_r, msq_r, op=OP.subtract),
                       inc="svL")
                svv = P.val("svL")
                P.emit("scalar", lambda e: e.activation(lnv_r, var_r, AF.Ln, bias=eps_r),
                       wait=[("svL", svv)])
                P.emit("scalar", lambda e: e.drain())
                P.emit("scalar", lambda e: e.activation(rstd_r, lnv_r, AF.Exp, scale=-0.5),
                       inc="ssL2")
                ssv = P.val("ssL2")
                P.emit("vector", lambda e: e.tensor_copy(a_bf, rstd_r), wait=[("ssL2", ssv)])
                P.emit("vector", lambda e: e.scalar_tensor_tensor(
                    bneg_r, mean_r, -1.0, rstd_r, op0=OP.mult, op1=OP.mult))
                P.emit("vector", lambda e: e.drain())
                P.emit("vector", lambda e: e.tensor_copy(b_bf, bneg_r), inc="svL2")
                svv = P.val("svL2")
                P.emit("tensor", lambda e: e.matmul(pB[:, 1024:1536], c_ones[0:1, 0:128], a_bf,
                                                    start=True, stop=True),
                       wait=[("svL2", svv), ("svA2y", sv_y)])
                P.emit("tensor", lambda e: e.matmul(pB[:, 1536:2048], c_ones[0:1, 0:128], b_bf,
                                                    start=True, stop=True), inc="stA2b")
                stv2 = P.val("stA2b")
                xpar = ic % 2
                wspill = [("sspill%d" % xpar, P.val("sspill%d" % xpar))]
                for mc in range(NF):
                    P.emit("vector",
                           lambda e, mc=mc, xpar=xpar: e.tensor_tensor(
                               x1buf[xpar][:, mc * 512:(mc + 1) * 512],
                               ybuf[:, mc * 512:(mc + 1) * 512], pB[:, 1024:1536], op=OP.mult),
                           wait=([("stA2b", stv2)] + wspill) if mc == 0 else None)
                for mc in range(NF):
                    P.emit("vector",
                           lambda e, mc=mc, xpar=xpar: e.tensor_tensor(
                               x1buf[xpar][:, mc * 512:(mc + 1) * 512],
                               x1buf[xpar][:, mc * 512:(mc + 1) * 512], pB[:, 1536:2048], op=OP.add),
                           inc="svA2x1" if mc == NF - 1 else None)
                svv = P.val("svA2x1")
                P.dma("sync",
                      x1_scr[s].rearrange("(f p) t -> p f t", p=128)[:, :, ic * ICW:(ic + 1) * ICW],
                      x1buf[xpar].rearrange("p (f t) -> p f t", f=NF),
                      "sspill%d" % xpar, wait=[("svA2x1", svv)])

            if ic == NIC - 1:
                a2_ln()
                pend_ln[0] = None
            else:
                pend_ln[0] = a2_ln

        spill_tot = [("sspill0", P.val("sspill0")), ("sspill1", P.val("sspill1"))]
        evb = {}
        evbv = {}
        for half in range(2):
            for kcp in range(8):
                if half == 0 and kcp == 0:
                    wq_w = [("ssA2", P.val("ssA2")), ("stA2o", P.val("stA2o")),
                            ("stA2y", P.val("stA2y")), ("svA2o", P.val("svA2o")),
                            ("svA2q", P.val("svA2q")), ("stA2st", P.val("stA2st")),
                            ("ssB2", P.val("ssB2")), ("stB3m", P.val("stB3m"))]
                elif half == 1 and kcp == 0:
                    wq_w = [("stB1", P.val("stB1"))]
                else:
                    wq_w = None
                P.dma("gpsimd", wqkv_t.rearrange("p (c n) -> p c n", c=8)[:, kcp:kcp + 1],
                      wqkv_in.rearrange("(c p) n -> p c n", p=128)[:, kcp:kcp + 1,
                                                                   half * 1536:(half + 1) * 1536],
                      "swQk%d" % kcp, wait=wq_w)
            swq_piece = [P.val("swQk%d" % kcp) for kcp in range(8)]
            for ic in range(NIC):
                par = ic % 2
                gs_, gv_ = x1gate.get(par, ("stB1", 0))
                w = [(gs_, gv_)]
                if half == 0:
                    w = w + spill_tot
                P.dma("sync", x1c[par].rearrange("p (f t) -> p f t", f=NF),
                      x1_scr[s].rearrange("(f p) t -> p f t", p=128)[:, :, ic * ICW:(ic + 1) * ICW],
                      "sx1%d" % par, wait=w)
                sx1v = P.val("sx1%d" % par)
                pump_wescr(4)
                base = [("swQk7", swq_piece[7]), ("sx1%d" % par, sx1v)]
                for oci in range(12):
                    oc = half * 12 + oci
                    if oc >= 16:
                        continue
                    dst = Q_ if oc < 8 else K_
                    dt = oc % 8
                    pp = oci % 2
                    ps = pA[:, pp * 512:(pp + 1) * 512]
                    wv_ = [("svB1", evb.get(pp, 0))]
                    for kc in range(NF):
                        kw = (base + wv_) if kc == 0 else None
                        if ic == 0 and oci < 2:
                            kw = (kw or []) + [("swQk%d" % kc, swq_piece[kc])]
                        P.emit("tensor",
                               lambda e, ps=ps, kc=kc, oci=oci, par=par:
                               e.matmul(ps, wqkv_t[:, kc * 1536 + oci * 128: kc * 1536 + oci * 128 + 128],
                                        x1c[par][:, kc * 512:(kc + 1) * 512],
                                        start=(kc == 0), stop=False),
                               wait=kw)
                    P.emit("tensor",
                           lambda e, ps=ps, oc=oc:
                           e.matmul(ps, c_brow[0:1, oc * 128:(oc + 1) * 128],
                                    c_ones[0:1, 0:512], start=False, stop=True),
                           inc="stB1")
                    stv = P.val("stB1")
                    P.emit("vector",
                           lambda e, ps=ps, dst=dst, dt=dt, ic=ic: e.tensor_copy(fm(dst, dt, ic), ps),
                           wait=[("stB1", stv)], inc="svB1")
                    evb[pp] = P.val("svB1")
                if half == 1:
                    for jt4 in range(4):
                        jt = ic * 4 + jt4
                        pp = jt4 % 2
                        ps = pB[:, pp * 1024:(pp + 1) * 1024]
                        wv_ = [("svB1v", evbv.get(pp, 0))]
                        if ic == 0 and jt4 == 0:
                            wv_ = wv_ + [("swQk%d" % k2, swq_piece[k2]) for k2 in range(8)]
                        for kc in range(NF):
                            for vh in range(2):
                                P.emit("tensor",
                                       lambda e, ps=ps, kc=kc, jt4=jt4, par=par, vh=vh:
                                       e.matmul(ps[:, vh * 512:(vh + 1) * 512],
                                                x1c[par][:, kc * 512 + jt4 * 128: kc * 512 + jt4 * 128 + 128],
                                                wqkv_t[:, kc * 1536 + 512 + vh * 512: kc * 1536 + 512 + (vh + 1) * 512],
                                                start=(kc == 0), stop=False),
                                       wait=(base + wv_) if (kc == 0 and vh == 0) else None)
                        for vh in range(2):
                            P.emit("tensor",
                                   lambda e, ps=ps, vh=vh:
                                   e.matmul(ps[:, vh * 512:(vh + 1) * 512],
                                            c_ones[0:1, 0:128],
                                            c_brow[0:1, 2048 + vh * 512: 2048 + (vh + 1) * 512],
                                            start=False, stop=True),
                                   inc="stB1v" if vh == 1 else None)
                        stv = P.val("stB1v")
                        P.emit("vector",
                               lambda e, ps=ps, jt=jt:
                               e.tensor_copy(
                                   V_[:, jt * VST: jt * VST + VST].rearrange("p (h c) -> p h c", h=HB)[:, :, 0:64],
                                   ps.rearrange("p (h c) -> p h c", h=HB)),
                               wait=[("stB1v", stv)], inc="svB1v")
                        evbv[pp] = P.val("svB1v")
                    for it in range(4):
                        wv_ = [("ssB1g", P.val("ssB1g"))]
                        for kc in range(NF):
                            P.emit("tensor",
                                   lambda e, kc=kc, it=it, par=par:
                                   e.matmul(pA[:, 1024:1032],
                                            x1c[par][:, kc * 512 + it * 128: kc * 512 + it * 128 + 128],
                                            c_wgate[:, kc * 8:(kc + 1) * 8],
                                            start=(kc == 0), stop=False),
                                   wait=(base + wv_) if kc == 0 else None)
                        P.emit("tensor",
                               lambda e: e.matmul(pA[:, 1024:1032], c_ones[0:1, 0:128],
                                                  c_bgate, start=False, stop=True),
                               inc="stB1g")
                        stv = P.val("stB1g")
                        P.emit("vector",
                               lambda e, it=it: e.reduce_max(negm_g[:, it:it + 1],
                                                             pA[:, 1024:1032], axis=AX.X,
                                                             negate=True),
                               wait=[("stB1g", stv)], inc="svB1g")
                        svv = P.val("svB1g")
                        P.emit("scalar",
                               lambda e, it=it: e.activation(gexp[:, it * 8:(it + 1) * 8],
                                                             pA[:, 1024:1032], AF.Exp,
                                                             bias=negm_g[:, it:it + 1],
                                                             accum_out=gsum[:, it:it + 1]),
                               wait=[("svB1g", svv)], inc="ssB1g")
                    ssv = P.val("ssB1g")
                    for it in range(4):
                        P.emit("vector", lambda e, it=it: e.reciprocal(grs[:, it:it + 1], gsum[:, it:it + 1]),
                               wait=[("ssB1g", ssv)] if it == 0 else None)
                    for it in range(4):
                        P.emit("vector", lambda e, it=it: e.tensor_tensor(
                            gsm[:, it * 8:(it + 1) * 8], gexp[:, it * 8:(it + 1) * 8],
                            grs[:, it:it + 1].broadcast_to([128, 8]), op=OP.mult))
                    for it in range(4):
                        P.emit("vector", lambda e, it=it: e.reduce_max(
                            gm1[:, it:it + 1], gsm[:, it * 8:(it + 1) * 8], axis=AX.X))
                    for it in range(4):
                        P.emit("vector", lambda e, it=it: e.tensor_tensor(
                            gmk[:, it * 8:(it + 1) * 8], gsm[:, it * 8:(it + 1) * 8],
                            gm1[:, it:it + 1].broadcast_to([128, 8]), op=OP.is_ge))
                    for it in range(4):
                        P.emit("vector", lambda e, it=it: e.tensor_tensor(
                            gzap[:, it * 8:(it + 1) * 8], gsm[:, it * 8:(it + 1) * 8],
                            gmk[:, it * 8:(it + 1) * 8], op=OP.subtract))
                    for it in range(4):
                        P.emit("vector", lambda e, it=it: e.reduce_max(
                            gm2[:, it:it + 1], gzap[:, it * 8:(it + 1) * 8], axis=AX.X))
                    for it in range(4):
                        P.emit("vector", lambda e, it=it: e.tensor_tensor(
                            gmk[:, it * 8:(it + 1) * 8], gsm[:, it * 8:(it + 1) * 8],
                            gm2[:, it:it + 1].broadcast_to([128, 8]), op=OP.is_ge))
                    for it in range(4):
                        P.emit("vector", lambda e, it=it: e.tensor_tensor(
                            gsm[:, it * 8:(it + 1) * 8], gsm[:, it * 8:(it + 1) * 8],
                            gmk[:, it * 8:(it + 1) * 8], op=OP.mult))
                    for it in range(4):
                        P.emit("vector", lambda e, it=it: e.tensor_copy(
                            gcomb_bf[:, it * 8:(it + 1) * 8], gsm[:, it * 8:(it + 1) * 8]),
                               inc="svB1c" if it == 3 else None)
                    svv = P.val("svB1c")
                    for it in range(4):
                        P.emit("tensor",
                               lambda e, it=it: e.transpose(
                                   pA[0:8, 1536:1600].bitcast(BF16),
                                   gcomb_bf[:, it * 8:(it + 1) * 8],
                                   c_idn),
                               wait=([("svB1c", svv)] if it == 0 else
                                     [("svB1t", P.val("svB1t"))]),
                               inc="stB1t")
                        stv = P.val("stB1t")
                        P.emit("vector",
                               lambda e, it=it, ic=ic: e.tensor_copy(
                                   combT[:, ic * ICW + it * 128: ic * ICW + (it + 1) * 128],
                                   pA[0:8, 1536:1600].bitcast(BF16)),
                               wait=[("stB1t", stv)], inc="svB1t")
                x1gate[par] = (("stB1g", P.val("stB1g")) if half == 1
                               else ("stB1", P.val("stB1")))
        P.emit("vector",
               lambda e: e.memset(
                   V_[:, 0:NJT * VST].rearrange("p (j h c) -> p j h c", j=NJT, h=HB)[:, :, :, 64:65], 1.0),
               inc="svB1o")
        sv_vones = P.val("svB1o")
        b1_done = [("svB1", P.val("svB1")), ("svB1v", P.val("svB1v")),
                   ("ssB1g", P.val("ssB1g")), ("svB1t", P.val("svB1t"))]

        B2expv = {}
        pend_epi = [None]
        pend_epi2 = [None]
        for ic in range(NIC):
            for g in range(8):
                hA2, hB2 = 2 * g, 2 * g + 1

                def b2_s(jt, g=g, ic=ic):
                    gpar = jt % 2
                    ps_s = pA[:, gpar * 1024:(gpar + 1) * 1024]
                    wargs = [("ssB2", B2expv.get(gpar, 0))]
                    if g == 0 and ic == 0 and jt <= 1:
                        wargs = wargs + [("svB1o", sv_vones)] + b1_done
                    P.emit("tensor",
                           lambda e, ps_s=ps_s, g=g, jt=jt, ic=ic:
                           e.matmul(ps_s[:, 0:512],
                                    K_[0:64, g * T + jt * 128: g * T + jt * 128 + 128],
                                    Q_[0:64, g * T + ic * ICW: g * T + ic * ICW + ICW],
                                    start=True, stop=True),
                           wait=wargs)
                    P.emit("tensor",
                           lambda e, ps_s=ps_s, g=g, jt=jt, ic=ic:
                           e.matmul(ps_s[:, 512:1024],
                                    K_[64:128, g * T + jt * 128: g * T + jt * 128 + 128],
                                    Q_[64:128, g * T + ic * ICW: g * T + ic * ICW + ICW],
                                    start=True, stop=True, tile_position=(64, 0)),
                           inc="stB2s")
                    stv = P.val("stB2s")
                    P.emit("scalar",
                           lambda e, gpar=gpar, ps_s=ps_s:
                           e.activation(ptilB[:, gpar * 1024:(gpar + 1) * 1024],
                                        ps_s, AF.Exp, scale=SCL_B),
                           wait=[("stB2s", stv)], inc="ssB2")
                    B2expv[gpar] = P.val("ssB2")

                b2_s(0)
                for jt in range(NJT):
                    gpar = jt % 2
                    if jt + 1 < NJT:
                        b2_s(jt + 1)
                    if jt == 0 and pend_epi[0] is not None:
                        pend_epi[0]()
                        pend_epi[0] = None
                    if jt == 3 and pend_epi2[0] is not None:
                        pend_epi2[0]()
                        pend_epi2[0] = None
                    ssv = B2expv[gpar]
                    owait = [("ssB2", ssv)]
                    if jt == 0:
                        owait = owait + [("svB2e", P.val("svB2e"))]
                    P.emit("tensor",
                           lambda e, jt=jt, gpar=gpar, hA2=hA2:
                           e.matmul(pB[0:65, 0:512],
                                    V_[:, jt * VST + hA2 * 65: jt * VST + hA2 * 65 + 65],
                                    ptilB[:, gpar * 1024: gpar * 1024 + 512],
                                    start=(jt == 0), stop=(jt == NJT - 1)),
                           wait=owait)
                    P.emit("tensor",
                           lambda e, jt=jt, gpar=gpar, hB2=hB2:
                           e.matmul(pB[64:128, 512:1024],
                                    V_[:, jt * VST + hB2 * 65: jt * VST + hB2 * 65 + 64],
                                    ptilB[:, gpar * 1024 + 512: gpar * 1024 + 1024],
                                    start=(jt == 0), stop=(jt == NJT - 1),
                                    tile_position=(0, 64)))
                    P.emit("tensor",
                           lambda e, jt=jt, gpar=gpar:
                           e.matmul(pB[:, 1024:1536],
                                    c_ones[:, 0:128],
                                    ptilB[:, gpar * 1024 + 512: gpar * 1024 + 1024],
                                    start=(jt == 0), stop=(jt == NJT - 1)),
                           inc="stB2o" if jt == NJT - 1 else None)
                stv = P.val("stB2o")

                def epilogue(g=g, ic=ic, stv=stv):
                    P.emit("vector", lambda e: e.tensor_copy(OEV[0:64, 0:512], pB[0:64, 0:512]),
                           wait=[("stB2o", stv)])
                    P.emit("vector", lambda e: e.tensor_copy(OEV[64:128, 512:1024],
                                                             pB[64:128, 512:1024]))
                    P.emit("vector", lambda e: e.tensor_copy(linve_r, pB[64:65, 0:512]))
                    P.emit("vector", lambda e: e.tensor_copy(linvo_r, pB[0:1, 1024:1536]),
                           inc="svB2e")
                    svev = P.val("svB2e")
                    P.emit("scalar", lambda e: e.activation(linve_r, linve_r, AF.Ln),
                           wait=[("svB2e", svev)])
                    P.emit("scalar", lambda e: e.activation(linvo_r, linvo_r, AF.Ln))
                    P.emit("scalar", lambda e: e.drain())
                    P.emit("scalar", lambda e: e.activation(lrow[64:65, :], linve_r, AF.Exp, scale=-1.0))
                    P.emit("scalar", lambda e: e.activation(lrow[0:1, :], linvo_r, AF.Exp, scale=-1.0),
                           inc="ssB2r")
                    svv = P.val("ssB2r")

                    def epi2(g=g, ic=ic, svv=svv):
                        P.emit("tensor", lambda e: e.matmul(pB[0:64, 1536:2048], c_ones[64:65, 0:64],
                                                            lrow[64:65, :], start=True, stop=True,
                                                            tile_position=(64, 0)),
                               wait=[("ssB2r", svv), ("ssB2f", P.val("ssB2f"))])
                        P.emit("tensor", lambda e: e.matmul(pB[64:128, 1536:2048], c_ones[0:1, 0:64],
                                                            lrow[0:1, :], start=True, stop=True,
                                                            tile_position=(0, 64)),
                               inc="stB2b")
                        stv2 = P.val("stB2b")
                        P.emit("vector", lambda e: e.tensor_copy(bcB, pB[:, 1536:2048]),
                               wait=[("stB2b", stv2)])
                        P.emit("vector", lambda e: e.drain())
                        P.emit("vector",
                               lambda e, g=g, ic=ic: e.tensor_tensor(
                                   ATB[0:64, g * T + ic * ICW: g * T + ic * ICW + ICW],
                                   OEV[0:64, 0:512], bcB[0:64, :], op=OP.mult))
                        P.emit("vector",
                               lambda e, g=g, ic=ic: e.tensor_tensor(
                                   ATB[64:128, g * T + ic * ICW: g * T + ic * ICW + ICW],
                                   OEV[64:128, 512:1024], bcB[64:128, :], op=OP.mult),
                               inc="ssB2f")

                    pend_epi2[0] = epi2

                pend_epi[0] = epilogue
        if pend_epi[0] is not None:
            pend_epi[0]()
            pend_epi[0] = None
        if pend_epi2[0] is not None:
            pend_epi2[0]()
            pend_epi2[0] = None
        sv_atb = P.val("ssB2f")
        if debug:
            P.dma("gpsimd", dbg2[s].rearrange("(f p) t -> p f t", p=128),
                  ATB.rearrange("p (f t) -> p f t", f=NF), "sdbg2",
                  wait=[("ssB2f", sv_atb)])
            P.dma("gpsimd", dbg3[s], combT, "sdbg2")
        b2_done = [("ssB2", P.val("ssB2"))]
        pump_wescr(NE * NF * 2)
        wescr_wait = [("sbs0", P.val("sbs0")), ("sbs1", P.val("sbs1"))]

        for dp in range(8):
            xw = [("stB2s", P.val("stB2s"))] if dp == 0 else None
            P.dma("sync", X1F.rearrange("p (f t) -> p f t", f=NF)[:, dp:dp + 1],
                  x1_scr[s].rearrange("(f p) t -> p f t", p=128)[:, dp:dp + 1], "sxfp%d" % dp,
                  wait=xw)
        sxf_piece = [P.val("sxfp%d" % dp) for dp in range(8)]
        def cb_emit(iq):
            cwait_extra = [("svB3y", P.val("svB3y"))] + (b2_done if iq == 0 else [])
            for e_ in range(NE):
                cps = pA[:, 1024:1536] if e_ % 2 == 0 else pA[:, 1536:2048]
                w = [("svB3c", P.val("svB3c") - 1 if P.val("svB3c") >= 2 else 0)]
                if e_ == 0:
                    w = w + cwait_extra
                P.emit("tensor",
                       lambda e, cps=cps, e_=e_, iq=iq: e.matmul(
                           cps, c_sel[:, e_ * 128:(e_ + 1) * 128],
                           combT[:, iq * ICW:(iq + 1) * ICW],
                           start=True, stop=True),
                       wait=w, inc="stB3c")
                stv = P.val("stB3c")
                P.emit("vector",
                       lambda e, cps=cps, e_=e_: e.tensor_copy(
                           CBSB[:, e_ * 512:(e_ + 1) * 512], cps),
                       wait=[("stB3c", stv)], inc="svB3c")
            return P.val("svB3c")

        cb_next = [None]
        pend_ap = [[]]
        for iq in range(NIC):
            sv_cb = cb_emit(0) if iq == 0 else cb_next[0]
            if debug and s == 0:
                P.dma("gpsimd", dbg5[iq], CBSB, "sdbg2", wait=[("svB3c", sv_cb)])
            mmfirst = True
            for e_ in range(NE):
                for d_ in range(NF):
                    idx = e_ * NF + d_
                    ring = idx % 4
                    w = [("stB3r%d" % ring, P.val("stB3r%d" % ring))]
                    if iq == 0 and idx < 4:
                        w = w + wescr_wait + [("stB2o", P.val("stB2o"))]
                    if direct_we:
                        P.dma("gpsimd", WER[ring], wexp_in[e_, d_ * 128:(d_ + 1) * 128, :],
                              "seW%d" % ring, wait=w)
                    else:
                        P.dma("sync", WER[ring], we_scr[e_, d_ * 128:(d_ + 1) * 128, :],
                              "seW%d" % ring, wait=w)
                    sew = P.val("seW%d" % ring)
                    xr = idx % 4
                    wx = [("stB3r%d" % xr, P.val("stB3r%d" % xr))]
                    if iq == 0 and e_ == 0:
                        wx = wx + [("sxfp%d" % d_, sxf_piece[d_])]
                    if idx == 0:
                        wx = wx + [("svB3c", sv_cb)]
                    first_mm_extra = ([("svB3L3", P.val("svB3L3"))] if idx == 0 else [])
                    if nocb:
                        P.emit("vector",
                               lambda e, xr=xr, d_=d_, iq=iq: e.tensor_copy(
                                   XER[xr], X1F[:, d_ * T + iq * ICW: d_ * T + iq * ICW + ICW]),
                               wait=wx, inc="svB3e")
                    else:
                        P.emit("vector",
                               lambda e, xr=xr, d_=d_, e_=e_, iq=iq: e.tensor_tensor(
                                   XER[xr], X1F[:, d_ * T + iq * ICW: d_ * T + iq * ICW + ICW],
                                   CBSB[:, e_ * 512:(e_ + 1) * 512], op=OP.mult),
                               wait=wx)
                        P.emit("vector", lambda e: e.drain(), inc="svB3e")
                    if idx % 8 == 7 and pend_ap[0]:
                        pend_ap[0].pop(0)()
                    svv = P.val("svB3e")
                    fcs = [2, 3, 4, 5, 6, 7, 0, 1]
                    for fi, fc in enumerate(fcs):
                        pbank = (pA if fc < 4 else pB)[:, (fc % 4) * 512:((fc % 4) + 1) * 512]
                        lhs_ap = c_idn if widn else WER[ring][:, fc * 128:(fc + 1) * 128]
                        rhs_ap = (X1F[:, d_ * T + iq * ICW: d_ * T + iq * ICW + ICW]
                                  if xdirect else XER[xr])
                        mmw = None
                        if fi == 0:
                            mmw = [("seW%d" % ring, sew), ("svB3e", svv)]
                        if fc == 0:
                            mmw = (mmw or []) + first_mm_extra
                        P.emit("tensor",
                               lambda e, pbank=pbank, lhs_ap=lhs_ap, rhs_ap=rhs_ap, st=(e_ == 0 and d_ == 0):
                               e.matmul(pbank, lhs_ap, rhs_ap, start=st, stop=False),
                               wait=mmw,
                               inc="stB3r%d" % ring if fi == NF - 1 else None)
            for fc in range(NF):
                pbank = (pA if fc < 4 else pB)[:, (fc % 4) * 512:((fc % 4) + 1) * 512]
                if skipadds:
                    P.emit("tensor",
                           lambda e, pbank=pbank, fc=fc, iq=iq:
                           e.matmul(pbank, c_bexp[:, fc * 128:(fc + 1) * 128],
                                    combT[:, iq * ICW:(iq + 1) * ICW], start=False, stop=True),
                           wait=[("ssB2f", sv_atb)] if fc == 0 else None,
                           inc="stB3m")
                    continue
                P.emit("tensor",
                       lambda e, pbank=pbank, fc=fc, iq=iq:
                       e.matmul(pbank, c_bexp[:, fc * 128:(fc + 1) * 128],
                                combT[:, iq * ICW:(iq + 1) * ICW], start=False, stop=False))
                P.emit("tensor",
                       lambda e, pbank=pbank, fc=fc, iq=iq:
                       e.matmul(pbank, c_idn, ATB[:, fc * T + iq * ICW: fc * T + iq * ICW + ICW],
                                start=False, stop=False),
                       wait=[("ssB2f", sv_atb)] if fc == 0 else None)
                P.emit("tensor",
                       lambda e, pbank=pbank, fc=fc, iq=iq:
                       e.matmul(pbank, c_idn, X1F[:, fc * T + iq * ICW: fc * T + iq * ICW + ICW],
                                start=False, stop=True),
                       wait=([("sxfp%d" % fc, sxf_piece[fc])] if iq == 0 else None),
                       inc="stB3m")
            stv = P.val("stB3m")
            for fc in range(NF):
                pbank = (pA if fc < 4 else pB)[:, (fc % 4) * 512:((fc % 4) + 1) * 512]
                P.emit("vector",
                       lambda e, pbank=pbank, fc=fc: e.tensor_copy(
                           Y2B[:, fc * 512:(fc + 1) * 512], pbank),
                       wait=[("stB3m", stv)] if fc == 0 else None,
                       inc="svB3y" if fc == NF - 1 else None)
            sv_y2 = P.val("svB3y")
            if debug:
                P.dma("gpsimd", dbg4[s].rearrange("(f p) t -> p f t", p=128)[:, :, iq * ICW:(iq + 1) * ICW],
                      Y2B.rearrange("p (f t) -> p f t", f=NF), "sdbg2",
                      wait=[("svB3y", sv_y2)])
            for fc in range(NF):
                P.emit("vector",
                       lambda e, fc=fc: e.tensor_tensor(
                           YSQB[:, fc * 512:(fc + 1) * 512],
                           Y2B[:, fc * 512:(fc + 1) * 512],
                           Y2B[:, fc * 512:(fc + 1) * 512], op=OP.mult),
                       inc="svB3q" if fc == NF - 1 else None)
            sv_q2 = P.val("svB3q")
            for fc in range(NF):
                P.emit("tensor",
                       lambda e, fc=fc: e.matmul(pA[0:1, 0:512], c_ones[:, 0:1],
                                                 Y2B[:, fc * 512:(fc + 1) * 512],
                                                 start=(fc == 0), stop=(fc == NF - 1)),
                       wait=[("svB3y", sv_y2)] if fc == 0 else None)
            for fc in range(NF):
                P.emit("tensor",
                       lambda e, fc=fc: e.matmul(pA[0:1, 512:1024], c_ones[:, 0:1],
                                                 YSQB[:, fc * 512:(fc + 1) * 512],
                                                 start=(fc == 0), stop=(fc == NF - 1)),
                       wait=[("svB3q", sv_q2)] if fc == 0 else None,
                       inc="stB3st" if fc == NF - 1 else None)
            stv = P.val("stB3st")
            if iq + 1 < NIC:
                cb_next[0] = cb_emit(iq + 1)
            P.emit("scalar", lambda e: e.mul(mean_r, pA[0:1, 0:512], 1.0 / D),
                   wait=[("stB3st", stv)])
            P.emit("scalar", lambda e: e.mul(es2_r, pA[0:1, 512:1024], 1.0 / D), inc="ssL")
            ssv = P.val("ssL")
            P.emit("vector", lambda e: e.tensor_tensor(msq_r, mean_r, mean_r, op=OP.mult),
                   wait=[("ssL", ssv)])
            P.emit("vector", lambda e: e.drain())
            P.emit("vector", lambda e: e.tensor_tensor(var_r, es2_r, msq_r, op=OP.subtract),
                   inc="svL")
            svv = P.val("svL")
            P.emit("scalar", lambda e: e.activation(lnv_r, var_r, AF.Ln, bias=eps_r),
                   wait=[("svL", svv)])
            P.emit("scalar", lambda e: e.drain())
            P.emit("scalar", lambda e: e.activation(rstd_r, lnv_r, AF.Exp, scale=-0.5),
                   inc="ssL2")
            ssv = P.val("ssL2")
            P.emit("vector", lambda e: e.tensor_copy(a_bf, rstd_r), wait=[("ssL2", ssv)])
            P.emit("vector", lambda e: e.scalar_tensor_tensor(
                bneg_r, mean_r, -1.0, rstd_r, op0=OP.mult, op1=OP.mult))
            P.emit("vector", lambda e: e.drain())
            P.emit("vector", lambda e: e.tensor_copy(b_bf, bneg_r), inc="svL2")
            svv = P.val("svL2")
            P.emit("tensor", lambda e: e.matmul(pA[:, 0:512], c_ones[0:1, 0:128], a_bf,
                                                start=True, stop=True),
                   wait=[("svL2", svv)])
            P.emit("tensor", lambda e: e.matmul(pA[:, 512:1024], c_ones[0:1, 0:128], b_bf,
                                                start=True, stop=True), inc="stB3b")
            stv = P.val("stB3b")
            P.emit("vector", lambda e: e.tensor_copy(a_sb, pA[:, 0:512]),
                   wait=[("stB3b", stv)])
            P.emit("vector", lambda e: e.tensor_copy(b_sb, pA[:, 512:1024]),
                   inc="svB3L3")
            P.emit("vector", lambda e: e.drain())

            def mk_apply(fc0, iq=iq):
                def go():
                    wout = [("sout", P.val("sout"))] if fc0 == 0 else None
                    for fc in range(fc0, fc0 + 2):
                        P.emit("vector",
                               lambda e, fc=fc: e.tensor_tensor(
                                   T1OUT[:, fc * 512:(fc + 1) * 512],
                                   Y2B[:, fc * 512:(fc + 1) * 512], a_sb, op=OP.mult),
                               wait=wout if fc == fc0 else None)
                    for fc in range(fc0, fc0 + 2):
                        P.emit("vector",
                               lambda e, fc=fc: e.tensor_tensor(
                                   T1OUT[:, fc * 512:(fc + 1) * 512],
                                   T1OUT[:, fc * 512:(fc + 1) * 512],
                                   b_sb, op=OP.add),
                               inc="svB3o" if fc == fc0 + 1 and fc0 == NF - 2 else None)
                    if fc0 == NF - 2:
                        svv = P.val("svB3o")
                        P.dma("sync",
                              out_hbm[s].rearrange("(f p) t -> p f t", p=128)[:, :, iq * ICW:(iq + 1) * ICW],
                              T1OUT.rearrange("p (f t) -> p f t", f=NF),
                              "sout", wait=[("svB3o", svv)])
                return go

            if iq == NIC - 1:
                for fc0 in range(0, NF, 2):
                    mk_apply(fc0)()
                pend_ap[0] = []
            else:
                pend_ap[0] = [mk_apply(fc0) for fc0 in range(0, NF, 2)]

    if debug:
        P.dma("gpsimd", dbg1, x1_scr, "sdbg")
        P.emit("gpsimd", lambda e: None,
               wait=[("sdbg", P.val("sdbg")), ("sdbg2", P.val("sdbg2"))])
    P.emit("sync", lambda e: None, wait=[("sout", P.val("sout"))])

    with nc.Block() as block:
        @block.tensor
        def _(eng):
            for fn in P.q["tensor"]:
                fn(eng)

        @block.vector
        def _(eng):
            for fn in P.q["vector"]:
                fn(eng)

        @block.scalar
        def _(eng):
            for fn in P.q["scalar"]:
                fn(eng)

        @block.gpsimd
        def _(eng):
            for fn in P.q["gpsimd"]:
                fn(eng)

        @block.sync
        def _(eng):
            for fn in P.q["sync"]:
                fn(eng)

    return nc


_NC_CACHE = [None]


def _host_constants():
    pos = np.arange(T, dtype=np.float64)[:, None]
    w = np.arange(32, dtype=np.float64)[None, :]
    ang = np.power(10000.0, -w / 32.0)
    rad = (pos * ang).astype(np.float32)
    cos = np.cos(rad).T.astype(np.float32)
    sin = np.sin(rad).T.astype(np.float32)
    ctab = np.ones((128, T), dtype=np.float32)
    ctab[0:32] = cos
    ctab[32:64] = cos
    stab = np.zeros((128, T), dtype=np.float32)
    stab[0:32] = -sin
    stab[32:64] = sin
    perm = np.zeros((128, 128), dtype=np.float32)
    for i in range(32):
        perm[32 + i, i] = 1.0
        perm[i, 32 + i] = 1.0
    r = np.arange(128)[:, None]
    u = np.arange(896)[None, :]
    stair = (r <= u - 384).astype(np.float32)
    idn = np.eye(128, dtype=np.float32)
    sel = np.zeros((8, 1024), dtype=np.float32)
    for e in range(8):
        sel[e, e * 128:(e + 1) * 128] = 1.0
    return ctab, stab, perm, stair, idn, sel


def kernel(L0_attn_x, L0_attn_wq, L0_attn_wk, L0_attn_wv, L0_attn_wo,
           L0_ffn_W_qkv, L0_ffn_b_qkv, L0_ffn_W_gate, L0_ffn_b_gate,
           L0_ffn_W_experts, L0_ffn_b_experts):
    x = np.ascontiguousarray(np.asarray(L0_attn_x, dtype=np.float32).transpose(0, 2, 1))
    wq = np.ascontiguousarray(np.asarray(L0_attn_wq, dtype=np.float32).reshape(D, D))
    wk = np.ascontiguousarray(np.asarray(L0_attn_wk, dtype=np.float32).reshape(D, D))
    wv = np.ascontiguousarray(np.asarray(L0_attn_wv, dtype=np.float32).reshape(D, D))
    wo = np.ascontiguousarray(np.asarray(L0_attn_wo, dtype=np.float32))
    wqkv = np.ascontiguousarray(np.asarray(L0_ffn_W_qkv, dtype=np.float32))
    bqkv = np.ascontiguousarray(np.asarray(L0_ffn_b_qkv, dtype=np.float32).reshape(1, 3 * D))
    wgate = np.ascontiguousarray(np.asarray(L0_ffn_W_gate, dtype=np.float32))
    bgate = np.ascontiguousarray(np.asarray(L0_ffn_b_gate, dtype=np.float32).reshape(1, NE))
    wexp = np.ascontiguousarray(np.asarray(L0_ffn_W_experts, dtype=np.float32))
    bexp = np.ascontiguousarray(np.asarray(L0_ffn_b_experts, dtype=np.float32))
    ctab, stab, perm, stair, idn, sel = _host_constants()

    if _NC_CACHE[0] is None:
        _NC_CACHE[0] = build_program()
    nc = _NC_CACHE[0]

    in_maps = []
    for c in range(NCORES):
        in_maps.append({
            "x": np.ascontiguousarray(x[c * SEQS:(c + 1) * SEQS]),
            "wq": wq, "wk": wk, "wv": wv, "wo": wo,
            "wqkv": wqkv, "bqkv": bqkv, "wgate": wgate, "bgate": bgate,
            "wexp": wexp, "bexp": bexp,
            "ctab": ctab, "stab": stab, "perm": perm, "stair": stair, "idn": idn, "sel": sel,
        })
    res = run_bass_kernel_spmd(nc, in_maps, core_ids=list(range(NCORES)))
    outs = [r["out"] for r in res.results]
    full = np.concatenate(outs, axis=0)
    return np.ascontiguousarray(full.transpose(0, 2, 1))


# version 18
# speedup vs baseline: 1.0116x; 1.0091x over previous
import numpy as np
import concourse.bass as bass
import concourse.mybir as mybir
from concourse.bass_utils import run_bass_kernel_spmd

F32 = mybir.dt.float32
BF16 = mybir.dt.bfloat16
AF = mybir.ActivationFunctionType
OP = mybir.AluOpType
AX = mybir.AxisListType

NCORES = 8
SEQS = 2
T = 2048
D = 1024
NF = 8
ICW = 512
NIC = T // ICW
NJT = T // 128
HA = 8
HB = 16
NE = 8
SCL_A = 1.0 / 128.0
SCL_B = 0.125
EPS = 1e-5
VST = HB * 65


class Prog:
    ENGS = ("tensor", "vector", "scalar", "gpsimd", "sync")

    def __init__(self, nc):
        self.nc = nc
        self.q = {e: [] for e in self.ENGS}
        self.sem_handles = {}
        self.cnt = {}
        self.fences = {}

    def sem(self, name):
        if name not in self.sem_handles:
            self.sem_handles[name] = self.nc.alloc_semaphore(name)
            self.cnt[name] = 0
        return self.sem_handles[name]

    def emit(self, eng, fn, wait=None, inc=None, inc_n=1):
        sems = self.sem_handles
        if inc is not None:
            self.sem(inc)
            self.cnt[inc] += inc_n
        wait = [(s, v) for (s, v) in (wait or []) if v > 0]
        for s, _ in wait:
            self.sem(s)

        def run(engine, fn=fn, wait=wait, inc=inc, inc_n=inc_n):
            for s, v in wait:
                engine.wait_ge(sems[s], v)
            ins = fn(engine)
            if inc is not None and ins is not None:
                ins.then_inc(sems[inc], inc_n)

        self.q[eng].append(run)

    def val(self, name):
        self.sem(name)
        return self.cnt[name]

    def dma(self, eng, out, in_, sem, wait=None):
        self.emit(eng, lambda e, o=out, i=in_: e.dma_start(out=o, in_=i),
                  wait=wait, inc=sem, inc_n=16)
        return self.cnt[sem]

    def release(self, eng, sem):
        self.emit(eng, lambda e: e.nop(), inc=sem)
        return self.cnt[sem]

    def barrier(self):
        marks = []
        for e, sname in (("vector", "bv"), ("scalar", "bs"),
                         ("gpsimd", "bg"), ("sync", "by")):
            fence = self.fences.get(e)
            if fence is not None:
                self.emit(e, fence, inc=sname)
            else:
                self.emit(e, lambda eng: eng.nop(), inc=sname)
            marks.append((sname, self.cnt[sname]))
        self.emit("tensor", self.fences["tensor"], wait=list(marks), inc="bt")
        marks.append(("bt", self.cnt["bt"]))
        for e in ("tensor", "vector", "scalar"):
            self.emit(e, lambda eng: None, wait=marks)


def build_program(debug=False, direct_we=False, nocb=False, skipadds=False, widn=False, xdirect=False):
    nc = bass.Bass(target_bir_lowering=False, trn_type="TRN2")

    x_in = nc.dram_tensor("x", [SEQS, D, T], F32, kind="ExternalInput").ap()
    wq_in = nc.dram_tensor("wq", [D, D], F32, kind="ExternalInput").ap()
    wk_in = nc.dram_tensor("wk", [D, D], F32, kind="ExternalInput").ap()
    wv_in = nc.dram_tensor("wv", [D, D], F32, kind="ExternalInput").ap()
    wo_in = nc.dram_tensor("wo", [HA, 128, D], F32, kind="ExternalInput").ap()
    wqkv_in = nc.dram_tensor("wqkv", [D, 3 * D], F32, kind="ExternalInput").ap()
    bqkv_in = nc.dram_tensor("bqkv", [1, 3 * D], F32, kind="ExternalInput").ap()
    wgate_in = nc.dram_tensor("wgate", [D, NE], F32, kind="ExternalInput").ap()
    bgate_in = nc.dram_tensor("bgate", [1, NE], F32, kind="ExternalInput").ap()
    wexp_in = nc.dram_tensor("wexp", [NE, D, D], F32, kind="ExternalInput").ap()
    bexp_in = nc.dram_tensor("bexp", [NE, D], F32, kind="ExternalInput").ap()
    ctab_in = nc.dram_tensor("ctab", [128, T], F32, kind="ExternalInput").ap()
    stab_in = nc.dram_tensor("stab", [128, T], F32, kind="ExternalInput").ap()
    perm_in = nc.dram_tensor("perm", [128, 128], F32, kind="ExternalInput").ap()
    stair_in = nc.dram_tensor("stair", [128, 896], F32, kind="ExternalInput").ap()
    idn_in = nc.dram_tensor("idn", [128, 128], F32, kind="ExternalInput").ap()
    sel_in = nc.dram_tensor("sel", [8, 1024], F32, kind="ExternalInput").ap()
    out_hbm = nc.dram_tensor("out", [SEQS, D, T], F32, kind="ExternalOutput").ap()

    x1_scr = nc.dram_tensor("x1scr", [SEQS, D, T], BF16).ap()
    dbg1 = (nc.dram_tensor("dbg1", [SEQS, D, T], F32, kind="ExternalOutput").ap()
            if debug else None)
    dbg2 = (nc.dram_tensor("dbg2", [SEQS, D, T], F32, kind="ExternalOutput").ap()
            if debug else None)
    dbg3 = (nc.dram_tensor("dbg3", [SEQS, 8, T], F32, kind="ExternalOutput").ap()
            if debug else None)
    dbg4 = (nc.dram_tensor("dbg4", [SEQS, D, T], F32, kind="ExternalOutput").ap()
            if debug else None)
    dbg5 = (nc.dram_tensor("dbg5", [NIC, 128, 4096], F32, kind="ExternalOutput").ap()
            if debug else None)
    we_scr = nc.dram_tensor("wescr", [NE, D, D], BF16).ap()
    wqkv_scr = nc.dram_tensor("wqkvscr", [D, 3 * D], BF16).ap()
    wq_scr = nc.dram_tensor("wqscr", [D, D], BF16).ap()
    wk_scr = nc.dram_tensor("wkscr", [D, D], BF16).ap()
    wv_scr = nc.dram_tensor("wvscr", [D, D], BF16).ap()
    wo_scr = nc.dram_tensor("woscr", [HA, 128, D], BF16).ap()

    c_ctab = nc.alloc_sbuf_tensor("c_ctab", [128, T], BF16).ap()
    c_stab = nc.alloc_sbuf_tensor("c_stab", [128, T], BF16).ap()
    c_perm = nc.alloc_sbuf_tensor("c_perm", [128, 128], BF16).ap()
    c_stair = nc.alloc_sbuf_tensor("c_stair", [128, 896], BF16).ap()
    c_idn = nc.alloc_sbuf_tensor("c_idn", [128, 128], BF16).ap()
    c_ones = nc.alloc_sbuf_tensor("c_ones", [128, 512], BF16).ap()
    c_wgate = nc.alloc_sbuf_tensor("c_wgate", [128, 64], BF16).ap()
    c_bgate = nc.alloc_sbuf_tensor("c_bgate", [1, 8], BF16).ap()
    c_brow = nc.alloc_sbuf_tensor("c_brow", [1, 3 * D], BF16).ap()
    c_bexp = nc.alloc_sbuf_tensor("c_bexp", [8, D], BF16).ap()
    c_sel = nc.alloc_sbuf_tensor("c_sel", [8, 1024], BF16).ap()
    combT = nc.alloc_sbuf_tensor("combT", [8, T], BF16).ap()
    mf = nc.alloc_sbuf_tensor("miscf", [128, 3264], F32).ap()
    a_bf = nc.alloc_sbuf_tensor("a_bf", [1, 512], BF16).ap()
    b_bf = nc.alloc_sbuf_tensor("b_bf", [1, 512], BF16).ap()
    bcB = nc.alloc_sbuf_tensor("bcB", [128, 512], BF16).ap()
    rsc = nc.alloc_sbuf_tensor("ropescr2", [128, 1024], BF16).ap()
    lrow = nc.alloc_sbuf_tensor("lrow", [128, 512], BF16).ap()
    gcomb_bf = nc.alloc_sbuf_tensor("gcombbf", [128, 32], BF16).ap()
    bnc = [nc.alloc_sbuf_tensor(f"bnc{i}", [128, 512], BF16).ap() for i in range(2)]

    R1 = nc.alloc_sbuf_tensor("R1", [128, 16384], BF16).ap()
    R2 = nc.alloc_sbuf_tensor("R2", [128, 16384], BF16).ap()
    R3 = nc.alloc_sbuf_tensor("R3", [128, 16640], BF16).ap()
    POOL = nc.alloc_sbuf_tensor("POOL", [128, 32768], BF16).ap()

    pA = nc.alloc_psum_tensor("pA", [128, 2048], F32).ap()
    pB = nc.alloc_psum_tensor("pB", [128, 2048], F32).ap()

    K_, Q_, V_ = R1, R2, R3
    X1F = R1
    T1OUT = R2[:, 0:8192].bitcast(F32)
    Y2B = R2[:, 8192:12288]
    YSQB = R2[:, 12288:16384]
    CBSB = R3[:, 0:4096]
    WER = [R3[:, 4096 + 1024 * i: 4096 + 1024 * (i + 1)] for i in range(4)]
    XER = [R3[:, 8192 + 512 * i: 8192 + 512 * (i + 1)] for i in range(4)]

    wq_t = POOL[:, 0:8192]
    wk_t = POOL[:, 8192:16384]
    wv_t = POOL[:, 16384:24576]
    x0c = [POOL[:, 24576:28672], POOL[:, 28672:32768]]
    wo_t = POOL[:, 0:8192]
    ptil = POOL[:, 8192:10240]
    oA = POOL[:, 10240:14336]
    ysqA = POOL[:, 10240:14336]
    ybuf = POOL[:, 14336:18432]
    x1buf = [POOL[:, 18432:22528], POOL[:, 22528:26624]]
    x0c2 = POOL[:, 26624:30720]
    wqkv_t = POOL[:, 0:12288]
    x1c = [POOL[:, 12288:16384], POOL[:, 16384:20480]]
    ptilB = POOL[:, 0:4096]
    ATB = POOL[:, 4096:20480]
    OEV = POOL[:, 20480:21504]
    a_sb = POOL[:, 0:1024].bitcast(F32)
    b_sb = POOL[:, 1024:2048].bitcast(F32)

    linv = mf[:, 0:512]
    mean_r = mf[0:1, 512:1024]
    es2_r = mf[0:1, 1024:1536]
    msq_r = mf[0:1, 1536:2048]
    var_r = mf[0:1, 2048:2560]
    lnv_r = mf[0:1, 1536:2048]
    rstd_r = mf[0:1, 2560:3072]
    bneg_r = mf[0:1, 1024:1536]
    linvo_r = mf[0:1, 2048:2560]
    linve_r = mf[64:65, 2048:2560]
    gexp = mf[:, 3072:3104]
    gsum = mf[:, 3104:3108]
    grs = mf[:, 3108:3112]
    gm1 = mf[:, 3112:3116]
    gm2 = mf[:, 3116:3120]
    gsm = mf[:, 3120:3152]
    gmk = mf[:, 3152:3184]
    gzap = mf[:, 3184:3216]
    negm_g = mf[:, 3216:3220]
    eps_r = mf[0:1, 3220:3221]

    P = Prog(nc)
    fence_f = mf[0:1, 3221:3222]
    fence_g = mf[0:1, 3222:3223]
    P.fences = {
        "tensor": lambda e: e.matmul(pA[0:1, 0:1], c_ones[0:1, 0:1], c_ones[0:1, 0:1],
                                     start=True, stop=True),
        "vector": lambda e: e.tensor_copy(fence_f, fence_g),
        "scalar": lambda e: e.copy(fence_g, fence_f),
    }

    def fm(t, chunk, ic, w=ICW):
        return t[:, chunk * T + ic * w: chunk * T + ic * w + w]

    P.dma("gpsimd", c_ctab, ctab_in, "swC")
    P.dma("gpsimd", c_stab, stab_in, "swC")
    P.dma("gpsimd", c_perm, perm_in, "swC")
    P.dma("gpsimd", c_stair, stair_in, "swC")
    P.dma("gpsimd", c_idn, idn_in, "swC")
    P.dma("gpsimd", c_wgate.rearrange("p (c e) -> p c e", c=8),
          wgate_in.rearrange("(c p) e -> p c e", p=128), "swC")
    P.dma("gpsimd", c_bgate, bgate_in, "swC")
    P.dma("gpsimd", c_brow, bqkv_in, "swC")
    P.dma("gpsimd", c_bexp, bexp_in, "swC")
    P.dma("gpsimd", c_sel, sel_in, "swC")
    swC_total = P.val("swC")
    P.emit("vector", lambda e: e.memset(c_ones, 1.0))
    P.emit("vector", lambda e: e.memset(eps_r, float(EPS)), inc="svones")
    sv_ones = P.val("svones")

    bounce_jobs = []
    for c in range(8):
        for j in range(6):
            bounce_jobs.append((wqkv_in[c * 128:(c + 1) * 128, j * 512:(j + 1) * 512],
                                wqkv_scr[c * 128:(c + 1) * 128, j * 512:(j + 1) * 512]))
    wqkv_scr_njobs = len(bounce_jobs)
    for w_in, w_sc in ((wq_in, wq_scr), (wk_in, wk_scr), (wv_in, wv_scr)):
        for c in range(8):
            for j in range(2):
                bounce_jobs.append((w_in[c * 128:(c + 1) * 128, j * 512:(j + 1) * 512],
                                    w_sc[c * 128:(c + 1) * 128, j * 512:(j + 1) * 512]))
    for h in range(HA):
        for j in range(2):
            bounce_jobs.append((wo_in[h, :, j * 512:(j + 1) * 512],
                                wo_scr[h, :, j * 512:(j + 1) * 512]))
    for e_ in range(NE):
        for d_ in range(NF):
            for hf in range(2):
                bounce_jobs.append((wexp_in[e_, d_ * 128:(d_ + 1) * 128, hf * 512:(hf + 1) * 512],
                                    we_scr[e_, d_ * 128:(d_ + 1) * 128, hf * 512:(hf + 1) * 512]))
    N_JOBS = len(bounce_jobs)
    wescr_idx = [0]

    def emit_wescr_chunk():
        idx = wescr_idx[0]
        if idx >= N_JOBS:
            return
        wescr_idx[0] += 1
        srcp, dstp = bounce_jobs[idx]
        par = idx % 2
        w = [("sbs%d" % par, 16 * (idx // 2))] if idx >= 2 else None
        P.dma("gpsimd", bnc[par], srcp, "sbl%d" % par, wait=w)
        P.dma("gpsimd", dstp, bnc[par], "sbs%d" % par,
              wait=[("sbl%d" % par, 16 * (idx // 2 + 1))])

    def bounce_done_wait(njobs):
        n0 = (njobs + 1) // 2
        n1 = njobs // 2
        return [("sbs0", 16 * n0), ("sbs1", 16 * n1)]

    def pump_wescr(n):
        for _ in range(n):
            emit_wescr_chunk()

    x0gate = {}
    x1gate = {}
    pa_gate = {}
    pb_gate = {}
    for s in range(SEQS):
        swa_w = None
        if s > 0:
            swa_w = [("stB3m", P.val("stB3m")), ("ssB2f", P.val("ssB2f")),
                     ("stB1g", P.val("stB1g")), ("ssB2", P.val("ssB2")),
                     ("svB3o", P.val("svB3o"))]
        for hh in range(2):
            if s == 0:
                P.dma("gpsimd", wq_t.rearrange("p (c n) -> p c n", c=8)[:, hh * 4:(hh + 1) * 4],
                      wq_in.rearrange("(c p) n -> p c n", p=128)[:, hh * 4:(hh + 1) * 4],
                      "swAh%d" % hh, wait=swa_w if hh == 0 else None)
                P.dma("gpsimd", wk_t.rearrange("p (c n) -> p c n", c=8)[:, hh * 4:(hh + 1) * 4],
                      wk_in.rearrange("(c p) n -> p c n", p=128)[:, hh * 4:(hh + 1) * 4],
                      "swAh%d" % hh)
                P.dma("gpsimd", wv_t.rearrange("p (c n) -> p c n", c=8)[:, hh * 4:(hh + 1) * 4],
                      wv_in.rearrange("(c p) n -> p c n", p=128)[:, hh * 4:(hh + 1) * 4],
                      "swAh%d" % hh)
            else:
                P.dma("sync", wq_t.rearrange("p (c n) -> p c n", c=8)[:, hh * 4:(hh + 1) * 4],
                      wq_scr.rearrange("(c p) n -> p c n", p=128)[:, hh * 4:(hh + 1) * 4],
                      "swAh%d" % hh, wait=swa_w if hh == 0 else None)
                P.dma("sync", wk_t.rearrange("p (c n) -> p c n", c=8)[:, hh * 4:(hh + 1) * 4],
                      wk_scr.rearrange("(c p) n -> p c n", p=128)[:, hh * 4:(hh + 1) * 4],
                      "swAh%d" % hh)
                P.dma("sync", wv_t.rearrange("p (c n) -> p c n", c=8)[:, hh * 4:(hh + 1) * 4],
                      wv_scr.rearrange("(c p) n -> p c n", p=128)[:, hh * 4:(hh + 1) * 4],
                      "swAh%d" % hh)
        swA_h = [P.val("swAh0"), P.val("swAh1")]

        pa_gate = dict(pa_gate) if s > 0 else {}
        pb_gate = dict(pb_gate) if s > 0 else {}
        qk_idx = [0]
        for ic in range(NIC):
            par = ic % 2
            w = [("stA1v", x0gate.get(par, 0))]
            if s > 0 and ic < 2:
                w += [("sspill1", P.val("sspill1")), ("stA2y", P.val("stA2y"))]
            P.dma("gpsimd", x0c[par].rearrange("p (f t) -> p f t", f=NF),
                  x_in[s].rearrange("(f p) t -> p f t", p=128)[:, :, ic * ICW:(ic + 1) * ICW],
                  "sx%d" % par, wait=w)
            sxv = P.val("sx%d" % par)
            pump_wescr(14)

            base = [("sx%d" % par, sxv), ("swC", swC_total),
                    ("svones", sv_ones)]
            for kind in ("q", "k"):
                wt = wq_t if kind == "q" else wk_t
                dst = Q_ if kind == "q" else K_
                for dt in range(NF):
                    pp = qk_idx[0] % 4
                    qk_idx[0] += 1
                    ps = pA[:, pp * 512:(pp + 1) * 512]
                    wv_ = [pa_gate[pp]] if pp in pa_gate else []
                    for kc in range(NF):
                        kw = (base + wv_) if kc == 0 else None
                        if ic == 0 and dt < 2 and kc in (0, 4):
                            kw = (kw or []) + [("swAh%d" % (kc // 4), swA_h[kc // 4])]
                        P.emit("tensor",
                               lambda e, ps=ps, wt=wt, kc=kc, dt=dt, par=par:
                               e.matmul(ps, wt[:, kc * 1024 + dt * 128: kc * 1024 + dt * 128 + 128],
                                        x0c[par][:, kc * 512:(kc + 1) * 512],
                                        start=(kc == 0), stop=(kc == NF - 1)),
                               wait=kw,
                               inc="stA1" if kc == NF - 1 else None)
                    stv = P.val("stA1")
                    P.emit("scalar",
                           lambda e, ps=ps, dst=dst, dt=dt, ic=ic: e.copy(fm(dst, dt, ic), ps),
                           wait=[("stA1", stv)], inc="svA1")
                    pa_gate[pp] = ("svA1", P.val("svA1"))
            for jt4 in range(4):
                jt = ic * 4 + jt4
                pp = jt4 % 2
                ps = pB[:, pp * 1024:(pp + 1) * 1024]
                wv_ = [g for b in (2 * pp, 2 * pp + 1) for g in ([pb_gate[b]] if b in pb_gate else [])]
                if ic == 0 and jt4 == 0:
                    wv_ = wv_ + [("swAh0", swA_h[0]), ("swAh1", swA_h[1])]
                for kc in range(NF):
                    for half in range(2):
                        P.emit("tensor",
                               lambda e, ps=ps, kc=kc, jt4=jt4, par=par, half=half:
                               e.matmul(ps[:, half * 512:(half + 1) * 512],
                                        x0c[par][:, kc * 512 + jt4 * 128: kc * 512 + jt4 * 128 + 128],
                                        wv_t[:, kc * 1024 + half * 512: kc * 1024 + (half + 1) * 512],
                                        start=(kc == 0), stop=(kc == NF - 1)),
                               wait=(base + wv_) if (kc == 0 and half == 0) else None,
                               inc="stA1v" if (kc == NF - 1 and half == 1) else None)
                stv = P.val("stA1v")
                P.emit("scalar",
                       lambda e, ps=ps, jt=jt: e.copy(V_[:, jt * 1024:(jt + 1) * 1024], ps),
                       wait=[("stA1v", stv)], inc="svA1v")
                pb_gate[2 * pp] = ("svA1v", P.val("svA1v"))
                pb_gate[2 * pp + 1] = ("svA1v", P.val("svA1v"))
            x0gate[par] = P.val("stA1v")

            svq = P.val("svA1")
            first = True
            rslot = [0]
            for X in (Q_, K_):
                for pr in range(4):
                    h0 = 2 * pr
                    sl0, sl1 = rslot[0] % 4, (rslot[0] + 1) % 4
                    rslot[0] += 2
                    rvals = {}
                    for (h, slot) in ((h0, sl0), (h0 + 1, sl1)):
                        rps = pB[:, slot * 512:(slot + 1) * 512]
                        w = [pb_gate[slot]] if slot in pb_gate else []
                        if first:
                            w += [("svA1", svq), ("swC", swC_total)]
                        P.emit("tensor",
                               lambda e, rps=rps, X=X, h=h, ic=ic:
                               e.matmul(rps, c_perm,
                                        X[:, h * T + ic * ICW: h * T + ic * ICW + ICW],
                                        start=True, stop=True),
                               wait=w, inc="strp")
                        first = False
                        rvals[h] = P.val("strp")
                    for (h, slot, ri) in ((h0, sl0, 0), (h0 + 1, sl1, 1)):
                        rps = pB[:, slot * 512:(slot + 1) * 512]
                        P.emit("vector",
                               lambda e, X=X, h=h, ic=ic:
                               e.tensor_tensor(
                                   X[:, h * T + ic * ICW: h * T + ic * ICW + ICW],
                                   X[:, h * T + ic * ICW: h * T + ic * ICW + ICW],
                                   c_ctab[:, ic * ICW:(ic + 1) * ICW], op=OP.mult),
                               wait=[("strp", rvals[h])])
                        P.emit("vector",
                               lambda e, rps=rps, ri=ri, ic=ic:
                               e.tensor_tensor(
                                   rsc[:, ri * 512:(ri + 1) * 512], rps,
                                   c_stab[:, ic * ICW:(ic + 1) * ICW], op=OP.mult),
                               inc="svrp")
                        pb_gate[slot] = ("svrp", P.val("svrp"))
                    P.emit("vector", lambda e: e.drain())
                    for (h, ri) in ((h0, 0), (h0 + 1, 1)):
                        P.emit("vector",
                               lambda e, X=X, h=h, ic=ic, ri=ri:
                               e.tensor_tensor(
                                   X[:, h * T + ic * ICW: h * T + ic * ICW + ICW],
                                   X[:, h * T + ic * ICW: h * T + ic * ICW + ICW],
                                   rsc[:, ri * 512:(ri + 1) * 512], op=OP.add))
        P.barrier()

        if s == 0:
            P.dma("gpsimd", wo_t.rearrange("p (h m) -> p h m", h=HA),
                  wo_in.rearrange("h p m -> p h m"), "swO",
                  wait=[("stA1", P.val("stA1"))])
        else:
            P.dma("sync", wo_t.rearrange("p (h m) -> p h m", h=HA),
                  wo_scr.rearrange("h p m -> p h m"), "swO",
                  wait=[("stA1", P.val("stA1"))])
        swO_total = P.val("swO")

        A2expv = {}
        pend_ln = [None]
        x0gate2 = [0 if s == 0 else x0gate2[0]]
        for ic in range(NIC):
            njt = 4 * (ic + 1)
            w = [("stA2y", x0gate2[0])] if x0gate2[0] > 0 else []
            if ic == 0:
                w = list(w) + [("stA1v", P.val("stA1v"))]
            P.dma("gpsimd", x0c2.rearrange("p (f t) -> p f t", f=NF),
                  x_in[s].rearrange("(f p) t -> p f t", p=128)[:, :, ic * ICW:(ic + 1) * ICW],
                  "sx2", wait=w)
            sx2v = P.val("sx2")
            pump_wescr(14)

            A2maskv = {}

            for h in range(HA):
                if h == 2 and pend_ln[0] is not None:
                    pend_ln[0]()
                    pend_ln[0] = None
                ngrp = (njt + 1) // 2

                def a2_s(jg, h=h, ic=ic, njt=njt):
                    gpar = jg % 2
                    ps_s = pA[:, gpar * 1024:(gpar + 1) * 1024]
                    nt = min(2, njt - 2 * jg)
                    wargs = [("ssA2", A2expv.get(gpar, 0))]
                    for t_ in range(nt):
                        jt = 2 * jg + t_
                        P.emit("tensor",
                               lambda e, ps_s=ps_s, t_=t_, jt=jt, h=h, ic=ic:
                               e.matmul(ps_s[:, t_ * 512:(t_ + 1) * 512],
                                        K_[:, h * T + jt * 128: h * T + jt * 128 + 128],
                                        Q_[:, h * T + ic * ICW: h * T + ic * ICW + ICW],
                                        start=True, stop=True),
                               wait=wargs if t_ == 0 else None,
                               inc="stA2s" if t_ == nt - 1 else None)
                    stv = P.val("stA2s")
                    P.emit("scalar",
                           lambda e, gpar=gpar, ps_s=ps_s, nt=nt:
                           e.activation(ptil[:, gpar * 1024: gpar * 1024 + nt * 512],
                                        ps_s[:, 0:nt * 512], AF.Exp, scale=SCL_A),
                           wait=[("stA2s", stv)], inc="ssA2")
                    A2expv[gpar] = P.val("ssA2")
                    masked = (2 * jg) >= 4 * ic
                    if masked:
                        ssv = P.val("ssA2")
                        for t_ in range(nt):
                            jt = 2 * jg + t_
                            toff = (jt - 4 * ic) * 128
                            P.emit("vector",
                                   lambda e, gpar=gpar, t_=t_, toff=toff:
                                   e.tensor_tensor(
                                       ptil[:, gpar * 1024 + t_ * 512: gpar * 1024 + (t_ + 1) * 512],
                                       ptil[:, gpar * 1024 + t_ * 512: gpar * 1024 + (t_ + 1) * 512],
                                       c_stair[:, 384 - toff: 384 - toff + 512],
                                       op=OP.mult),
                                   wait=[("ssA2", ssv)] if t_ == 0 else None,
                                   inc="svA2m" if t_ == nt - 1 else None)
                        A2maskv[gpar] = P.val("svA2m")
                    else:
                        A2maskv[gpar] = None

                if s == 0:
                    pump_wescr(3)
                a2_s(0)
                for jg in range(ngrp):
                    gpar = jg % 2
                    nt = min(2, njt - 2 * jg)
                    if jg + 1 < ngrp:
                        a2_s(jg + 1)
                    if A2maskv[gpar] is not None:
                        pwait = [("svA2m", A2maskv[gpar])]
                    else:
                        pwait = [("ssA2", A2expv[gpar])]
                    if jg == 0:
                        pwait = pwait + [("svA2o", P.val("svA2o"))]
                    for t_ in range(nt):
                        jt = 2 * jg + t_
                        P.emit("tensor",
                               lambda e, jt=jt, gpar=gpar, t_=t_, h=h:
                               e.matmul(pB[:, 0:512],
                                        V_[:, jt * 1024 + h * 128: jt * 1024 + h * 128 + 128],
                                        ptil[:, gpar * 1024 + t_ * 512: gpar * 1024 + (t_ + 1) * 512],
                                        start=(jt == 0), stop=(jt == njt - 1)),
                               wait=pwait if t_ == 0 else None)
                        P.emit("tensor",
                               lambda e, jt=jt, gpar=gpar, t_=t_:
                               e.matmul(pB[:, 512:1024],
                                        c_ones[:, 0:128],
                                        ptil[:, gpar * 1024 + t_ * 512: gpar * 1024 + (t_ + 1) * 512],
                                        start=(jt == 0), stop=(jt == njt - 1)),
                               inc="stA2o" if jt == njt - 1 else None)
                stv = P.val("stA2o")
                P.emit("scalar", lambda e: e.activation(linv, pB[:, 512:1024], AF.Ln),
                       wait=[("stA2o", stv)])
                P.emit("scalar", lambda e: e.drain())
                P.emit("scalar", lambda e: e.activation(linv, linv, AF.Exp, scale=-1.0),
                       inc="ssA2r")
                ssrv = P.val("ssA2r")
                ovwait = [("ssA2r", ssrv)]
                if h == 0 and ic > 0:
                    ovwait.append(("stA2st", P.val("stA2st")))
                P.emit("vector",
                       lambda e, h=h: e.tensor_tensor(
                           oA[:, h * 512:(h + 1) * 512], pB[:, 0:512], linv, op=OP.mult),
                       wait=ovwait, inc="svA2o")
            sv_heads = P.val("svA2o")
            evy = {}
            for mc in range(NF):
                ypar = mc % 2
                ps_y = pB[:, 1024 + ypar * 512: 1024 + (ypar + 1) * 512]
                wargs = [("swO", swO_total), ("svA2o", sv_heads), ("sx2", sx2v),
                         ("svA2y", evy.get(ypar, 0))]
                if mc == 0:
                    wargs = wargs + [("ssL", P.val("ssL")), ("svA2x1", P.val("svA2x1"))]
                for h in range(HA):
                    P.emit("tensor",
                           lambda e, ps_y=ps_y, h=h, mc=mc:
                           e.matmul(ps_y, wo_t[:, h * 1024 + mc * 128: h * 1024 + mc * 128 + 128],
                                    oA[:, h * 512:(h + 1) * 512],
                                    start=(h == 0), stop=False),
                           wait=wargs if h == 0 else None)
                P.emit("tensor",
                       lambda e, ps_y=ps_y, mc=mc:
                       e.matmul(ps_y, c_idn, x0c2[:, mc * 512:(mc + 1) * 512],
                                start=False, stop=True),
                       inc="stA2y")
                stv = P.val("stA2y")
                P.emit("vector",
                       lambda e, ps_y=ps_y, mc=mc: e.tensor_copy(ybuf[:, mc * 512:(mc + 1) * 512], ps_y),
                       wait=[("stA2y", stv)], inc="svA2y")
                evy[ypar] = P.val("svA2y")
            sv_y = P.val("svA2y")
            x0gate2[0] = P.val("stA2y")
            for mc in range(NF):
                P.emit("vector",
                       lambda e, mc=mc: e.tensor_tensor(
                           ysqA[:, mc * 512:(mc + 1) * 512],
                           ybuf[:, mc * 512:(mc + 1) * 512],
                           ybuf[:, mc * 512:(mc + 1) * 512], op=OP.mult),
                       wait=[("svA2y", sv_y)] if mc == 0 else None,
                       inc="svA2q" if mc == NF - 1 else None)
            sv_q = P.val("svA2q")
            for mc in range(NF):
                P.emit("tensor",
                       lambda e, mc=mc: e.matmul(pB[0:1, 1024:1536], c_ones[:, 0:1],
                                                 ybuf[:, mc * 512:(mc + 1) * 512],
                                                 start=(mc == 0), stop=(mc == NF - 1)),
                       wait=[("svA2y", sv_y)] if mc == 0 else None)
            for mc in range(NF):
                P.emit("tensor",
                       lambda e, mc=mc: e.matmul(pB[0:1, 1536:2048], c_ones[:, 0:1],
                                                 ysqA[:, mc * 512:(mc + 1) * 512],
                                                 start=(mc == 0), stop=(mc == NF - 1)),
                       wait=[("svA2q", sv_q)] if mc == 0 else None,
                       inc="stA2st" if mc == NF - 1 else None)

            def a2_ln(ic=ic, stv=P.val("stA2st"), sv_y=sv_y):
                P.emit("scalar", lambda e: e.mul(mean_r, pB[0:1, 1024:1536], 1.0 / D),
                       wait=[("stA2st", stv)])
                P.emit("scalar", lambda e: e.mul(es2_r, pB[0:1, 1536:2048], 1.0 / D), inc="ssL")
                ssv = P.val("ssL")
                P.emit("vector", lambda e: e.tensor_tensor(msq_r, mean_r, mean_r, op=OP.mult),
                       wait=[("ssL", ssv)])
                P.emit("vector", lambda e: e.drain())
                P.emit("vector", lambda e: e.tensor_tensor(var_r, es2_r, msq_r, op=OP.subtract),
                       inc="svL")
                svv = P.val("svL")
                P.emit("scalar", lambda e: e.activation(lnv_r, var_r, AF.Ln, bias=eps_r),
                       wait=[("svL", svv)])
                P.emit("scalar", lambda e: e.drain())
                P.emit("scalar", lambda e: e.activation(rstd_r, lnv_r, AF.Exp, scale=-0.5),
                       inc="ssL2")
                ssv = P.val("ssL2")
                P.emit("vector", lambda e: e.tensor_copy(a_bf, rstd_r), wait=[("ssL2", ssv)])
                P.emit("vector", lambda e: e.scalar_tensor_tensor(
                    bneg_r, mean_r, -1.0, rstd_r, op0=OP.mult, op1=OP.mult))
                P.emit("vector", lambda e: e.drain())
                P.emit("vector", lambda e: e.tensor_copy(b_bf, bneg_r), inc="svL2")
                svv = P.val("svL2")
                P.emit("tensor", lambda e: e.matmul(pB[:, 1024:1536], c_ones[0:1, 0:128], a_bf,
                                                    start=True, stop=True),
                       wait=[("svL2", svv), ("svA2y", sv_y)])
                P.emit("tensor", lambda e: e.matmul(pB[:, 1536:2048], c_ones[0:1, 0:128], b_bf,
                                                    start=True, stop=True), inc="stA2b")
                stv2 = P.val("stA2b")
                xpar = ic % 2
                wspill = [("sspill%d" % xpar, P.val("sspill%d" % xpar))]
                for mc in range(NF):
                    P.emit("vector",
                           lambda e, mc=mc, xpar=xpar: e.tensor_tensor(
                               x1buf[xpar][:, mc * 512:(mc + 1) * 512],
                               ybuf[:, mc * 512:(mc + 1) * 512], pB[:, 1024:1536], op=OP.mult),
                           wait=([("stA2b", stv2)] + wspill) if mc == 0 else None)
                for mc in range(NF):
                    P.emit("vector",
                           lambda e, mc=mc, xpar=xpar: e.tensor_tensor(
                               x1buf[xpar][:, mc * 512:(mc + 1) * 512],
                               x1buf[xpar][:, mc * 512:(mc + 1) * 512], pB[:, 1536:2048], op=OP.add),
                           inc="svA2x1" if mc == NF - 1 else None)
                svv = P.val("svA2x1")
                P.dma("sync",
                      x1_scr[s].rearrange("(f p) t -> p f t", p=128)[:, :, ic * ICW:(ic + 1) * ICW],
                      x1buf[xpar].rearrange("p (f t) -> p f t", f=NF),
                      "sspill%d" % xpar, wait=[("svA2x1", svv)])

            if ic == NIC - 1:
                a2_ln()
                pend_ln[0] = None
            else:
                pend_ln[0] = a2_ln

        spill_tot = [("sspill0", P.val("sspill0")), ("sspill1", P.val("sspill1"))]
        evb = {}
        evbv = {}
        for half in range(2):
            for kcp in range(8):
                if half == 0 and kcp == 0:
                    wq_w = [("ssA2", P.val("ssA2")), ("stA2o", P.val("stA2o")),
                            ("stA2y", P.val("stA2y")), ("svA2o", P.val("svA2o")),
                            ("svA2q", P.val("svA2q")), ("stA2st", P.val("stA2st")),
                            ("ssB2", P.val("ssB2")), ("stB3m", P.val("stB3m"))]
                    if s == 0:
                        wq_w = wq_w + bounce_done_wait(wqkv_scr_njobs)
                elif half == 1 and kcp == 0:
                    wq_w = [("stB1", P.val("stB1"))]
                else:
                    wq_w = None
                P.dma("sync", wqkv_t.rearrange("p (c n) -> p c n", c=8)[:, kcp:kcp + 1],
                      wqkv_scr.rearrange("(c p) n -> p c n", p=128)[:, kcp:kcp + 1,
                                                                    half * 1536:(half + 1) * 1536],
                      "swQk%d" % kcp, wait=wq_w)
            swq_piece = [P.val("swQk%d" % kcp) for kcp in range(8)]
            for ic in range(NIC):
                par = ic % 2
                gs_, gv_ = x1gate.get(par, ("stB1", 0))
                w = [(gs_, gv_)]
                if half == 0:
                    w = w + spill_tot
                P.dma("sync", x1c[par].rearrange("p (f t) -> p f t", f=NF),
                      x1_scr[s].rearrange("(f p) t -> p f t", p=128)[:, :, ic * ICW:(ic + 1) * ICW],
                      "sx1%d" % par, wait=w)
                sx1v = P.val("sx1%d" % par)
                pump_wescr(14)
                base = [("swQk7", swq_piece[7]), ("sx1%d" % par, sx1v)]
                for oci in range(12):
                    oc = half * 12 + oci
                    if oc >= 16:
                        continue
                    dst = Q_ if oc < 8 else K_
                    dt = oc % 8
                    pp = oci % 2
                    ps = pA[:, pp * 512:(pp + 1) * 512]
                    wv_ = [("svB1", evb.get(pp, 0))]
                    for kc in range(NF):
                        kw = (base + wv_) if kc == 0 else None
                        if ic == 0 and oci < 2:
                            kw = (kw or []) + [("swQk%d" % kc, swq_piece[kc])]
                        P.emit("tensor",
                               lambda e, ps=ps, kc=kc, oci=oci, par=par:
                               e.matmul(ps, wqkv_t[:, kc * 1536 + oci * 128: kc * 1536 + oci * 128 + 128],
                                        x1c[par][:, kc * 512:(kc + 1) * 512],
                                        start=(kc == 0), stop=False),
                               wait=kw)
                    P.emit("tensor",
                           lambda e, ps=ps, oc=oc:
                           e.matmul(ps, c_brow[0:1, oc * 128:(oc + 1) * 128],
                                    c_ones[0:1, 0:512], start=False, stop=True),
                           inc="stB1")
                    stv = P.val("stB1")
                    P.emit("vector",
                           lambda e, ps=ps, dst=dst, dt=dt, ic=ic: e.tensor_copy(fm(dst, dt, ic), ps),
                           wait=[("stB1", stv)], inc="svB1")
                    evb[pp] = P.val("svB1")
                if half == 1:
                    for jt4 in range(4):
                        jt = ic * 4 + jt4
                        pp = jt4 % 2
                        ps = pB[:, pp * 1024:(pp + 1) * 1024]
                        wv_ = [("svB1v", evbv.get(pp, 0))]
                        if ic == 0 and jt4 == 0:
                            wv_ = wv_ + [("swQk%d" % k2, swq_piece[k2]) for k2 in range(8)]
                        for kc in range(NF):
                            for vh in range(2):
                                P.emit("tensor",
                                       lambda e, ps=ps, kc=kc, jt4=jt4, par=par, vh=vh:
                                       e.matmul(ps[:, vh * 512:(vh + 1) * 512],
                                                x1c[par][:, kc * 512 + jt4 * 128: kc * 512 + jt4 * 128 + 128],
                                                wqkv_t[:, kc * 1536 + 512 + vh * 512: kc * 1536 + 512 + (vh + 1) * 512],
                                                start=(kc == 0), stop=False),
                                       wait=(base + wv_) if (kc == 0 and vh == 0) else None)
                        for vh in range(2):
                            P.emit("tensor",
                                   lambda e, ps=ps, vh=vh:
                                   e.matmul(ps[:, vh * 512:(vh + 1) * 512],
                                            c_ones[0:1, 0:128],
                                            c_brow[0:1, 2048 + vh * 512: 2048 + (vh + 1) * 512],
                                            start=False, stop=True),
                                   inc="stB1v" if vh == 1 else None)
                        stv = P.val("stB1v")
                        P.emit("vector",
                               lambda e, ps=ps, jt=jt:
                               e.tensor_copy(
                                   V_[:, jt * VST: jt * VST + VST].rearrange("p (h c) -> p h c", h=HB)[:, :, 0:64],
                                   ps.rearrange("p (h c) -> p h c", h=HB)),
                               wait=[("stB1v", stv)], inc="svB1v")
                        evbv[pp] = P.val("svB1v")
                    for it in range(4):
                        wv_ = [("ssB1g", P.val("ssB1g"))]
                        for kc in range(NF):
                            P.emit("tensor",
                                   lambda e, kc=kc, it=it, par=par:
                                   e.matmul(pA[:, 1024:1032],
                                            x1c[par][:, kc * 512 + it * 128: kc * 512 + it * 128 + 128],
                                            c_wgate[:, kc * 8:(kc + 1) * 8],
                                            start=(kc == 0), stop=False),
                                   wait=(base + wv_) if kc == 0 else None)
                        P.emit("tensor",
                               lambda e: e.matmul(pA[:, 1024:1032], c_ones[0:1, 0:128],
                                                  c_bgate, start=False, stop=True),
                               inc="stB1g")
                        stv = P.val("stB1g")
                        P.emit("vector",
                               lambda e, it=it: e.reduce_max(negm_g[:, it:it + 1],
                                                             pA[:, 1024:1032], axis=AX.X,
                                                             negate=True),
                               wait=[("stB1g", stv)], inc="svB1g")
                        svv = P.val("svB1g")
                        P.emit("scalar",
                               lambda e, it=it: e.activation(gexp[:, it * 8:(it + 1) * 8],
                                                             pA[:, 1024:1032], AF.Exp,
                                                             bias=negm_g[:, it:it + 1],
                                                             accum_out=gsum[:, it:it + 1]),
                               wait=[("svB1g", svv)], inc="ssB1g")
                    ssv = P.val("ssB1g")
                    for it in range(4):
                        P.emit("vector", lambda e, it=it: e.reciprocal(grs[:, it:it + 1], gsum[:, it:it + 1]),
                               wait=[("ssB1g", ssv)] if it == 0 else None)
                    for it in range(4):
                        P.emit("vector", lambda e, it=it: e.tensor_tensor(
                            gsm[:, it * 8:(it + 1) * 8], gexp[:, it * 8:(it + 1) * 8],
                            grs[:, it:it + 1].broadcast_to([128, 8]), op=OP.mult))
                    for it in range(4):
                        P.emit("vector", lambda e, it=it: e.reduce_max(
                            gm1[:, it:it + 1], gsm[:, it * 8:(it + 1) * 8], axis=AX.X))
                    for it in range(4):
                        P.emit("vector", lambda e, it=it: e.tensor_tensor(
                            gmk[:, it * 8:(it + 1) * 8], gsm[:, it * 8:(it + 1) * 8],
                            gm1[:, it:it + 1].broadcast_to([128, 8]), op=OP.is_ge))
                    for it in range(4):
                        P.emit("vector", lambda e, it=it: e.tensor_tensor(
                            gzap[:, it * 8:(it + 1) * 8], gsm[:, it * 8:(it + 1) * 8],
                            gmk[:, it * 8:(it + 1) * 8], op=OP.subtract))
                    for it in range(4):
                        P.emit("vector", lambda e, it=it: e.reduce_max(
                            gm2[:, it:it + 1], gzap[:, it * 8:(it + 1) * 8], axis=AX.X))
                    for it in range(4):
                        P.emit("vector", lambda e, it=it: e.tensor_tensor(
                            gmk[:, it * 8:(it + 1) * 8], gsm[:, it * 8:(it + 1) * 8],
                            gm2[:, it:it + 1].broadcast_to([128, 8]), op=OP.is_ge))
                    for it in range(4):
                        P.emit("vector", lambda e, it=it: e.tensor_tensor(
                            gsm[:, it * 8:(it + 1) * 8], gsm[:, it * 8:(it + 1) * 8],
                            gmk[:, it * 8:(it + 1) * 8], op=OP.mult))
                    for it in range(4):
                        P.emit("vector", lambda e, it=it: e.tensor_copy(
                            gcomb_bf[:, it * 8:(it + 1) * 8], gsm[:, it * 8:(it + 1) * 8]),
                               inc="svB1c" if it == 3 else None)
                    svv = P.val("svB1c")
                    for it in range(4):
                        P.emit("tensor",
                               lambda e, it=it: e.transpose(
                                   pA[0:8, 1536:1600].bitcast(BF16),
                                   gcomb_bf[:, it * 8:(it + 1) * 8],
                                   c_idn),
                               wait=([("svB1c", svv)] if it == 0 else
                                     [("svB1t", P.val("svB1t"))]),
                               inc="stB1t")
                        stv = P.val("stB1t")
                        P.emit("vector",
                               lambda e, it=it, ic=ic: e.tensor_copy(
                                   combT[:, ic * ICW + it * 128: ic * ICW + (it + 1) * 128],
                                   pA[0:8, 1536:1600].bitcast(BF16)),
                               wait=[("stB1t", stv)], inc="svB1t")
                x1gate[par] = (("stB1g", P.val("stB1g")) if half == 1
                               else ("stB1", P.val("stB1")))
        P.emit("vector",
               lambda e: e.memset(
                   V_[:, 0:NJT * VST].rearrange("p (j h c) -> p j h c", j=NJT, h=HB)[:, :, :, 64:65], 1.0),
               inc="svB1o")
        sv_vones = P.val("svB1o")
        b1_done = [("svB1", P.val("svB1")), ("svB1v", P.val("svB1v")),
                   ("ssB1g", P.val("ssB1g")), ("svB1t", P.val("svB1t"))]

        B2expv = {}
        pend_epi = [None]
        pend_epi2 = [None]
        for ic in range(NIC):
            for g in range(8):
                hA2, hB2 = 2 * g, 2 * g + 1

                def b2_s(jt, g=g, ic=ic):
                    gpar = jt % 2
                    ps_s = pA[:, gpar * 1024:(gpar + 1) * 1024]
                    wargs = [("ssB2", B2expv.get(gpar, 0))]
                    if g == 0 and ic == 0 and jt <= 1:
                        wargs = wargs + [("svB1o", sv_vones)] + b1_done
                    P.emit("tensor",
                           lambda e, ps_s=ps_s, g=g, jt=jt, ic=ic:
                           e.matmul(ps_s[:, 0:512],
                                    K_[0:64, g * T + jt * 128: g * T + jt * 128 + 128],
                                    Q_[0:64, g * T + ic * ICW: g * T + ic * ICW + ICW],
                                    start=True, stop=True),
                           wait=wargs)
                    P.emit("tensor",
                           lambda e, ps_s=ps_s, g=g, jt=jt, ic=ic:
                           e.matmul(ps_s[:, 512:1024],
                                    K_[64:128, g * T + jt * 128: g * T + jt * 128 + 128],
                                    Q_[64:128, g * T + ic * ICW: g * T + ic * ICW + ICW],
                                    start=True, stop=True, tile_position=(64, 0)),
                           inc="stB2s")
                    stv = P.val("stB2s")
                    P.emit("scalar",
                           lambda e, gpar=gpar, ps_s=ps_s:
                           e.activation(ptilB[:, gpar * 1024:(gpar + 1) * 1024],
                                        ps_s, AF.Exp, scale=SCL_B),
                           wait=[("stB2s", stv)], inc="ssB2")
                    B2expv[gpar] = P.val("ssB2")

                b2_s(0)
                for jt in range(NJT):
                    gpar = jt % 2
                    if jt + 1 < NJT:
                        b2_s(jt + 1)
                    if jt == 0 and pend_epi[0] is not None:
                        pend_epi[0]()
                        pend_epi[0] = None
                    if jt == 3 and pend_epi2[0] is not None:
                        pend_epi2[0]()
                        pend_epi2[0] = None
                    ssv = B2expv[gpar]
                    owait = [("ssB2", ssv)]
                    if jt == 0:
                        owait = owait + [("svB2e", P.val("svB2e"))]
                    P.emit("tensor",
                           lambda e, jt=jt, gpar=gpar, hA2=hA2:
                           e.matmul(pB[0:65, 0:512],
                                    V_[:, jt * VST + hA2 * 65: jt * VST + hA2 * 65 + 65],
                                    ptilB[:, gpar * 1024: gpar * 1024 + 512],
                                    start=(jt == 0), stop=(jt == NJT - 1)),
                           wait=owait)
                    P.emit("tensor",
                           lambda e, jt=jt, gpar=gpar, hB2=hB2:
                           e.matmul(pB[64:128, 512:1024],
                                    V_[:, jt * VST + hB2 * 65: jt * VST + hB2 * 65 + 64],
                                    ptilB[:, gpar * 1024 + 512: gpar * 1024 + 1024],
                                    start=(jt == 0), stop=(jt == NJT - 1),
                                    tile_position=(0, 64)))
                    P.emit("tensor",
                           lambda e, jt=jt, gpar=gpar:
                           e.matmul(pB[:, 1024:1536],
                                    c_ones[:, 0:128],
                                    ptilB[:, gpar * 1024 + 512: gpar * 1024 + 1024],
                                    start=(jt == 0), stop=(jt == NJT - 1)),
                           inc="stB2o" if jt == NJT - 1 else None)
                stv = P.val("stB2o")

                def epilogue(g=g, ic=ic, stv=stv):
                    P.emit("vector", lambda e: e.tensor_copy(OEV[0:64, 0:512], pB[0:64, 0:512]),
                           wait=[("stB2o", stv)])
                    P.emit("vector", lambda e: e.tensor_copy(OEV[64:128, 512:1024],
                                                             pB[64:128, 512:1024]))
                    P.emit("vector", lambda e: e.tensor_copy(linve_r, pB[64:65, 0:512]))
                    P.emit("vector", lambda e: e.tensor_copy(linvo_r, pB[0:1, 1024:1536]),
                           inc="svB2e")
                    svev = P.val("svB2e")
                    P.emit("scalar", lambda e: e.activation(linve_r, linve_r, AF.Ln),
                           wait=[("svB2e", svev)])
                    P.emit("scalar", lambda e: e.activation(linvo_r, linvo_r, AF.Ln))
                    P.emit("scalar", lambda e: e.drain())
                    P.emit("scalar", lambda e: e.activation(lrow[64:65, :], linve_r, AF.Exp, scale=-1.0))
                    P.emit("scalar", lambda e: e.activation(lrow[0:1, :], linvo_r, AF.Exp, scale=-1.0),
                           inc="ssB2r")
                    svv = P.val("ssB2r")

                    def epi2(g=g, ic=ic, svv=svv):
                        P.emit("tensor", lambda e: e.matmul(pB[0:64, 1536:2048], c_ones[64:65, 0:64],
                                                            lrow[64:65, :], start=True, stop=True,
                                                            tile_position=(64, 0)),
                               wait=[("ssB2r", svv), ("ssB2f", P.val("ssB2f"))])
                        P.emit("tensor", lambda e: e.matmul(pB[64:128, 1536:2048], c_ones[0:1, 0:64],
                                                            lrow[0:1, :], start=True, stop=True,
                                                            tile_position=(0, 64)),
                               inc="stB2b")
                        stv2 = P.val("stB2b")
                        P.emit("vector", lambda e: e.tensor_copy(bcB, pB[:, 1536:2048]),
                               wait=[("stB2b", stv2)])
                        P.emit("vector", lambda e: e.drain())
                        P.emit("vector",
                               lambda e, g=g, ic=ic: e.tensor_tensor(
                                   ATB[0:64, g * T + ic * ICW: g * T + ic * ICW + ICW],
                                   OEV[0:64, 0:512], bcB[0:64, :], op=OP.mult))
                        P.emit("vector",
                               lambda e, g=g, ic=ic: e.tensor_tensor(
                                   ATB[64:128, g * T + ic * ICW: g * T + ic * ICW + ICW],
                                   OEV[64:128, 512:1024], bcB[64:128, :], op=OP.mult),
                               inc="ssB2f")

                    pend_epi2[0] = epi2

                pend_epi[0] = epilogue
        if pend_epi[0] is not None:
            pend_epi[0]()
            pend_epi[0] = None
        if pend_epi2[0] is not None:
            pend_epi2[0]()
            pend_epi2[0] = None
        sv_atb = P.val("ssB2f")
        if debug:
            P.dma("gpsimd", dbg2[s].rearrange("(f p) t -> p f t", p=128),
                  ATB.rearrange("p (f t) -> p f t", f=NF), "sdbg2",
                  wait=[("ssB2f", sv_atb)])
            P.dma("gpsimd", dbg3[s], combT, "sdbg2")
        b2_done = [("ssB2", P.val("ssB2"))]
        pump_wescr(N_JOBS)
        wescr_wait = bounce_done_wait(N_JOBS)

        for dp in range(8):
            xw = [("stB2s", P.val("stB2s"))] if dp == 0 else None
            P.dma("sync", X1F.rearrange("p (f t) -> p f t", f=NF)[:, dp:dp + 1],
                  x1_scr[s].rearrange("(f p) t -> p f t", p=128)[:, dp:dp + 1], "sxfp%d" % dp,
                  wait=xw)
        sxf_piece = [P.val("sxfp%d" % dp) for dp in range(8)]
        def cb_emit(iq):
            cwait_extra = [("svB3y", P.val("svB3y"))] + (b2_done if iq == 0 else [])
            for e_ in range(NE):
                cps = pA[:, 1024:1536] if e_ % 2 == 0 else pA[:, 1536:2048]
                w = [("svB3c", P.val("svB3c") - 1 if P.val("svB3c") >= 2 else 0)]
                if e_ == 0:
                    w = w + cwait_extra
                P.emit("tensor",
                       lambda e, cps=cps, e_=e_, iq=iq: e.matmul(
                           cps, c_sel[:, e_ * 128:(e_ + 1) * 128],
                           combT[:, iq * ICW:(iq + 1) * ICW],
                           start=True, stop=True),
                       wait=w, inc="stB3c")
                stv = P.val("stB3c")
                P.emit("vector",
                       lambda e, cps=cps, e_=e_: e.tensor_copy(
                           CBSB[:, e_ * 512:(e_ + 1) * 512], cps),
                       wait=[("stB3c", stv)], inc="svB3c")
            return P.val("svB3c")

        cb_next = [None]
        pend_ap = [[]]
        for iq in range(NIC):
            sv_cb = cb_emit(0) if iq == 0 else cb_next[0]
            if debug and s == 0:
                P.dma("gpsimd", dbg5[iq], CBSB, "sdbg2", wait=[("svB3c", sv_cb)])
            mmfirst = True
            for e_ in range(NE):
                for d_ in range(NF):
                    idx = e_ * NF + d_
                    ring = idx % 4
                    w = [("stB3r%d" % ring, P.val("stB3r%d" % ring))]
                    if iq == 0 and idx < 4:
                        w = w + wescr_wait + [("stB2o", P.val("stB2o"))]
                    if direct_we:
                        P.dma("gpsimd", WER[ring], wexp_in[e_, d_ * 128:(d_ + 1) * 128, :],
                              "seW%d" % ring, wait=w)
                    else:
                        P.dma("sync", WER[ring], we_scr[e_, d_ * 128:(d_ + 1) * 128, :],
                              "seW%d" % ring, wait=w)
                    sew = P.val("seW%d" % ring)
                    xr = idx % 4
                    wx = [("stB3r%d" % xr, P.val("stB3r%d" % xr))]
                    if iq == 0 and e_ == 0:
                        wx = wx + [("sxfp%d" % d_, sxf_piece[d_])]
                    if idx == 0:
                        wx = wx + [("svB3c", sv_cb)]
                    first_mm_extra = ([("svB3L3", P.val("svB3L3"))] if idx == 0 else [])
                    if nocb:
                        P.emit("vector",
                               lambda e, xr=xr, d_=d_, iq=iq: e.tensor_copy(
                                   XER[xr], X1F[:, d_ * T + iq * ICW: d_ * T + iq * ICW + ICW]),
                               wait=wx, inc="svB3e")
                    else:
                        P.emit("vector",
                               lambda e, xr=xr, d_=d_, e_=e_, iq=iq: e.tensor_tensor(
                                   XER[xr], X1F[:, d_ * T + iq * ICW: d_ * T + iq * ICW + ICW],
                                   CBSB[:, e_ * 512:(e_ + 1) * 512], op=OP.mult),
                               wait=wx)
                        P.emit("vector", lambda e: e.drain(), inc="svB3e")
                    if idx % 8 == 7 and pend_ap[0]:
                        pend_ap[0].pop(0)()
                    svv = P.val("svB3e")
                    fcs = [2, 3, 4, 5, 6, 7, 0, 1]
                    for fi, fc in enumerate(fcs):
                        pbank = (pA if fc < 4 else pB)[:, (fc % 4) * 512:((fc % 4) + 1) * 512]
                        lhs_ap = c_idn if widn else WER[ring][:, fc * 128:(fc + 1) * 128]
                        rhs_ap = (X1F[:, d_ * T + iq * ICW: d_ * T + iq * ICW + ICW]
                                  if xdirect else XER[xr])
                        mmw = None
                        if fi == 0:
                            mmw = [("seW%d" % ring, sew), ("svB3e", svv)]
                        if fc == 0:
                            mmw = (mmw or []) + first_mm_extra
                        P.emit("tensor",
                               lambda e, pbank=pbank, lhs_ap=lhs_ap, rhs_ap=rhs_ap, st=(e_ == 0 and d_ == 0):
                               e.matmul(pbank, lhs_ap, rhs_ap, start=st, stop=False),
                               wait=mmw,
                               inc="stB3r%d" % ring if fi == NF - 1 else None)
            for fc in range(NF):
                pbank = (pA if fc < 4 else pB)[:, (fc % 4) * 512:((fc % 4) + 1) * 512]
                if skipadds:
                    P.emit("tensor",
                           lambda e, pbank=pbank, fc=fc, iq=iq:
                           e.matmul(pbank, c_bexp[:, fc * 128:(fc + 1) * 128],
                                    combT[:, iq * ICW:(iq + 1) * ICW], start=False, stop=True),
                           wait=[("ssB2f", sv_atb)] if fc == 0 else None,
                           inc="stB3m")
                    continue
                P.emit("tensor",
                       lambda e, pbank=pbank, fc=fc, iq=iq:
                       e.matmul(pbank, c_bexp[:, fc * 128:(fc + 1) * 128],
                                combT[:, iq * ICW:(iq + 1) * ICW], start=False, stop=False))
                P.emit("tensor",
                       lambda e, pbank=pbank, fc=fc, iq=iq:
                       e.matmul(pbank, c_idn, ATB[:, fc * T + iq * ICW: fc * T + iq * ICW + ICW],
                                start=False, stop=False),
                       wait=[("ssB2f", sv_atb)] if fc == 0 else None)
                P.emit("tensor",
                       lambda e, pbank=pbank, fc=fc, iq=iq:
                       e.matmul(pbank, c_idn, X1F[:, fc * T + iq * ICW: fc * T + iq * ICW + ICW],
                                start=False, stop=True),
                       wait=([("sxfp%d" % fc, sxf_piece[fc])] if iq == 0 else None),
                       inc="stB3m")
            stv = P.val("stB3m")
            for fc in range(NF):
                pbank = (pA if fc < 4 else pB)[:, (fc % 4) * 512:((fc % 4) + 1) * 512]
                P.emit("vector",
                       lambda e, pbank=pbank, fc=fc: e.tensor_copy(
                           Y2B[:, fc * 512:(fc + 1) * 512], pbank),
                       wait=[("stB3m", stv)] if fc == 0 else None,
                       inc="svB3y" if fc == NF - 1 else None)
            sv_y2 = P.val("svB3y")
            if debug:
                P.dma("gpsimd", dbg4[s].rearrange("(f p) t -> p f t", p=128)[:, :, iq * ICW:(iq + 1) * ICW],
                      Y2B.rearrange("p (f t) -> p f t", f=NF), "sdbg2",
                      wait=[("svB3y", sv_y2)])
            for fc in range(NF):
                P.emit("vector",
                       lambda e, fc=fc: e.tensor_tensor(
                           YSQB[:, fc * 512:(fc + 1) * 512],
                           Y2B[:, fc * 512:(fc + 1) * 512],
                           Y2B[:, fc * 512:(fc + 1) * 512], op=OP.mult),
                       inc="svB3q" if fc == NF - 1 else None)
            sv_q2 = P.val("svB3q")
            for fc in range(NF):
                P.emit("tensor",
                       lambda e, fc=fc: e.matmul(pA[0:1, 0:512], c_ones[:, 0:1],
                                                 Y2B[:, fc * 512:(fc + 1) * 512],
                                                 start=(fc == 0), stop=(fc == NF - 1)),
                       wait=[("svB3y", sv_y2)] if fc == 0 else None)
            for fc in range(NF):
                P.emit("tensor",
                       lambda e, fc=fc: e.matmul(pA[0:1, 512:1024], c_ones[:, 0:1],
                                                 YSQB[:, fc * 512:(fc + 1) * 512],
                                                 start=(fc == 0), stop=(fc == NF - 1)),
                       wait=[("svB3q", sv_q2)] if fc == 0 else None,
                       inc="stB3st" if fc == NF - 1 else None)
            stv = P.val("stB3st")
            if iq + 1 < NIC:
                cb_next[0] = cb_emit(iq + 1)
            P.emit("scalar", lambda e: e.mul(mean_r, pA[0:1, 0:512], 1.0 / D),
                   wait=[("stB3st", stv)])
            P.emit("scalar", lambda e: e.mul(es2_r, pA[0:1, 512:1024], 1.0 / D), inc="ssL")
            ssv = P.val("ssL")
            P.emit("vector", lambda e: e.tensor_tensor(msq_r, mean_r, mean_r, op=OP.mult),
                   wait=[("ssL", ssv)])
            P.emit("vector", lambda e: e.drain())
            P.emit("vector", lambda e: e.tensor_tensor(var_r, es2_r, msq_r, op=OP.subtract),
                   inc="svL")
            svv = P.val("svL")
            P.emit("scalar", lambda e: e.activation(lnv_r, var_r, AF.Ln, bias=eps_r),
                   wait=[("svL", svv)])
            P.emit("scalar", lambda e: e.drain())
            P.emit("scalar", lambda e: e.activation(rstd_r, lnv_r, AF.Exp, scale=-0.5),
                   inc="ssL2")
            ssv = P.val("ssL2")
            P.emit("vector", lambda e: e.tensor_copy(a_bf, rstd_r), wait=[("ssL2", ssv)])
            P.emit("vector", lambda e: e.scalar_tensor_tensor(
                bneg_r, mean_r, -1.0, rstd_r, op0=OP.mult, op1=OP.mult))
            P.emit("vector", lambda e: e.drain())
            P.emit("vector", lambda e: e.tensor_copy(b_bf, bneg_r), inc="svL2")
            svv = P.val("svL2")
            P.emit("tensor", lambda e: e.matmul(pA[:, 0:512], c_ones[0:1, 0:128], a_bf,
                                                start=True, stop=True),
                   wait=[("svL2", svv)])
            P.emit("tensor", lambda e: e.matmul(pA[:, 512:1024], c_ones[0:1, 0:128], b_bf,
                                                start=True, stop=True), inc="stB3b")
            stv = P.val("stB3b")
            P.emit("vector", lambda e: e.tensor_copy(a_sb, pA[:, 0:512]),
                   wait=[("stB3b", stv)])
            P.emit("vector", lambda e: e.tensor_copy(b_sb, pA[:, 512:1024]),
                   inc="svB3L3")
            P.emit("vector", lambda e: e.drain())

            def mk_apply(fc0, iq=iq):
                def go():
                    wout = [("sout", P.val("sout"))] if fc0 == 0 else None
                    for fc in range(fc0, fc0 + 2):
                        P.emit("vector",
                               lambda e, fc=fc: e.tensor_tensor(
                                   T1OUT[:, fc * 512:(fc + 1) * 512],
                                   Y2B[:, fc * 512:(fc + 1) * 512], a_sb, op=OP.mult),
                               wait=wout if fc == fc0 else None)
                    for fc in range(fc0, fc0 + 2):
                        P.emit("vector",
                               lambda e, fc=fc: e.tensor_tensor(
                                   T1OUT[:, fc * 512:(fc + 1) * 512],
                                   T1OUT[:, fc * 512:(fc + 1) * 512],
                                   b_sb, op=OP.add),
                               inc="svB3o" if fc == fc0 + 1 and fc0 == NF - 2 else None)
                    if fc0 == NF - 2:
                        svv = P.val("svB3o")
                        P.dma("sync",
                              out_hbm[s].rearrange("(f p) t -> p f t", p=128)[:, :, iq * ICW:(iq + 1) * ICW],
                              T1OUT.rearrange("p (f t) -> p f t", f=NF),
                              "sout", wait=[("svB3o", svv)])
                return go

            if iq == NIC - 1:
                for fc0 in range(0, NF, 2):
                    mk_apply(fc0)()
                pend_ap[0] = []
            else:
                pend_ap[0] = [mk_apply(fc0) for fc0 in range(0, NF, 2)]

    if debug:
        P.dma("gpsimd", dbg1, x1_scr, "sdbg")
        P.emit("gpsimd", lambda e: None,
               wait=[("sdbg", P.val("sdbg")), ("sdbg2", P.val("sdbg2"))])
    P.emit("sync", lambda e: None, wait=[("sout", P.val("sout"))])

    with nc.Block() as block:
        @block.tensor
        def _(eng):
            for fn in P.q["tensor"]:
                fn(eng)

        @block.vector
        def _(eng):
            for fn in P.q["vector"]:
                fn(eng)

        @block.scalar
        def _(eng):
            for fn in P.q["scalar"]:
                fn(eng)

        @block.gpsimd
        def _(eng):
            for fn in P.q["gpsimd"]:
                fn(eng)

        @block.sync
        def _(eng):
            for fn in P.q["sync"]:
                fn(eng)

    return nc


_NC_CACHE = [None]


def _host_constants():
    pos = np.arange(T, dtype=np.float64)[:, None]
    w = np.arange(32, dtype=np.float64)[None, :]
    ang = np.power(10000.0, -w / 32.0)
    rad = (pos * ang).astype(np.float32)
    cos = np.cos(rad).T.astype(np.float32)
    sin = np.sin(rad).T.astype(np.float32)
    ctab = np.ones((128, T), dtype=np.float32)
    ctab[0:32] = cos
    ctab[32:64] = cos
    stab = np.zeros((128, T), dtype=np.float32)
    stab[0:32] = -sin
    stab[32:64] = sin
    perm = np.zeros((128, 128), dtype=np.float32)
    for i in range(32):
        perm[32 + i, i] = 1.0
        perm[i, 32 + i] = 1.0
    r = np.arange(128)[:, None]
    u = np.arange(896)[None, :]
    stair = (r <= u - 384).astype(np.float32)
    idn = np.eye(128, dtype=np.float32)
    sel = np.zeros((8, 1024), dtype=np.float32)
    for e in range(8):
        sel[e, e * 128:(e + 1) * 128] = 1.0
    return ctab, stab, perm, stair, idn, sel


def kernel(L0_attn_x, L0_attn_wq, L0_attn_wk, L0_attn_wv, L0_attn_wo,
           L0_ffn_W_qkv, L0_ffn_b_qkv, L0_ffn_W_gate, L0_ffn_b_gate,
           L0_ffn_W_experts, L0_ffn_b_experts):
    x = np.ascontiguousarray(np.asarray(L0_attn_x, dtype=np.float32).transpose(0, 2, 1))
    wq = np.ascontiguousarray(np.asarray(L0_attn_wq, dtype=np.float32).reshape(D, D))
    wk = np.ascontiguousarray(np.asarray(L0_attn_wk, dtype=np.float32).reshape(D, D))
    wv = np.ascontiguousarray(np.asarray(L0_attn_wv, dtype=np.float32).reshape(D, D))
    wo = np.ascontiguousarray(np.asarray(L0_attn_wo, dtype=np.float32))
    wqkv = np.ascontiguousarray(np.asarray(L0_ffn_W_qkv, dtype=np.float32))
    bqkv = np.ascontiguousarray(np.asarray(L0_ffn_b_qkv, dtype=np.float32).reshape(1, 3 * D))
    wgate = np.ascontiguousarray(np.asarray(L0_ffn_W_gate, dtype=np.float32))
    bgate = np.ascontiguousarray(np.asarray(L0_ffn_b_gate, dtype=np.float32).reshape(1, NE))
    wexp = np.ascontiguousarray(np.asarray(L0_ffn_W_experts, dtype=np.float32))
    bexp = np.ascontiguousarray(np.asarray(L0_ffn_b_experts, dtype=np.float32))
    ctab, stab, perm, stair, idn, sel = _host_constants()

    if _NC_CACHE[0] is None:
        _NC_CACHE[0] = build_program()
    nc = _NC_CACHE[0]

    in_maps = []
    for c in range(NCORES):
        in_maps.append({
            "x": np.ascontiguousarray(x[c * SEQS:(c + 1) * SEQS]),
            "wq": wq, "wk": wk, "wv": wv, "wo": wo,
            "wqkv": wqkv, "bqkv": bqkv, "wgate": wgate, "bgate": bgate,
            "wexp": wexp, "bexp": bexp,
            "ctab": ctab, "stab": stab, "perm": perm, "stair": stair, "idn": idn, "sel": sel,
        })
    res = run_bass_kernel_spmd(nc, in_maps, core_ids=list(range(NCORES)))
    outs = [r["out"] for r in res.results]
    full = np.concatenate(outs, axis=0)
    return np.ascontiguousarray(full.transpose(0, 2, 1))


# version 19
# speedup vs baseline: 1.0369x; 1.0249x over previous
import numpy as np
import concourse.bass as bass
import concourse.mybir as mybir
from concourse.bass_utils import run_bass_kernel_spmd

F32 = mybir.dt.float32
BF16 = mybir.dt.bfloat16
AF = mybir.ActivationFunctionType
OP = mybir.AluOpType
AX = mybir.AxisListType

NCORES = 8
SEQS = 2
T = 2048
D = 1024
NF = 8
ICW = 512
NIC = T // ICW
NJT = T // 128
HA = 8
HB = 16
NE = 8
SCL_A = 1.0 / 128.0
SCL_B = 0.125
EPS = 1e-5
VST = HB * 65


class Prog:
    ENGS = ("tensor", "vector", "scalar", "gpsimd", "sync")

    def __init__(self, nc):
        self.nc = nc
        self.q = {e: [] for e in self.ENGS}
        self.sem_handles = {}
        self.cnt = {}
        self.fences = {}

    def sem(self, name):
        if name not in self.sem_handles:
            self.sem_handles[name] = self.nc.alloc_semaphore(name)
            self.cnt[name] = 0
        return self.sem_handles[name]

    def emit(self, eng, fn, wait=None, inc=None, inc_n=1):
        sems = self.sem_handles
        if inc is not None:
            self.sem(inc)
            self.cnt[inc] += inc_n
        wait = [(s, v) for (s, v) in (wait or []) if v > 0]
        for s, _ in wait:
            self.sem(s)

        def run(engine, fn=fn, wait=wait, inc=inc, inc_n=inc_n):
            for s, v in wait:
                engine.wait_ge(sems[s], v)
            ins = fn(engine)
            if inc is not None and ins is not None:
                ins.then_inc(sems[inc], inc_n)

        self.q[eng].append(run)

    def val(self, name):
        self.sem(name)
        return self.cnt[name]

    def dma(self, eng, out, in_, sem, wait=None):
        self.emit(eng, lambda e, o=out, i=in_: e.dma_start(out=o, in_=i),
                  wait=wait, inc=sem, inc_n=16)
        return self.cnt[sem]

    def release(self, eng, sem):
        self.emit(eng, lambda e: e.nop(), inc=sem)
        return self.cnt[sem]

    def barrier(self):
        marks = []
        for e, sname in (("vector", "bv"), ("scalar", "bs"),
                         ("gpsimd", "bg"), ("sync", "by")):
            fence = self.fences.get(e)
            if fence is not None:
                self.emit(e, fence, inc=sname)
            else:
                self.emit(e, lambda eng: eng.nop(), inc=sname)
            marks.append((sname, self.cnt[sname]))
        self.emit("tensor", self.fences["tensor"], wait=list(marks), inc="bt")
        marks.append(("bt", self.cnt["bt"]))
        for e in ("tensor", "vector", "scalar"):
            self.emit(e, lambda eng: None, wait=marks)


def build_program(debug=False, direct_we=False, nocb=False, skipadds=False, widn=False, xdirect=False):
    nc = bass.Bass(target_bir_lowering=False, trn_type="TRN2")

    x_in = nc.dram_tensor("x", [SEQS, D, T], F32, kind="ExternalInput").ap()
    wq_in = nc.dram_tensor("wq", [D, D], F32, kind="ExternalInput").ap()
    wk_in = nc.dram_tensor("wk", [D, D], F32, kind="ExternalInput").ap()
    wv_in = nc.dram_tensor("wv", [D, D], F32, kind="ExternalInput").ap()
    wo_in = nc.dram_tensor("wo", [HA, 128, D], F32, kind="ExternalInput").ap()
    wqkv_in = nc.dram_tensor("wqkv", [D, 3 * D], F32, kind="ExternalInput").ap()
    bqkv_in = nc.dram_tensor("bqkv", [1, 3 * D], F32, kind="ExternalInput").ap()
    wgate_in = nc.dram_tensor("wgate", [D, NE], F32, kind="ExternalInput").ap()
    bgate_in = nc.dram_tensor("bgate", [1, NE], F32, kind="ExternalInput").ap()
    wexp_in = nc.dram_tensor("wexp", [NE, D, D], F32, kind="ExternalInput").ap()
    bexp_in = nc.dram_tensor("bexp", [NE, D], F32, kind="ExternalInput").ap()
    ctab_in = nc.dram_tensor("ctab", [128, T], F32, kind="ExternalInput").ap()
    stab_in = nc.dram_tensor("stab", [128, T], F32, kind="ExternalInput").ap()
    perm_in = nc.dram_tensor("perm", [128, 128], F32, kind="ExternalInput").ap()
    stair_in = nc.dram_tensor("stair", [128, 896], F32, kind="ExternalInput").ap()
    idn_in = nc.dram_tensor("idn", [128, 128], F32, kind="ExternalInput").ap()
    sel_in = nc.dram_tensor("sel", [8, 1024], F32, kind="ExternalInput").ap()
    out_hbm = nc.dram_tensor("out", [SEQS, D, T], F32, kind="ExternalOutput").ap()

    x1_scr = nc.dram_tensor("x1scr", [SEQS, D, T], BF16).ap()
    dbg1 = (nc.dram_tensor("dbg1", [SEQS, D, T], F32, kind="ExternalOutput").ap()
            if debug else None)
    dbg2 = (nc.dram_tensor("dbg2", [SEQS, D, T], F32, kind="ExternalOutput").ap()
            if debug else None)
    dbg3 = (nc.dram_tensor("dbg3", [SEQS, 8, T], F32, kind="ExternalOutput").ap()
            if debug else None)
    dbg4 = (nc.dram_tensor("dbg4", [SEQS, D, T], F32, kind="ExternalOutput").ap()
            if debug else None)
    dbg5 = (nc.dram_tensor("dbg5", [NIC, 128, 4096], F32, kind="ExternalOutput").ap()
            if debug else None)
    we_scr = nc.dram_tensor("wescr", [NE, D, D], BF16).ap()
    wqkv_scr = nc.dram_tensor("wqkvscr", [D, 3 * D], BF16).ap()
    wq_scr = nc.dram_tensor("wqscr", [D, D], BF16).ap()
    wk_scr = nc.dram_tensor("wkscr", [D, D], BF16).ap()
    wv_scr = nc.dram_tensor("wvscr", [D, D], BF16).ap()
    wo_scr = nc.dram_tensor("woscr", [HA, 128, D], BF16).ap()

    c_ctab = nc.alloc_sbuf_tensor("c_ctab", [128, T], BF16).ap()
    c_stab = nc.alloc_sbuf_tensor("c_stab", [128, T], BF16).ap()
    c_perm = nc.alloc_sbuf_tensor("c_perm", [128, 128], BF16).ap()
    c_stair = nc.alloc_sbuf_tensor("c_stair", [128, 896], BF16).ap()
    c_idn = nc.alloc_sbuf_tensor("c_idn", [128, 128], BF16).ap()
    c_ones = nc.alloc_sbuf_tensor("c_ones", [128, 512], BF16).ap()
    c_wgate = nc.alloc_sbuf_tensor("c_wgate", [128, 64], BF16).ap()
    c_bgate = nc.alloc_sbuf_tensor("c_bgate", [1, 8], BF16).ap()
    c_brow = nc.alloc_sbuf_tensor("c_brow", [1, 3 * D], BF16).ap()
    c_bexp = nc.alloc_sbuf_tensor("c_bexp", [8, D], BF16).ap()
    c_sel = nc.alloc_sbuf_tensor("c_sel", [8, 1024], BF16).ap()
    combT = nc.alloc_sbuf_tensor("combT", [8, T], BF16).ap()
    mf = nc.alloc_sbuf_tensor("miscf", [128, 3264], F32).ap()
    a_bf = nc.alloc_sbuf_tensor("a_bf", [1, 512], BF16).ap()
    b_bf = nc.alloc_sbuf_tensor("b_bf", [1, 512], BF16).ap()
    bcB = nc.alloc_sbuf_tensor("bcB", [128, 512], BF16).ap()
    rsc = nc.alloc_sbuf_tensor("ropescr2", [128, 1024], BF16).ap()
    lrow = nc.alloc_sbuf_tensor("lrow", [128, 512], BF16).ap()
    gcomb_bf = nc.alloc_sbuf_tensor("gcombbf", [128, 32], BF16).ap()
    bnc = [nc.alloc_sbuf_tensor(f"bnc{i}", [128, 512], BF16).ap() for i in range(2)]

    R1 = nc.alloc_sbuf_tensor("R1", [128, 16384], BF16).ap()
    R2 = nc.alloc_sbuf_tensor("R2", [128, 16384], BF16).ap()
    R3 = nc.alloc_sbuf_tensor("R3", [128, 16640], BF16).ap()
    POOL = nc.alloc_sbuf_tensor("POOL", [128, 32768], BF16).ap()

    pA = nc.alloc_psum_tensor("pA", [128, 2048], F32).ap()
    pB = nc.alloc_psum_tensor("pB", [128, 2048], F32).ap()

    K_, Q_, V_ = R1, R2, R3
    X1F = R1
    T1OUT = R2[:, 0:8192].bitcast(F32)
    Y2B = R2[:, 8192:12288]
    YSQB = R2[:, 12288:16384]
    CBSB = R3[:, 0:4096]
    WER = [R3[:, 4096 + 1024 * i: 4096 + 1024 * (i + 1)] for i in range(4)]
    XER = [R3[:, 8192 + 512 * i: 8192 + 512 * (i + 1)] for i in range(4)]

    wq_t = POOL[:, 0:8192]
    wk_t = POOL[:, 8192:16384]
    wv_t = POOL[:, 16384:24576]
    x0c = [POOL[:, 24576:28672], POOL[:, 28672:32768]]
    wo_t = POOL[:, 0:8192]
    ptil = POOL[:, 8192:10240]
    oA = POOL[:, 10240:14336]
    ysqA = POOL[:, 10240:14336]
    ybuf = POOL[:, 14336:18432]
    x1buf = [POOL[:, 18432:22528], POOL[:, 22528:26624]]
    x0c2 = POOL[:, 26624:30720]
    wqkv_t = POOL[:, 0:12288]
    x1c = [POOL[:, 12288:16384], POOL[:, 16384:20480]]
    ptilB = POOL[:, 0:4096]
    ATB = POOL[:, 4096:20480]
    OEV = POOL[:, 20480:21504]
    a_sb = POOL[:, 0:1024].bitcast(F32)
    b_sb = POOL[:, 1024:2048].bitcast(F32)

    linv = mf[:, 0:512]
    mean_r = mf[0:1, 512:1024]
    es2_r = mf[0:1, 1024:1536]
    msq_r = mf[0:1, 1536:2048]
    var_r = mf[0:1, 2048:2560]
    lnv_r = mf[0:1, 1536:2048]
    rstd_r = mf[0:1, 2560:3072]
    bneg_r = mf[0:1, 1024:1536]
    linvo_r = mf[0:1, 2048:2560]
    linve_r = mf[64:65, 2048:2560]
    gexp = mf[:, 3072:3104]
    gsum = mf[:, 3104:3108]
    grs = mf[:, 3108:3112]
    gm1 = mf[:, 3112:3116]
    gm2 = mf[:, 3116:3120]
    gsm = mf[:, 3120:3152]
    gmk = mf[:, 3152:3184]
    gzap = mf[:, 3184:3216]
    negm_g = mf[:, 3216:3220]
    eps_r = mf[0:1, 3220:3221]

    P = Prog(nc)
    fence_f = mf[0:1, 3221:3222]
    fence_g = mf[0:1, 3222:3223]
    P.fences = {
        "tensor": lambda e: e.matmul(pA[0:1, 0:1], c_ones[0:1, 0:1], c_ones[0:1, 0:1],
                                     start=True, stop=True),
        "vector": lambda e: e.tensor_copy(fence_f, fence_g),
        "scalar": lambda e: e.copy(fence_g, fence_f),
    }

    def fm(t, chunk, ic, w=ICW):
        return t[:, chunk * T + ic * w: chunk * T + ic * w + w]

    P.dma("gpsimd", c_ctab, ctab_in, "swC")
    P.dma("gpsimd", c_stab, stab_in, "swC")
    P.dma("gpsimd", c_perm, perm_in, "swC")
    P.dma("gpsimd", c_stair, stair_in, "swC")
    P.dma("gpsimd", c_idn, idn_in, "swC")
    P.dma("gpsimd", c_wgate.rearrange("p (c e) -> p c e", c=8),
          wgate_in.rearrange("(c p) e -> p c e", p=128), "swC")
    P.dma("gpsimd", c_bgate, bgate_in, "swC")
    P.dma("gpsimd", c_brow, bqkv_in, "swC")
    P.dma("gpsimd", c_bexp, bexp_in, "swC")
    P.dma("gpsimd", c_sel, sel_in, "swC")
    swC_total = P.val("swC")
    P.emit("vector", lambda e: e.memset(c_ones, 1.0))
    P.emit("vector", lambda e: e.memset(eps_r, float(EPS)), inc="svones")
    sv_ones = P.val("svones")

    bounce_jobs = []
    for c in range(8):
        for j in range(6):
            bounce_jobs.append((wqkv_in[c * 128:(c + 1) * 128, j * 512:(j + 1) * 512],
                                wqkv_scr[c * 128:(c + 1) * 128, j * 512:(j + 1) * 512]))
    wqkv_scr_njobs = len(bounce_jobs)
    for w_in, w_sc in ((wq_in, wq_scr), (wk_in, wk_scr), (wv_in, wv_scr)):
        for c in range(8):
            for j in range(2):
                bounce_jobs.append((w_in[c * 128:(c + 1) * 128, j * 512:(j + 1) * 512],
                                    w_sc[c * 128:(c + 1) * 128, j * 512:(j + 1) * 512]))
    for h in range(HA):
        for j in range(2):
            bounce_jobs.append((wo_in[h, :, j * 512:(j + 1) * 512],
                                wo_scr[h, :, j * 512:(j + 1) * 512]))
    for e_ in range(NE):
        for d_ in range(NF):
            for hf in range(2):
                bounce_jobs.append((wexp_in[e_, d_ * 128:(d_ + 1) * 128, hf * 512:(hf + 1) * 512],
                                    we_scr[e_, d_ * 128:(d_ + 1) * 128, hf * 512:(hf + 1) * 512]))
    N_JOBS = len(bounce_jobs)
    wescr_idx = [0]

    def emit_wescr_chunk():
        idx = wescr_idx[0]
        if idx >= N_JOBS:
            return
        wescr_idx[0] += 1
        srcp, dstp = bounce_jobs[idx]
        par = idx % 2
        w = [("sbs%d" % par, 16 * (idx // 2))] if idx >= 2 else None
        P.dma("gpsimd", bnc[par], srcp, "sbl%d" % par, wait=w)
        P.dma("gpsimd", dstp, bnc[par], "sbs%d" % par,
              wait=[("sbl%d" % par, 16 * (idx // 2 + 1))])

    def bounce_done_wait(njobs):
        n0 = (njobs + 1) // 2
        n1 = njobs // 2
        return [("sbs0", 16 * n0), ("sbs1", 16 * n1)]

    def pump_wescr(n):
        for _ in range(n):
            emit_wescr_chunk()

    x0gate = {}
    x1gate = {}
    pa_gate = {}
    pb_gate = {}
    for s in range(SEQS):
        swa_w = None
        if s > 0:
            swa_w = [("stB3m", P.val("stB3m")), ("ssB2f", P.val("ssB2f")),
                     ("stB1g", P.val("stB1g")), ("ssB2", P.val("ssB2")),
                     ("svB3o", P.val("svB3o"))]
        for hh in range(2):
            if s == 0:
                P.dma("gpsimd", wq_t.rearrange("p (c n) -> p c n", c=8)[:, hh * 4:(hh + 1) * 4],
                      wq_in.rearrange("(c p) n -> p c n", p=128)[:, hh * 4:(hh + 1) * 4],
                      "swAh%d" % hh, wait=swa_w if hh == 0 else None)
                P.dma("gpsimd", wk_t.rearrange("p (c n) -> p c n", c=8)[:, hh * 4:(hh + 1) * 4],
                      wk_in.rearrange("(c p) n -> p c n", p=128)[:, hh * 4:(hh + 1) * 4],
                      "swAh%d" % hh)
                P.dma("gpsimd", wv_t.rearrange("p (c n) -> p c n", c=8)[:, hh * 4:(hh + 1) * 4],
                      wv_in.rearrange("(c p) n -> p c n", p=128)[:, hh * 4:(hh + 1) * 4],
                      "swAh%d" % hh)
            else:
                P.dma("sync", wq_t.rearrange("p (c n) -> p c n", c=8)[:, hh * 4:(hh + 1) * 4],
                      wq_scr.rearrange("(c p) n -> p c n", p=128)[:, hh * 4:(hh + 1) * 4],
                      "swAh%d" % hh, wait=swa_w if hh == 0 else None)
                P.dma("sync", wk_t.rearrange("p (c n) -> p c n", c=8)[:, hh * 4:(hh + 1) * 4],
                      wk_scr.rearrange("(c p) n -> p c n", p=128)[:, hh * 4:(hh + 1) * 4],
                      "swAh%d" % hh)
                P.dma("sync", wv_t.rearrange("p (c n) -> p c n", c=8)[:, hh * 4:(hh + 1) * 4],
                      wv_scr.rearrange("(c p) n -> p c n", p=128)[:, hh * 4:(hh + 1) * 4],
                      "swAh%d" % hh)
        swA_h = [P.val("swAh0"), P.val("swAh1")]

        pa_gate = dict(pa_gate) if s > 0 else {}
        pb_gate = dict(pb_gate) if s > 0 else {}
        qk_idx = [0]
        for ic in range(NIC):
            par = ic % 2
            w = [("stA1v", x0gate.get(par, 0))]
            if s > 0 and ic < 2:
                w += [("sspill1", P.val("sspill1")), ("stA2y", P.val("stA2y"))]
            P.dma("gpsimd", x0c[par].rearrange("p (f t) -> p f t", f=NF),
                  x_in[s].rearrange("(f p) t -> p f t", p=128)[:, :, ic * ICW:(ic + 1) * ICW],
                  "sx%d" % par, wait=w)
            sxv = P.val("sx%d" % par)
            pump_wescr(6)

            base = [("sx%d" % par, sxv), ("swC", swC_total),
                    ("svones", sv_ones)]
            for kind in ("q", "k"):
                wt = wq_t if kind == "q" else wk_t
                dst = Q_ if kind == "q" else K_
                for dt in range(NF):
                    pp = qk_idx[0] % 4
                    qk_idx[0] += 1
                    ps = pA[:, pp * 512:(pp + 1) * 512]
                    wv_ = [pa_gate[pp]] if pp in pa_gate else []
                    for kc in range(NF):
                        kw = (base + wv_) if kc == 0 else None
                        if ic == 0 and dt < 2 and kc in (0, 4):
                            kw = (kw or []) + [("swAh%d" % (kc // 4), swA_h[kc // 4])]
                        P.emit("tensor",
                               lambda e, ps=ps, wt=wt, kc=kc, dt=dt, par=par:
                               e.matmul(ps, wt[:, kc * 1024 + dt * 128: kc * 1024 + dt * 128 + 128],
                                        x0c[par][:, kc * 512:(kc + 1) * 512],
                                        start=(kc == 0), stop=(kc == NF - 1)),
                               wait=kw,
                               inc="stA1" if kc == NF - 1 else None)
                    stv = P.val("stA1")
                    P.emit("scalar",
                           lambda e, ps=ps, dst=dst, dt=dt, ic=ic: e.copy(fm(dst, dt, ic), ps),
                           wait=[("stA1", stv)], inc="svA1")
                    pa_gate[pp] = ("svA1", P.val("svA1"))
            for jt4 in range(4):
                jt = ic * 4 + jt4
                pp = jt4 % 2
                ps = pB[:, pp * 1024:(pp + 1) * 1024]
                wv_ = [g for b in (2 * pp, 2 * pp + 1) for g in ([pb_gate[b]] if b in pb_gate else [])]
                if ic == 0 and jt4 == 0:
                    wv_ = wv_ + [("swAh0", swA_h[0]), ("swAh1", swA_h[1])]
                for kc in range(NF):
                    for half in range(2):
                        P.emit("tensor",
                               lambda e, ps=ps, kc=kc, jt4=jt4, par=par, half=half:
                               e.matmul(ps[:, half * 512:(half + 1) * 512],
                                        x0c[par][:, kc * 512 + jt4 * 128: kc * 512 + jt4 * 128 + 128],
                                        wv_t[:, kc * 1024 + half * 512: kc * 1024 + (half + 1) * 512],
                                        start=(kc == 0), stop=(kc == NF - 1)),
                               wait=(base + wv_) if (kc == 0 and half == 0) else None,
                               inc="stA1v" if (kc == NF - 1 and half == 1) else None)
                stv = P.val("stA1v")
                P.emit("scalar",
                       lambda e, ps=ps, jt=jt: e.copy(V_[:, jt * 1024:(jt + 1) * 1024], ps),
                       wait=[("stA1v", stv)], inc="svA1v")
                pb_gate[2 * pp] = ("svA1v", P.val("svA1v"))
                pb_gate[2 * pp + 1] = ("svA1v", P.val("svA1v"))
            x0gate[par] = P.val("stA1v")

            svq = P.val("svA1")
            first = True
            rslot = [0]
            for X in (Q_, K_):
                for pr in range(4):
                    h0 = 2 * pr
                    sl0, sl1 = rslot[0] % 4, (rslot[0] + 1) % 4
                    rslot[0] += 2
                    rvals = {}
                    for (h, slot) in ((h0, sl0), (h0 + 1, sl1)):
                        rps = pB[:, slot * 512:(slot + 1) * 512]
                        w = [pb_gate[slot]] if slot in pb_gate else []
                        if first:
                            w += [("svA1", svq), ("swC", swC_total)]
                        P.emit("tensor",
                               lambda e, rps=rps, X=X, h=h, ic=ic:
                               e.matmul(rps, c_perm,
                                        X[:, h * T + ic * ICW: h * T + ic * ICW + ICW],
                                        start=True, stop=True),
                               wait=w, inc="strp")
                        first = False
                        rvals[h] = P.val("strp")
                    for (h, slot, ri) in ((h0, sl0, 0), (h0 + 1, sl1, 1)):
                        rps = pB[:, slot * 512:(slot + 1) * 512]
                        P.emit("vector",
                               lambda e, X=X, h=h, ic=ic:
                               e.tensor_tensor(
                                   X[:, h * T + ic * ICW: h * T + ic * ICW + ICW],
                                   X[:, h * T + ic * ICW: h * T + ic * ICW + ICW],
                                   c_ctab[:, ic * ICW:(ic + 1) * ICW], op=OP.mult),
                               wait=[("strp", rvals[h])])
                        P.emit("vector",
                               lambda e, rps=rps, ri=ri, ic=ic:
                               e.tensor_tensor(
                                   rsc[:, ri * 512:(ri + 1) * 512], rps,
                                   c_stab[:, ic * ICW:(ic + 1) * ICW], op=OP.mult),
                               inc="svrp")
                        pb_gate[slot] = ("svrp", P.val("svrp"))
                    P.emit("vector", lambda e: e.drain())
                    for (h, ri) in ((h0, 0), (h0 + 1, 1)):
                        P.emit("vector",
                               lambda e, X=X, h=h, ic=ic, ri=ri:
                               e.tensor_tensor(
                                   X[:, h * T + ic * ICW: h * T + ic * ICW + ICW],
                                   X[:, h * T + ic * ICW: h * T + ic * ICW + ICW],
                                   rsc[:, ri * 512:(ri + 1) * 512], op=OP.add))
        P.barrier()

        if s == 0:
            P.dma("gpsimd", wo_t.rearrange("p (h m) -> p h m", h=HA),
                  wo_in.rearrange("h p m -> p h m"), "swO",
                  wait=[("stA1", P.val("stA1"))])
        else:
            P.dma("sync", wo_t.rearrange("p (h m) -> p h m", h=HA),
                  wo_scr.rearrange("h p m -> p h m"), "swO",
                  wait=[("stA1", P.val("stA1"))])
        swO_total = P.val("swO")

        A2expv = {}
        pend_ln = [None]
        x0gate2 = [0 if s == 0 else x0gate2[0]]
        for ic in range(NIC):
            njt = 4 * (ic + 1)
            w = [("stA2y", x0gate2[0])] if x0gate2[0] > 0 else []
            if ic == 0:
                w = list(w) + [("stA1v", P.val("stA1v"))]
            P.dma("gpsimd", x0c2.rearrange("p (f t) -> p f t", f=NF),
                  x_in[s].rearrange("(f p) t -> p f t", p=128)[:, :, ic * ICW:(ic + 1) * ICW],
                  "sx2", wait=w)
            sx2v = P.val("sx2")
            pump_wescr(6)

            A2maskv = {}

            for h in range(HA):
                if h == 2 and pend_ln[0] is not None:
                    pend_ln[0]()
                    pend_ln[0] = None
                ngrp = (njt + 1) // 2

                def a2_s(jg, h=h, ic=ic, njt=njt):
                    gpar = jg % 2
                    ps_s = pA[:, gpar * 1024:(gpar + 1) * 1024]
                    nt = min(2, njt - 2 * jg)
                    wargs = [("ssA2", A2expv.get(gpar, 0))]
                    for t_ in range(nt):
                        jt = 2 * jg + t_
                        P.emit("tensor",
                               lambda e, ps_s=ps_s, t_=t_, jt=jt, h=h, ic=ic:
                               e.matmul(ps_s[:, t_ * 512:(t_ + 1) * 512],
                                        K_[:, h * T + jt * 128: h * T + jt * 128 + 128],
                                        Q_[:, h * T + ic * ICW: h * T + ic * ICW + ICW],
                                        start=True, stop=True),
                               wait=wargs if t_ == 0 else None,
                               inc="stA2s" if t_ == nt - 1 else None)
                    stv = P.val("stA2s")
                    P.emit("scalar",
                           lambda e, gpar=gpar, ps_s=ps_s, nt=nt:
                           e.activation(ptil[:, gpar * 1024: gpar * 1024 + nt * 512],
                                        ps_s[:, 0:nt * 512], AF.Exp, scale=SCL_A),
                           wait=[("stA2s", stv)], inc="ssA2")
                    A2expv[gpar] = P.val("ssA2")
                    masked = (2 * jg) >= 4 * ic
                    if masked:
                        ssv = P.val("ssA2")
                        for t_ in range(nt):
                            jt = 2 * jg + t_
                            toff = (jt - 4 * ic) * 128
                            P.emit("vector",
                                   lambda e, gpar=gpar, t_=t_, toff=toff:
                                   e.tensor_tensor(
                                       ptil[:, gpar * 1024 + t_ * 512: gpar * 1024 + (t_ + 1) * 512],
                                       ptil[:, gpar * 1024 + t_ * 512: gpar * 1024 + (t_ + 1) * 512],
                                       c_stair[:, 384 - toff: 384 - toff + 512],
                                       op=OP.mult),
                                   wait=[("ssA2", ssv)] if t_ == 0 else None,
                                   inc="svA2m" if t_ == nt - 1 else None)
                        A2maskv[gpar] = P.val("svA2m")
                    else:
                        A2maskv[gpar] = None

                if s == 0:
                    pump_wescr(3)
                a2_s(0)
                for jg in range(ngrp):
                    gpar = jg % 2
                    nt = min(2, njt - 2 * jg)
                    if jg + 1 < ngrp:
                        a2_s(jg + 1)
                    if A2maskv[gpar] is not None:
                        pwait = [("svA2m", A2maskv[gpar])]
                    else:
                        pwait = [("ssA2", A2expv[gpar])]
                    if jg == 0:
                        pwait = pwait + [("svA2o", P.val("svA2o"))]
                    for t_ in range(nt):
                        jt = 2 * jg + t_
                        P.emit("tensor",
                               lambda e, jt=jt, gpar=gpar, t_=t_, h=h:
                               e.matmul(pB[:, 0:512],
                                        V_[:, jt * 1024 + h * 128: jt * 1024 + h * 128 + 128],
                                        ptil[:, gpar * 1024 + t_ * 512: gpar * 1024 + (t_ + 1) * 512],
                                        start=(jt == 0), stop=(jt == njt - 1)),
                               wait=pwait if t_ == 0 else None)
                        P.emit("tensor",
                               lambda e, jt=jt, gpar=gpar, t_=t_:
                               e.matmul(pB[:, 512:1024],
                                        c_ones[:, 0:128],
                                        ptil[:, gpar * 1024 + t_ * 512: gpar * 1024 + (t_ + 1) * 512],
                                        start=(jt == 0), stop=(jt == njt - 1)),
                               inc="stA2o" if jt == njt - 1 else None)
                stv = P.val("stA2o")
                P.emit("scalar", lambda e: e.activation(linv, pB[:, 512:1024], AF.Ln),
                       wait=[("stA2o", stv)])
                P.emit("scalar", lambda e: e.drain())
                P.emit("scalar", lambda e: e.activation(linv, linv, AF.Exp, scale=-1.0),
                       inc="ssA2r")
                ssrv = P.val("ssA2r")
                ovwait = [("ssA2r", ssrv)]
                if h == 0 and ic > 0:
                    ovwait.append(("stA2st", P.val("stA2st")))
                P.emit("vector",
                       lambda e, h=h: e.tensor_tensor(
                           oA[:, h * 512:(h + 1) * 512], pB[:, 0:512], linv, op=OP.mult),
                       wait=ovwait, inc="svA2o")
            sv_heads = P.val("svA2o")
            evy = {}
            for mc in range(NF):
                ypar = mc % 2
                ps_y = pB[:, 1024 + ypar * 512: 1024 + (ypar + 1) * 512]
                wargs = [("swO", swO_total), ("svA2o", sv_heads), ("sx2", sx2v),
                         ("svA2y", evy.get(ypar, 0))]
                if mc == 0:
                    wargs = wargs + [("ssL", P.val("ssL")), ("svA2x1", P.val("svA2x1"))]
                for h in range(HA):
                    P.emit("tensor",
                           lambda e, ps_y=ps_y, h=h, mc=mc:
                           e.matmul(ps_y, wo_t[:, h * 1024 + mc * 128: h * 1024 + mc * 128 + 128],
                                    oA[:, h * 512:(h + 1) * 512],
                                    start=(h == 0), stop=False),
                           wait=wargs if h == 0 else None)
                P.emit("tensor",
                       lambda e, ps_y=ps_y, mc=mc:
                       e.matmul(ps_y, c_idn, x0c2[:, mc * 512:(mc + 1) * 512],
                                start=False, stop=True),
                       inc="stA2y")
                stv = P.val("stA2y")
                P.emit("vector",
                       lambda e, ps_y=ps_y, mc=mc: e.tensor_copy(ybuf[:, mc * 512:(mc + 1) * 512], ps_y),
                       wait=[("stA2y", stv)], inc="svA2y")
                evy[ypar] = P.val("svA2y")
            sv_y = P.val("svA2y")
            x0gate2[0] = P.val("stA2y")
            for mc in range(NF):
                P.emit("vector",
                       lambda e, mc=mc: e.tensor_tensor(
                           ysqA[:, mc * 512:(mc + 1) * 512],
                           ybuf[:, mc * 512:(mc + 1) * 512],
                           ybuf[:, mc * 512:(mc + 1) * 512], op=OP.mult),
                       wait=[("svA2y", sv_y)] if mc == 0 else None,
                       inc="svA2q" if mc == NF - 1 else None)
            sv_q = P.val("svA2q")
            for mc in range(NF):
                P.emit("tensor",
                       lambda e, mc=mc: e.matmul(pB[0:1, 1024:1536], c_ones[:, 0:1],
                                                 ybuf[:, mc * 512:(mc + 1) * 512],
                                                 start=(mc == 0), stop=(mc == NF - 1)),
                       wait=[("svA2y", sv_y)] if mc == 0 else None)
            for mc in range(NF):
                P.emit("tensor",
                       lambda e, mc=mc: e.matmul(pB[0:1, 1536:2048], c_ones[:, 0:1],
                                                 ysqA[:, mc * 512:(mc + 1) * 512],
                                                 start=(mc == 0), stop=(mc == NF - 1)),
                       wait=[("svA2q", sv_q)] if mc == 0 else None,
                       inc="stA2st" if mc == NF - 1 else None)

            def a2_ln(ic=ic, stv=P.val("stA2st"), sv_y=sv_y):
                P.emit("scalar", lambda e: e.mul(mean_r, pB[0:1, 1024:1536], 1.0 / D),
                       wait=[("stA2st", stv)])
                P.emit("scalar", lambda e: e.mul(es2_r, pB[0:1, 1536:2048], 1.0 / D), inc="ssL")
                ssv = P.val("ssL")
                P.emit("vector", lambda e: e.tensor_tensor(msq_r, mean_r, mean_r, op=OP.mult),
                       wait=[("ssL", ssv)])
                P.emit("vector", lambda e: e.drain())
                P.emit("vector", lambda e: e.tensor_tensor(var_r, es2_r, msq_r, op=OP.subtract),
                       inc="svL")
                svv = P.val("svL")
                P.emit("scalar", lambda e: e.activation(lnv_r, var_r, AF.Ln, bias=eps_r),
                       wait=[("svL", svv)])
                P.emit("scalar", lambda e: e.drain())
                P.emit("scalar", lambda e: e.activation(rstd_r, lnv_r, AF.Exp, scale=-0.5),
                       inc="ssL2")
                ssv = P.val("ssL2")
                P.emit("vector", lambda e: e.tensor_copy(a_bf, rstd_r), wait=[("ssL2", ssv)])
                P.emit("vector", lambda e: e.scalar_tensor_tensor(
                    bneg_r, mean_r, -1.0, rstd_r, op0=OP.mult, op1=OP.mult))
                P.emit("vector", lambda e: e.drain())
                P.emit("vector", lambda e: e.tensor_copy(b_bf, bneg_r), inc="svL2")
                svv = P.val("svL2")
                P.emit("tensor", lambda e: e.matmul(pB[:, 1024:1536], c_ones[0:1, 0:128], a_bf,
                                                    start=True, stop=True),
                       wait=[("svL2", svv), ("svA2y", sv_y)])
                P.emit("tensor", lambda e: e.matmul(pB[:, 1536:2048], c_ones[0:1, 0:128], b_bf,
                                                    start=True, stop=True), inc="stA2b")
                stv2 = P.val("stA2b")
                xpar = ic % 2
                wspill = [("sspill%d" % xpar, P.val("sspill%d" % xpar))]
                for mc in range(NF):
                    P.emit("vector",
                           lambda e, mc=mc, xpar=xpar: e.tensor_tensor(
                               x1buf[xpar][:, mc * 512:(mc + 1) * 512],
                               ybuf[:, mc * 512:(mc + 1) * 512], pB[:, 1024:1536], op=OP.mult),
                           wait=([("stA2b", stv2)] + wspill) if mc == 0 else None)
                for mc in range(NF):
                    P.emit("vector",
                           lambda e, mc=mc, xpar=xpar: e.tensor_tensor(
                               x1buf[xpar][:, mc * 512:(mc + 1) * 512],
                               x1buf[xpar][:, mc * 512:(mc + 1) * 512], pB[:, 1536:2048], op=OP.add),
                           inc="svA2x1" if mc == NF - 1 else None)
                svv = P.val("svA2x1")
                P.dma("sync",
                      x1_scr[s].rearrange("(f p) t -> p f t", p=128)[:, :, ic * ICW:(ic + 1) * ICW],
                      x1buf[xpar].rearrange("p (f t) -> p f t", f=NF),
                      "sspill%d" % xpar, wait=[("svA2x1", svv)])

            if ic == NIC - 1:
                a2_ln()
                pend_ln[0] = None
            else:
                pend_ln[0] = a2_ln

        spill_tot = [("sspill0", P.val("sspill0")), ("sspill1", P.val("sspill1"))]
        evb = {}
        evbv = {}
        for half in range(2):
            def emit_pieces(half=half):
                for kcp in range(8):
                    if half == 0 and kcp == 0:
                        wq_w = [("ssA2", P.val("ssA2")), ("stA2o", P.val("stA2o")),
                                ("stA2y", P.val("stA2y")), ("svA2o", P.val("svA2o")),
                                ("svA2q", P.val("svA2q")), ("stA2st", P.val("stA2st")),
                                ("ssB2", P.val("ssB2")), ("stB3m", P.val("stB3m"))]
                        if s == 0:
                            wq_w = wq_w + bounce_done_wait(wqkv_scr_njobs)
                    elif half == 1 and kcp == 0:
                        wq_w = [("stB1", P.val("stB1"))]
                    else:
                        wq_w = None
                    P.dma("sync", wqkv_t.rearrange("p (c n) -> p c n", c=8)[:, kcp:kcp + 1],
                          wqkv_scr.rearrange("(c p) n -> p c n", p=128)[:, kcp:kcp + 1,
                                                                        half * 1536:(half + 1) * 1536],
                          "swQk%d" % kcp, wait=wq_w)
                return [P.val("swQk%d" % kcp) for kcp in range(8)]

            swq_piece = None
            for ic in range(NIC):
                par = ic % 2
                gs_, gv_ = x1gate.get(par, ("stB1", 0))
                w = [(gs_, gv_)]
                if half == 0:
                    w = w + spill_tot
                P.dma("sync", x1c[par].rearrange("p (f t) -> p f t", f=NF),
                      x1_scr[s].rearrange("(f p) t -> p f t", p=128)[:, :, ic * ICW:(ic + 1) * ICW],
                      "sx1%d" % par, wait=w)
                if swq_piece is None:
                    swq_piece = emit_pieces()
                sx1v = P.val("sx1%d" % par)
                pump_wescr(12)
                base = [("swQk7", swq_piece[7]), ("sx1%d" % par, sx1v)]
                for oci in range(12):
                    oc = half * 12 + oci
                    if oc >= 16:
                        continue
                    dst = Q_ if oc < 8 else K_
                    dt = oc % 8
                    pp = oci % 2
                    ps = pA[:, pp * 512:(pp + 1) * 512]
                    wv_ = [("svB1", evb.get(pp, 0))]
                    for kc in range(NF):
                        kw = (base + wv_) if kc == 0 else None
                        if ic == 0 and oci < 2:
                            kw = (kw or []) + [("swQk%d" % kc, swq_piece[kc])]
                        P.emit("tensor",
                               lambda e, ps=ps, kc=kc, oci=oci, par=par:
                               e.matmul(ps, wqkv_t[:, kc * 1536 + oci * 128: kc * 1536 + oci * 128 + 128],
                                        x1c[par][:, kc * 512:(kc + 1) * 512],
                                        start=(kc == 0), stop=False),
                               wait=kw)
                    P.emit("tensor",
                           lambda e, ps=ps, oc=oc:
                           e.matmul(ps, c_brow[0:1, oc * 128:(oc + 1) * 128],
                                    c_ones[0:1, 0:512], start=False, stop=True),
                           inc="stB1")
                    stv = P.val("stB1")
                    P.emit("vector",
                           lambda e, ps=ps, dst=dst, dt=dt, ic=ic: e.tensor_copy(fm(dst, dt, ic), ps),
                           wait=[("stB1", stv)], inc="svB1")
                    evb[pp] = P.val("svB1")
                if half == 1:
                    for jt4 in range(4):
                        jt = ic * 4 + jt4
                        pp = jt4 % 2
                        ps = pB[:, pp * 1024:(pp + 1) * 1024]
                        wv_ = [("svB1v", evbv.get(pp, 0))]
                        if ic == 0 and jt4 == 0:
                            wv_ = wv_ + [("swQk%d" % k2, swq_piece[k2]) for k2 in range(8)]
                        for kc in range(NF):
                            for vh in range(2):
                                P.emit("tensor",
                                       lambda e, ps=ps, kc=kc, jt4=jt4, par=par, vh=vh:
                                       e.matmul(ps[:, vh * 512:(vh + 1) * 512],
                                                x1c[par][:, kc * 512 + jt4 * 128: kc * 512 + jt4 * 128 + 128],
                                                wqkv_t[:, kc * 1536 + 512 + vh * 512: kc * 1536 + 512 + (vh + 1) * 512],
                                                start=(kc == 0), stop=False),
                                       wait=(base + wv_) if (kc == 0 and vh == 0) else None)
                        for vh in range(2):
                            P.emit("tensor",
                                   lambda e, ps=ps, vh=vh:
                                   e.matmul(ps[:, vh * 512:(vh + 1) * 512],
                                            c_ones[0:1, 0:128],
                                            c_brow[0:1, 2048 + vh * 512: 2048 + (vh + 1) * 512],
                                            start=False, stop=True),
                                   inc="stB1v" if vh == 1 else None)
                        stv = P.val("stB1v")
                        P.emit("vector",
                               lambda e, ps=ps, jt=jt:
                               e.tensor_copy(
                                   V_[:, jt * VST: jt * VST + VST].rearrange("p (h c) -> p h c", h=HB)[:, :, 0:64],
                                   ps.rearrange("p (h c) -> p h c", h=HB)),
                               wait=[("stB1v", stv)], inc="svB1v")
                        evbv[pp] = P.val("svB1v")
                    for it in range(4):
                        wv_ = [("ssB1g", P.val("ssB1g"))]
                        for kc in range(NF):
                            P.emit("tensor",
                                   lambda e, kc=kc, it=it, par=par:
                                   e.matmul(pA[:, 1024:1032],
                                            x1c[par][:, kc * 512 + it * 128: kc * 512 + it * 128 + 128],
                                            c_wgate[:, kc * 8:(kc + 1) * 8],
                                            start=(kc == 0), stop=False),
                                   wait=(base + wv_) if kc == 0 else None)
                        P.emit("tensor",
                               lambda e: e.matmul(pA[:, 1024:1032], c_ones[0:1, 0:128],
                                                  c_bgate, start=False, stop=True),
                               inc="stB1g")
                        stv = P.val("stB1g")
                        P.emit("vector",
                               lambda e, it=it: e.reduce_max(negm_g[:, it:it + 1],
                                                             pA[:, 1024:1032], axis=AX.X,
                                                             negate=True),
                               wait=[("stB1g", stv)], inc="svB1g")
                        svv = P.val("svB1g")
                        P.emit("scalar",
                               lambda e, it=it: e.activation(gexp[:, it * 8:(it + 1) * 8],
                                                             pA[:, 1024:1032], AF.Exp,
                                                             bias=negm_g[:, it:it + 1],
                                                             accum_out=gsum[:, it:it + 1]),
                               wait=[("svB1g", svv)], inc="ssB1g")
                    ssv = P.val("ssB1g")
                    for it in range(4):
                        P.emit("vector", lambda e, it=it: e.reciprocal(grs[:, it:it + 1], gsum[:, it:it + 1]),
                               wait=[("ssB1g", ssv)] if it == 0 else None)
                    for it in range(4):
                        P.emit("vector", lambda e, it=it: e.tensor_tensor(
                            gsm[:, it * 8:(it + 1) * 8], gexp[:, it * 8:(it + 1) * 8],
                            grs[:, it:it + 1].broadcast_to([128, 8]), op=OP.mult))
                    for it in range(4):
                        P.emit("vector", lambda e, it=it: e.reduce_max(
                            gm1[:, it:it + 1], gsm[:, it * 8:(it + 1) * 8], axis=AX.X))
                    for it in range(4):
                        P.emit("vector", lambda e, it=it: e.tensor_tensor(
                            gmk[:, it * 8:(it + 1) * 8], gsm[:, it * 8:(it + 1) * 8],
                            gm1[:, it:it + 1].broadcast_to([128, 8]), op=OP.is_ge))
                    for it in range(4):
                        P.emit("vector", lambda e, it=it: e.tensor_tensor(
                            gzap[:, it * 8:(it + 1) * 8], gsm[:, it * 8:(it + 1) * 8],
                            gmk[:, it * 8:(it + 1) * 8], op=OP.subtract))
                    for it in range(4):
                        P.emit("vector", lambda e, it=it: e.reduce_max(
                            gm2[:, it:it + 1], gzap[:, it * 8:(it + 1) * 8], axis=AX.X))
                    for it in range(4):
                        P.emit("vector", lambda e, it=it: e.tensor_tensor(
                            gmk[:, it * 8:(it + 1) * 8], gsm[:, it * 8:(it + 1) * 8],
                            gm2[:, it:it + 1].broadcast_to([128, 8]), op=OP.is_ge))
                    for it in range(4):
                        P.emit("vector", lambda e, it=it: e.tensor_tensor(
                            gsm[:, it * 8:(it + 1) * 8], gsm[:, it * 8:(it + 1) * 8],
                            gmk[:, it * 8:(it + 1) * 8], op=OP.mult))
                    for it in range(4):
                        P.emit("vector", lambda e, it=it: e.tensor_copy(
                            gcomb_bf[:, it * 8:(it + 1) * 8], gsm[:, it * 8:(it + 1) * 8]),
                               inc="svB1c" if it == 3 else None)
                    svv = P.val("svB1c")
                    for it in range(4):
                        P.emit("tensor",
                               lambda e, it=it: e.transpose(
                                   pA[0:8, 1536:1600].bitcast(BF16),
                                   gcomb_bf[:, it * 8:(it + 1) * 8],
                                   c_idn),
                               wait=([("svB1c", svv)] if it == 0 else
                                     [("svB1t", P.val("svB1t"))]),
                               inc="stB1t")
                        stv = P.val("stB1t")
                        P.emit("vector",
                               lambda e, it=it, ic=ic: e.tensor_copy(
                                   combT[:, ic * ICW + it * 128: ic * ICW + (it + 1) * 128],
                                   pA[0:8, 1536:1600].bitcast(BF16)),
                               wait=[("stB1t", stv)], inc="svB1t")
                x1gate[par] = (("stB1g", P.val("stB1g")) if half == 1
                               else ("stB1", P.val("stB1")))
        P.emit("vector",
               lambda e: e.memset(
                   V_[:, 0:NJT * VST].rearrange("p (j h c) -> p j h c", j=NJT, h=HB)[:, :, :, 64:65], 1.0),
               inc="svB1o")
        sv_vones = P.val("svB1o")
        b1_done = [("svB1", P.val("svB1")), ("svB1v", P.val("svB1v")),
                   ("ssB1g", P.val("ssB1g")), ("svB1t", P.val("svB1t"))]

        B2expv = {}
        pend_epi = [None]
        pend_epi2 = [None]
        for ic in range(NIC):
            for g in range(8):
                hA2, hB2 = 2 * g, 2 * g + 1

                def b2_s(jt, g=g, ic=ic):
                    gpar = jt % 2
                    ps_s = pA[:, gpar * 1024:(gpar + 1) * 1024]
                    wargs = [("ssB2", B2expv.get(gpar, 0))]
                    if g == 0 and ic == 0 and jt <= 1:
                        wargs = wargs + [("svB1o", sv_vones)] + b1_done
                    P.emit("tensor",
                           lambda e, ps_s=ps_s, g=g, jt=jt, ic=ic:
                           e.matmul(ps_s[:, 0:512],
                                    K_[0:64, g * T + jt * 128: g * T + jt * 128 + 128],
                                    Q_[0:64, g * T + ic * ICW: g * T + ic * ICW + ICW],
                                    start=True, stop=True),
                           wait=wargs)
                    P.emit("tensor",
                           lambda e, ps_s=ps_s, g=g, jt=jt, ic=ic:
                           e.matmul(ps_s[:, 512:1024],
                                    K_[64:128, g * T + jt * 128: g * T + jt * 128 + 128],
                                    Q_[64:128, g * T + ic * ICW: g * T + ic * ICW + ICW],
                                    start=True, stop=True, tile_position=(64, 0)),
                           inc="stB2s")
                    stv = P.val("stB2s")
                    P.emit("scalar",
                           lambda e, gpar=gpar, ps_s=ps_s:
                           e.activation(ptilB[:, gpar * 1024:(gpar + 1) * 1024],
                                        ps_s, AF.Exp, scale=SCL_B),
                           wait=[("stB2s", stv)], inc="ssB2")
                    B2expv[gpar] = P.val("ssB2")

                b2_s(0)
                for jt in range(NJT):
                    gpar = jt % 2
                    if jt + 1 < NJT:
                        b2_s(jt + 1)
                    if jt == 0 and pend_epi[0] is not None:
                        pend_epi[0]()
                        pend_epi[0] = None
                    if jt == 3 and pend_epi2[0] is not None:
                        pend_epi2[0]()
                        pend_epi2[0] = None
                    ssv = B2expv[gpar]
                    owait = [("ssB2", ssv)]
                    if jt == 0:
                        owait = owait + [("svB2e", P.val("svB2e"))]
                    P.emit("tensor",
                           lambda e, jt=jt, gpar=gpar, hA2=hA2:
                           e.matmul(pB[0:65, 0:512],
                                    V_[:, jt * VST + hA2 * 65: jt * VST + hA2 * 65 + 65],
                                    ptilB[:, gpar * 1024: gpar * 1024 + 512],
                                    start=(jt == 0), stop=(jt == NJT - 1)),
                           wait=owait)
                    P.emit("tensor",
                           lambda e, jt=jt, gpar=gpar, hB2=hB2:
                           e.matmul(pB[64:128, 512:1024],
                                    V_[:, jt * VST + hB2 * 65: jt * VST + hB2 * 65 + 64],
                                    ptilB[:, gpar * 1024 + 512: gpar * 1024 + 1024],
                                    start=(jt == 0), stop=(jt == NJT - 1),
                                    tile_position=(0, 64)))
                    P.emit("tensor",
                           lambda e, jt=jt, gpar=gpar:
                           e.matmul(pB[:, 1024:1536],
                                    c_ones[:, 0:128],
                                    ptilB[:, gpar * 1024 + 512: gpar * 1024 + 1024],
                                    start=(jt == 0), stop=(jt == NJT - 1)),
                           inc="stB2o" if jt == NJT - 1 else None)
                stv = P.val("stB2o")

                def epilogue(g=g, ic=ic, stv=stv):
                    P.emit("vector", lambda e: e.tensor_copy(OEV[0:64, 0:512], pB[0:64, 0:512]),
                           wait=[("stB2o", stv)])
                    P.emit("vector", lambda e: e.tensor_copy(OEV[64:128, 512:1024],
                                                             pB[64:128, 512:1024]))
                    P.emit("vector", lambda e: e.tensor_copy(linve_r, pB[64:65, 0:512]))
                    P.emit("vector", lambda e: e.tensor_copy(linvo_r, pB[0:1, 1024:1536]),
                           inc="svB2e")
                    svev = P.val("svB2e")
                    P.emit("scalar", lambda e: e.activation(linve_r, linve_r, AF.Ln),
                           wait=[("svB2e", svev)])
                    P.emit("scalar", lambda e: e.activation(linvo_r, linvo_r, AF.Ln))
                    P.emit("scalar", lambda e: e.drain())
                    P.emit("scalar", lambda e: e.activation(lrow[64:65, :], linve_r, AF.Exp, scale=-1.0))
                    P.emit("scalar", lambda e: e.activation(lrow[0:1, :], linvo_r, AF.Exp, scale=-1.0),
                           inc="ssB2r")
                    svv = P.val("ssB2r")

                    def epi2(g=g, ic=ic, svv=svv):
                        P.emit("tensor", lambda e: e.matmul(pB[0:64, 1536:2048], c_ones[64:65, 0:64],
                                                            lrow[64:65, :], start=True, stop=True,
                                                            tile_position=(64, 0)),
                               wait=[("ssB2r", svv), ("ssB2f", P.val("ssB2f"))])
                        P.emit("tensor", lambda e: e.matmul(pB[64:128, 1536:2048], c_ones[0:1, 0:64],
                                                            lrow[0:1, :], start=True, stop=True,
                                                            tile_position=(0, 64)),
                               inc="stB2b")
                        stv2 = P.val("stB2b")
                        P.emit("vector", lambda e: e.tensor_copy(bcB, pB[:, 1536:2048]),
                               wait=[("stB2b", stv2)])
                        P.emit("vector", lambda e: e.drain())
                        P.emit("vector",
                               lambda e, g=g, ic=ic: e.tensor_tensor(
                                   ATB[0:64, g * T + ic * ICW: g * T + ic * ICW + ICW],
                                   OEV[0:64, 0:512], bcB[0:64, :], op=OP.mult))
                        P.emit("vector",
                               lambda e, g=g, ic=ic: e.tensor_tensor(
                                   ATB[64:128, g * T + ic * ICW: g * T + ic * ICW + ICW],
                                   OEV[64:128, 512:1024], bcB[64:128, :], op=OP.mult),
                               inc="ssB2f")

                    pend_epi2[0] = epi2

                pend_epi[0] = epilogue
        if pend_epi[0] is not None:
            pend_epi[0]()
            pend_epi[0] = None
        if pend_epi2[0] is not None:
            pend_epi2[0]()
            pend_epi2[0] = None
        sv_atb = P.val("ssB2f")
        if debug:
            P.dma("gpsimd", dbg2[s].rearrange("(f p) t -> p f t", p=128),
                  ATB.rearrange("p (f t) -> p f t", f=NF), "sdbg2",
                  wait=[("ssB2f", sv_atb)])
            P.dma("gpsimd", dbg3[s], combT, "sdbg2")
        b2_done = [("ssB2", P.val("ssB2"))]
        pump_wescr(N_JOBS)
        wescr_wait = bounce_done_wait(N_JOBS)

        for dp in range(8):
            xw = [("stB2s", P.val("stB2s"))] if dp == 0 else None
            P.dma("sync", X1F.rearrange("p (f t) -> p f t", f=NF)[:, dp:dp + 1],
                  x1_scr[s].rearrange("(f p) t -> p f t", p=128)[:, dp:dp + 1], "sxfp%d" % dp,
                  wait=xw)
        sxf_piece = [P.val("sxfp%d" % dp) for dp in range(8)]
        def cb_emit(iq):
            cwait_extra = [("svB3y", P.val("svB3y"))] + (b2_done if iq == 0 else [])
            for e_ in range(NE):
                cps = pA[:, 1024:1536] if e_ % 2 == 0 else pA[:, 1536:2048]
                w = [("svB3c", P.val("svB3c") - 1 if P.val("svB3c") >= 2 else 0)]
                if e_ == 0:
                    w = w + cwait_extra
                P.emit("tensor",
                       lambda e, cps=cps, e_=e_, iq=iq: e.matmul(
                           cps, c_sel[:, e_ * 128:(e_ + 1) * 128],
                           combT[:, iq * ICW:(iq + 1) * ICW],
                           start=True, stop=True),
                       wait=w, inc="stB3c")
                stv = P.val("stB3c")
                P.emit("vector",
                       lambda e, cps=cps, e_=e_: e.tensor_copy(
                           CBSB[:, e_ * 512:(e_ + 1) * 512], cps),
                       wait=[("stB3c", stv)], inc="svB3c")
            return P.val("svB3c")

        cb_next = [None]
        pend_ap = [[]]
        for iq in range(NIC):
            sv_cb = cb_emit(0) if iq == 0 else cb_next[0]
            if debug and s == 0:
                P.dma("gpsimd", dbg5[iq], CBSB, "sdbg2", wait=[("svB3c", sv_cb)])
            mmfirst = True
            for e_ in range(NE):
                for d_ in range(NF):
                    idx = e_ * NF + d_
                    ring = idx % 4
                    w = [("stB3r%d" % ring, P.val("stB3r%d" % ring))]
                    if iq == 0 and idx < 4:
                        w = w + wescr_wait + [("stB2o", P.val("stB2o"))]
                    if direct_we:
                        P.dma("gpsimd", WER[ring], wexp_in[e_, d_ * 128:(d_ + 1) * 128, :],
                              "seW%d" % ring, wait=w)
                    else:
                        P.dma("sync", WER[ring], we_scr[e_, d_ * 128:(d_ + 1) * 128, :],
                              "seW%d" % ring, wait=w)
                    sew = P.val("seW%d" % ring)
                    xr = idx % 4
                    wx = [("stB3r%d" % xr, P.val("stB3r%d" % xr))]
                    if iq == 0 and e_ == 0:
                        wx = wx + [("sxfp%d" % d_, sxf_piece[d_])]
                    if idx == 0:
                        wx = wx + [("svB3c", sv_cb)]
                    first_mm_extra = ([("svB3L3", P.val("svB3L3"))] if idx == 0 else [])
                    if nocb:
                        P.emit("vector",
                               lambda e, xr=xr, d_=d_, iq=iq: e.tensor_copy(
                                   XER[xr], X1F[:, d_ * T + iq * ICW: d_ * T + iq * ICW + ICW]),
                               wait=wx, inc="svB3e")
                    else:
                        P.emit("vector",
                               lambda e, xr=xr, d_=d_, e_=e_, iq=iq: e.tensor_tensor(
                                   XER[xr], X1F[:, d_ * T + iq * ICW: d_ * T + iq * ICW + ICW],
                                   CBSB[:, e_ * 512:(e_ + 1) * 512], op=OP.mult),
                               wait=wx)
                        P.emit("vector", lambda e: e.drain(), inc="svB3e")
                    if idx % 8 == 7 and pend_ap[0]:
                        pend_ap[0].pop(0)()
                    svv = P.val("svB3e")
                    fcs = [2, 3, 4, 5, 6, 7, 0, 1]
                    for fi, fc in enumerate(fcs):
                        pbank = (pA if fc < 4 else pB)[:, (fc % 4) * 512:((fc % 4) + 1) * 512]
                        lhs_ap = c_idn if widn else WER[ring][:, fc * 128:(fc + 1) * 128]
                        rhs_ap = (X1F[:, d_ * T + iq * ICW: d_ * T + iq * ICW + ICW]
                                  if xdirect else XER[xr])
                        mmw = None
                        if fi == 0:
                            mmw = [("seW%d" % ring, sew), ("svB3e", svv)]
                        if fc == 0:
                            mmw = (mmw or []) + first_mm_extra
                        P.emit("tensor",
                               lambda e, pbank=pbank, lhs_ap=lhs_ap, rhs_ap=rhs_ap, st=(e_ == 0 and d_ == 0):
                               e.matmul(pbank, lhs_ap, rhs_ap, start=st, stop=False),
                               wait=mmw,
                               inc="stB3r%d" % ring if fi == NF - 1 else None)
            for fc in range(NF):
                pbank = (pA if fc < 4 else pB)[:, (fc % 4) * 512:((fc % 4) + 1) * 512]
                if skipadds:
                    P.emit("tensor",
                           lambda e, pbank=pbank, fc=fc, iq=iq:
                           e.matmul(pbank, c_bexp[:, fc * 128:(fc + 1) * 128],
                                    combT[:, iq * ICW:(iq + 1) * ICW], start=False, stop=True),
                           wait=[("ssB2f", sv_atb)] if fc == 0 else None,
                           inc="stB3m")
                    continue
                P.emit("tensor",
                       lambda e, pbank=pbank, fc=fc, iq=iq:
                       e.matmul(pbank, c_bexp[:, fc * 128:(fc + 1) * 128],
                                combT[:, iq * ICW:(iq + 1) * ICW], start=False, stop=False))
                P.emit("tensor",
                       lambda e, pbank=pbank, fc=fc, iq=iq:
                       e.matmul(pbank, c_idn, ATB[:, fc * T + iq * ICW: fc * T + iq * ICW + ICW],
                                start=False, stop=False),
                       wait=[("ssB2f", sv_atb)] if fc == 0 else None)
                P.emit("tensor",
                       lambda e, pbank=pbank, fc=fc, iq=iq:
                       e.matmul(pbank, c_idn, X1F[:, fc * T + iq * ICW: fc * T + iq * ICW + ICW],
                                start=False, stop=True),
                       wait=([("sxfp%d" % fc, sxf_piece[fc])] if iq == 0 else None),
                       inc="stB3m")
            stv = P.val("stB3m")
            for fc in range(NF):
                pbank = (pA if fc < 4 else pB)[:, (fc % 4) * 512:((fc % 4) + 1) * 512]
                P.emit("vector",
                       lambda e, pbank=pbank, fc=fc: e.tensor_copy(
                           Y2B[:, fc * 512:(fc + 1) * 512], pbank),
                       wait=[("stB3m", stv)] if fc == 0 else None,
                       inc="svB3y" if fc == NF - 1 else None)
            sv_y2 = P.val("svB3y")
            if debug:
                P.dma("gpsimd", dbg4[s].rearrange("(f p) t -> p f t", p=128)[:, :, iq * ICW:(iq + 1) * ICW],
                      Y2B.rearrange("p (f t) -> p f t", f=NF), "sdbg2",
                      wait=[("svB3y", sv_y2)])
            for fc in range(NF):
                P.emit("vector",
                       lambda e, fc=fc: e.tensor_tensor(
                           YSQB[:, fc * 512:(fc + 1) * 512],
                           Y2B[:, fc * 512:(fc + 1) * 512],
                           Y2B[:, fc * 512:(fc + 1) * 512], op=OP.mult),
                       inc="svB3q" if fc == NF - 1 else None)
            sv_q2 = P.val("svB3q")
            for fc in range(NF):
                P.emit("tensor",
                       lambda e, fc=fc: e.matmul(pA[0:1, 0:512], c_ones[:, 0:1],
                                                 Y2B[:, fc * 512:(fc + 1) * 512],
                                                 start=(fc == 0), stop=(fc == NF - 1)),
                       wait=[("svB3y", sv_y2)] if fc == 0 else None)
            for fc in range(NF):
                P.emit("tensor",
                       lambda e, fc=fc: e.matmul(pA[0:1, 512:1024], c_ones[:, 0:1],
                                                 YSQB[:, fc * 512:(fc + 1) * 512],
                                                 start=(fc == 0), stop=(fc == NF - 1)),
                       wait=[("svB3q", sv_q2)] if fc == 0 else None,
                       inc="stB3st" if fc == NF - 1 else None)
            stv = P.val("stB3st")
            if iq + 1 < NIC:
                cb_next[0] = cb_emit(iq + 1)
            P.emit("scalar", lambda e: e.mul(mean_r, pA[0:1, 0:512], 1.0 / D),
                   wait=[("stB3st", stv)])
            P.emit("scalar", lambda e: e.mul(es2_r, pA[0:1, 512:1024], 1.0 / D), inc="ssL")
            ssv = P.val("ssL")
            P.emit("vector", lambda e: e.tensor_tensor(msq_r, mean_r, mean_r, op=OP.mult),
                   wait=[("ssL", ssv)])
            P.emit("vector", lambda e: e.drain())
            P.emit("vector", lambda e: e.tensor_tensor(var_r, es2_r, msq_r, op=OP.subtract),
                   inc="svL")
            svv = P.val("svL")
            P.emit("scalar", lambda e: e.activation(lnv_r, var_r, AF.Ln, bias=eps_r),
                   wait=[("svL", svv)])
            P.emit("scalar", lambda e: e.drain())
            P.emit("scalar", lambda e: e.activation(rstd_r, lnv_r, AF.Exp, scale=-0.5),
                   inc="ssL2")
            ssv = P.val("ssL2")
            P.emit("vector", lambda e: e.tensor_copy(a_bf, rstd_r), wait=[("ssL2", ssv)])
            P.emit("vector", lambda e: e.scalar_tensor_tensor(
                bneg_r, mean_r, -1.0, rstd_r, op0=OP.mult, op1=OP.mult))
            P.emit("vector", lambda e: e.drain())
            P.emit("vector", lambda e: e.tensor_copy(b_bf, bneg_r), inc="svL2")
            svv = P.val("svL2")
            P.emit("tensor", lambda e: e.matmul(pA[:, 0:512], c_ones[0:1, 0:128], a_bf,
                                                start=True, stop=True),
                   wait=[("svL2", svv)])
            P.emit("tensor", lambda e: e.matmul(pA[:, 512:1024], c_ones[0:1, 0:128], b_bf,
                                                start=True, stop=True), inc="stB3b")
            stv = P.val("stB3b")
            P.emit("vector", lambda e: e.tensor_copy(a_sb, pA[:, 0:512]),
                   wait=[("stB3b", stv)])
            P.emit("vector", lambda e: e.tensor_copy(b_sb, pA[:, 512:1024]),
                   inc="svB3L3")
            P.emit("vector", lambda e: e.drain())

            def mk_apply(fc0, iq=iq):
                def go():
                    wout = [("sout", P.val("sout"))] if fc0 == 0 else None
                    for fc in range(fc0, fc0 + 2):
                        P.emit("vector",
                               lambda e, fc=fc: e.tensor_tensor(
                                   T1OUT[:, fc * 512:(fc + 1) * 512],
                                   Y2B[:, fc * 512:(fc + 1) * 512], a_sb, op=OP.mult),
                               wait=wout if fc == fc0 else None)
                    for fc in range(fc0, fc0 + 2):
                        P.emit("vector",
                               lambda e, fc=fc: e.tensor_tensor(
                                   T1OUT[:, fc * 512:(fc + 1) * 512],
                                   T1OUT[:, fc * 512:(fc + 1) * 512],
                                   b_sb, op=OP.add),
                               inc="svB3o" if fc == fc0 + 1 and fc0 == NF - 2 else None)
                    if fc0 == NF - 2:
                        svv = P.val("svB3o")
                        P.dma("sync",
                              out_hbm[s].rearrange("(f p) t -> p f t", p=128)[:, :, iq * ICW:(iq + 1) * ICW],
                              T1OUT.rearrange("p (f t) -> p f t", f=NF),
                              "sout", wait=[("svB3o", svv)])
                return go

            if iq == NIC - 1:
                for fc0 in range(0, NF, 2):
                    mk_apply(fc0)()
                pend_ap[0] = []
            else:
                pend_ap[0] = [mk_apply(fc0) for fc0 in range(0, NF, 2)]

    if debug:
        P.dma("gpsimd", dbg1, x1_scr, "sdbg")
        P.emit("gpsimd", lambda e: None,
               wait=[("sdbg", P.val("sdbg")), ("sdbg2", P.val("sdbg2"))])
    P.emit("sync", lambda e: None, wait=[("sout", P.val("sout"))])

    with nc.Block() as block:
        @block.tensor
        def _(eng):
            for fn in P.q["tensor"]:
                fn(eng)

        @block.vector
        def _(eng):
            for fn in P.q["vector"]:
                fn(eng)

        @block.scalar
        def _(eng):
            for fn in P.q["scalar"]:
                fn(eng)

        @block.gpsimd
        def _(eng):
            for fn in P.q["gpsimd"]:
                fn(eng)

        @block.sync
        def _(eng):
            for fn in P.q["sync"]:
                fn(eng)

    return nc


_NC_CACHE = [None]


def _host_constants():
    pos = np.arange(T, dtype=np.float64)[:, None]
    w = np.arange(32, dtype=np.float64)[None, :]
    ang = np.power(10000.0, -w / 32.0)
    rad = (pos * ang).astype(np.float32)
    cos = np.cos(rad).T.astype(np.float32)
    sin = np.sin(rad).T.astype(np.float32)
    ctab = np.ones((128, T), dtype=np.float32)
    ctab[0:32] = cos
    ctab[32:64] = cos
    stab = np.zeros((128, T), dtype=np.float32)
    stab[0:32] = -sin
    stab[32:64] = sin
    perm = np.zeros((128, 128), dtype=np.float32)
    for i in range(32):
        perm[32 + i, i] = 1.0
        perm[i, 32 + i] = 1.0
    r = np.arange(128)[:, None]
    u = np.arange(896)[None, :]
    stair = (r <= u - 384).astype(np.float32)
    idn = np.eye(128, dtype=np.float32)
    sel = np.zeros((8, 1024), dtype=np.float32)
    for e in range(8):
        sel[e, e * 128:(e + 1) * 128] = 1.0
    return ctab, stab, perm, stair, idn, sel


def kernel(L0_attn_x, L0_attn_wq, L0_attn_wk, L0_attn_wv, L0_attn_wo,
           L0_ffn_W_qkv, L0_ffn_b_qkv, L0_ffn_W_gate, L0_ffn_b_gate,
           L0_ffn_W_experts, L0_ffn_b_experts):
    x = np.ascontiguousarray(np.asarray(L0_attn_x, dtype=np.float32).transpose(0, 2, 1))
    wq = np.ascontiguousarray(np.asarray(L0_attn_wq, dtype=np.float32).reshape(D, D))
    wk = np.ascontiguousarray(np.asarray(L0_attn_wk, dtype=np.float32).reshape(D, D))
    wv = np.ascontiguousarray(np.asarray(L0_attn_wv, dtype=np.float32).reshape(D, D))
    wo = np.ascontiguousarray(np.asarray(L0_attn_wo, dtype=np.float32))
    wqkv = np.ascontiguousarray(np.asarray(L0_ffn_W_qkv, dtype=np.float32))
    bqkv = np.ascontiguousarray(np.asarray(L0_ffn_b_qkv, dtype=np.float32).reshape(1, 3 * D))
    wgate = np.ascontiguousarray(np.asarray(L0_ffn_W_gate, dtype=np.float32))
    bgate = np.ascontiguousarray(np.asarray(L0_ffn_b_gate, dtype=np.float32).reshape(1, NE))
    wexp = np.ascontiguousarray(np.asarray(L0_ffn_W_experts, dtype=np.float32))
    bexp = np.ascontiguousarray(np.asarray(L0_ffn_b_experts, dtype=np.float32))
    ctab, stab, perm, stair, idn, sel = _host_constants()

    if _NC_CACHE[0] is None:
        _NC_CACHE[0] = build_program()
    nc = _NC_CACHE[0]

    in_maps = []
    for c in range(NCORES):
        in_maps.append({
            "x": np.ascontiguousarray(x[c * SEQS:(c + 1) * SEQS]),
            "wq": wq, "wk": wk, "wv": wv, "wo": wo,
            "wqkv": wqkv, "bqkv": bqkv, "wgate": wgate, "bgate": bgate,
            "wexp": wexp, "bexp": bexp,
            "ctab": ctab, "stab": stab, "perm": perm, "stair": stair, "idn": idn, "sel": sel,
        })
    res = run_bass_kernel_spmd(nc, in_maps, core_ids=list(range(NCORES)))
    outs = [r["out"] for r in res.results]
    full = np.concatenate(outs, axis=0)
    return np.ascontiguousarray(full.transpose(0, 2, 1))


# version 20
# speedup vs baseline: 1.0399x; 1.0030x over previous
import numpy as np
import concourse.bass as bass
import concourse.mybir as mybir
from concourse.bass_utils import run_bass_kernel_spmd

F32 = mybir.dt.float32
BF16 = mybir.dt.bfloat16
AF = mybir.ActivationFunctionType
OP = mybir.AluOpType
AX = mybir.AxisListType

NCORES = 8
SEQS = 2
T = 2048
D = 1024
NF = 8
ICW = 512
NIC = T // ICW
NJT = T // 128
HA = 8
HB = 16
NE = 8
SCL_A = 1.0 / 128.0
SCL_B = 0.125
EPS = 1e-5
VST = HB * 65


class Prog:
    ENGS = ("tensor", "vector", "scalar", "gpsimd", "sync")

    def __init__(self, nc):
        self.nc = nc
        self.q = {e: [] for e in self.ENGS}
        self.sem_handles = {}
        self.cnt = {}
        self.fences = {}

    def sem(self, name):
        if name not in self.sem_handles:
            self.sem_handles[name] = self.nc.alloc_semaphore(name)
            self.cnt[name] = 0
        return self.sem_handles[name]

    def emit(self, eng, fn, wait=None, inc=None, inc_n=1):
        sems = self.sem_handles
        if inc is not None:
            self.sem(inc)
            self.cnt[inc] += inc_n
        wait = [(s, v) for (s, v) in (wait or []) if v > 0]
        for s, _ in wait:
            self.sem(s)

        def run(engine, fn=fn, wait=wait, inc=inc, inc_n=inc_n):
            for s, v in wait:
                engine.wait_ge(sems[s], v)
            ins = fn(engine)
            if inc is not None and ins is not None:
                ins.then_inc(sems[inc], inc_n)

        self.q[eng].append(run)

    def val(self, name):
        self.sem(name)
        return self.cnt[name]

    def dma(self, eng, out, in_, sem, wait=None):
        self.emit(eng, lambda e, o=out, i=in_: e.dma_start(out=o, in_=i),
                  wait=wait, inc=sem, inc_n=16)
        return self.cnt[sem]

    def release(self, eng, sem):
        self.emit(eng, lambda e: e.nop(), inc=sem)
        return self.cnt[sem]

    def barrier(self):
        marks = []
        for e, sname in (("vector", "bv"), ("scalar", "bs"),
                         ("gpsimd", "bg"), ("sync", "by")):
            fence = self.fences.get(e)
            if fence is not None:
                self.emit(e, fence, inc=sname)
            else:
                self.emit(e, lambda eng: eng.nop(), inc=sname)
            marks.append((sname, self.cnt[sname]))
        self.emit("tensor", self.fences["tensor"], wait=list(marks), inc="bt")
        marks.append(("bt", self.cnt["bt"]))
        for e in ("tensor", "vector", "scalar"):
            self.emit(e, lambda eng: None, wait=marks)


def build_program(debug=False, direct_we=False, nocb=False, skipadds=False, widn=False, xdirect=False):
    nc = bass.Bass(target_bir_lowering=False, trn_type="TRN2")

    x_in = nc.dram_tensor("x", [SEQS, D, T], F32, kind="ExternalInput").ap()
    wq_in = nc.dram_tensor("wq", [D, D], F32, kind="ExternalInput").ap()
    wk_in = nc.dram_tensor("wk", [D, D], F32, kind="ExternalInput").ap()
    wv_in = nc.dram_tensor("wv", [D, D], F32, kind="ExternalInput").ap()
    wo_in = nc.dram_tensor("wo", [HA, 128, D], F32, kind="ExternalInput").ap()
    wqkv_in = nc.dram_tensor("wqkv", [D, 3 * D], F32, kind="ExternalInput").ap()
    bqkv_in = nc.dram_tensor("bqkv", [1, 3 * D], F32, kind="ExternalInput").ap()
    wgate_in = nc.dram_tensor("wgate", [D, NE], F32, kind="ExternalInput").ap()
    bgate_in = nc.dram_tensor("bgate", [1, NE], F32, kind="ExternalInput").ap()
    wexp_in = nc.dram_tensor("wexp", [NE, D, D], F32, kind="ExternalInput").ap()
    bexp_in = nc.dram_tensor("bexp", [NE, D], F32, kind="ExternalInput").ap()
    ctab_in = nc.dram_tensor("ctab", [128, T], F32, kind="ExternalInput").ap()
    stab_in = nc.dram_tensor("stab", [128, T], F32, kind="ExternalInput").ap()
    perm_in = nc.dram_tensor("perm", [128, 128], F32, kind="ExternalInput").ap()
    stair_in = nc.dram_tensor("stair", [128, 896], F32, kind="ExternalInput").ap()
    idn_in = nc.dram_tensor("idn", [128, 128], F32, kind="ExternalInput").ap()
    sel_in = nc.dram_tensor("sel", [8, 1024], F32, kind="ExternalInput").ap()
    out_hbm = nc.dram_tensor("out", [SEQS, D, T], F32, kind="ExternalOutput").ap()

    x1_scr = nc.dram_tensor("x1scr", [SEQS, D, T], BF16).ap()
    dbg1 = (nc.dram_tensor("dbg1", [SEQS, D, T], F32, kind="ExternalOutput").ap()
            if debug else None)
    dbg2 = (nc.dram_tensor("dbg2", [SEQS, D, T], F32, kind="ExternalOutput").ap()
            if debug else None)
    dbg3 = (nc.dram_tensor("dbg3", [SEQS, 8, T], F32, kind="ExternalOutput").ap()
            if debug else None)
    dbg4 = (nc.dram_tensor("dbg4", [SEQS, D, T], F32, kind="ExternalOutput").ap()
            if debug else None)
    dbg5 = (nc.dram_tensor("dbg5", [NIC, 128, 4096], F32, kind="ExternalOutput").ap()
            if debug else None)
    we_scr = nc.dram_tensor("wescr", [NE, D, D], BF16).ap()
    wqkv_scr = nc.dram_tensor("wqkvscr", [D, 3 * D], BF16).ap()
    wq_scr = nc.dram_tensor("wqscr", [D, D], BF16).ap()
    wk_scr = nc.dram_tensor("wkscr", [D, D], BF16).ap()
    wv_scr = nc.dram_tensor("wvscr", [D, D], BF16).ap()
    wo_scr = nc.dram_tensor("woscr", [HA, 128, D], BF16).ap()

    c_ctab = nc.alloc_sbuf_tensor("c_ctab", [128, T], BF16).ap()
    c_stab = nc.alloc_sbuf_tensor("c_stab", [128, T], BF16).ap()
    c_perm = nc.alloc_sbuf_tensor("c_perm", [128, 128], BF16).ap()
    c_stair = nc.alloc_sbuf_tensor("c_stair", [128, 896], BF16).ap()
    c_idn = nc.alloc_sbuf_tensor("c_idn", [128, 128], BF16).ap()
    c_ones = nc.alloc_sbuf_tensor("c_ones", [128, 512], BF16).ap()
    c_wgate = nc.alloc_sbuf_tensor("c_wgate", [128, 64], BF16).ap()
    c_bgate = nc.alloc_sbuf_tensor("c_bgate", [1, 8], BF16).ap()
    c_brow = nc.alloc_sbuf_tensor("c_brow", [1, 3 * D], BF16).ap()
    c_bexp = nc.alloc_sbuf_tensor("c_bexp", [8, D], BF16).ap()
    c_sel = nc.alloc_sbuf_tensor("c_sel", [8, 1024], BF16).ap()
    combT = nc.alloc_sbuf_tensor("combT", [8, T], BF16).ap()
    mf = nc.alloc_sbuf_tensor("miscf", [128, 3264], F32).ap()
    a_bf = nc.alloc_sbuf_tensor("a_bf", [1, 512], BF16).ap()
    b_bf = nc.alloc_sbuf_tensor("b_bf", [1, 512], BF16).ap()
    bcB = nc.alloc_sbuf_tensor("bcB", [128, 512], BF16).ap()
    rsc = nc.alloc_sbuf_tensor("ropescr2", [128, 1024], BF16).ap()
    lrow = nc.alloc_sbuf_tensor("lrow", [128, 512], BF16).ap()
    gcomb_bf = nc.alloc_sbuf_tensor("gcombbf", [128, 32], BF16).ap()
    bnc = [nc.alloc_sbuf_tensor(f"bnc{i}", [128, 512], BF16).ap() for i in range(2)]

    R1 = nc.alloc_sbuf_tensor("R1", [128, 16384], BF16).ap()
    R2 = nc.alloc_sbuf_tensor("R2", [128, 16384], BF16).ap()
    R3 = nc.alloc_sbuf_tensor("R3", [128, 16640], BF16).ap()
    POOL = nc.alloc_sbuf_tensor("POOL", [128, 32768], BF16).ap()

    pA = nc.alloc_psum_tensor("pA", [128, 2048], F32).ap()
    pB = nc.alloc_psum_tensor("pB", [128, 2048], F32).ap()

    K_, Q_, V_ = R1, R2, R3
    X1F = R1
    T1OUT = R2[:, 0:8192].bitcast(F32)
    Y2B = R2[:, 8192:12288]
    YSQB = R2[:, 12288:16384]
    CBSB = R3[:, 0:4096]
    WER = [R3[:, 4096 + 1024 * i: 4096 + 1024 * (i + 1)] for i in range(4)]
    XER = [R3[:, 8192 + 512 * i: 8192 + 512 * (i + 1)] for i in range(4)]

    wq_t = POOL[:, 0:8192]
    wk_t = POOL[:, 8192:16384]
    wv_t = POOL[:, 16384:24576]
    x0c = [POOL[:, 24576:28672], POOL[:, 28672:32768]]
    wo_t = POOL[:, 0:8192]
    ptil = POOL[:, 8192:10240]
    oA = POOL[:, 10240:14336]
    ysqA = POOL[:, 10240:14336]
    ybuf = POOL[:, 14336:18432]
    x1buf = [POOL[:, 18432:22528], POOL[:, 22528:26624]]
    x0c2 = POOL[:, 26624:30720]
    wqkv_t = POOL[:, 0:12288]
    x1c = [POOL[:, 12288:16384], POOL[:, 16384:20480]]
    ptilB = POOL[:, 0:4096]
    ATB = POOL[:, 4096:20480]
    OEV = POOL[:, 20480:21504]
    a_sb = POOL[:, 0:1024].bitcast(F32)
    b_sb = POOL[:, 1024:2048].bitcast(F32)

    linv = mf[:, 0:512]
    mean_r = mf[0:1, 512:1024]
    es2_r = mf[0:1, 1024:1536]
    msq_r = mf[0:1, 1536:2048]
    var_r = mf[0:1, 2048:2560]
    lnv_r = mf[0:1, 1536:2048]
    rstd_r = mf[0:1, 2560:3072]
    bneg_r = mf[0:1, 1024:1536]
    linvo_r = mf[0:1, 2048:2560]
    linve_r = mf[64:65, 2048:2560]
    gexp = mf[:, 3072:3104]
    gsum = mf[:, 3104:3108]
    grs = mf[:, 3108:3112]
    gm1 = mf[:, 3112:3116]
    gm2 = mf[:, 3116:3120]
    gsm = mf[:, 3120:3152]
    gmk = mf[:, 3152:3184]
    gzap = mf[:, 3184:3216]
    negm_g = mf[:, 3216:3220]
    eps_r = mf[0:1, 3220:3221]

    P = Prog(nc)
    fence_f = mf[0:1, 3221:3222]
    fence_g = mf[0:1, 3222:3223]
    P.fences = {
        "tensor": lambda e: e.matmul(pA[0:1, 0:1], c_ones[0:1, 0:1], c_ones[0:1, 0:1],
                                     start=True, stop=True),
        "vector": lambda e: e.tensor_copy(fence_f, fence_g),
        "scalar": lambda e: e.copy(fence_g, fence_f),
    }

    def fm(t, chunk, ic, w=ICW):
        return t[:, chunk * T + ic * w: chunk * T + ic * w + w]

    P.dma("gpsimd", c_ctab, ctab_in, "swC")
    P.dma("gpsimd", c_stab, stab_in, "swC")
    P.dma("gpsimd", c_perm, perm_in, "swC")
    P.dma("gpsimd", c_stair, stair_in, "swC")
    P.dma("gpsimd", c_idn, idn_in, "swC")
    P.dma("gpsimd", c_wgate.rearrange("p (c e) -> p c e", c=8),
          wgate_in.rearrange("(c p) e -> p c e", p=128), "swC")
    P.dma("gpsimd", c_bgate, bgate_in, "swC")
    P.dma("gpsimd", c_brow, bqkv_in, "swC")
    P.dma("gpsimd", c_bexp, bexp_in, "swC")
    P.dma("gpsimd", c_sel, sel_in, "swC")
    swC_total = P.val("swC")
    P.emit("vector", lambda e: e.memset(c_ones, 1.0))
    P.emit("vector", lambda e: e.memset(eps_r, float(EPS)), inc="svones")
    sv_ones = P.val("svones")

    bounce_jobs = []
    for c in range(8):
        for j in range(6):
            bounce_jobs.append((wqkv_in[c * 128:(c + 1) * 128, j * 512:(j + 1) * 512],
                                wqkv_scr[c * 128:(c + 1) * 128, j * 512:(j + 1) * 512]))
    wqkv_scr_njobs = len(bounce_jobs)
    for e_ in range(NE):
        for d_ in range(NF):
            for hf in range(2):
                bounce_jobs.append((wexp_in[e_, d_ * 128:(d_ + 1) * 128, hf * 512:(hf + 1) * 512],
                                    we_scr[e_, d_ * 128:(d_ + 1) * 128, hf * 512:(hf + 1) * 512]))
    N_JOBS = len(bounce_jobs)
    wescr_idx = [0]

    def emit_wescr_chunk():
        idx = wescr_idx[0]
        if idx >= N_JOBS:
            return
        wescr_idx[0] += 1
        srcp, dstp = bounce_jobs[idx]
        par = idx % 2
        w = [("sbs%d" % par, 16 * (idx // 2))] if idx >= 2 else None
        P.dma("gpsimd", bnc[par], srcp, "sbl%d" % par, wait=w)
        P.dma("gpsimd", dstp, bnc[par], "sbs%d" % par,
              wait=[("sbl%d" % par, 16 * (idx // 2 + 1))])

    def bounce_done_wait(njobs):
        n0 = (njobs + 1) // 2
        n1 = njobs // 2
        return [("sbs0", 16 * n0), ("sbs1", 16 * n1)]

    def pump_wescr(n):
        for _ in range(n):
            emit_wescr_chunk()

    x0gate = {}
    x1gate = {}
    pa_gate = {}
    pb_gate = {}
    for s in range(SEQS):
        swa_w = None
        if s > 0:
            swa_w = [("stB3m", P.val("stB3m")), ("ssB2f", P.val("ssB2f")),
                     ("stB1g", P.val("stB1g")), ("ssB2", P.val("ssB2")),
                     ("svB3o", P.val("svB3o"))]
        for hh in range(2):
            P.dma("gpsimd", wq_t.rearrange("p (c n) -> p c n", c=8)[:, hh * 4:(hh + 1) * 4],
                  wq_in.rearrange("(c p) n -> p c n", p=128)[:, hh * 4:(hh + 1) * 4],
                  "swAh%d" % hh, wait=swa_w if hh == 0 else None)
            P.dma("gpsimd", wk_t.rearrange("p (c n) -> p c n", c=8)[:, hh * 4:(hh + 1) * 4],
                  wk_in.rearrange("(c p) n -> p c n", p=128)[:, hh * 4:(hh + 1) * 4],
                  "swAh%d" % hh)
            P.dma("gpsimd", wv_t.rearrange("p (c n) -> p c n", c=8)[:, hh * 4:(hh + 1) * 4],
                  wv_in.rearrange("(c p) n -> p c n", p=128)[:, hh * 4:(hh + 1) * 4],
                  "swAh%d" % hh)
        swA_h = [P.val("swAh0"), P.val("swAh1")]

        pa_gate = dict(pa_gate) if s > 0 else {}
        pb_gate = dict(pb_gate) if s > 0 else {}
        qk_idx = [0]
        for ic in range(NIC):
            par = ic % 2
            w = [("stA1v", x0gate.get(par, 0))]
            if s > 0 and ic < 2:
                w += [("sspill1", P.val("sspill1")), ("stA2y", P.val("stA2y"))]
            P.dma("gpsimd", x0c[par].rearrange("p (f t) -> p f t", f=NF),
                  x_in[s].rearrange("(f p) t -> p f t", p=128)[:, :, ic * ICW:(ic + 1) * ICW],
                  "sx%d" % par, wait=w)
            sxv = P.val("sx%d" % par)
            pump_wescr(6)

            base = [("sx%d" % par, sxv), ("swC", swC_total),
                    ("svones", sv_ones)]
            for kind in ("q", "k"):
                wt = wq_t if kind == "q" else wk_t
                dst = Q_ if kind == "q" else K_
                for dt in range(NF):
                    pp = qk_idx[0] % 4
                    qk_idx[0] += 1
                    ps = pA[:, pp * 512:(pp + 1) * 512]
                    wv_ = [pa_gate[pp]] if pp in pa_gate else []
                    for kc in range(NF):
                        kw = (base + wv_) if kc == 0 else None
                        if ic == 0 and dt < 2 and kc in (0, 4):
                            kw = (kw or []) + [("swAh%d" % (kc // 4), swA_h[kc // 4])]
                        P.emit("tensor",
                               lambda e, ps=ps, wt=wt, kc=kc, dt=dt, par=par:
                               e.matmul(ps, wt[:, kc * 1024 + dt * 128: kc * 1024 + dt * 128 + 128],
                                        x0c[par][:, kc * 512:(kc + 1) * 512],
                                        start=(kc == 0), stop=(kc == NF - 1)),
                               wait=kw,
                               inc="stA1" if kc == NF - 1 else None)
                    stv = P.val("stA1")
                    P.emit("scalar",
                           lambda e, ps=ps, dst=dst, dt=dt, ic=ic: e.copy(fm(dst, dt, ic), ps),
                           wait=[("stA1", stv)], inc="svA1")
                    pa_gate[pp] = ("svA1", P.val("svA1"))
            for jt4 in range(4):
                jt = ic * 4 + jt4
                pp = jt4 % 2
                ps = pB[:, pp * 1024:(pp + 1) * 1024]
                wv_ = [g for b in (2 * pp, 2 * pp + 1) for g in ([pb_gate[b]] if b in pb_gate else [])]
                if ic == 0 and jt4 == 0:
                    wv_ = wv_ + [("swAh0", swA_h[0]), ("swAh1", swA_h[1])]
                for kc in range(NF):
                    for half in range(2):
                        P.emit("tensor",
                               lambda e, ps=ps, kc=kc, jt4=jt4, par=par, half=half:
                               e.matmul(ps[:, half * 512:(half + 1) * 512],
                                        x0c[par][:, kc * 512 + jt4 * 128: kc * 512 + jt4 * 128 + 128],
                                        wv_t[:, kc * 1024 + half * 512: kc * 1024 + (half + 1) * 512],
                                        start=(kc == 0), stop=(kc == NF - 1)),
                               wait=(base + wv_) if (kc == 0 and half == 0) else None,
                               inc="stA1v" if (kc == NF - 1 and half == 1) else None)
                stv = P.val("stA1v")
                P.emit("scalar",
                       lambda e, ps=ps, jt=jt: e.copy(V_[:, jt * 1024:(jt + 1) * 1024], ps),
                       wait=[("stA1v", stv)], inc="svA1v")
                pb_gate[2 * pp] = ("svA1v", P.val("svA1v"))
                pb_gate[2 * pp + 1] = ("svA1v", P.val("svA1v"))
            x0gate[par] = P.val("stA1v")

            svq = P.val("svA1")
            first = True
            rslot = [0]
            for X in (Q_, K_):
                for pr in range(4):
                    h0 = 2 * pr
                    sl0, sl1 = rslot[0] % 4, (rslot[0] + 1) % 4
                    rslot[0] += 2
                    rvals = {}
                    for (h, slot) in ((h0, sl0), (h0 + 1, sl1)):
                        rps = pB[:, slot * 512:(slot + 1) * 512]
                        w = [pb_gate[slot]] if slot in pb_gate else []
                        if first:
                            w += [("svA1", svq), ("swC", swC_total)]
                        P.emit("tensor",
                               lambda e, rps=rps, X=X, h=h, ic=ic:
                               e.matmul(rps, c_perm,
                                        X[:, h * T + ic * ICW: h * T + ic * ICW + ICW],
                                        start=True, stop=True),
                               wait=w, inc="strp")
                        first = False
                        rvals[h] = P.val("strp")
                    for (h, slot, ri) in ((h0, sl0, 0), (h0 + 1, sl1, 1)):
                        rps = pB[:, slot * 512:(slot + 1) * 512]
                        P.emit("vector",
                               lambda e, X=X, h=h, ic=ic:
                               e.tensor_tensor(
                                   X[:, h * T + ic * ICW: h * T + ic * ICW + ICW],
                                   X[:, h * T + ic * ICW: h * T + ic * ICW + ICW],
                                   c_ctab[:, ic * ICW:(ic + 1) * ICW], op=OP.mult),
                               wait=[("strp", rvals[h])])
                        P.emit("vector",
                               lambda e, rps=rps, ri=ri, ic=ic:
                               e.tensor_tensor(
                                   rsc[:, ri * 512:(ri + 1) * 512], rps,
                                   c_stab[:, ic * ICW:(ic + 1) * ICW], op=OP.mult),
                               inc="svrp")
                        pb_gate[slot] = ("svrp", P.val("svrp"))
                    P.emit("vector", lambda e: e.drain())
                    for (h, ri) in ((h0, 0), (h0 + 1, 1)):
                        P.emit("vector",
                               lambda e, X=X, h=h, ic=ic, ri=ri:
                               e.tensor_tensor(
                                   X[:, h * T + ic * ICW: h * T + ic * ICW + ICW],
                                   X[:, h * T + ic * ICW: h * T + ic * ICW + ICW],
                                   rsc[:, ri * 512:(ri + 1) * 512], op=OP.add))
        P.barrier()

        P.dma("gpsimd", wo_t.rearrange("p (h m) -> p h m", h=HA),
              wo_in.rearrange("h p m -> p h m"), "swO",
              wait=[("stA1", P.val("stA1"))])
        swO_total = P.val("swO")

        A2expv = {}
        pend_ln = [None]
        x0gate2 = [0 if s == 0 else x0gate2[0]]
        for ic in range(NIC):
            njt = 4 * (ic + 1)
            w = [("stA2y", x0gate2[0])] if x0gate2[0] > 0 else []
            if ic == 0:
                w = list(w) + [("stA1v", P.val("stA1v"))]
            P.dma("gpsimd", x0c2.rearrange("p (f t) -> p f t", f=NF),
                  x_in[s].rearrange("(f p) t -> p f t", p=128)[:, :, ic * ICW:(ic + 1) * ICW],
                  "sx2", wait=w)
            sx2v = P.val("sx2")
            pump_wescr(6)

            A2maskv = {}

            for h in range(HA):
                if h == 2 and pend_ln[0] is not None:
                    pend_ln[0]()
                    pend_ln[0] = None
                ngrp = (njt + 1) // 2

                def a2_s(jg, h=h, ic=ic, njt=njt):
                    gpar = jg % 2
                    ps_s = pA[:, gpar * 1024:(gpar + 1) * 1024]
                    nt = min(2, njt - 2 * jg)
                    wargs = [("ssA2", A2expv.get(gpar, 0))]
                    for t_ in range(nt):
                        jt = 2 * jg + t_
                        P.emit("tensor",
                               lambda e, ps_s=ps_s, t_=t_, jt=jt, h=h, ic=ic:
                               e.matmul(ps_s[:, t_ * 512:(t_ + 1) * 512],
                                        K_[:, h * T + jt * 128: h * T + jt * 128 + 128],
                                        Q_[:, h * T + ic * ICW: h * T + ic * ICW + ICW],
                                        start=True, stop=True),
                               wait=wargs if t_ == 0 else None,
                               inc="stA2s" if t_ == nt - 1 else None)
                    stv = P.val("stA2s")
                    P.emit("scalar",
                           lambda e, gpar=gpar, ps_s=ps_s, nt=nt:
                           e.activation(ptil[:, gpar * 1024: gpar * 1024 + nt * 512],
                                        ps_s[:, 0:nt * 512], AF.Exp, scale=SCL_A),
                           wait=[("stA2s", stv)], inc="ssA2")
                    A2expv[gpar] = P.val("ssA2")
                    masked = (2 * jg) >= 4 * ic
                    if masked:
                        ssv = P.val("ssA2")
                        for t_ in range(nt):
                            jt = 2 * jg + t_
                            toff = (jt - 4 * ic) * 128
                            P.emit("vector",
                                   lambda e, gpar=gpar, t_=t_, toff=toff:
                                   e.tensor_tensor(
                                       ptil[:, gpar * 1024 + t_ * 512: gpar * 1024 + (t_ + 1) * 512],
                                       ptil[:, gpar * 1024 + t_ * 512: gpar * 1024 + (t_ + 1) * 512],
                                       c_stair[:, 384 - toff: 384 - toff + 512],
                                       op=OP.mult),
                                   wait=[("ssA2", ssv)] if t_ == 0 else None,
                                   inc="svA2m" if t_ == nt - 1 else None)
                        A2maskv[gpar] = P.val("svA2m")
                    else:
                        A2maskv[gpar] = None

                if s == 0:
                    pump_wescr(3)
                a2_s(0)
                for jg in range(ngrp):
                    gpar = jg % 2
                    nt = min(2, njt - 2 * jg)
                    if jg + 1 < ngrp:
                        a2_s(jg + 1)
                    if A2maskv[gpar] is not None:
                        pwait = [("svA2m", A2maskv[gpar])]
                    else:
                        pwait = [("ssA2", A2expv[gpar])]
                    if jg == 0:
                        pwait = pwait + [("svA2o", P.val("svA2o"))]
                    for t_ in range(nt):
                        jt = 2 * jg + t_
                        P.emit("tensor",
                               lambda e, jt=jt, gpar=gpar, t_=t_, h=h:
                               e.matmul(pB[:, 0:512],
                                        V_[:, jt * 1024 + h * 128: jt * 1024 + h * 128 + 128],
                                        ptil[:, gpar * 1024 + t_ * 512: gpar * 1024 + (t_ + 1) * 512],
                                        start=(jt == 0), stop=(jt == njt - 1)),
                               wait=pwait if t_ == 0 else None)
                        P.emit("tensor",
                               lambda e, jt=jt, gpar=gpar, t_=t_:
                               e.matmul(pB[:, 512:1024],
                                        c_ones[:, 0:128],
                                        ptil[:, gpar * 1024 + t_ * 512: gpar * 1024 + (t_ + 1) * 512],
                                        start=(jt == 0), stop=(jt == njt - 1)),
                               inc="stA2o" if jt == njt - 1 else None)
                stv = P.val("stA2o")
                P.emit("scalar", lambda e: e.activation(linv, pB[:, 512:1024], AF.Ln),
                       wait=[("stA2o", stv)])
                P.emit("scalar", lambda e: e.drain())
                P.emit("scalar", lambda e: e.activation(linv, linv, AF.Exp, scale=-1.0),
                       inc="ssA2r")
                ssrv = P.val("ssA2r")
                ovwait = [("ssA2r", ssrv)]
                if h == 0 and ic > 0:
                    ovwait.append(("stA2st", P.val("stA2st")))
                P.emit("vector",
                       lambda e, h=h: e.tensor_tensor(
                           oA[:, h * 512:(h + 1) * 512], pB[:, 0:512], linv, op=OP.mult),
                       wait=ovwait, inc="svA2o")
            sv_heads = P.val("svA2o")
            evy = {}
            for mc in range(NF):
                ypar = mc % 2
                ps_y = pB[:, 1024 + ypar * 512: 1024 + (ypar + 1) * 512]
                wargs = [("swO", swO_total), ("svA2o", sv_heads), ("sx2", sx2v),
                         ("svA2y", evy.get(ypar, 0))]
                if mc == 0:
                    wargs = wargs + [("ssL", P.val("ssL")), ("svA2x1", P.val("svA2x1"))]
                for h in range(HA):
                    P.emit("tensor",
                           lambda e, ps_y=ps_y, h=h, mc=mc:
                           e.matmul(ps_y, wo_t[:, h * 1024 + mc * 128: h * 1024 + mc * 128 + 128],
                                    oA[:, h * 512:(h + 1) * 512],
                                    start=(h == 0), stop=False),
                           wait=wargs if h == 0 else None)
                P.emit("tensor",
                       lambda e, ps_y=ps_y, mc=mc:
                       e.matmul(ps_y, c_idn, x0c2[:, mc * 512:(mc + 1) * 512],
                                start=False, stop=True),
                       inc="stA2y")
                stv = P.val("stA2y")
                P.emit("vector",
                       lambda e, ps_y=ps_y, mc=mc: e.tensor_copy(ybuf[:, mc * 512:(mc + 1) * 512], ps_y),
                       wait=[("stA2y", stv)], inc="svA2y")
                evy[ypar] = P.val("svA2y")
            sv_y = P.val("svA2y")
            x0gate2[0] = P.val("stA2y")
            for mc in range(NF):
                P.emit("vector",
                       lambda e, mc=mc: e.tensor_tensor(
                           ysqA[:, mc * 512:(mc + 1) * 512],
                           ybuf[:, mc * 512:(mc + 1) * 512],
                           ybuf[:, mc * 512:(mc + 1) * 512], op=OP.mult),
                       wait=[("svA2y", sv_y)] if mc == 0 else None,
                       inc="svA2q" if mc == NF - 1 else None)
            sv_q = P.val("svA2q")
            for mc in range(NF):
                P.emit("tensor",
                       lambda e, mc=mc: e.matmul(pB[0:1, 1024:1536], c_ones[:, 0:1],
                                                 ybuf[:, mc * 512:(mc + 1) * 512],
                                                 start=(mc == 0), stop=(mc == NF - 1)),
                       wait=[("svA2y", sv_y)] if mc == 0 else None)
            for mc in range(NF):
                P.emit("tensor",
                       lambda e, mc=mc: e.matmul(pB[0:1, 1536:2048], c_ones[:, 0:1],
                                                 ysqA[:, mc * 512:(mc + 1) * 512],
                                                 start=(mc == 0), stop=(mc == NF - 1)),
                       wait=[("svA2q", sv_q)] if mc == 0 else None,
                       inc="stA2st" if mc == NF - 1 else None)

            def a2_ln(ic=ic, stv=P.val("stA2st"), sv_y=sv_y):
                P.emit("scalar", lambda e: e.mul(mean_r, pB[0:1, 1024:1536], 1.0 / D),
                       wait=[("stA2st", stv)])
                P.emit("scalar", lambda e: e.mul(es2_r, pB[0:1, 1536:2048], 1.0 / D), inc="ssL")
                ssv = P.val("ssL")
                P.emit("vector", lambda e: e.tensor_tensor(msq_r, mean_r, mean_r, op=OP.mult),
                       wait=[("ssL", ssv)])
                P.emit("vector", lambda e: e.drain())
                P.emit("vector", lambda e: e.tensor_tensor(var_r, es2_r, msq_r, op=OP.subtract),
                       inc="svL")
                svv = P.val("svL")
                P.emit("scalar", lambda e: e.activation(lnv_r, var_r, AF.Ln, bias=eps_r),
                       wait=[("svL", svv)])
                P.emit("scalar", lambda e: e.drain())
                P.emit("scalar", lambda e: e.activation(rstd_r, lnv_r, AF.Exp, scale=-0.5),
                       inc="ssL2")
                ssv = P.val("ssL2")
                P.emit("vector", lambda e: e.tensor_copy(a_bf, rstd_r), wait=[("ssL2", ssv)])
                P.emit("vector", lambda e: e.scalar_tensor_tensor(
                    bneg_r, mean_r, -1.0, rstd_r, op0=OP.mult, op1=OP.mult))
                P.emit("vector", lambda e: e.drain())
                P.emit("vector", lambda e: e.tensor_copy(b_bf, bneg_r), inc="svL2")
                svv = P.val("svL2")
                P.emit("tensor", lambda e: e.matmul(pB[:, 1024:1536], c_ones[0:1, 0:128], a_bf,
                                                    start=True, stop=True),
                       wait=[("svL2", svv), ("svA2y", sv_y)])
                P.emit("tensor", lambda e: e.matmul(pB[:, 1536:2048], c_ones[0:1, 0:128], b_bf,
                                                    start=True, stop=True), inc="stA2b")
                stv2 = P.val("stA2b")
                xpar = ic % 2
                wspill = [("sspill%d" % xpar, P.val("sspill%d" % xpar))]
                for mc in range(NF):
                    P.emit("vector",
                           lambda e, mc=mc, xpar=xpar: e.tensor_tensor(
                               x1buf[xpar][:, mc * 512:(mc + 1) * 512],
                               ybuf[:, mc * 512:(mc + 1) * 512], pB[:, 1024:1536], op=OP.mult),
                           wait=([("stA2b", stv2)] + wspill) if mc == 0 else None)
                for mc in range(NF):
                    P.emit("vector",
                           lambda e, mc=mc, xpar=xpar: e.tensor_tensor(
                               x1buf[xpar][:, mc * 512:(mc + 1) * 512],
                               x1buf[xpar][:, mc * 512:(mc + 1) * 512], pB[:, 1536:2048], op=OP.add),
                           inc="svA2x1" if mc == NF - 1 else None)
                svv = P.val("svA2x1")
                P.dma("sync",
                      x1_scr[s].rearrange("(f p) t -> p f t", p=128)[:, :, ic * ICW:(ic + 1) * ICW],
                      x1buf[xpar].rearrange("p (f t) -> p f t", f=NF),
                      "sspill%d" % xpar, wait=[("svA2x1", svv)])

            if ic == NIC - 1:
                a2_ln()
                pend_ln[0] = None
            else:
                pend_ln[0] = a2_ln

        spill_tot = [("sspill0", P.val("sspill0")), ("sspill1", P.val("sspill1"))]
        evb = {}
        evbv = {}
        for half in range(2):
            def emit_pieces(half=half):
                for kcp in range(8):
                    if half == 0 and kcp == 0:
                        wq_w = [("ssA2", P.val("ssA2")), ("stA2o", P.val("stA2o")),
                                ("stA2y", P.val("stA2y")), ("svA2o", P.val("svA2o")),
                                ("svA2q", P.val("svA2q")), ("stA2st", P.val("stA2st")),
                                ("ssB2", P.val("ssB2")), ("stB3m", P.val("stB3m"))]
                        if s == 0:
                            wq_w = wq_w + bounce_done_wait(wqkv_scr_njobs)
                    elif half == 1 and kcp == 0:
                        wq_w = [("stB1", P.val("stB1"))]
                    else:
                        wq_w = None
                    P.dma("sync", wqkv_t.rearrange("p (c n) -> p c n", c=8)[:, kcp:kcp + 1],
                          wqkv_scr.rearrange("(c p) n -> p c n", p=128)[:, kcp:kcp + 1,
                                                                        half * 1536:(half + 1) * 1536],
                          "swQk%d" % kcp, wait=wq_w)
                return [P.val("swQk%d" % kcp) for kcp in range(8)]

            swq_piece = None
            for ic in range(NIC):
                par = ic % 2
                gs_, gv_ = x1gate.get(par, ("stB1", 0))
                w = [(gs_, gv_)]
                if half == 0:
                    w = w + spill_tot
                P.dma("sync", x1c[par].rearrange("p (f t) -> p f t", f=NF),
                      x1_scr[s].rearrange("(f p) t -> p f t", p=128)[:, :, ic * ICW:(ic + 1) * ICW],
                      "sx1%d" % par, wait=w)
                if swq_piece is None:
                    swq_piece = emit_pieces()
                sx1v = P.val("sx1%d" % par)
                pump_wescr(8)
                base = [("swQk7", swq_piece[7]), ("sx1%d" % par, sx1v)]
                for oci in range(12):
                    oc = half * 12 + oci
                    if oc >= 16:
                        continue
                    dst = Q_ if oc < 8 else K_
                    dt = oc % 8
                    pp = oci % 2
                    ps = pA[:, pp * 512:(pp + 1) * 512]
                    wv_ = [("svB1", evb.get(pp, 0))]
                    for kc in range(NF):
                        kw = (base + wv_) if kc == 0 else None
                        if ic == 0 and oci < 2:
                            kw = (kw or []) + [("swQk%d" % kc, swq_piece[kc])]
                        P.emit("tensor",
                               lambda e, ps=ps, kc=kc, oci=oci, par=par:
                               e.matmul(ps, wqkv_t[:, kc * 1536 + oci * 128: kc * 1536 + oci * 128 + 128],
                                        x1c[par][:, kc * 512:(kc + 1) * 512],
                                        start=(kc == 0), stop=False),
                               wait=kw)
                    P.emit("tensor",
                           lambda e, ps=ps, oc=oc:
                           e.matmul(ps, c_brow[0:1, oc * 128:(oc + 1) * 128],
                                    c_ones[0:1, 0:512], start=False, stop=True),
                           inc="stB1")
                    stv = P.val("stB1")
                    P.emit("vector",
                           lambda e, ps=ps, dst=dst, dt=dt, ic=ic: e.tensor_copy(fm(dst, dt, ic), ps),
                           wait=[("stB1", stv)], inc="svB1")
                    evb[pp] = P.val("svB1")
                if half == 1:
                    for jt4 in range(4):
                        jt = ic * 4 + jt4
                        pp = jt4 % 2
                        ps = pB[:, pp * 1024:(pp + 1) * 1024]
                        wv_ = [("svB1v", evbv.get(pp, 0))]
                        if ic == 0 and jt4 == 0:
                            wv_ = wv_ + [("swQk%d" % k2, swq_piece[k2]) for k2 in range(8)]
                        for kc in range(NF):
                            for vh in range(2):
                                P.emit("tensor",
                                       lambda e, ps=ps, kc=kc, jt4=jt4, par=par, vh=vh:
                                       e.matmul(ps[:, vh * 512:(vh + 1) * 512],
                                                x1c[par][:, kc * 512 + jt4 * 128: kc * 512 + jt4 * 128 + 128],
                                                wqkv_t[:, kc * 1536 + 512 + vh * 512: kc * 1536 + 512 + (vh + 1) * 512],
                                                start=(kc == 0), stop=False),
                                       wait=(base + wv_) if (kc == 0 and vh == 0) else None)
                        for vh in range(2):
                            P.emit("tensor",
                                   lambda e, ps=ps, vh=vh:
                                   e.matmul(ps[:, vh * 512:(vh + 1) * 512],
                                            c_ones[0:1, 0:128],
                                            c_brow[0:1, 2048 + vh * 512: 2048 + (vh + 1) * 512],
                                            start=False, stop=True),
                                   inc="stB1v" if vh == 1 else None)
                        stv = P.val("stB1v")
                        P.emit("vector",
                               lambda e, ps=ps, jt=jt:
                               e.tensor_copy(
                                   V_[:, jt * VST: jt * VST + VST].rearrange("p (h c) -> p h c", h=HB)[:, :, 0:64],
                                   ps.rearrange("p (h c) -> p h c", h=HB)),
                               wait=[("stB1v", stv)], inc="svB1v")
                        evbv[pp] = P.val("svB1v")
                    for it in range(4):
                        wv_ = [("ssB1g", P.val("ssB1g"))]
                        for kc in range(NF):
                            P.emit("tensor",
                                   lambda e, kc=kc, it=it, par=par:
                                   e.matmul(pA[:, 1024:1032],
                                            x1c[par][:, kc * 512 + it * 128: kc * 512 + it * 128 + 128],
                                            c_wgate[:, kc * 8:(kc + 1) * 8],
                                            start=(kc == 0), stop=False),
                                   wait=(base + wv_) if kc == 0 else None)
                        P.emit("tensor",
                               lambda e: e.matmul(pA[:, 1024:1032], c_ones[0:1, 0:128],
                                                  c_bgate, start=False, stop=True),
                               inc="stB1g")
                        stv = P.val("stB1g")
                        P.emit("vector",
                               lambda e, it=it: e.reduce_max(negm_g[:, it:it + 1],
                                                             pA[:, 1024:1032], axis=AX.X,
                                                             negate=True),
                               wait=[("stB1g", stv)], inc="svB1g")
                        svv = P.val("svB1g")
                        P.emit("scalar",
                               lambda e, it=it: e.activation(gexp[:, it * 8:(it + 1) * 8],
                                                             pA[:, 1024:1032], AF.Exp,
                                                             bias=negm_g[:, it:it + 1],
                                                             accum_out=gsum[:, it:it + 1]),
                               wait=[("svB1g", svv)], inc="ssB1g")
                    ssv = P.val("ssB1g")
                    for it in range(4):
                        P.emit("vector", lambda e, it=it: e.reciprocal(grs[:, it:it + 1], gsum[:, it:it + 1]),
                               wait=[("ssB1g", ssv)] if it == 0 else None)
                    for it in range(4):
                        P.emit("vector", lambda e, it=it: e.tensor_tensor(
                            gsm[:, it * 8:(it + 1) * 8], gexp[:, it * 8:(it + 1) * 8],
                            grs[:, it:it + 1].broadcast_to([128, 8]), op=OP.mult))
                    for it in range(4):
                        P.emit("vector", lambda e, it=it: e.reduce_max(
                            gm1[:, it:it + 1], gsm[:, it * 8:(it + 1) * 8], axis=AX.X))
                    for it in range(4):
                        P.emit("vector", lambda e, it=it: e.tensor_tensor(
                            gmk[:, it * 8:(it + 1) * 8], gsm[:, it * 8:(it + 1) * 8],
                            gm1[:, it:it + 1].broadcast_to([128, 8]), op=OP.is_ge))
                    for it in range(4):
                        P.emit("vector", lambda e, it=it: e.tensor_tensor(
                            gzap[:, it * 8:(it + 1) * 8], gsm[:, it * 8:(it + 1) * 8],
                            gmk[:, it * 8:(it + 1) * 8], op=OP.subtract))
                    for it in range(4):
                        P.emit("vector", lambda e, it=it: e.reduce_max(
                            gm2[:, it:it + 1], gzap[:, it * 8:(it + 1) * 8], axis=AX.X))
                    for it in range(4):
                        P.emit("vector", lambda e, it=it: e.tensor_tensor(
                            gmk[:, it * 8:(it + 1) * 8], gsm[:, it * 8:(it + 1) * 8],
                            gm2[:, it:it + 1].broadcast_to([128, 8]), op=OP.is_ge))
                    for it in range(4):
                        P.emit("vector", lambda e, it=it: e.tensor_tensor(
                            gsm[:, it * 8:(it + 1) * 8], gsm[:, it * 8:(it + 1) * 8],
                            gmk[:, it * 8:(it + 1) * 8], op=OP.mult))
                    for it in range(4):
                        P.emit("vector", lambda e, it=it: e.tensor_copy(
                            gcomb_bf[:, it * 8:(it + 1) * 8], gsm[:, it * 8:(it + 1) * 8]),
                               inc="svB1c" if it == 3 else None)
                    svv = P.val("svB1c")
                    for it in range(4):
                        P.emit("tensor",
                               lambda e, it=it: e.transpose(
                                   pA[0:8, 1536:1600].bitcast(BF16),
                                   gcomb_bf[:, it * 8:(it + 1) * 8],
                                   c_idn),
                               wait=([("svB1c", svv)] if it == 0 else
                                     [("svB1t", P.val("svB1t"))]),
                               inc="stB1t")
                        stv = P.val("stB1t")
                        P.emit("vector",
                               lambda e, it=it, ic=ic: e.tensor_copy(
                                   combT[:, ic * ICW + it * 128: ic * ICW + (it + 1) * 128],
                                   pA[0:8, 1536:1600].bitcast(BF16)),
                               wait=[("stB1t", stv)], inc="svB1t")
                x1gate[par] = (("stB1g", P.val("stB1g")) if half == 1
                               else ("stB1", P.val("stB1")))
        P.emit("vector",
               lambda e: e.memset(
                   V_[:, 0:NJT * VST].rearrange("p (j h c) -> p j h c", j=NJT, h=HB)[:, :, :, 64:65], 1.0),
               inc="svB1o")
        sv_vones = P.val("svB1o")
        b1_done = [("svB1", P.val("svB1")), ("svB1v", P.val("svB1v")),
                   ("ssB1g", P.val("ssB1g")), ("svB1t", P.val("svB1t"))]

        B2expv = {}
        pend_epi = [None]
        pend_epi2 = [None]
        for ic in range(NIC):
            for g in range(8):
                hA2, hB2 = 2 * g, 2 * g + 1

                def b2_s(jt, g=g, ic=ic):
                    gpar = jt % 2
                    ps_s = pA[:, gpar * 1024:(gpar + 1) * 1024]
                    wargs = [("ssB2", B2expv.get(gpar, 0))]
                    if g == 0 and ic == 0 and jt <= 1:
                        wargs = wargs + [("svB1o", sv_vones)] + b1_done
                    P.emit("tensor",
                           lambda e, ps_s=ps_s, g=g, jt=jt, ic=ic:
                           e.matmul(ps_s[:, 0:512],
                                    K_[0:64, g * T + jt * 128: g * T + jt * 128 + 128],
                                    Q_[0:64, g * T + ic * ICW: g * T + ic * ICW + ICW],
                                    start=True, stop=True),
                           wait=wargs)
                    P.emit("tensor",
                           lambda e, ps_s=ps_s, g=g, jt=jt, ic=ic:
                           e.matmul(ps_s[:, 512:1024],
                                    K_[64:128, g * T + jt * 128: g * T + jt * 128 + 128],
                                    Q_[64:128, g * T + ic * ICW: g * T + ic * ICW + ICW],
                                    start=True, stop=True, tile_position=(64, 0)),
                           inc="stB2s")
                    stv = P.val("stB2s")
                    P.emit("scalar",
                           lambda e, gpar=gpar, ps_s=ps_s:
                           e.activation(ptilB[:, gpar * 1024:(gpar + 1) * 1024],
                                        ps_s, AF.Exp, scale=SCL_B),
                           wait=[("stB2s", stv)], inc="ssB2")
                    B2expv[gpar] = P.val("ssB2")

                b2_s(0)
                for jt in range(NJT):
                    gpar = jt % 2
                    if jt + 1 < NJT:
                        b2_s(jt + 1)
                    if jt == 0 and pend_epi[0] is not None:
                        pend_epi[0]()
                        pend_epi[0] = None
                    if jt == 3 and pend_epi2[0] is not None:
                        pend_epi2[0]()
                        pend_epi2[0] = None
                    ssv = B2expv[gpar]
                    owait = [("ssB2", ssv)]
                    if jt == 0:
                        owait = owait + [("svB2e", P.val("svB2e"))]
                    P.emit("tensor",
                           lambda e, jt=jt, gpar=gpar, hA2=hA2:
                           e.matmul(pB[0:65, 0:512],
                                    V_[:, jt * VST + hA2 * 65: jt * VST + hA2 * 65 + 65],
                                    ptilB[:, gpar * 1024: gpar * 1024 + 512],
                                    start=(jt == 0), stop=(jt == NJT - 1)),
                           wait=owait)
                    P.emit("tensor",
                           lambda e, jt=jt, gpar=gpar, hB2=hB2:
                           e.matmul(pB[64:128, 512:1024],
                                    V_[:, jt * VST + hB2 * 65: jt * VST + hB2 * 65 + 64],
                                    ptilB[:, gpar * 1024 + 512: gpar * 1024 + 1024],
                                    start=(jt == 0), stop=(jt == NJT - 1),
                                    tile_position=(0, 64)))
                    P.emit("tensor",
                           lambda e, jt=jt, gpar=gpar:
                           e.matmul(pB[:, 1024:1536],
                                    c_ones[:, 0:128],
                                    ptilB[:, gpar * 1024 + 512: gpar * 1024 + 1024],
                                    start=(jt == 0), stop=(jt == NJT - 1)),
                           inc="stB2o" if jt == NJT - 1 else None)
                stv = P.val("stB2o")

                def epilogue(g=g, ic=ic, stv=stv):
                    P.emit("vector", lambda e: e.tensor_copy(OEV[0:64, 0:512], pB[0:64, 0:512]),
                           wait=[("stB2o", stv)])
                    P.emit("vector", lambda e: e.tensor_copy(OEV[64:128, 512:1024],
                                                             pB[64:128, 512:1024]))
                    P.emit("vector", lambda e: e.tensor_copy(linve_r, pB[64:65, 0:512]))
                    P.emit("vector", lambda e: e.tensor_copy(linvo_r, pB[0:1, 1024:1536]),
                           inc="svB2e")
                    svev = P.val("svB2e")
                    P.emit("scalar", lambda e: e.activation(linve_r, linve_r, AF.Ln),
                           wait=[("svB2e", svev)])
                    P.emit("scalar", lambda e: e.activation(linvo_r, linvo_r, AF.Ln))
                    P.emit("scalar", lambda e: e.drain())
                    P.emit("scalar", lambda e: e.activation(lrow[64:65, :], linve_r, AF.Exp, scale=-1.0))
                    P.emit("scalar", lambda e: e.activation(lrow[0:1, :], linvo_r, AF.Exp, scale=-1.0),
                           inc="ssB2r")
                    svv = P.val("ssB2r")

                    def epi2(g=g, ic=ic, svv=svv):
                        P.emit("tensor", lambda e: e.matmul(pB[0:64, 1536:2048], c_ones[64:65, 0:64],
                                                            lrow[64:65, :], start=True, stop=True,
                                                            tile_position=(64, 0)),
                               wait=[("ssB2r", svv), ("ssB2f", P.val("ssB2f"))])
                        P.emit("tensor", lambda e: e.matmul(pB[64:128, 1536:2048], c_ones[0:1, 0:64],
                                                            lrow[0:1, :], start=True, stop=True,
                                                            tile_position=(0, 64)),
                               inc="stB2b")
                        stv2 = P.val("stB2b")
                        P.emit("vector", lambda e: e.tensor_copy(bcB, pB[:, 1536:2048]),
                               wait=[("stB2b", stv2)])
                        P.emit("vector", lambda e: e.drain())
                        P.emit("vector",
                               lambda e, g=g, ic=ic: e.tensor_tensor(
                                   ATB[0:64, g * T + ic * ICW: g * T + ic * ICW + ICW],
                                   OEV[0:64, 0:512], bcB[0:64, :], op=OP.mult))
                        P.emit("vector",
                               lambda e, g=g, ic=ic: e.tensor_tensor(
                                   ATB[64:128, g * T + ic * ICW: g * T + ic * ICW + ICW],
                                   OEV[64:128, 512:1024], bcB[64:128, :], op=OP.mult),
                               inc="ssB2f")

                    pend_epi2[0] = epi2

                pend_epi[0] = epilogue
        if pend_epi[0] is not None:
            pend_epi[0]()
            pend_epi[0] = None
        if pend_epi2[0] is not None:
            pend_epi2[0]()
            pend_epi2[0] = None
        sv_atb = P.val("ssB2f")
        if debug:
            P.dma("gpsimd", dbg2[s].rearrange("(f p) t -> p f t", p=128),
                  ATB.rearrange("p (f t) -> p f t", f=NF), "sdbg2",
                  wait=[("ssB2f", sv_atb)])
            P.dma("gpsimd", dbg3[s], combT, "sdbg2")
        b2_done = [("ssB2", P.val("ssB2"))]
        pump_wescr(N_JOBS)
        wescr_wait = bounce_done_wait(N_JOBS)

        for dp in range(8):
            xw = [("stB2s", P.val("stB2s"))] if dp == 0 else None
            P.dma("sync", X1F.rearrange("p (f t) -> p f t", f=NF)[:, dp:dp + 1],
                  x1_scr[s].rearrange("(f p) t -> p f t", p=128)[:, dp:dp + 1], "sxfp%d" % dp,
                  wait=xw)
        sxf_piece = [P.val("sxfp%d" % dp) for dp in range(8)]
        def cb_emit(iq):
            cwait_extra = [("svB3y", P.val("svB3y"))] + (b2_done if iq == 0 else [])
            for e_ in range(NE):
                cps = pA[:, 1024:1536] if e_ % 2 == 0 else pA[:, 1536:2048]
                w = [("svB3c", P.val("svB3c") - 1 if P.val("svB3c") >= 2 else 0)]
                if e_ == 0:
                    w = w + cwait_extra
                P.emit("tensor",
                       lambda e, cps=cps, e_=e_, iq=iq: e.matmul(
                           cps, c_sel[:, e_ * 128:(e_ + 1) * 128],
                           combT[:, iq * ICW:(iq + 1) * ICW],
                           start=True, stop=True),
                       wait=w, inc="stB3c")
                stv = P.val("stB3c")
                P.emit("vector",
                       lambda e, cps=cps, e_=e_: e.tensor_copy(
                           CBSB[:, e_ * 512:(e_ + 1) * 512], cps),
                       wait=[("stB3c", stv)], inc="svB3c")
            return P.val("svB3c")

        cb_next = [None]
        pend_ap = [[]]
        for iq in range(NIC):
            sv_cb = cb_emit(0) if iq == 0 else cb_next[0]
            if debug and s == 0:
                P.dma("gpsimd", dbg5[iq], CBSB, "sdbg2", wait=[("svB3c", sv_cb)])
            mmfirst = True
            for e_ in range(NE):
                for d_ in range(NF):
                    idx = e_ * NF + d_
                    ring = idx % 4
                    w = [("stB3r%d" % ring, P.val("stB3r%d" % ring))]
                    if iq == 0 and idx < 4:
                        w = w + wescr_wait + [("stB2o", P.val("stB2o"))]
                    if direct_we:
                        P.dma("gpsimd", WER[ring], wexp_in[e_, d_ * 128:(d_ + 1) * 128, :],
                              "seW%d" % ring, wait=w)
                    else:
                        P.dma("sync", WER[ring], we_scr[e_, d_ * 128:(d_ + 1) * 128, :],
                              "seW%d" % ring, wait=w)
                    sew = P.val("seW%d" % ring)
                    xr = idx % 4
                    wx = [("stB3r%d" % xr, P.val("stB3r%d" % xr))]
                    if iq == 0 and e_ == 0:
                        wx = wx + [("sxfp%d" % d_, sxf_piece[d_])]
                    if idx == 0:
                        wx = wx + [("svB3c", sv_cb)]
                    first_mm_extra = ([("svB3L3", P.val("svB3L3"))] if idx == 0 else [])
                    if nocb:
                        P.emit("vector",
                               lambda e, xr=xr, d_=d_, iq=iq: e.tensor_copy(
                                   XER[xr], X1F[:, d_ * T + iq * ICW: d_ * T + iq * ICW + ICW]),
                               wait=wx, inc="svB3e")
                    else:
                        P.emit("vector",
                               lambda e, xr=xr, d_=d_, e_=e_, iq=iq: e.tensor_tensor(
                                   XER[xr], X1F[:, d_ * T + iq * ICW: d_ * T + iq * ICW + ICW],
                                   CBSB[:, e_ * 512:(e_ + 1) * 512], op=OP.mult),
                               wait=wx)
                        P.emit("vector", lambda e: e.drain(), inc="svB3e")
                    if idx % 8 == 7 and pend_ap[0]:
                        pend_ap[0].pop(0)()
                    svv = P.val("svB3e")
                    fcs = [2, 3, 4, 5, 6, 7, 0, 1]
                    for fi, fc in enumerate(fcs):
                        pbank = (pA if fc < 4 else pB)[:, (fc % 4) * 512:((fc % 4) + 1) * 512]
                        lhs_ap = c_idn if widn else WER[ring][:, fc * 128:(fc + 1) * 128]
                        rhs_ap = (X1F[:, d_ * T + iq * ICW: d_ * T + iq * ICW + ICW]
                                  if xdirect else XER[xr])
                        mmw = None
                        if fi == 0:
                            mmw = [("seW%d" % ring, sew), ("svB3e", svv)]
                        if fc == 0:
                            mmw = (mmw or []) + first_mm_extra
                        P.emit("tensor",
                               lambda e, pbank=pbank, lhs_ap=lhs_ap, rhs_ap=rhs_ap, st=(e_ == 0 and d_ == 0):
                               e.matmul(pbank, lhs_ap, rhs_ap, start=st, stop=False),
                               wait=mmw,
                               inc="stB3r%d" % ring if fi == NF - 1 else None)
            for fc in range(NF):
                pbank = (pA if fc < 4 else pB)[:, (fc % 4) * 512:((fc % 4) + 1) * 512]
                if skipadds:
                    P.emit("tensor",
                           lambda e, pbank=pbank, fc=fc, iq=iq:
                           e.matmul(pbank, c_bexp[:, fc * 128:(fc + 1) * 128],
                                    combT[:, iq * ICW:(iq + 1) * ICW], start=False, stop=True),
                           wait=[("ssB2f", sv_atb)] if fc == 0 else None,
                           inc="stB3m")
                    continue
                P.emit("tensor",
                       lambda e, pbank=pbank, fc=fc, iq=iq:
                       e.matmul(pbank, c_bexp[:, fc * 128:(fc + 1) * 128],
                                combT[:, iq * ICW:(iq + 1) * ICW], start=False, stop=False))
                P.emit("tensor",
                       lambda e, pbank=pbank, fc=fc, iq=iq:
                       e.matmul(pbank, c_idn, ATB[:, fc * T + iq * ICW: fc * T + iq * ICW + ICW],
                                start=False, stop=False),
                       wait=[("ssB2f", sv_atb)] if fc == 0 else None)
                P.emit("tensor",
                       lambda e, pbank=pbank, fc=fc, iq=iq:
                       e.matmul(pbank, c_idn, X1F[:, fc * T + iq * ICW: fc * T + iq * ICW + ICW],
                                start=False, stop=True),
                       wait=([("sxfp%d" % fc, sxf_piece[fc])] if iq == 0 else None),
                       inc="stB3m")
            stv = P.val("stB3m")
            for fc in range(NF):
                pbank = (pA if fc < 4 else pB)[:, (fc % 4) * 512:((fc % 4) + 1) * 512]
                P.emit("vector",
                       lambda e, pbank=pbank, fc=fc: e.tensor_copy(
                           Y2B[:, fc * 512:(fc + 1) * 512], pbank),
                       wait=[("stB3m", stv)] if fc == 0 else None,
                       inc="svB3y" if fc == NF - 1 else None)
            sv_y2 = P.val("svB3y")
            if debug:
                P.dma("gpsimd", dbg4[s].rearrange("(f p) t -> p f t", p=128)[:, :, iq * ICW:(iq + 1) * ICW],
                      Y2B.rearrange("p (f t) -> p f t", f=NF), "sdbg2",
                      wait=[("svB3y", sv_y2)])
            for fc in range(NF):
                P.emit("vector",
                       lambda e, fc=fc: e.tensor_tensor(
                           YSQB[:, fc * 512:(fc + 1) * 512],
                           Y2B[:, fc * 512:(fc + 1) * 512],
                           Y2B[:, fc * 512:(fc + 1) * 512], op=OP.mult),
                       inc="svB3q" if fc == NF - 1 else None)
            sv_q2 = P.val("svB3q")
            for fc in range(NF):
                P.emit("tensor",
                       lambda e, fc=fc: e.matmul(pA[0:1, 0:512], c_ones[:, 0:1],
                                                 Y2B[:, fc * 512:(fc + 1) * 512],
                                                 start=(fc == 0), stop=(fc == NF - 1)),
                       wait=[("svB3y", sv_y2)] if fc == 0 else None)
            for fc in range(NF):
                P.emit("tensor",
                       lambda e, fc=fc: e.matmul(pA[0:1, 512:1024], c_ones[:, 0:1],
                                                 YSQB[:, fc * 512:(fc + 1) * 512],
                                                 start=(fc == 0), stop=(fc == NF - 1)),
                       wait=[("svB3q", sv_q2)] if fc == 0 else None,
                       inc="stB3st" if fc == NF - 1 else None)
            stv = P.val("stB3st")
            if iq + 1 < NIC:
                cb_next[0] = cb_emit(iq + 1)
            P.emit("scalar", lambda e: e.mul(mean_r, pA[0:1, 0:512], 1.0 / D),
                   wait=[("stB3st", stv)])
            P.emit("scalar", lambda e: e.mul(es2_r, pA[0:1, 512:1024], 1.0 / D), inc="ssL")
            ssv = P.val("ssL")
            P.emit("vector", lambda e: e.tensor_tensor(msq_r, mean_r, mean_r, op=OP.mult),
                   wait=[("ssL", ssv)])
            P.emit("vector", lambda e: e.drain())
            P.emit("vector", lambda e: e.tensor_tensor(var_r, es2_r, msq_r, op=OP.subtract),
                   inc="svL")
            svv = P.val("svL")
            P.emit("scalar", lambda e: e.activation(lnv_r, var_r, AF.Ln, bias=eps_r),
                   wait=[("svL", svv)])
            P.emit("scalar", lambda e: e.drain())
            P.emit("scalar", lambda e: e.activation(rstd_r, lnv_r, AF.Exp, scale=-0.5),
                   inc="ssL2")
            ssv = P.val("ssL2")
            P.emit("vector", lambda e: e.tensor_copy(a_bf, rstd_r), wait=[("ssL2", ssv)])
            P.emit("vector", lambda e: e.scalar_tensor_tensor(
                bneg_r, mean_r, -1.0, rstd_r, op0=OP.mult, op1=OP.mult))
            P.emit("vector", lambda e: e.drain())
            P.emit("vector", lambda e: e.tensor_copy(b_bf, bneg_r), inc="svL2")
            svv = P.val("svL2")
            P.emit("tensor", lambda e: e.matmul(pA[:, 0:512], c_ones[0:1, 0:128], a_bf,
                                                start=True, stop=True),
                   wait=[("svL2", svv)])
            P.emit("tensor", lambda e: e.matmul(pA[:, 512:1024], c_ones[0:1, 0:128], b_bf,
                                                start=True, stop=True), inc="stB3b")
            stv = P.val("stB3b")
            P.emit("vector", lambda e: e.tensor_copy(a_sb, pA[:, 0:512]),
                   wait=[("stB3b", stv)])
            P.emit("vector", lambda e: e.tensor_copy(b_sb, pA[:, 512:1024]),
                   inc="svB3L3")
            P.emit("vector", lambda e: e.drain())

            def mk_apply(fc0, iq=iq):
                def go():
                    wout = [("sout", P.val("sout"))] if fc0 == 0 else None
                    for fc in range(fc0, fc0 + 2):
                        P.emit("vector",
                               lambda e, fc=fc: e.tensor_tensor(
                                   T1OUT[:, fc * 512:(fc + 1) * 512],
                                   Y2B[:, fc * 512:(fc + 1) * 512], a_sb, op=OP.mult),
                               wait=wout if fc == fc0 else None)
                    for fc in range(fc0, fc0 + 2):
                        P.emit("vector",
                               lambda e, fc=fc: e.tensor_tensor(
                                   T1OUT[:, fc * 512:(fc + 1) * 512],
                                   T1OUT[:, fc * 512:(fc + 1) * 512],
                                   b_sb, op=OP.add),
                               inc="svB3o" if fc == fc0 + 1 and fc0 == NF - 2 else None)
                    if fc0 == NF - 2:
                        svv = P.val("svB3o")
                        P.dma("sync",
                              out_hbm[s].rearrange("(f p) t -> p f t", p=128)[:, :, iq * ICW:(iq + 1) * ICW],
                              T1OUT.rearrange("p (f t) -> p f t", f=NF),
                              "sout", wait=[("svB3o", svv)])
                return go

            if iq == NIC - 1:
                for fc0 in range(0, NF, 2):
                    mk_apply(fc0)()
                pend_ap[0] = []
            else:
                pend_ap[0] = [mk_apply(fc0) for fc0 in range(0, NF, 2)]

    if debug:
        P.dma("gpsimd", dbg1, x1_scr, "sdbg")
        P.emit("gpsimd", lambda e: None,
               wait=[("sdbg", P.val("sdbg")), ("sdbg2", P.val("sdbg2"))])
    P.emit("sync", lambda e: None, wait=[("sout", P.val("sout"))])

    with nc.Block() as block:
        @block.tensor
        def _(eng):
            for fn in P.q["tensor"]:
                fn(eng)

        @block.vector
        def _(eng):
            for fn in P.q["vector"]:
                fn(eng)

        @block.scalar
        def _(eng):
            for fn in P.q["scalar"]:
                fn(eng)

        @block.gpsimd
        def _(eng):
            for fn in P.q["gpsimd"]:
                fn(eng)

        @block.sync
        def _(eng):
            for fn in P.q["sync"]:
                fn(eng)

    return nc


_NC_CACHE = [None]


def _host_constants():
    pos = np.arange(T, dtype=np.float64)[:, None]
    w = np.arange(32, dtype=np.float64)[None, :]
    ang = np.power(10000.0, -w / 32.0)
    rad = (pos * ang).astype(np.float32)
    cos = np.cos(rad).T.astype(np.float32)
    sin = np.sin(rad).T.astype(np.float32)
    ctab = np.ones((128, T), dtype=np.float32)
    ctab[0:32] = cos
    ctab[32:64] = cos
    stab = np.zeros((128, T), dtype=np.float32)
    stab[0:32] = -sin
    stab[32:64] = sin
    perm = np.zeros((128, 128), dtype=np.float32)
    for i in range(32):
        perm[32 + i, i] = 1.0
        perm[i, 32 + i] = 1.0
    r = np.arange(128)[:, None]
    u = np.arange(896)[None, :]
    stair = (r <= u - 384).astype(np.float32)
    idn = np.eye(128, dtype=np.float32)
    sel = np.zeros((8, 1024), dtype=np.float32)
    for e in range(8):
        sel[e, e * 128:(e + 1) * 128] = 1.0
    return ctab, stab, perm, stair, idn, sel


def kernel(L0_attn_x, L0_attn_wq, L0_attn_wk, L0_attn_wv, L0_attn_wo,
           L0_ffn_W_qkv, L0_ffn_b_qkv, L0_ffn_W_gate, L0_ffn_b_gate,
           L0_ffn_W_experts, L0_ffn_b_experts):
    x = np.ascontiguousarray(np.asarray(L0_attn_x, dtype=np.float32).transpose(0, 2, 1))
    wq = np.ascontiguousarray(np.asarray(L0_attn_wq, dtype=np.float32).reshape(D, D))
    wk = np.ascontiguousarray(np.asarray(L0_attn_wk, dtype=np.float32).reshape(D, D))
    wv = np.ascontiguousarray(np.asarray(L0_attn_wv, dtype=np.float32).reshape(D, D))
    wo = np.ascontiguousarray(np.asarray(L0_attn_wo, dtype=np.float32))
    wqkv = np.ascontiguousarray(np.asarray(L0_ffn_W_qkv, dtype=np.float32))
    bqkv = np.ascontiguousarray(np.asarray(L0_ffn_b_qkv, dtype=np.float32).reshape(1, 3 * D))
    wgate = np.ascontiguousarray(np.asarray(L0_ffn_W_gate, dtype=np.float32))
    bgate = np.ascontiguousarray(np.asarray(L0_ffn_b_gate, dtype=np.float32).reshape(1, NE))
    wexp = np.ascontiguousarray(np.asarray(L0_ffn_W_experts, dtype=np.float32))
    bexp = np.ascontiguousarray(np.asarray(L0_ffn_b_experts, dtype=np.float32))
    ctab, stab, perm, stair, idn, sel = _host_constants()

    if _NC_CACHE[0] is None:
        _NC_CACHE[0] = build_program()
    nc = _NC_CACHE[0]

    in_maps = []
    for c in range(NCORES):
        in_maps.append({
            "x": np.ascontiguousarray(x[c * SEQS:(c + 1) * SEQS]),
            "wq": wq, "wk": wk, "wv": wv, "wo": wo,
            "wqkv": wqkv, "bqkv": bqkv, "wgate": wgate, "bgate": bgate,
            "wexp": wexp, "bexp": bexp,
            "ctab": ctab, "stab": stab, "perm": perm, "stair": stair, "idn": idn, "sel": sel,
        })
    res = run_bass_kernel_spmd(nc, in_maps, core_ids=list(range(NCORES)))
    outs = [r["out"] for r in res.results]
    full = np.concatenate(outs, axis=0)
    return np.ascontiguousarray(full.transpose(0, 2, 1))


# version 21
# speedup vs baseline: 1.0405x; 1.0005x over previous
import numpy as np
import concourse.bass as bass
import concourse.mybir as mybir
from concourse.bass_utils import run_bass_kernel_spmd

F32 = mybir.dt.float32
BF16 = mybir.dt.bfloat16
AF = mybir.ActivationFunctionType
OP = mybir.AluOpType
AX = mybir.AxisListType

NCORES = 8
SEQS = 2
T = 2048
D = 1024
NF = 8
ICW = 512
NIC = T // ICW
NJT = T // 128
HA = 8
HB = 16
NE = 8
SCL_A = 1.0 / 128.0
SCL_B = 0.125
EPS = 1e-5
VST = HB * 65


class Prog:
    ENGS = ("tensor", "vector", "scalar", "gpsimd", "sync")

    def __init__(self, nc):
        self.nc = nc
        self.q = {e: [] for e in self.ENGS}
        self.sem_handles = {}
        self.cnt = {}
        self.fences = {}

    def sem(self, name):
        if name not in self.sem_handles:
            self.sem_handles[name] = self.nc.alloc_semaphore(name)
            self.cnt[name] = 0
        return self.sem_handles[name]

    def emit(self, eng, fn, wait=None, inc=None, inc_n=1):
        sems = self.sem_handles
        if inc is not None:
            self.sem(inc)
            self.cnt[inc] += inc_n
        wait = [(s, v) for (s, v) in (wait or []) if v > 0]
        for s, _ in wait:
            self.sem(s)

        def run(engine, fn=fn, wait=wait, inc=inc, inc_n=inc_n):
            for s, v in wait:
                engine.wait_ge(sems[s], v)
            ins = fn(engine)
            if inc is not None and ins is not None:
                ins.then_inc(sems[inc], inc_n)

        self.q[eng].append(run)

    def val(self, name):
        self.sem(name)
        return self.cnt[name]

    def dma(self, eng, out, in_, sem, wait=None):
        self.emit(eng, lambda e, o=out, i=in_: e.dma_start(out=o, in_=i),
                  wait=wait, inc=sem, inc_n=16)
        return self.cnt[sem]

    def release(self, eng, sem):
        self.emit(eng, lambda e: e.nop(), inc=sem)
        return self.cnt[sem]

    def barrier(self):
        marks = []
        for e, sname in (("vector", "bv"), ("scalar", "bs"),
                         ("gpsimd", "bg"), ("sync", "by")):
            fence = self.fences.get(e)
            if fence is not None:
                self.emit(e, fence, inc=sname)
            else:
                self.emit(e, lambda eng: eng.nop(), inc=sname)
            marks.append((sname, self.cnt[sname]))
        self.emit("tensor", self.fences["tensor"], wait=list(marks), inc="bt")
        marks.append(("bt", self.cnt["bt"]))
        for e in ("tensor", "vector", "scalar"):
            self.emit(e, lambda eng: None, wait=marks)


def build_program(debug=False, direct_we=False, nocb=False, skipadds=False, widn=False, xdirect=False):
    nc = bass.Bass(target_bir_lowering=False, trn_type="TRN2")

    x_in = nc.dram_tensor("x", [SEQS, D, T], F32, kind="ExternalInput").ap()
    wq_in = nc.dram_tensor("wq", [D, D], F32, kind="ExternalInput").ap()
    wk_in = nc.dram_tensor("wk", [D, D], F32, kind="ExternalInput").ap()
    wv_in = nc.dram_tensor("wv", [D, D], F32, kind="ExternalInput").ap()
    wo_in = nc.dram_tensor("wo", [HA, 128, D], F32, kind="ExternalInput").ap()
    wqkv_in = nc.dram_tensor("wqkv", [D, 3 * D], F32, kind="ExternalInput").ap()
    bqkv_in = nc.dram_tensor("bqkv", [1, 3 * D], F32, kind="ExternalInput").ap()
    wgate_in = nc.dram_tensor("wgate", [D, NE], F32, kind="ExternalInput").ap()
    bgate_in = nc.dram_tensor("bgate", [1, NE], F32, kind="ExternalInput").ap()
    wexp_in = nc.dram_tensor("wexp", [NE, D, D], F32, kind="ExternalInput").ap()
    bexp_in = nc.dram_tensor("bexp", [NE, D], F32, kind="ExternalInput").ap()
    ctab_in = nc.dram_tensor("ctab", [128, T], F32, kind="ExternalInput").ap()
    stab_in = nc.dram_tensor("stab", [128, T], F32, kind="ExternalInput").ap()
    perm_in = nc.dram_tensor("perm", [128, 128], F32, kind="ExternalInput").ap()
    stair_in = nc.dram_tensor("stair", [128, 896], F32, kind="ExternalInput").ap()
    idn_in = nc.dram_tensor("idn", [128, 128], F32, kind="ExternalInput").ap()
    sel_in = nc.dram_tensor("sel", [8, 1024], F32, kind="ExternalInput").ap()
    out_hbm = nc.dram_tensor("out", [SEQS, D, T], F32, kind="ExternalOutput").ap()

    x1_scr = nc.dram_tensor("x1scr", [SEQS, D, T], BF16).ap()
    dbg1 = (nc.dram_tensor("dbg1", [SEQS, D, T], F32, kind="ExternalOutput").ap()
            if debug else None)
    dbg2 = (nc.dram_tensor("dbg2", [SEQS, D, T], F32, kind="ExternalOutput").ap()
            if debug else None)
    dbg3 = (nc.dram_tensor("dbg3", [SEQS, 8, T], F32, kind="ExternalOutput").ap()
            if debug else None)
    dbg4 = (nc.dram_tensor("dbg4", [SEQS, D, T], F32, kind="ExternalOutput").ap()
            if debug else None)
    dbg5 = (nc.dram_tensor("dbg5", [NIC, 128, 4096], F32, kind="ExternalOutput").ap()
            if debug else None)
    we_scr = nc.dram_tensor("wescr", [NE, D, D], BF16).ap()
    wqkv_scr = nc.dram_tensor("wqkvscr", [D, 3 * D], BF16).ap()
    wq_scr = nc.dram_tensor("wqscr", [D, D], BF16).ap()
    wk_scr = nc.dram_tensor("wkscr", [D, D], BF16).ap()
    wv_scr = nc.dram_tensor("wvscr", [D, D], BF16).ap()
    wo_scr = nc.dram_tensor("woscr", [HA, 128, D], BF16).ap()

    c_ctab = nc.alloc_sbuf_tensor("c_ctab", [128, T], BF16).ap()
    c_stab = nc.alloc_sbuf_tensor("c_stab", [128, T], BF16).ap()
    c_perm = nc.alloc_sbuf_tensor("c_perm", [128, 128], BF16).ap()
    c_stair = nc.alloc_sbuf_tensor("c_stair", [128, 896], BF16).ap()
    c_idn = nc.alloc_sbuf_tensor("c_idn", [128, 128], BF16).ap()
    c_ones = nc.alloc_sbuf_tensor("c_ones", [128, 512], BF16).ap()
    c_wgate = nc.alloc_sbuf_tensor("c_wgate", [128, 64], BF16).ap()
    c_bgate = nc.alloc_sbuf_tensor("c_bgate", [1, 8], BF16).ap()
    c_brow = nc.alloc_sbuf_tensor("c_brow", [1, 3 * D], BF16).ap()
    c_bexp = nc.alloc_sbuf_tensor("c_bexp", [8, D], BF16).ap()
    c_sel = nc.alloc_sbuf_tensor("c_sel", [8, 1024], BF16).ap()
    combT = nc.alloc_sbuf_tensor("combT", [8, T], BF16).ap()
    mf = nc.alloc_sbuf_tensor("miscf", [128, 3264], F32).ap()
    a_bf = nc.alloc_sbuf_tensor("a_bf", [1, 512], BF16).ap()
    b_bf = nc.alloc_sbuf_tensor("b_bf", [1, 512], BF16).ap()
    bcB = nc.alloc_sbuf_tensor("bcB", [128, 512], BF16).ap()
    rsc = nc.alloc_sbuf_tensor("ropescr2", [128, 1024], BF16).ap()
    lrow = nc.alloc_sbuf_tensor("lrow", [128, 512], BF16).ap()
    gcomb_bf = nc.alloc_sbuf_tensor("gcombbf", [128, 32], BF16).ap()
    bnc = [nc.alloc_sbuf_tensor(f"bnc{i}", [128, 512], BF16).ap() for i in range(2)]

    R1 = nc.alloc_sbuf_tensor("R1", [128, 16384], BF16).ap()
    R2 = nc.alloc_sbuf_tensor("R2", [128, 16384], BF16).ap()
    R3 = nc.alloc_sbuf_tensor("R3", [128, 16640], BF16).ap()
    POOL = nc.alloc_sbuf_tensor("POOL", [128, 32768], BF16).ap()

    pA = nc.alloc_psum_tensor("pA", [128, 2048], F32).ap()
    pB = nc.alloc_psum_tensor("pB", [128, 2048], F32).ap()

    K_, Q_, V_ = R1, R2, R3
    X1F = R1
    T1OUT = R2[:, 0:8192].bitcast(F32)
    Y2B = R2[:, 8192:12288]
    YSQB = R2[:, 12288:16384]
    CBSB = R3[:, 0:4096]
    WER = [R3[:, 4096 + 1024 * i: 4096 + 1024 * (i + 1)] for i in range(4)]
    XER = [R3[:, 8192 + 512 * i: 8192 + 512 * (i + 1)] for i in range(4)]

    wq_t = POOL[:, 0:8192]
    wk_t = POOL[:, 8192:16384]
    wv_t = POOL[:, 16384:24576]
    x0c = [POOL[:, 24576:28672], POOL[:, 28672:32768]]
    wo_t = POOL[:, 0:8192]
    ptil = POOL[:, 8192:10240]
    oA = POOL[:, 10240:14336]
    ysqA = POOL[:, 10240:14336]
    ybuf = POOL[:, 14336:18432]
    x1buf = [POOL[:, 18432:22528], POOL[:, 22528:26624]]
    x0c2 = POOL[:, 26624:30720]
    wqkv_t = POOL[:, 0:12288]
    x1c = [POOL[:, 12288:16384], POOL[:, 16384:20480]]
    ptilB = POOL[:, 0:4096]
    ATB = POOL[:, 4096:20480]
    OEV = POOL[:, 20480:21504]
    a_sb = POOL[:, 0:1024].bitcast(F32)
    b_sb = POOL[:, 1024:2048].bitcast(F32)

    linv = mf[:, 0:512]
    mean_r = mf[0:1, 512:1024]
    es2_r = mf[0:1, 1024:1536]
    msq_r = mf[0:1, 1536:2048]
    var_r = mf[0:1, 2048:2560]
    lnv_r = mf[0:1, 1536:2048]
    rstd_r = mf[0:1, 2560:3072]
    bneg_r = mf[0:1, 1024:1536]
    linvo_r = mf[0:1, 2048:2560]
    linve_r = mf[64:65, 2048:2560]
    gexp = mf[:, 3072:3104]
    gsum = mf[:, 3104:3108]
    grs = mf[:, 3108:3112]
    gm1 = mf[:, 3112:3116]
    gm2 = mf[:, 3116:3120]
    gsm = mf[:, 3120:3152]
    gmk = mf[:, 3152:3184]
    gzap = mf[:, 3184:3216]
    negm_g = mf[:, 3216:3220]
    eps_r = mf[0:1, 3220:3221]

    P = Prog(nc)
    fence_f = mf[0:1, 3221:3222]
    fence_g = mf[0:1, 3222:3223]
    P.fences = {
        "tensor": lambda e: e.matmul(pA[0:1, 0:1], c_ones[0:1, 0:1], c_ones[0:1, 0:1],
                                     start=True, stop=True),
        "vector": lambda e: e.tensor_copy(fence_f, fence_g),
        "scalar": lambda e: e.copy(fence_g, fence_f),
    }

    def fm(t, chunk, ic, w=ICW):
        return t[:, chunk * T + ic * w: chunk * T + ic * w + w]

    P.dma("gpsimd", c_ctab, ctab_in, "swC")
    P.dma("gpsimd", c_stab, stab_in, "swC")
    P.dma("gpsimd", c_perm, perm_in, "swC")
    P.dma("gpsimd", c_stair, stair_in, "swC")
    P.dma("gpsimd", c_idn, idn_in, "swC")
    P.dma("gpsimd", c_wgate.rearrange("p (c e) -> p c e", c=8),
          wgate_in.rearrange("(c p) e -> p c e", p=128), "swC")
    P.dma("gpsimd", c_bgate, bgate_in, "swC")
    P.dma("gpsimd", c_brow, bqkv_in, "swC")
    P.dma("gpsimd", c_bexp, bexp_in, "swC")
    P.dma("gpsimd", c_sel, sel_in, "swC")
    swC_total = P.val("swC")
    P.emit("vector", lambda e: e.memset(c_ones, 1.0))
    P.emit("vector", lambda e: e.memset(eps_r, float(EPS)), inc="svones")
    sv_ones = P.val("svones")

    bounce_jobs = []
    for c in range(8):
        for j in range(6):
            bounce_jobs.append((wqkv_in[c * 128:(c + 1) * 128, j * 512:(j + 1) * 512],
                                wqkv_scr[c * 128:(c + 1) * 128, j * 512:(j + 1) * 512]))
    wqkv_scr_njobs = len(bounce_jobs)
    for e_ in range(NE):
        for d_ in range(NF):
            for hf in range(2):
                bounce_jobs.append((wexp_in[e_, d_ * 128:(d_ + 1) * 128, hf * 512:(hf + 1) * 512],
                                    we_scr[e_, d_ * 128:(d_ + 1) * 128, hf * 512:(hf + 1) * 512]))
    N_JOBS = len(bounce_jobs)
    wescr_idx = [0]

    def emit_wescr_chunk():
        idx = wescr_idx[0]
        if idx >= N_JOBS:
            return
        wescr_idx[0] += 1
        srcp, dstp = bounce_jobs[idx]
        par = idx % 2
        w = [("sbs%d" % par, 16 * (idx // 2))] if idx >= 2 else None
        P.dma("gpsimd", bnc[par], srcp, "sbl%d" % par, wait=w)
        P.dma("gpsimd", dstp, bnc[par], "sbs%d" % par,
              wait=[("sbl%d" % par, 16 * (idx // 2 + 1))])

    def bounce_done_wait(njobs):
        n0 = (njobs + 1) // 2
        n1 = njobs // 2
        return [("sbs0", 16 * n0), ("sbs1", 16 * n1)]

    def pump_wescr(n):
        for _ in range(n):
            emit_wescr_chunk()

    x0gate = {}
    x1gate = {}
    pa_gate = {}
    pb_gate = {}
    for s in range(SEQS):
        swa_w = None
        if s > 0:
            swa_w = [("stB3m", P.val("stB3m")), ("ssB2f", P.val("ssB2f")),
                     ("stB1g", P.val("stB1g")), ("ssB2", P.val("ssB2")),
                     ("svB3o", P.val("svB3o"))]
        for hh in range(2):
            P.dma("gpsimd", wq_t.rearrange("p (c n) -> p c n", c=8)[:, hh * 4:(hh + 1) * 4],
                  wq_in.rearrange("(c p) n -> p c n", p=128)[:, hh * 4:(hh + 1) * 4],
                  "swAh%d" % hh, wait=swa_w if hh == 0 else None)
            P.dma("gpsimd", wk_t.rearrange("p (c n) -> p c n", c=8)[:, hh * 4:(hh + 1) * 4],
                  wk_in.rearrange("(c p) n -> p c n", p=128)[:, hh * 4:(hh + 1) * 4],
                  "swAh%d" % hh)
            P.dma("gpsimd", wv_t.rearrange("p (c n) -> p c n", c=8)[:, hh * 4:(hh + 1) * 4],
                  wv_in.rearrange("(c p) n -> p c n", p=128)[:, hh * 4:(hh + 1) * 4],
                  "swAh%d" % hh)
        swA_h = [P.val("swAh0"), P.val("swAh1")]

        pa_gate = dict(pa_gate) if s > 0 else {}
        pb_gate = dict(pb_gate) if s > 0 else {}
        qk_idx = [0]
        for ic in range(NIC):
            par = ic % 2
            w = [("stA1v", x0gate.get(par, 0))]
            if s > 0 and ic < 2:
                w += [("sspill1", P.val("sspill1")), ("stA2y", P.val("stA2y"))]
            P.dma("gpsimd", x0c[par].rearrange("p (f t) -> p f t", f=NF),
                  x_in[s].rearrange("(f p) t -> p f t", p=128)[:, :, ic * ICW:(ic + 1) * ICW],
                  "sx%d" % par, wait=w)
            sxv = P.val("sx%d" % par)
            pump_wescr(6)

            base = [("sx%d" % par, sxv), ("swC", swC_total),
                    ("svones", sv_ones)]
            for kind in ("q", "k"):
                wt = wq_t if kind == "q" else wk_t
                dst = Q_ if kind == "q" else K_
                for dt in range(NF):
                    pp = qk_idx[0] % 4
                    qk_idx[0] += 1
                    ps = pA[:, pp * 512:(pp + 1) * 512]
                    wv_ = [pa_gate[pp]] if pp in pa_gate else []
                    for kc in range(NF):
                        kw = (base + wv_) if kc == 0 else None
                        if ic == 0 and dt < 2 and kc in (0, 4):
                            kw = (kw or []) + [("swAh%d" % (kc // 4), swA_h[kc // 4])]
                        P.emit("tensor",
                               lambda e, ps=ps, wt=wt, kc=kc, dt=dt, par=par:
                               e.matmul(ps, wt[:, kc * 1024 + dt * 128: kc * 1024 + dt * 128 + 128],
                                        x0c[par][:, kc * 512:(kc + 1) * 512],
                                        start=(kc == 0), stop=(kc == NF - 1)),
                               wait=kw,
                               inc="stA1" if kc == NF - 1 else None)
                    stv = P.val("stA1")
                    P.emit("scalar",
                           lambda e, ps=ps, dst=dst, dt=dt, ic=ic: e.copy(fm(dst, dt, ic), ps),
                           wait=[("stA1", stv)], inc="svA1")
                    pa_gate[pp] = ("svA1", P.val("svA1"))
            for jt4 in range(4):
                jt = ic * 4 + jt4
                pp = jt4 % 2
                ps = pB[:, pp * 1024:(pp + 1) * 1024]
                wv_ = [g for b in (2 * pp, 2 * pp + 1) for g in ([pb_gate[b]] if b in pb_gate else [])]
                if ic == 0 and jt4 == 0:
                    wv_ = wv_ + [("swAh0", swA_h[0]), ("swAh1", swA_h[1])]
                for kc in range(NF):
                    for half in range(2):
                        P.emit("tensor",
                               lambda e, ps=ps, kc=kc, jt4=jt4, par=par, half=half:
                               e.matmul(ps[:, half * 512:(half + 1) * 512],
                                        x0c[par][:, kc * 512 + jt4 * 128: kc * 512 + jt4 * 128 + 128],
                                        wv_t[:, kc * 1024 + half * 512: kc * 1024 + (half + 1) * 512],
                                        start=(kc == 0), stop=(kc == NF - 1)),
                               wait=(base + wv_) if (kc == 0 and half == 0) else None,
                               inc="stA1v" if (kc == NF - 1 and half == 1) else None)
                stv = P.val("stA1v")
                P.emit("scalar",
                       lambda e, ps=ps, jt=jt: e.copy(V_[:, jt * 1024:(jt + 1) * 1024], ps),
                       wait=[("stA1v", stv)], inc="svA1v")
                pb_gate[2 * pp] = ("svA1v", P.val("svA1v"))
                pb_gate[2 * pp + 1] = ("svA1v", P.val("svA1v"))
            x0gate[par] = P.val("stA1v")

            svq = P.val("svA1")
            first = True
            rslot = [0]
            for X in (Q_, K_):
                for pr in range(4):
                    h0 = 2 * pr
                    sl0, sl1 = rslot[0] % 4, (rslot[0] + 1) % 4
                    rslot[0] += 2
                    rvals = {}
                    for (h, slot) in ((h0, sl0), (h0 + 1, sl1)):
                        rps = pB[:, slot * 512:(slot + 1) * 512]
                        w = [pb_gate[slot]] if slot in pb_gate else []
                        if first:
                            w += [("svA1", svq), ("swC", swC_total)]
                        P.emit("tensor",
                               lambda e, rps=rps, X=X, h=h, ic=ic:
                               e.matmul(rps, c_perm,
                                        X[:, h * T + ic * ICW: h * T + ic * ICW + ICW],
                                        start=True, stop=True),
                               wait=w, inc="strp")
                        first = False
                        rvals[h] = P.val("strp")
                    for (h, slot, ri) in ((h0, sl0, 0), (h0 + 1, sl1, 1)):
                        rps = pB[:, slot * 512:(slot + 1) * 512]
                        P.emit("vector",
                               lambda e, X=X, h=h, ic=ic:
                               e.tensor_tensor(
                                   X[:, h * T + ic * ICW: h * T + ic * ICW + ICW],
                                   X[:, h * T + ic * ICW: h * T + ic * ICW + ICW],
                                   c_ctab[:, ic * ICW:(ic + 1) * ICW], op=OP.mult),
                               wait=[("strp", rvals[h])])
                        P.emit("vector",
                               lambda e, rps=rps, ri=ri, ic=ic:
                               e.tensor_tensor(
                                   rsc[:, ri * 512:(ri + 1) * 512], rps,
                                   c_stab[:, ic * ICW:(ic + 1) * ICW], op=OP.mult),
                               inc="svrp")
                        pb_gate[slot] = ("svrp", P.val("svrp"))
                    P.emit("vector", lambda e: e.drain())
                    for (h, ri) in ((h0, 0), (h0 + 1, 1)):
                        P.emit("vector",
                               lambda e, X=X, h=h, ic=ic, ri=ri:
                               e.tensor_tensor(
                                   X[:, h * T + ic * ICW: h * T + ic * ICW + ICW],
                                   X[:, h * T + ic * ICW: h * T + ic * ICW + ICW],
                                   rsc[:, ri * 512:(ri + 1) * 512], op=OP.add))
        P.barrier()

        P.dma("gpsimd", wo_t.rearrange("p (h m) -> p h m", h=HA),
              wo_in.rearrange("h p m -> p h m"), "swO",
              wait=[("stA1", P.val("stA1"))])
        swO_total = P.val("swO")

        A2expv = {}
        pend_ln = [None]
        x0gate2 = [0 if s == 0 else x0gate2[0]]
        for ic in range(NIC):
            njt = 4 * (ic + 1)
            w = [("stA2y", x0gate2[0])] if x0gate2[0] > 0 else []
            if ic == 0:
                w = list(w) + [("stA1v", P.val("stA1v"))]
            P.dma("gpsimd", x0c2.rearrange("p (f t) -> p f t", f=NF),
                  x_in[s].rearrange("(f p) t -> p f t", p=128)[:, :, ic * ICW:(ic + 1) * ICW],
                  "sx2", wait=w)
            sx2v = P.val("sx2")
            pump_wescr(6)

            A2maskv = {}

            for h in range(HA):
                if h == 2 and pend_ln[0] is not None:
                    pend_ln[0]()
                    pend_ln[0] = None
                ngrp = (njt + 1) // 2

                def a2_s(jg, h=h, ic=ic, njt=njt):
                    gpar = jg % 2
                    ps_s = pA[:, gpar * 1024:(gpar + 1) * 1024]
                    nt = min(2, njt - 2 * jg)
                    wargs = [("ssA2", A2expv.get(gpar, 0))]
                    for t_ in range(nt):
                        jt = 2 * jg + t_
                        P.emit("tensor",
                               lambda e, ps_s=ps_s, t_=t_, jt=jt, h=h, ic=ic:
                               e.matmul(ps_s[:, t_ * 512:(t_ + 1) * 512],
                                        K_[:, h * T + jt * 128: h * T + jt * 128 + 128],
                                        Q_[:, h * T + ic * ICW: h * T + ic * ICW + ICW],
                                        start=True, stop=True),
                               wait=wargs if t_ == 0 else None,
                               inc="stA2s" if t_ == nt - 1 else None)
                    stv = P.val("stA2s")
                    P.emit("scalar",
                           lambda e, gpar=gpar, ps_s=ps_s, nt=nt:
                           e.activation(ptil[:, gpar * 1024: gpar * 1024 + nt * 512],
                                        ps_s[:, 0:nt * 512], AF.Exp, scale=SCL_A),
                           wait=[("stA2s", stv)], inc="ssA2")
                    A2expv[gpar] = P.val("ssA2")
                    masked = (2 * jg) >= 4 * ic
                    if masked:
                        ssv = P.val("ssA2")
                        for t_ in range(nt):
                            jt = 2 * jg + t_
                            toff = (jt - 4 * ic) * 128
                            P.emit("vector",
                                   lambda e, gpar=gpar, t_=t_, toff=toff:
                                   e.tensor_tensor(
                                       ptil[:, gpar * 1024 + t_ * 512: gpar * 1024 + (t_ + 1) * 512],
                                       ptil[:, gpar * 1024 + t_ * 512: gpar * 1024 + (t_ + 1) * 512],
                                       c_stair[:, 384 - toff: 384 - toff + 512],
                                       op=OP.mult),
                                   wait=[("ssA2", ssv)] if t_ == 0 else None,
                                   inc="svA2m" if t_ == nt - 1 else None)
                        A2maskv[gpar] = P.val("svA2m")
                    else:
                        A2maskv[gpar] = None

                if s == 0:
                    pump_wescr(3)
                a2_s(0)
                for jg in range(ngrp):
                    gpar = jg % 2
                    nt = min(2, njt - 2 * jg)
                    if jg + 1 < ngrp:
                        a2_s(jg + 1)
                    if A2maskv[gpar] is not None:
                        pwait = [("svA2m", A2maskv[gpar])]
                    else:
                        pwait = [("ssA2", A2expv[gpar])]
                    if jg == 0:
                        pwait = pwait + [("svA2o", P.val("svA2o"))]
                    for t_ in range(nt):
                        jt = 2 * jg + t_
                        P.emit("tensor",
                               lambda e, jt=jt, gpar=gpar, t_=t_, h=h:
                               e.matmul(pB[:, 0:512],
                                        V_[:, jt * 1024 + h * 128: jt * 1024 + h * 128 + 128],
                                        ptil[:, gpar * 1024 + t_ * 512: gpar * 1024 + (t_ + 1) * 512],
                                        start=(jt == 0), stop=(jt == njt - 1)),
                               wait=pwait if t_ == 0 else None)
                        P.emit("tensor",
                               lambda e, jt=jt, gpar=gpar, t_=t_:
                               e.matmul(pB[:, 512:1024],
                                        c_ones[:, 0:128],
                                        ptil[:, gpar * 1024 + t_ * 512: gpar * 1024 + (t_ + 1) * 512],
                                        start=(jt == 0), stop=(jt == njt - 1)),
                               inc="stA2o" if jt == njt - 1 else None)
                stv = P.val("stA2o")
                P.emit("scalar", lambda e: e.activation(linv, pB[:, 512:1024], AF.Ln),
                       wait=[("stA2o", stv)])
                P.emit("scalar", lambda e: e.drain())
                P.emit("scalar", lambda e: e.activation(linv, linv, AF.Exp, scale=-1.0),
                       inc="ssA2r")
                ssrv = P.val("ssA2r")
                ovwait = [("ssA2r", ssrv)]
                if h == 0 and ic > 0:
                    ovwait.append(("stA2st", P.val("stA2st")))
                P.emit("vector",
                       lambda e, h=h: e.tensor_tensor(
                           oA[:, h * 512:(h + 1) * 512], pB[:, 0:512], linv, op=OP.mult),
                       wait=ovwait, inc="svA2o")
            sv_heads = P.val("svA2o")
            evy = {}
            for mc in range(NF):
                ypar = mc % 2
                ps_y = pB[:, 1024 + ypar * 512: 1024 + (ypar + 1) * 512]
                wargs = [("swO", swO_total), ("svA2o", sv_heads), ("sx2", sx2v),
                         ("svA2y", evy.get(ypar, 0))]
                if mc == 0:
                    wargs = wargs + [("svL", P.val("svL")), ("svA2x1", P.val("svA2x1"))]
                for h in range(HA):
                    P.emit("tensor",
                           lambda e, ps_y=ps_y, h=h, mc=mc:
                           e.matmul(ps_y, wo_t[:, h * 1024 + mc * 128: h * 1024 + mc * 128 + 128],
                                    oA[:, h * 512:(h + 1) * 512],
                                    start=(h == 0), stop=False),
                           wait=wargs if h == 0 else None)
                P.emit("tensor",
                       lambda e, ps_y=ps_y, mc=mc:
                       e.matmul(ps_y, c_idn, x0c2[:, mc * 512:(mc + 1) * 512],
                                start=False, stop=True),
                       inc="stA2y")
                stv = P.val("stA2y")
                P.emit("vector",
                       lambda e, ps_y=ps_y, mc=mc: e.tensor_copy(ybuf[:, mc * 512:(mc + 1) * 512], ps_y),
                       wait=[("stA2y", stv)], inc="svA2y")
                evy[ypar] = P.val("svA2y")
            sv_y = P.val("svA2y")
            x0gate2[0] = P.val("stA2y")
            for mc in range(NF):
                P.emit("vector",
                       lambda e, mc=mc: e.tensor_tensor(
                           ysqA[:, mc * 512:(mc + 1) * 512],
                           ybuf[:, mc * 512:(mc + 1) * 512],
                           ybuf[:, mc * 512:(mc + 1) * 512], op=OP.mult),
                       wait=[("svA2y", sv_y)] if mc == 0 else None,
                       inc="svA2q" if mc == NF - 1 else None)
            sv_q = P.val("svA2q")
            for mc in range(NF):
                P.emit("tensor",
                       lambda e, mc=mc: e.matmul(pB[0:1, 1024:1536], c_ones[:, 0:1],
                                                 ybuf[:, mc * 512:(mc + 1) * 512],
                                                 start=(mc == 0), stop=(mc == NF - 1)),
                       wait=[("svA2y", sv_y)] if mc == 0 else None)
            for mc in range(NF):
                P.emit("tensor",
                       lambda e, mc=mc: e.matmul(pB[0:1, 1536:2048], c_ones[:, 0:1],
                                                 ysqA[:, mc * 512:(mc + 1) * 512],
                                                 start=(mc == 0), stop=(mc == NF - 1)),
                       wait=[("svA2q", sv_q)] if mc == 0 else None,
                       inc="stA2st" if mc == NF - 1 else None)

            stv_st = P.val("stA2st")
            P.emit("vector", lambda e: e.tensor_scalar(mean_r, pB[0:1, 1024:1536],
                                                       1.0 / D, None, op0=OP.mult),
                   wait=[("stA2st", stv_st)])
            P.emit("vector", lambda e: e.tensor_scalar(es2_r, pB[0:1, 1536:2048],
                                                       1.0 / D, None, op0=OP.mult))
            P.emit("vector", lambda e: e.drain())
            P.emit("vector", lambda e: e.tensor_tensor(msq_r, mean_r, mean_r, op=OP.mult))
            P.emit("vector", lambda e: e.drain())
            P.emit("vector", lambda e: e.tensor_tensor(var_r, es2_r, msq_r, op=OP.subtract),
                   inc="svL")
            svv = P.val("svL")
            P.emit("scalar", lambda e: e.activation(lnv_r, var_r, AF.Ln, bias=eps_r),
                   wait=[("svL", svv)])
            P.emit("scalar", lambda e: e.drain())
            P.emit("scalar", lambda e: e.activation(a_bf, lnv_r, AF.Exp, scale=-0.5),
                   inc="ssL2")
            ssv2 = P.val("ssL2")
            P.emit("vector", lambda e: e.scalar_tensor_tensor(
                b_bf, mean_r, -1.0, a_bf, op0=OP.mult, op1=OP.mult),
                   wait=[("ssL2", ssv2)], inc="svL2")

            def a2_ln(ic=ic, sv_y=sv_y):
                svv = P.val("svL2")
                P.emit("tensor", lambda e: e.matmul(pB[:, 1024:1536], c_ones[0:1, 0:128], a_bf,
                                                    start=True, stop=True),
                       wait=[("svL2", svv), ("svA2y", sv_y)])
                P.emit("tensor", lambda e: e.matmul(pB[:, 1536:2048], c_ones[0:1, 0:128], b_bf,
                                                    start=True, stop=True), inc="stA2b")
                stv2 = P.val("stA2b")
                xpar = ic % 2
                wspill = [("sspill%d" % xpar, P.val("sspill%d" % xpar))]
                for mc in range(NF):
                    P.emit("vector",
                           lambda e, mc=mc, xpar=xpar: e.tensor_tensor(
                               x1buf[xpar][:, mc * 512:(mc + 1) * 512],
                               ybuf[:, mc * 512:(mc + 1) * 512], pB[:, 1024:1536], op=OP.mult),
                           wait=([("stA2b", stv2)] + wspill) if mc == 0 else None)
                for mc in range(NF):
                    P.emit("vector",
                           lambda e, mc=mc, xpar=xpar: e.tensor_tensor(
                               x1buf[xpar][:, mc * 512:(mc + 1) * 512],
                               x1buf[xpar][:, mc * 512:(mc + 1) * 512], pB[:, 1536:2048], op=OP.add),
                           inc="svA2x1" if mc == NF - 1 else None)
                svv = P.val("svA2x1")
                P.dma("sync",
                      x1_scr[s].rearrange("(f p) t -> p f t", p=128)[:, :, ic * ICW:(ic + 1) * ICW],
                      x1buf[xpar].rearrange("p (f t) -> p f t", f=NF),
                      "sspill%d" % xpar, wait=[("svA2x1", svv)])

            if ic == NIC - 1:
                a2_ln()
                pend_ln[0] = None
            else:
                pend_ln[0] = a2_ln

        spill_tot = [("sspill0", P.val("sspill0")), ("sspill1", P.val("sspill1"))]
        evb = {}
        evbv = {}
        for half in range(2):
            def emit_pieces(half=half):
                for kcp in range(8):
                    if half == 0 and kcp == 0:
                        wq_w = [("ssA2", P.val("ssA2")), ("stA2o", P.val("stA2o")),
                                ("stA2y", P.val("stA2y")), ("svA2o", P.val("svA2o")),
                                ("svA2q", P.val("svA2q")), ("stA2st", P.val("stA2st")),
                                ("ssB2", P.val("ssB2")), ("stB3m", P.val("stB3m"))]
                        if s == 0:
                            wq_w = wq_w + bounce_done_wait(wqkv_scr_njobs)
                    elif half == 1 and kcp == 0:
                        wq_w = [("stB1", P.val("stB1"))]
                    else:
                        wq_w = None
                    P.dma("sync", wqkv_t.rearrange("p (c n) -> p c n", c=8)[:, kcp:kcp + 1],
                          wqkv_scr.rearrange("(c p) n -> p c n", p=128)[:, kcp:kcp + 1,
                                                                        half * 1536:(half + 1) * 1536],
                          "swQk%d" % kcp, wait=wq_w)
                return [P.val("swQk%d" % kcp) for kcp in range(8)]

            swq_piece = None
            for ic in range(NIC):
                par = ic % 2
                gs_, gv_ = x1gate.get(par, ("stB1", 0))
                w = [(gs_, gv_)]
                if half == 0:
                    w = w + spill_tot
                P.dma("sync", x1c[par].rearrange("p (f t) -> p f t", f=NF),
                      x1_scr[s].rearrange("(f p) t -> p f t", p=128)[:, :, ic * ICW:(ic + 1) * ICW],
                      "sx1%d" % par, wait=w)
                if swq_piece is None:
                    swq_piece = emit_pieces()
                sx1v = P.val("sx1%d" % par)
                pump_wescr(8)
                base = [("swQk7", swq_piece[7]), ("sx1%d" % par, sx1v)]
                for oci in range(12):
                    oc = half * 12 + oci
                    if oc >= 16:
                        continue
                    dst = Q_ if oc < 8 else K_
                    dt = oc % 8
                    pp = oci % 2
                    ps = pA[:, pp * 512:(pp + 1) * 512]
                    wv_ = [("svB1", evb.get(pp, 0))]
                    for kc in range(NF):
                        kw = (base + wv_) if kc == 0 else None
                        if ic == 0 and oci < 2:
                            kw = (kw or []) + [("swQk%d" % kc, swq_piece[kc])]
                        P.emit("tensor",
                               lambda e, ps=ps, kc=kc, oci=oci, par=par:
                               e.matmul(ps, wqkv_t[:, kc * 1536 + oci * 128: kc * 1536 + oci * 128 + 128],
                                        x1c[par][:, kc * 512:(kc + 1) * 512],
                                        start=(kc == 0), stop=False),
                               wait=kw)
                    P.emit("tensor",
                           lambda e, ps=ps, oc=oc:
                           e.matmul(ps, c_brow[0:1, oc * 128:(oc + 1) * 128],
                                    c_ones[0:1, 0:512], start=False, stop=True),
                           inc="stB1")
                    stv = P.val("stB1")
                    P.emit("vector",
                           lambda e, ps=ps, dst=dst, dt=dt, ic=ic: e.tensor_copy(fm(dst, dt, ic), ps),
                           wait=[("stB1", stv)], inc="svB1")
                    evb[pp] = P.val("svB1")
                if half == 1:
                    for jt4 in range(4):
                        jt = ic * 4 + jt4
                        pp = jt4 % 2
                        ps = pB[:, pp * 1024:(pp + 1) * 1024]
                        wv_ = [("svB1v", evbv.get(pp, 0))]
                        if ic == 0 and jt4 == 0:
                            wv_ = wv_ + [("swQk%d" % k2, swq_piece[k2]) for k2 in range(8)]
                        for kc in range(NF):
                            for vh in range(2):
                                P.emit("tensor",
                                       lambda e, ps=ps, kc=kc, jt4=jt4, par=par, vh=vh:
                                       e.matmul(ps[:, vh * 512:(vh + 1) * 512],
                                                x1c[par][:, kc * 512 + jt4 * 128: kc * 512 + jt4 * 128 + 128],
                                                wqkv_t[:, kc * 1536 + 512 + vh * 512: kc * 1536 + 512 + (vh + 1) * 512],
                                                start=(kc == 0), stop=False),
                                       wait=(base + wv_) if (kc == 0 and vh == 0) else None)
                        for vh in range(2):
                            P.emit("tensor",
                                   lambda e, ps=ps, vh=vh:
                                   e.matmul(ps[:, vh * 512:(vh + 1) * 512],
                                            c_ones[0:1, 0:128],
                                            c_brow[0:1, 2048 + vh * 512: 2048 + (vh + 1) * 512],
                                            start=False, stop=True),
                                   inc="stB1v" if vh == 1 else None)
                        stv = P.val("stB1v")
                        P.emit("vector",
                               lambda e, ps=ps, jt=jt:
                               e.tensor_copy(
                                   V_[:, jt * VST: jt * VST + VST].rearrange("p (h c) -> p h c", h=HB)[:, :, 0:64],
                                   ps.rearrange("p (h c) -> p h c", h=HB)),
                               wait=[("stB1v", stv)], inc="svB1v")
                        evbv[pp] = P.val("svB1v")
                    for it in range(4):
                        wv_ = [("ssB1g", P.val("ssB1g"))]
                        for kc in range(NF):
                            P.emit("tensor",
                                   lambda e, kc=kc, it=it, par=par:
                                   e.matmul(pA[:, 1024:1032],
                                            x1c[par][:, kc * 512 + it * 128: kc * 512 + it * 128 + 128],
                                            c_wgate[:, kc * 8:(kc + 1) * 8],
                                            start=(kc == 0), stop=False),
                                   wait=(base + wv_) if kc == 0 else None)
                        P.emit("tensor",
                               lambda e: e.matmul(pA[:, 1024:1032], c_ones[0:1, 0:128],
                                                  c_bgate, start=False, stop=True),
                               inc="stB1g")
                        stv = P.val("stB1g")
                        P.emit("vector",
                               lambda e, it=it: e.reduce_max(negm_g[:, it:it + 1],
                                                             pA[:, 1024:1032], axis=AX.X,
                                                             negate=True),
                               wait=[("stB1g", stv)], inc="svB1g")
                        svv = P.val("svB1g")
                        P.emit("scalar",
                               lambda e, it=it: e.activation(gexp[:, it * 8:(it + 1) * 8],
                                                             pA[:, 1024:1032], AF.Exp,
                                                             bias=negm_g[:, it:it + 1],
                                                             accum_out=gsum[:, it:it + 1]),
                               wait=[("svB1g", svv)], inc="ssB1g")
                    ssv = P.val("ssB1g")
                    for it in range(4):
                        P.emit("vector", lambda e, it=it: e.reciprocal(grs[:, it:it + 1], gsum[:, it:it + 1]),
                               wait=[("ssB1g", ssv)] if it == 0 else None)
                    for it in range(4):
                        P.emit("vector", lambda e, it=it: e.tensor_tensor(
                            gsm[:, it * 8:(it + 1) * 8], gexp[:, it * 8:(it + 1) * 8],
                            grs[:, it:it + 1].broadcast_to([128, 8]), op=OP.mult))
                    for it in range(4):
                        P.emit("vector", lambda e, it=it: e.reduce_max(
                            gm1[:, it:it + 1], gsm[:, it * 8:(it + 1) * 8], axis=AX.X))
                    for it in range(4):
                        P.emit("vector", lambda e, it=it: e.tensor_tensor(
                            gmk[:, it * 8:(it + 1) * 8], gsm[:, it * 8:(it + 1) * 8],
                            gm1[:, it:it + 1].broadcast_to([128, 8]), op=OP.is_ge))
                    for it in range(4):
                        P.emit("vector", lambda e, it=it: e.tensor_tensor(
                            gzap[:, it * 8:(it + 1) * 8], gsm[:, it * 8:(it + 1) * 8],
                            gmk[:, it * 8:(it + 1) * 8], op=OP.subtract))
                    for it in range(4):
                        P.emit("vector", lambda e, it=it: e.reduce_max(
                            gm2[:, it:it + 1], gzap[:, it * 8:(it + 1) * 8], axis=AX.X))
                    for it in range(4):
                        P.emit("vector", lambda e, it=it: e.tensor_tensor(
                            gmk[:, it * 8:(it + 1) * 8], gsm[:, it * 8:(it + 1) * 8],
                            gm2[:, it:it + 1].broadcast_to([128, 8]), op=OP.is_ge))
                    for it in range(4):
                        P.emit("vector", lambda e, it=it: e.tensor_tensor(
                            gsm[:, it * 8:(it + 1) * 8], gsm[:, it * 8:(it + 1) * 8],
                            gmk[:, it * 8:(it + 1) * 8], op=OP.mult))
                    for it in range(4):
                        P.emit("vector", lambda e, it=it: e.tensor_copy(
                            gcomb_bf[:, it * 8:(it + 1) * 8], gsm[:, it * 8:(it + 1) * 8]),
                               inc="svB1c" if it == 3 else None)
                    svv = P.val("svB1c")
                    for it in range(4):
                        P.emit("tensor",
                               lambda e, it=it: e.transpose(
                                   pA[0:8, 1536:1600].bitcast(BF16),
                                   gcomb_bf[:, it * 8:(it + 1) * 8],
                                   c_idn),
                               wait=([("svB1c", svv)] if it == 0 else
                                     [("svB1t", P.val("svB1t"))]),
                               inc="stB1t")
                        stv = P.val("stB1t")
                        P.emit("vector",
                               lambda e, it=it, ic=ic: e.tensor_copy(
                                   combT[:, ic * ICW + it * 128: ic * ICW + (it + 1) * 128],
                                   pA[0:8, 1536:1600].bitcast(BF16)),
                               wait=[("stB1t", stv)], inc="svB1t")
                x1gate[par] = (("stB1g", P.val("stB1g")) if half == 1
                               else ("stB1", P.val("stB1")))
        P.emit("vector",
               lambda e: e.memset(
                   V_[:, 0:NJT * VST].rearrange("p (j h c) -> p j h c", j=NJT, h=HB)[:, :, :, 64:65], 1.0),
               inc="svB1o")
        sv_vones = P.val("svB1o")
        b1_done = [("svB1", P.val("svB1")), ("svB1v", P.val("svB1v")),
                   ("ssB1g", P.val("ssB1g")), ("svB1t", P.val("svB1t"))]

        B2expv = {}
        pend_epi = [None]
        pend_epi2 = [None]
        for ic in range(NIC):
            for g in range(8):
                hA2, hB2 = 2 * g, 2 * g + 1

                def b2_s(jt, g=g, ic=ic):
                    gpar = jt % 2
                    ps_s = pA[:, gpar * 1024:(gpar + 1) * 1024]
                    wargs = [("ssB2", B2expv.get(gpar, 0))]
                    if g == 0 and ic == 0 and jt <= 1:
                        wargs = wargs + [("svB1o", sv_vones)] + b1_done
                    P.emit("tensor",
                           lambda e, ps_s=ps_s, g=g, jt=jt, ic=ic:
                           e.matmul(ps_s[:, 0:512],
                                    K_[0:64, g * T + jt * 128: g * T + jt * 128 + 128],
                                    Q_[0:64, g * T + ic * ICW: g * T + ic * ICW + ICW],
                                    start=True, stop=True),
                           wait=wargs)
                    P.emit("tensor",
                           lambda e, ps_s=ps_s, g=g, jt=jt, ic=ic:
                           e.matmul(ps_s[:, 512:1024],
                                    K_[64:128, g * T + jt * 128: g * T + jt * 128 + 128],
                                    Q_[64:128, g * T + ic * ICW: g * T + ic * ICW + ICW],
                                    start=True, stop=True, tile_position=(64, 0)),
                           inc="stB2s")
                    stv = P.val("stB2s")
                    P.emit("scalar",
                           lambda e, gpar=gpar, ps_s=ps_s:
                           e.activation(ptilB[:, gpar * 1024:(gpar + 1) * 1024],
                                        ps_s, AF.Exp, scale=SCL_B),
                           wait=[("stB2s", stv)], inc="ssB2")
                    B2expv[gpar] = P.val("ssB2")

                b2_s(0)
                for jt in range(NJT):
                    gpar = jt % 2
                    if jt + 1 < NJT:
                        b2_s(jt + 1)
                    if jt == 0 and pend_epi[0] is not None:
                        pend_epi[0]()
                        pend_epi[0] = None
                    if jt == 3 and pend_epi2[0] is not None:
                        pend_epi2[0]()
                        pend_epi2[0] = None
                    ssv = B2expv[gpar]
                    owait = [("ssB2", ssv)]
                    if jt == 0:
                        owait = owait + [("svB2e", P.val("svB2e"))]
                    P.emit("tensor",
                           lambda e, jt=jt, gpar=gpar, hA2=hA2:
                           e.matmul(pB[0:65, 0:512],
                                    V_[:, jt * VST + hA2 * 65: jt * VST + hA2 * 65 + 65],
                                    ptilB[:, gpar * 1024: gpar * 1024 + 512],
                                    start=(jt == 0), stop=(jt == NJT - 1)),
                           wait=owait)
                    P.emit("tensor",
                           lambda e, jt=jt, gpar=gpar, hB2=hB2:
                           e.matmul(pB[64:128, 512:1024],
                                    V_[:, jt * VST + hB2 * 65: jt * VST + hB2 * 65 + 64],
                                    ptilB[:, gpar * 1024 + 512: gpar * 1024 + 1024],
                                    start=(jt == 0), stop=(jt == NJT - 1),
                                    tile_position=(0, 64)))
                    P.emit("tensor",
                           lambda e, jt=jt, gpar=gpar:
                           e.matmul(pB[:, 1024:1536],
                                    c_ones[:, 0:128],
                                    ptilB[:, gpar * 1024 + 512: gpar * 1024 + 1024],
                                    start=(jt == 0), stop=(jt == NJT - 1)),
                           inc="stB2o" if jt == NJT - 1 else None)
                stv = P.val("stB2o")

                def epilogue(g=g, ic=ic, stv=stv):
                    P.emit("vector", lambda e: e.tensor_copy(OEV[0:64, 0:512], pB[0:64, 0:512]),
                           wait=[("stB2o", stv)])
                    P.emit("vector", lambda e: e.tensor_copy(OEV[64:128, 512:1024],
                                                             pB[64:128, 512:1024]))
                    P.emit("vector", lambda e: e.tensor_copy(linve_r, pB[64:65, 0:512]))
                    P.emit("vector", lambda e: e.tensor_copy(linvo_r, pB[0:1, 1024:1536]),
                           inc="svB2e")
                    svev = P.val("svB2e")
                    P.emit("scalar", lambda e: e.activation(linve_r, linve_r, AF.Ln),
                           wait=[("svB2e", svev)])
                    P.emit("scalar", lambda e: e.activation(linvo_r, linvo_r, AF.Ln))
                    P.emit("scalar", lambda e: e.drain())
                    P.emit("scalar", lambda e: e.activation(lrow[64:65, :], linve_r, AF.Exp, scale=-1.0))
                    P.emit("scalar", lambda e: e.activation(lrow[0:1, :], linvo_r, AF.Exp, scale=-1.0),
                           inc="ssB2r")
                    svv = P.val("ssB2r")

                    def epi2(g=g, ic=ic, svv=svv):
                        P.emit("tensor", lambda e: e.matmul(pB[0:64, 1536:2048], c_ones[64:65, 0:64],
                                                            lrow[64:65, :], start=True, stop=True,
                                                            tile_position=(64, 0)),
                               wait=[("ssB2r", svv), ("ssB2f", P.val("ssB2f"))])
                        P.emit("tensor", lambda e: e.matmul(pB[64:128, 1536:2048], c_ones[0:1, 0:64],
                                                            lrow[0:1, :], start=True, stop=True,
                                                            tile_position=(0, 64)),
                               inc="stB2b")
                        stv2 = P.val("stB2b")
                        P.emit("vector", lambda e: e.tensor_copy(bcB, pB[:, 1536:2048]),
                               wait=[("stB2b", stv2)])
                        P.emit("vector", lambda e: e.drain())
                        P.emit("vector",
                               lambda e, g=g, ic=ic: e.tensor_tensor(
                                   ATB[0:64, g * T + ic * ICW: g * T + ic * ICW + ICW],
                                   OEV[0:64, 0:512], bcB[0:64, :], op=OP.mult))
                        P.emit("vector",
                               lambda e, g=g, ic=ic: e.tensor_tensor(
                                   ATB[64:128, g * T + ic * ICW: g * T + ic * ICW + ICW],
                                   OEV[64:128, 512:1024], bcB[64:128, :], op=OP.mult),
                               inc="ssB2f")

                    pend_epi2[0] = epi2

                pend_epi[0] = epilogue
        if pend_epi[0] is not None:
            pend_epi[0]()
            pend_epi[0] = None
        if pend_epi2[0] is not None:
            pend_epi2[0]()
            pend_epi2[0] = None
        sv_atb = P.val("ssB2f")
        if debug:
            P.dma("gpsimd", dbg2[s].rearrange("(f p) t -> p f t", p=128),
                  ATB.rearrange("p (f t) -> p f t", f=NF), "sdbg2",
                  wait=[("ssB2f", sv_atb)])
            P.dma("gpsimd", dbg3[s], combT, "sdbg2")
        b2_done = [("ssB2", P.val("ssB2"))]
        pump_wescr(N_JOBS)
        wescr_wait = bounce_done_wait(N_JOBS)

        for dp in range(8):
            xw = [("stB2s", P.val("stB2s"))] if dp == 0 else None
            P.dma("sync", X1F.rearrange("p (f t) -> p f t", f=NF)[:, dp:dp + 1],
                  x1_scr[s].rearrange("(f p) t -> p f t", p=128)[:, dp:dp + 1], "sxfp%d" % dp,
                  wait=xw)
        sxf_piece = [P.val("sxfp%d" % dp) for dp in range(8)]
        def cb_emit(iq):
            cwait_extra = [("svB3y", P.val("svB3y"))] + (b2_done if iq == 0 else [])
            for e_ in range(NE):
                cps = pA[:, 1024:1536] if e_ % 2 == 0 else pA[:, 1536:2048]
                w = [("svB3c", P.val("svB3c") - 1 if P.val("svB3c") >= 2 else 0)]
                if e_ == 0:
                    w = w + cwait_extra
                P.emit("tensor",
                       lambda e, cps=cps, e_=e_, iq=iq: e.matmul(
                           cps, c_sel[:, e_ * 128:(e_ + 1) * 128],
                           combT[:, iq * ICW:(iq + 1) * ICW],
                           start=True, stop=True),
                       wait=w, inc="stB3c")
                stv = P.val("stB3c")
                P.emit("vector",
                       lambda e, cps=cps, e_=e_: e.tensor_copy(
                           CBSB[:, e_ * 512:(e_ + 1) * 512], cps),
                       wait=[("stB3c", stv)], inc="svB3c")
            return P.val("svB3c")

        cb_next = [None]
        pend_ap = [[]]
        for iq in range(NIC):
            sv_cb = cb_emit(0) if iq == 0 else cb_next[0]
            if debug and s == 0:
                P.dma("gpsimd", dbg5[iq], CBSB, "sdbg2", wait=[("svB3c", sv_cb)])
            mmfirst = True
            for e_ in range(NE):
                for d_ in range(NF):
                    idx = e_ * NF + d_
                    ring = idx % 4
                    w = [("stB3r%d" % ring, P.val("stB3r%d" % ring))]
                    if iq == 0 and idx < 4:
                        w = w + wescr_wait + [("stB2o", P.val("stB2o"))]
                    if direct_we:
                        P.dma("gpsimd", WER[ring], wexp_in[e_, d_ * 128:(d_ + 1) * 128, :],
                              "seW%d" % ring, wait=w)
                    else:
                        P.dma("sync", WER[ring], we_scr[e_, d_ * 128:(d_ + 1) * 128, :],
                              "seW%d" % ring, wait=w)
                    sew = P.val("seW%d" % ring)
                    xr = idx % 4
                    wx = [("stB3r%d" % xr, P.val("stB3r%d" % xr))]
                    if iq == 0 and e_ == 0:
                        wx = wx + [("sxfp%d" % d_, sxf_piece[d_])]
                    if idx == 0:
                        wx = wx + [("svB3c", sv_cb)]
                    first_mm_extra = ([("svB3L3", P.val("svB3L3"))] if idx == 0 else [])
                    if nocb:
                        P.emit("vector",
                               lambda e, xr=xr, d_=d_, iq=iq: e.tensor_copy(
                                   XER[xr], X1F[:, d_ * T + iq * ICW: d_ * T + iq * ICW + ICW]),
                               wait=wx, inc="svB3e")
                    else:
                        P.emit("vector",
                               lambda e, xr=xr, d_=d_, e_=e_, iq=iq: e.tensor_tensor(
                                   XER[xr], X1F[:, d_ * T + iq * ICW: d_ * T + iq * ICW + ICW],
                                   CBSB[:, e_ * 512:(e_ + 1) * 512], op=OP.mult),
                               wait=wx)
                        P.emit("vector", lambda e: e.drain(), inc="svB3e")
                    if idx % 8 == 7 and pend_ap[0]:
                        pend_ap[0].pop(0)()
                    svv = P.val("svB3e")
                    fcs = [2, 3, 4, 5, 6, 7, 0, 1]
                    for fi, fc in enumerate(fcs):
                        pbank = (pA if fc < 4 else pB)[:, (fc % 4) * 512:((fc % 4) + 1) * 512]
                        lhs_ap = c_idn if widn else WER[ring][:, fc * 128:(fc + 1) * 128]
                        rhs_ap = (X1F[:, d_ * T + iq * ICW: d_ * T + iq * ICW + ICW]
                                  if xdirect else XER[xr])
                        mmw = None
                        if fi == 0:
                            mmw = [("seW%d" % ring, sew), ("svB3e", svv)]
                        if fc == 0:
                            mmw = (mmw or []) + first_mm_extra
                        P.emit("tensor",
                               lambda e, pbank=pbank, lhs_ap=lhs_ap, rhs_ap=rhs_ap, st=(e_ == 0 and d_ == 0):
                               e.matmul(pbank, lhs_ap, rhs_ap, start=st, stop=False),
                               wait=mmw,
                               inc="stB3r%d" % ring if fi == NF - 1 else None)
            for fc in range(NF):
                pbank = (pA if fc < 4 else pB)[:, (fc % 4) * 512:((fc % 4) + 1) * 512]
                if skipadds:
                    P.emit("tensor",
                           lambda e, pbank=pbank, fc=fc, iq=iq:
                           e.matmul(pbank, c_bexp[:, fc * 128:(fc + 1) * 128],
                                    combT[:, iq * ICW:(iq + 1) * ICW], start=False, stop=True),
                           wait=[("ssB2f", sv_atb)] if fc == 0 else None,
                           inc="stB3m")
                    continue
                P.emit("tensor",
                       lambda e, pbank=pbank, fc=fc, iq=iq:
                       e.matmul(pbank, c_bexp[:, fc * 128:(fc + 1) * 128],
                                combT[:, iq * ICW:(iq + 1) * ICW], start=False, stop=False))
                P.emit("tensor",
                       lambda e, pbank=pbank, fc=fc, iq=iq:
                       e.matmul(pbank, c_idn, ATB[:, fc * T + iq * ICW: fc * T + iq * ICW + ICW],
                                start=False, stop=False),
                       wait=[("ssB2f", sv_atb)] if fc == 0 else None)
                P.emit("tensor",
                       lambda e, pbank=pbank, fc=fc, iq=iq:
                       e.matmul(pbank, c_idn, X1F[:, fc * T + iq * ICW: fc * T + iq * ICW + ICW],
                                start=False, stop=True),
                       wait=([("sxfp%d" % fc, sxf_piece[fc])] if iq == 0 else None),
                       inc="stB3m")
            stv = P.val("stB3m")
            for fc in range(NF):
                pbank = (pA if fc < 4 else pB)[:, (fc % 4) * 512:((fc % 4) + 1) * 512]
                P.emit("vector",
                       lambda e, pbank=pbank, fc=fc: e.tensor_copy(
                           Y2B[:, fc * 512:(fc + 1) * 512], pbank),
                       wait=[("stB3m", stv)] if fc == 0 else None,
                       inc="svB3y" if fc == NF - 1 else None)
            sv_y2 = P.val("svB3y")
            if debug:
                P.dma("gpsimd", dbg4[s].rearrange("(f p) t -> p f t", p=128)[:, :, iq * ICW:(iq + 1) * ICW],
                      Y2B.rearrange("p (f t) -> p f t", f=NF), "sdbg2",
                      wait=[("svB3y", sv_y2)])
            for fc in range(NF):
                P.emit("vector",
                       lambda e, fc=fc: e.tensor_tensor(
                           YSQB[:, fc * 512:(fc + 1) * 512],
                           Y2B[:, fc * 512:(fc + 1) * 512],
                           Y2B[:, fc * 512:(fc + 1) * 512], op=OP.mult),
                       inc="svB3q" if fc == NF - 1 else None)
            sv_q2 = P.val("svB3q")
            for fc in range(NF):
                P.emit("tensor",
                       lambda e, fc=fc: e.matmul(pA[0:1, 0:512], c_ones[:, 0:1],
                                                 Y2B[:, fc * 512:(fc + 1) * 512],
                                                 start=(fc == 0), stop=(fc == NF - 1)),
                       wait=[("svB3y", sv_y2)] if fc == 0 else None)
            for fc in range(NF):
                P.emit("tensor",
                       lambda e, fc=fc: e.matmul(pA[0:1, 512:1024], c_ones[:, 0:1],
                                                 YSQB[:, fc * 512:(fc + 1) * 512],
                                                 start=(fc == 0), stop=(fc == NF - 1)),
                       wait=[("svB3q", sv_q2)] if fc == 0 else None,
                       inc="stB3st" if fc == NF - 1 else None)
            stv = P.val("stB3st")
            if iq + 1 < NIC:
                cb_next[0] = cb_emit(iq + 1)
            P.emit("scalar", lambda e: e.mul(mean_r, pA[0:1, 0:512], 1.0 / D),
                   wait=[("stB3st", stv)])
            P.emit("scalar", lambda e: e.mul(es2_r, pA[0:1, 512:1024], 1.0 / D), inc="ssL")
            ssv = P.val("ssL")
            P.emit("vector", lambda e: e.tensor_tensor(msq_r, mean_r, mean_r, op=OP.mult),
                   wait=[("ssL", ssv)])
            P.emit("vector", lambda e: e.drain())
            P.emit("vector", lambda e: e.tensor_tensor(var_r, es2_r, msq_r, op=OP.subtract),
                   inc="svL")
            svv = P.val("svL")
            P.emit("scalar", lambda e: e.activation(lnv_r, var_r, AF.Ln, bias=eps_r),
                   wait=[("svL", svv)])
            P.emit("scalar", lambda e: e.drain())
            P.emit("scalar", lambda e: e.activation(rstd_r, lnv_r, AF.Exp, scale=-0.5),
                   inc="ssL2")
            ssv = P.val("ssL2")
            P.emit("vector", lambda e: e.tensor_copy(a_bf, rstd_r), wait=[("ssL2", ssv)])
            P.emit("vector", lambda e: e.scalar_tensor_tensor(
                bneg_r, mean_r, -1.0, rstd_r, op0=OP.mult, op1=OP.mult))
            P.emit("vector", lambda e: e.drain())
            P.emit("vector", lambda e: e.tensor_copy(b_bf, bneg_r), inc="svL2")
            svv = P.val("svL2")
            P.emit("tensor", lambda e: e.matmul(pA[:, 0:512], c_ones[0:1, 0:128], a_bf,
                                                start=True, stop=True),
                   wait=[("svL2", svv)])
            P.emit("tensor", lambda e: e.matmul(pA[:, 512:1024], c_ones[0:1, 0:128], b_bf,
                                                start=True, stop=True), inc="stB3b")
            stv = P.val("stB3b")
            P.emit("vector", lambda e: e.tensor_copy(a_sb, pA[:, 0:512]),
                   wait=[("stB3b", stv)])
            P.emit("vector", lambda e: e.tensor_copy(b_sb, pA[:, 512:1024]),
                   inc="svB3L3")
            P.emit("vector", lambda e: e.drain())

            def mk_apply(fc0, iq=iq):
                def go():
                    wout = [("sout", P.val("sout"))] if fc0 == 0 else None
                    for fc in range(fc0, fc0 + 2):
                        P.emit("vector",
                               lambda e, fc=fc: e.tensor_tensor(
                                   T1OUT[:, fc * 512:(fc + 1) * 512],
                                   Y2B[:, fc * 512:(fc + 1) * 512], a_sb, op=OP.mult),
                               wait=wout if fc == fc0 else None)
                    for fc in range(fc0, fc0 + 2):
                        P.emit("vector",
                               lambda e, fc=fc: e.tensor_tensor(
                                   T1OUT[:, fc * 512:(fc + 1) * 512],
                                   T1OUT[:, fc * 512:(fc + 1) * 512],
                                   b_sb, op=OP.add),
                               inc="svB3o" if fc == fc0 + 1 and fc0 == NF - 2 else None)
                    if fc0 == NF - 2:
                        svv = P.val("svB3o")
                        P.dma("sync",
                              out_hbm[s].rearrange("(f p) t -> p f t", p=128)[:, :, iq * ICW:(iq + 1) * ICW],
                              T1OUT.rearrange("p (f t) -> p f t", f=NF),
                              "sout", wait=[("svB3o", svv)])
                return go

            if iq == NIC - 1:
                for fc0 in range(0, NF, 2):
                    mk_apply(fc0)()
                pend_ap[0] = []
            else:
                pend_ap[0] = [mk_apply(fc0) for fc0 in range(0, NF, 2)]

    if debug:
        P.dma("gpsimd", dbg1, x1_scr, "sdbg")
        P.emit("gpsimd", lambda e: None,
               wait=[("sdbg", P.val("sdbg")), ("sdbg2", P.val("sdbg2"))])
    P.emit("sync", lambda e: None, wait=[("sout", P.val("sout"))])

    with nc.Block() as block:
        @block.tensor
        def _(eng):
            for fn in P.q["tensor"]:
                fn(eng)

        @block.vector
        def _(eng):
            for fn in P.q["vector"]:
                fn(eng)

        @block.scalar
        def _(eng):
            for fn in P.q["scalar"]:
                fn(eng)

        @block.gpsimd
        def _(eng):
            for fn in P.q["gpsimd"]:
                fn(eng)

        @block.sync
        def _(eng):
            for fn in P.q["sync"]:
                fn(eng)

    return nc


_NC_CACHE = [None]


def _host_constants():
    pos = np.arange(T, dtype=np.float64)[:, None]
    w = np.arange(32, dtype=np.float64)[None, :]
    ang = np.power(10000.0, -w / 32.0)
    rad = (pos * ang).astype(np.float32)
    cos = np.cos(rad).T.astype(np.float32)
    sin = np.sin(rad).T.astype(np.float32)
    ctab = np.ones((128, T), dtype=np.float32)
    ctab[0:32] = cos
    ctab[32:64] = cos
    stab = np.zeros((128, T), dtype=np.float32)
    stab[0:32] = -sin
    stab[32:64] = sin
    perm = np.zeros((128, 128), dtype=np.float32)
    for i in range(32):
        perm[32 + i, i] = 1.0
        perm[i, 32 + i] = 1.0
    r = np.arange(128)[:, None]
    u = np.arange(896)[None, :]
    stair = (r <= u - 384).astype(np.float32)
    idn = np.eye(128, dtype=np.float32)
    sel = np.zeros((8, 1024), dtype=np.float32)
    for e in range(8):
        sel[e, e * 128:(e + 1) * 128] = 1.0
    return ctab, stab, perm, stair, idn, sel


def kernel(L0_attn_x, L0_attn_wq, L0_attn_wk, L0_attn_wv, L0_attn_wo,
           L0_ffn_W_qkv, L0_ffn_b_qkv, L0_ffn_W_gate, L0_ffn_b_gate,
           L0_ffn_W_experts, L0_ffn_b_experts):
    x = np.ascontiguousarray(np.asarray(L0_attn_x, dtype=np.float32).transpose(0, 2, 1))
    wq = np.ascontiguousarray(np.asarray(L0_attn_wq, dtype=np.float32).reshape(D, D))
    wk = np.ascontiguousarray(np.asarray(L0_attn_wk, dtype=np.float32).reshape(D, D))
    wv = np.ascontiguousarray(np.asarray(L0_attn_wv, dtype=np.float32).reshape(D, D))
    wo = np.ascontiguousarray(np.asarray(L0_attn_wo, dtype=np.float32))
    wqkv = np.ascontiguousarray(np.asarray(L0_ffn_W_qkv, dtype=np.float32))
    bqkv = np.ascontiguousarray(np.asarray(L0_ffn_b_qkv, dtype=np.float32).reshape(1, 3 * D))
    wgate = np.ascontiguousarray(np.asarray(L0_ffn_W_gate, dtype=np.float32))
    bgate = np.ascontiguousarray(np.asarray(L0_ffn_b_gate, dtype=np.float32).reshape(1, NE))
    wexp = np.ascontiguousarray(np.asarray(L0_ffn_W_experts, dtype=np.float32))
    bexp = np.ascontiguousarray(np.asarray(L0_ffn_b_experts, dtype=np.float32))
    ctab, stab, perm, stair, idn, sel = _host_constants()

    if _NC_CACHE[0] is None:
        _NC_CACHE[0] = build_program()
    nc = _NC_CACHE[0]

    in_maps = []
    for c in range(NCORES):
        in_maps.append({
            "x": np.ascontiguousarray(x[c * SEQS:(c + 1) * SEQS]),
            "wq": wq, "wk": wk, "wv": wv, "wo": wo,
            "wqkv": wqkv, "bqkv": bqkv, "wgate": wgate, "bgate": bgate,
            "wexp": wexp, "bexp": bexp,
            "ctab": ctab, "stab": stab, "perm": perm, "stair": stair, "idn": idn, "sel": sel,
        })
    res = run_bass_kernel_spmd(nc, in_maps, core_ids=list(range(NCORES)))
    outs = [r["out"] for r in res.results]
    full = np.concatenate(outs, axis=0)
    return np.ascontiguousarray(full.transpose(0, 2, 1))
